# version 06
# baseline (speedup 1.0000x reference)
import numpy as np
import ml_dtypes
import concourse.bass as bass
import concourse.mybir as mybir
from concourse.bass_utils import run_bass_kernel_spmd
from contextlib import ExitStack

F32 = mybir.dt.float32
BF16 = mybir.dt.bfloat16
ALU = mybir.AluOpType
AF = mybir.ActivationFunctionType
AX = mybir.AxisListType

NCORES = 8
T = 16384
TL = T // NCORES
D = 2048
NH = 16
HD = 128
CT = 128
NCH = TL // CT
CGI = 30
E = 8
IM = 1408
ISH = 5632
NT = TL // 128
NDT = D // 128
LAMB_LB = 0.25

BF = ml_dtypes.bfloat16
DO_ATTN = True


class Ctx:
    pass


def _emit_common(ctx):
    nc = ctx.nc
    ee = ctx.stack.enter_context
    ctx.sems = {}
    ctx.semcnt = {}
    ctx.semgen = [0]

    def sem(name):
        if name not in ctx.sems:
            ctx.sems[name] = ee(nc.semaphore(f"{name}_g{ctx.semgen[0]}"))
            ctx.semcnt[name] = 0
        return ctx.sems[name]

    def inc(instr, name, k=1):
        instr.then_inc(sem(name), k)
        ctx.semcnt[name] += k
        return ctx.semcnt[name]

    def wait(eng, name, val=None):
        s = sem(name)
        eng.wait_ge(s, ctx.semcnt[name] if val is None else val)

    def cnt(name):
        sem(name)
        return ctx.semcnt[name]

    def barrier():
        nc.all_engine_barrier(sem_only=True)

    def flush_sems():
        names = list(ctx.sems)
        if not names:
            return
        for n in names:
            nc.gpsimd.wait_ge(ctx.sems[n], ctx.semcnt[n])
        barrier()
        nc.clear_and_free_semaphores([ctx.sems[n] for n in names])
        barrier()
        for n in names:
            del ctx.sems[n]
            del ctx.semcnt[n]
        ctx.semgen[0] += 1
    ctx.flush_sems = flush_sems

    _ENG_NAME = {}

    def edge(eng, instr):
        nm = "selfedge_" + str(id(eng) if eng not in _ENG_NAME else _ENG_NAME[eng])
        nm = "se_" + type(eng).__name__
        inc(instr, nm, 1)
        wait(eng, nm)
        return instr

    def dma(eng, name, out, in_, **kw):
        wait(eng, name)
        d = eng.dma_start(out=out, in_=in_, **kw)
        inc(d, name, 16)
        return d
    ctx.dma = dma

    ctx.sem, ctx.inc, ctx.wait, ctx.cnt, ctx.barrier, ctx.edge = sem, inc, wait, cnt, barrier, edge


def build_program(do_attn=DO_ATTN, taps=(), stages=99):
    nc = bass.Bass("TRN2", target_bir_lowering=False)
    ctx = Ctx()
    ctx.nc = nc
    ctx.stack = ExitStack()
    ctx.stack.__enter__()
    _emit_common(ctx)
    ee = ctx.stack.enter_context

    id_bf_d = nc.inline_tensor(np.eye(128, dtype=BF), "id_bf_d").ap()
    id_f32_d = nc.inline_tensor(np.eye(128, dtype=np.float32), "id_f32_d").ap()
    id16_f32_d = nc.inline_tensor(np.eye(16, dtype=np.float32), "id16_f32_d").ap()
    tri = (np.arange(128)[:, None] <= np.arange(128)[None, :]).astype(np.float32)
    tri_f32_d = nc.inline_tensor(tri, "tri_f32_d").ap()
    ones_f32_d = nc.inline_tensor(np.ones((1, 128), dtype=np.float32), "ones_f32_d").ap()
    onescol_f32_d = nc.inline_tensor(np.ones((128, 1), dtype=np.float32), "onescol_f32_d").ap()
    onesmat_bf_d = nc.inline_tensor(np.ones((128, 128), dtype=BF), "onesmat_bf_d").ap()
    ctx.id_bf = ee(nc.sbuf_tensor("id_bf", [128, 128], BF16))[:]
    ctx.id_f32 = ee(nc.sbuf_tensor("id_f32", [128, 128], F32))[:]
    ctx.id16_f32 = ee(nc.sbuf_tensor("id16_f32", [16, 16], F32))[:]
    ctx.tri_f32 = ee(nc.sbuf_tensor("tri_f32", [128, 128], F32))[:]
    ctx.ones_f32 = ee(nc.sbuf_tensor("ones_f32", [1, 128], F32))[:]
    ctx.onescol_f32 = ee(nc.sbuf_tensor("onescol_f32", [128, 1], F32))[:]
    ctx.onesmat_bf = ee(nc.sbuf_tensor("onesmat_bf", [128, 128], BF16))[:]

    di = {}
    di["x"] = nc.dram_tensor("x", [TL, D], F32, kind="ExternalInput").ap()
    di["gate_up"] = nc.dram_tensor("gate_up", [E, 2 * IM, D], F32, kind="ExternalInput").ap()
    di["down"] = nc.dram_tensor("down", [E, D, IM], F32, kind="ExternalInput").ap()
    di["sh_gate"] = nc.dram_tensor("sh_gate", [ISH, D], F32, kind="ExternalInput").ap()
    di["sh_up"] = nc.dram_tensor("sh_up", [ISH, D], F32, kind="ExternalInput").ap()
    di["sh_down"] = nc.dram_tensor("sh_down", [D, ISH], F32, kind="ExternalInput").ap()
    di["shared_gate"] = nc.dram_tensor("shared_gate", [1, D], F32, kind="ExternalInput").ap()
    di["router"] = nc.dram_tensor("router", [E, D], F32, kind="ExternalInput").ap()
    di["Wq"] = nc.dram_tensor("Wq", [D, D], F32, kind="ExternalInput").ap()
    di["Wk"] = nc.dram_tensor("Wk", [D, D], F32, kind="ExternalInput").ap()
    di["Wv"] = nc.dram_tensor("Wv", [D, D], F32, kind="ExternalInput").ap()
    di["Wo"] = nc.dram_tensor("Wo", [D, D], F32, kind="ExternalInput").ap()
    di["Wa"] = nc.dram_tensor("Wa", [D, NH], F32, kind="ExternalInput").ap()
    di["Wb"] = nc.dram_tensor("Wb", [D, NH], F32, kind="ExternalInput").ap()
    di["ba"] = nc.dram_tensor("ba", [1, NH], F32, kind="ExternalInput").ap()
    di["bb"] = nc.dram_tensor("bb", [1, NH], F32, kind="ExternalInput").ap()
    di["lamb"] = nc.dram_tensor("lamb", [NH, HD], F32, kind="ExternalInput").ap()
    di["onw"] = nc.dram_tensor("onw", [HD, 1], F32, kind="ExternalInput").ap()
    ctx.di = di

    ctx.y = nc.dram_tensor("y", [TL, D], F32, kind="ExternalOutput").ap()

    ds = {}
    ds["gu_bf"] = nc.dram_tensor("gu_bf", [E, 2 * IM, D], BF16).ap()
    ds["down_bf"] = nc.dram_tensor("down_bf", [E, D, IM], BF16).ap()
    ds["shg_bf"] = nc.dram_tensor("shg_bf", [ISH, D], BF16).ap()
    ds["shu_bf"] = nc.dram_tensor("shu_bf", [ISH, D], BF16).ap()
    ds["shd_bf"] = nc.dram_tensor("shd_bf", [D, ISH], BF16).ap()
    ds["acc"] = nc.dram_tensor("acc", [D, TL], F32).ap()
    ds["x1"] = nc.dram_tensor("x1", [TL, D], F32).ap()
    ds["x2"] = nc.dram_tensor("x2", [TL, D], F32).ap()
    ds["crow"] = nc.dram_tensor("crow", [1, 9 * TL], F32).ap()
    ds["bq"] = nc.dram_tensor("bq", [TL, NH * HD], BF16).ap()
    ds["knat"] = nc.dram_tensor("knat_s", [TL, NH * HD], BF16).ap()
    ds["vnat"] = nc.dram_tensor("vnat_s", [TL, NH * HD], BF16).ap()
    ds["knT"] = nc.dram_tensor("knT_s", [NH * HD, TL], BF16).ap()
    ds["vT"] = nc.dram_tensor("vT_s", [NH * HD, TL], BF16).ap()
    ds["masks"] = nc.dram_tensor("masks_s", [NCH, NH, CT, CT], BF16).ap()
    ds["partials"] = nc.dram_tensor("partials_s", [NCH, 2, HD, NH * HD], F32).ap()
    ds["lpref"] = nc.dram_tensor("lpref_s", [NCH, 2, HD, NH * HD], F32).ap()
    ds["h0"] = nc.dram_tensor("h0_s", [NCH, 2, HD, NH * HD], BF16).ap()
    ds["cc_in"] = nc.dram_tensor("cc_in", [2 * HD + 1, NH * HD], F32).ap()
    ds["cc_out"] = nc.dram_tensor("cc_out", [NCORES, 2 * HD + 1, NH * HD], F32).ap()
    ds["prefixes"] = nc.dram_tensor("prefixes_s", [NCORES, 2 * HD, NH * HD], F32).ap()
    ctx.ds = ds

    ctx.taps = {}
    for name in taps:
        if name in ("ln1T", "ln3T"):
            ctx.taps[name] = nc.dram_tensor("tap_" + name, [128, NDT * TL], BF16, kind="ExternalOutput").ap()
        elif name == "comb":
            ctx.taps[name] = nc.dram_tensor("tap_" + name, [1, 9 * TL], F32, kind="ExternalOutput").ap()
        elif name in ("x1",):
            ctx.taps[name] = nc.dram_tensor("tap_" + name, [TL, D], F32, kind="ExternalOutput").ap()
        elif name == "acc1":
            ctx.taps[name] = nc.dram_tensor("tap_" + name, [D, TL], F32, kind="ExternalOutput").ap()
        elif name == "act":
            ctx.taps[name] = nc.dram_tensor("tap_" + name, [128, 11 * TL], BF16, kind="ExternalOutput").ap()

    ctx.lnT = ee(nc.sbuf_tensor("lnT", [128, NDT * TL], BF16))
    ctx.rt9 = ee(nc.sbuf_tensor("rt9", [128, NDT * 9], F32))
    ctx.tabs = ee(nc.sbuf_tensor("tabs", [128, 32], F32))

    for dstc, srcc in ((ctx.id_bf, id_bf_d), (ctx.id_f32, id_f32_d), (ctx.id16_f32, id16_f32_d),
                       (ctx.tri_f32, tri_f32_d), (ctx.ones_f32, ones_f32_d),
                       (ctx.onescol_f32, onescol_f32_d), (ctx.onesmat_bf, onesmat_bf_d)):
        ctx.dma(nc.sync, "cst", dstc, srcc)
    for eng in (nc.tensor, nc.vector, nc.scalar, nc.gpsimd, nc.sync):
        ctx.wait(eng, "cst")
    _weights_prologue(ctx)
    _build_router_T(ctx)
    _build_small_tables(ctx)
    ctx.wait(nc.gpsimd, "wcast")
    ctx.barrier()

    if stages < 2:
        ctx.stack.__exit__(None, None, None)
        return nc
    ctx.flush_sems()
    _ln_router_stage(ctx, src=di["x"], tag="L1", tap="ln1T")
    ctx.barrier()
    if "comb" in ctx.taps:
        ctx.inc(nc.gpsimd.dma_start(out=ctx.taps["comb"], in_=ds["crow"]), "tapc", 16)
        ctx.wait(nc.gpsimd, "tapc")
    ctx.barrier()
    if stages < 3:
        ctx.stack.__exit__(None, None, None)
        return nc
    ctx.flush_sems()
    _ffn_experts(ctx, tag="F1")
    ctx.barrier()
    if "acc1" in ctx.taps:
        ctx.inc(nc.gpsimd.dma_start(out=ctx.taps["acc1"], in_=ds["acc"]), "tapa", 16)
        ctx.wait(nc.gpsimd, "tapa")
    if stages < 4:
        ctx.stack.__exit__(None, None, None)
        return nc
    ctx.flush_sems()
    _residual_stage(ctx, src=di["x"], dst=ds["x1"], tag="R1", tap="x1")
    ctx.barrier()

    if stages < 5:
        ctx.stack.__exit__(None, None, None)
        return nc
    if do_attn:
        _attention(ctx)
    else:
        ctx.inc(nc.gpsimd.dma_start(out=ds["x2"], in_=ds["x1"]), "xcp", 16)
        ctx.wait(nc.gpsimd, "xcp")
    ctx.barrier()

    if stages >= 6:
        ctx.flush_sems()
        _ln_router_stage(ctx, src=ds["x2"], tag="L3", tap="ln3T")
        ctx.barrier()
    if stages >= 7:
        ctx.flush_sems()
        _ffn_experts(ctx, tag="F2")
        ctx.barrier()
    if stages >= 8:
        ctx.flush_sems()
        _residual_stage(ctx, src=ds["x2"], dst=ctx.y, tag="R2", tap=None)
        ctx.barrier()

    ctx.stack.__exit__(None, None, None)
    return nc


def _weights_prologue(ctx):
    nc = ctx.nc
    g = nc.gpsimd
    di, ds = ctx.di, ctx.ds
    for e in range(E):
        ctx.dma(g, "wcast", ds["gu_bf"][e], di["gate_up"][e])
        ctx.dma(g, "wcast", ds["down_bf"][e], di["down"][e])
    ctx.dma(g, "wcast", ds["shg_bf"], di["sh_gate"])
    ctx.dma(g, "wcast", ds["shu_bf"], di["sh_up"])
    ctx.dma(g, "wcast", ds["shd_bf"], di["sh_down"])


def _build_router_T(ctx):
    nc = ctx.nc
    with ExitStack() as es:
        stage = es.enter_context(nc.sbuf_tensor("rt_stage", [16, D], F32))
        ps = es.enter_context(nc.psum_tensor("ps_rt", [128, 16], F32))
        ctx.dma(nc.sync, "rtld", stage[0:E, :], ctx.di["router"])
        ctx.dma(nc.sync, "rtld", stage[E:E + 1, :], ctx.di["shared_gate"])
        ctx.wait(nc.tensor, "rtld")
        for dt in range(NDT):
            if dt >= 1:
                ctx.wait(nc.tensor, "rtev", dt)
            mm = nc.tensor.matmul(out=ps[:, 0:9], lhsT=stage[0:9, dt * 128:(dt + 1) * 128],
                                  rhs=ctx.id16_f32[0:9, 0:9], start=True, stop=True)
            ctx.inc(mm, "rtmm", 1)
            ctx.wait(nc.vector, "rtmm")
            ctx.inc(nc.vector.tensor_copy(out=ctx.rt9[:, dt * 9:(dt + 1) * 9], in_=ps[:, 0:9]), "rtev", 1)
        ctx.wait(nc.tensor, "rtev")
        ctx.wait(nc.sync, "rtev")


def _build_small_tables(ctx):
    nc = ctx.nc
    tabs = ctx.tabs
    with ExitStack() as es:
        lp = es.enter_context(nc.sbuf_tensor("lp_stage", [NH, HD], F32))
        t1 = es.enter_context(nc.sbuf_tensor("lp_t1", [NH, HD], F32))
        sp = es.enter_context(nc.sbuf_tensor("lp_sp", [NH, HD], F32))
        ps = es.enter_context(nc.psum_tensor("ps_lp", [128, 16], F32))
        ctx.dma(nc.sync, "lpld", lp[:], ctx.di["lamb"])
        ctx.dma(nc.sync, "lpld", tabs[0:HD, 16:17], ctx.di["onw"])
        v, s = nc.vector, nc.scalar
        ctx.wait(s, "lpld")
        ctx.inc(s.activation(out=t1[:], in_=lp[:], func=AF.Abs), "lps", 1)
        ctx.wait(v, "lps")
        ctx.inc(v.tensor_scalar(out=t1[:], in0=t1[:], scalar1=-1.0, scalar2=None, op0=ALU.mult), "lpv", 1)
        ctx.wait(s, "lpv")
        ctx.inc(s.activation(out=t1[:], in_=t1[:], func=AF.Exp), "lps", 1)
        ctx.wait(v, "lps")
        ctx.inc(v.tensor_scalar(out=t1[:], in0=t1[:], scalar1=1.0, scalar2=None, op0=ALU.add), "lpv", 1)
        ctx.wait(s, "lpv")
        ctx.inc(s.activation(out=t1[:], in_=t1[:], func=AF.Ln), "lps", 1)
        ctx.wait(v, "lps")
        ctx.edge(v, v.tensor_scalar(out=sp[:], in0=lp[:], scalar1=0.0, scalar2=None, op0=ALU.max))
        ctx.edge(v, v.tensor_tensor(out=sp[:], in0=sp[:], in1=t1[:], op=ALU.add))
        ctx.inc(v.tensor_scalar(out=sp[:], in0=sp[:], scalar1=LAMB_LB, scalar2=None, op0=ALU.add), "lpv", 1)
        ctx.wait(nc.tensor, "lpv")
        ctx.inc(nc.tensor.matmul(out=ps[:, 0:16], lhsT=sp[:], rhs=ctx.id16_f32, start=True, stop=True), "lpmm", 1)
        ctx.wait(v, "lpmm")
        ctx.inc(v.tensor_copy(out=tabs[:, 0:16], in_=ps[:, 0:16]), "lpev", 1)
        ctx.wait(nc.tensor, "lpev")
        ctx.wait(nc.sync, "lpev")


def _ln_router_stage(ctx, src, tag, tap, do_router=True):
    import os
    RMODE = int(os.environ.get("RMODE", "2")) if do_router else 0
    TMODE = int(os.environ.get("TMODE", "2"))
    LTILES = int(os.environ.get("LTILES", str(NT)))
    """LN(src) -> ctx.lnT (bf16 feature-major [:, dt*TL+tok]); fp32 router -> ctx.crow
    (cols c*TL+tok: c 0..7 = 0.5*combine_c, c 8 = 0.5*sigmoid(shared))."""
    nc = ctx.nc
    lnT = ctx.lnT
    crow = ctx.ds["crow"]
    tg = tag
    v, s = nc.vector, nc.scalar
    with ExitStack() as es:
        xs = [es.enter_context(nc.sbuf_tensor(f"{tg}_x{i}", [128, D], F32)) for i in range(2)]
        lntok = es.enter_context(nc.sbuf_tensor(tg + "_lntok", [128, D], F32))
        lnf = [es.enter_context(nc.sbuf_tensor(f"{tg}_lnf{i}", [128, 128], F32)) for i in range(2)]
        st = es.enter_context(nc.sbuf_tensor(tg + "_st", [128, 8], F32))
        pr = es.enter_context(nc.sbuf_tensor(tg + "_pr", [128, 16], F32))
        q2 = es.enter_context(nc.sbuf_tensor(tg + "_q2", [128, 16], F32))
        comb = es.enter_context(nc.sbuf_tensor(tg + "_comb", [128, 16], F32))
        ps_t = [es.enter_context(nc.psum_tensor(f"{tg}_ps_t{i}", [128, 128], F32)) for i in range(2)]
        ps_lg = es.enter_context(nc.psum_tensor(tg + "_ps_lg", [128, 16], F32))
        ps_cr = es.enter_context(nc.psum_tensor(tg + "_ps_cr", [1, 128], F32))
        crs = es.enter_context(nc.sbuf_tensor(tg + "_crs", [1, 128], F32))

        for tt in range(LTILES):
            buf = xs[tt % 2]
            if tt >= 2:
                ctx.wait(nc.sync, tg + "app", tt - 1)
            ctx.dma(nc.sync, tg + "x%d" % (tt % 2), buf[:], src[tt * 128:(tt + 1) * 128, :])
            if tt >= 1:
                ctx.wait(s, tg + "t", tt * NDT)
            ctx.wait(s, tg + "x%d" % (tt % 2))
            a1 = s.activation(out=lntok[:], in_=buf[:], func=AF.Copy, accum_out=st[:, 0:1])
            ctx.inc(a1, tg + "s", 1)
            ctx.wait(s, tg + "s")
            ctx.inc(s.activation(out=lntok[:], in_=buf[:], func=AF.Square, accum_out=st[:, 1:2]), tg + "s", 1)
            ctx.wait(v, tg + "s")
            v.tensor_scalar(out=st[:, 2:3], in0=st[:, 0:1], scalar1=1.0 / D, scalar2=None, op0=ALU.mult)
            ctx.edge(v, v.tensor_scalar(out=st[:, 3:4], in0=st[:, 1:2], scalar1=1.0 / D, scalar2=None, op0=ALU.mult))
            ctx.edge(v, v.scalar_tensor_tensor(out=st[:, 4:5], in0=st[:, 2:3], scalar=st[:, 2:3], in1=st[:, 3:4],
                                               op0=ALU.mult, op1=ALU.subtract))
            ctx.inc(v.tensor_scalar(out=st[:, 4:5], in0=st[:, 4:5], scalar1=-1.0, scalar2=1e-5,
                                    op0=ALU.mult, op1=ALU.add), tg + "v", 1)
            ctx.wait(s, tg + "v")
            ctx.inc(s.activation(out=st[:, 5:6], in_=st[:, 4:5], func=AF.Sqrt), tg + "s", 1)
            if TMODE < 1:
                ctx.inc(v.nop(), tg + "app", 1)
                continue
            ctx.wait(v, tg + "s")
            v.reciprocal(out=st[:, 6:7], in_=st[:, 5:6])
            ctx.edge(v, v.tensor_scalar(out=st[:, 7:8], in0=st[:, 2:3], scalar1=-1.0, scalar2=None, op0=ALU.mult))
            app = v.tensor_scalar(out=lntok[:], in0=buf[:], scalar1=st[:, 7:8], scalar2=st[:, 6:7],
                                  op0=ALU.add, op1=ALU.mult)
            ctx.inc(app, tg + "app", 1)
            ctx.wait(nc.tensor, tg + "app")
            if TMODE < 2:
                continue
            for dt in range(NDT):
                gidx = tt * NDT + dt
                half = dt % 2
                if gidx >= 2:
                    ctx.wait(nc.tensor, tg + "ev", (gidx - 1) * 2)
                mm = nc.tensor.matmul(out=ps_t[half][:],
                                      lhsT=lntok[:, dt * 128:(dt + 1) * 128],
                                      rhs=ctx.id_f32, start=True, stop=True)
                ctx.inc(mm, tg + "t", 1)
                if RMODE >= 1 and gidx >= 2:
                    ctx.wait(v, tg + "lgm", gidx - 1)
                ctx.wait(v, tg + "t")
                c1 = v.tensor_copy(out=lnf[half][:], in_=ps_t[half][:])
                ctx.inc(c1, tg + "ev", 1)
                ctx.wait(v, tg + "ev")
                ctx.inc(v.tensor_copy(out=lnT[:, dt * TL + tt * 128: dt * TL + (tt + 1) * 128],
                                      in_=lnf[half][:]), tg + "ev", 1)
                if RMODE >= 1:
                    if dt == 0 and tt >= 1:
                        ctx.wait(nc.tensor, tg + "prc", tt)
                    ctx.wait(nc.tensor, tg + "ev")
                    mm2 = nc.tensor.matmul(out=ps_lg[:, 0:9], lhsT=lnf[half][:],
                                           rhs=ctx.rt9[:, dt * 9:(dt + 1) * 9],
                                           start=(dt == 0), stop=(dt == NDT - 1))
                    ctx.inc(mm2, tg + "lgm", 1)
            if RMODE < 1:
                continue
            if tt >= 1:
                ctx.wait(v, tg + "crmm", tt * 9)
            ctx.wait(v, tg + "lgm")
            ctx.wait(v, tg + "sg")
            ctx.inc(v.tensor_copy(out=pr[:, 0:9], in_=ps_lg[:, 0:9]), tg + "prc", 1)
            ctx.wait(s, tg + "prc")
            ctx.inc(s.activation(out=pr[:, 9:10], in_=pr[:, 8:9], func=AF.Sigmoid), tg + "sg", 1)
            ctx.wait(v, tg + "prc")
            ctx.edge(v, v.reduce_max(out=q2[:, 0:1], in_=pr[:, 0:8], axis=AX.X))
            ctx.inc(v.tensor_scalar(out=q2[:, 1:2], in0=q2[:, 0:1], scalar1=-1.0, scalar2=None, op0=ALU.mult),
                    tg + "pv", 1)
            ctx.wait(s, tg + "pv")
            ctx.inc(s.activation(out=pr[:, 0:8], in_=pr[:, 0:8], func=AF.Exp, bias=q2[:, 1:2]), tg + "sg", 1)
            ctx.wait(v, tg + "sg")
            ctx.edge(v, v.tensor_scalar(out=pr[:, 0:8], in0=pr[:, 0:8], scalar1=1.0, scalar2=0.0,
                                         op0=ALU.mult, op1=ALU.add, accum_out=q2[:, 2:3]))
            ctx.edge(v, v.reciprocal(out=q2[:, 3:4], in_=q2[:, 2:3]))
            ctx.edge(v, v.tensor_scalar(out=pr[:, 0:8], in0=pr[:, 0:8], scalar1=q2[:, 3:4], scalar2=None, op0=ALU.mult))
            ctx.edge(v, v.reduce_max(out=q2[:, 4:5], in_=pr[:, 0:8], axis=AX.X))
            ctx.edge(v, v.tensor_scalar(out=q2[:, 8:16], in0=pr[:, 0:8], scalar1=q2[:, 4:5], scalar2=None, op0=ALU.is_equal))
            ctx.edge(v, v.scalar_tensor_tensor(out=comb[:, 8:16], in0=q2[:, 8:16], scalar=-1e30, in1=pr[:, 0:8],
                                               op0=ALU.mult, op1=ALU.add))
            ctx.edge(v, v.reduce_max(out=q2[:, 5:6], in_=comb[:, 8:16], axis=AX.X))
            v.tensor_scalar(out=comb[:, 0:8], in0=comb[:, 8:16], scalar1=q2[:, 5:6], scalar2=None, op0=ALU.is_equal)
            ctx.edge(v, v.tensor_tensor(out=q2[:, 6:7], in0=q2[:, 4:5], in1=q2[:, 5:6], op=ALU.add))
            ctx.edge(v, v.reciprocal(out=q2[:, 7:8], in_=q2[:, 6:7]))
            ctx.edge(v, v.tensor_tensor(out=comb[:, 0:8], in0=comb[:, 0:8], in1=q2[:, 8:16], op=ALU.add))
            ctx.edge(v, v.tensor_tensor(out=comb[:, 0:8], in0=comb[:, 0:8], in1=pr[:, 0:8], op=ALU.mult))
            ctx.edge(v, v.tensor_scalar(out=comb[:, 0:8], in0=comb[:, 0:8], scalar1=q2[:, 7:8], scalar2=0.5,
                                        op0=ALU.mult, op1=ALU.mult))
            ctx.wait(v, tg + "sg")
            ctx.inc(v.tensor_scalar(out=comb[:, 8:9], in0=pr[:, 9:10], scalar1=0.5, scalar2=None, op0=ALU.mult),
                    tg + "cmb", 1)
            if RMODE < 2:
                continue
            ctx.wait(nc.tensor, tg + "cmb")
            for c in range(9):
                cidx = tt * 9 + c
                if cidx >= 1:
                    ctx.wait(nc.tensor, tg + "crev", cidx)
                mmc = nc.tensor.matmul(out=ps_cr[0:1, :], lhsT=comb[:, c:c + 1], rhs=ctx.id_f32,
                                       start=True, stop=True)
                ctx.inc(mmc, tg + "crmm", 1)
                if cidx >= 1:
                    ctx.wait(v, tg + "crst", 16 * cidx)
                ctx.wait(v, tg + "crmm")
                ctx.inc(v.tensor_copy(out=crs[0:1, :], in_=ps_cr[0:1, :]), tg + "crev", 1)
                ctx.wait(nc.sync, tg + "crev")
                ctx.inc(nc.sync.dma_start(out=crow[0:1, c * TL + tt * 128: c * TL + (tt + 1) * 128],
                                          in_=crs[0:1, :]), tg + "crst", 16)
        ctx.wait(nc.sync, tg + "ev")
        if RMODE >= 2:
            ctx.wait(nc.sync, tg + "crst")
    if tap in ctx.taps:
        ctx.inc(nc.sync.dma_start(out=ctx.taps[tap], in_=lnT[:]), tg + "tap", 16)
        ctx.wait(nc.sync, tg + "tap")


def _ffn_experts(ctx, tag):
    nc = ctx.nc
    tg = tag
    lnT, ds = ctx.lnT, ctx.ds
    crow = ds["crow"]
    v, s, g = nc.vector, nc.scalar, nc.gpsimd
    import os
    NPE = int(os.environ.get("NPE", "0"))
    NHT = 11
    pes = []
    for j in range(4):
        gblocks = []
        for srcname in ("shg_bf", "shu_bf"):
            base = j * IM
            for off, nt_ in ((0, 4), (512, 4), (1024, 3)):
                gblocks.append((ds[srcname], base + off, nt_))
        pes.append((gblocks, ds["shd_bf"], j * IM, 8))
    for e in range(E):
        gblocks = []
        for off, nt_ in ((0, 4), (512, 4), (1024, 3), (IM, 4), (IM + 512, 4), (IM + 1024, 3)):
            gblocks.append((ds["gu_bf"][e], off, nt_))
        pes.append((gblocks, ds["down_bf"][e], 0, e))
    if NPE:
        pes = pes[:NPE]

    with ExitStack() as es:
        guT = [es.enter_context(nc.sbuf_tensor(f"{tg}_guT{i}", [128, 16 * 512], BF16)) for i in range(2)]
        dnT = [es.enter_context(nc.sbuf_tensor(f"{tg}_dnT{i}", [128, NHT * 1024], BF16)) for i in range(2)]
        crow_sb = es.enter_context(nc.sbuf_tensor(tg + "_crow_sb", [1, TL], F32))
        act = es.enter_context(nc.sbuf_tensor(tg + "_act", [128, NHT * TL], BF16))
        cbc = es.enter_context(nc.sbuf_tensor(tg + "_cbc", [128, TL], BF16))
        stg = [es.enter_context(nc.sbuf_tensor(f"{tg}_stg{i}", [128, 512], F32)) for i in range(2)]
        ps_gu = [es.enter_context(nc.psum_tensor(f"{tg}_psgu{i}", [128, 512], F32)) for i in range(2)]
        ps_o = [es.enter_context(nc.psum_tensor(f"{tg}_pso{i}", [128, 512], F32)) for i in range(2)]

        gu_slot_ev = [None, None]
        o_slot_ev = [0, 0]
        acq_cnt = [0, 0]
        gud_cnt = [0, 0]
        dnd_cnt = [0, 0]
        blk_ev_need = [0, 0]

        for p, (gblocks, dn_src, dn_coloff, c_idx) in enumerate(pes):
            if p >= 1:
                ctx.wait(s, tg + "om", p * 64)
                ctx.wait(v, tg + "om", p * 64)
            if p >= 1:
                ctx.wait(nc.gpsimd, tg + "cbm", p * 4)
            ctx.inc(nc.gpsimd.dma_start(out=crow_sb[0:1, :], in_=crow[0:1, c_idx * TL:(c_idx + 1) * TL]),
                    tg + "crl", 16)
            ctx.wait(nc.tensor, tg + "crl", 16 * (p + 1))
            if gu_slot_ev[0] is not None:
                ctx.wait(nc.tensor, gu_slot_ev[0][0], gu_slot_ev[0][1])
            ctx.wait(s, tg + "guevV")
            for hh in range(4):
                mcb = nc.tensor.matmul(out=ps_gu[0][:], lhsT=ctx.ones_f32,
                                       rhs=crow_sb[0:1, hh * 512:(hh + 1) * 512],
                                       start=True, stop=True)
                ctx.inc(mcb, tg + "cbm", 1)
                ctx.wait(s, tg + "cbm")
                ctx.inc(s.activation(out=cbc[:, hh * 512:(hh + 1) * 512], in_=ps_gu[0][:], func=AF.Copy),
                        tg + "cbev", 1)
                ctx.wait(nc.tensor, tg + "cbev")


            ht_base = 0
            blki = 0
            ht_global = 0
            for (gsrc, row_off, ntile) in gblocks:
                par = blki % 2
                if os.environ.get("GUSER", "0") == "1":
                    ctx.wait(nc.sync, tg + "gum", ctx.cnt(tg + "gum"))
                else:
                    ctx.wait(nc.sync, tg + "gum", blk_ev_need[par])
                for dt in range(NDT):
                    ctx.dma(nc.sync, tg + "gud%d" % par,
                            guT[par][:, dt * 512: dt * 512 + ntile * 128],
                            gsrc[row_off:row_off + ntile * 128, dt * 128:(dt + 1) * 128],
                            transpose=True)
                gud_cnt[par] = ctx.cnt(tg + "gud%d" % par)
                ctx.wait(nc.tensor, tg + "gud%d" % par, gud_cnt[par])
                for htb in range(ntile):
                    ht = ht_global
                    ht_global += 1
                    for tq in range(4):
                        slot = (ht * 4 + tq) % 2
                        if gu_slot_ev[slot] is not None:
                            ctx.wait(nc.tensor, gu_slot_ev[slot][0], gu_slot_ev[slot][1])
                        for dt in range(NDT):
                            mmg = nc.tensor.matmul(out=ps_gu[slot][:],
                                                   lhsT=guT[par][:, dt * 512 + htb * 128: dt * 512 + (htb + 1) * 128],
                                                   rhs=lnT[:, dt * TL + tq * 512: dt * TL + (tq + 1) * 512],
                                                   start=(dt == 0), stop=(dt == NDT - 1))
                        gum = ctx.inc(mmg, tg + "gum", 1)
                        if ht < NHT:
                            ctx.wait(s, tg + "gum", gum)
                            ev = s.activation(out=act[:, ht * TL + tq * 512: ht * TL + (tq + 1) * 512],
                                              in_=ps_gu[slot][:], func=AF.Silu)
                            gu_slot_ev[slot] = (tg + "guevS", ctx.inc(ev, tg + "guevS", 1))
                        else:
                            jslot = ht - NHT
                            asl = act[:, jslot * TL + tq * 512: jslot * TL + (tq + 1) * 512]
                            ctx.wait(v, tg + "gum", gum)
                            ctx.edge(v, v.tensor_tensor(out=asl, in0=asl, in1=cbc[:, tq * 512:(tq + 1) * 512],
                                                        op=ALU.mult))
                            ev = v.tensor_tensor(out=asl, in0=asl, in1=ps_gu[slot][:], op=ALU.mult)
                            gu_slot_ev[slot] = (tg + "guevV", ctx.inc(ev, tg + "guevV", 1))
                blki += 1
                if blki >= 2:
                    pass
                blk_ev_need[par] = ctx.cnt(tg + "gum")

            if os.environ.get("NODND", "0") == "1":
                continue
            ctx.wait(nc.tensor, tg + "guevS")
            ctx.wait(nc.tensor, tg + "guevV")
            for dh in range(2):
                dpar = dh % 2
                if p * 2 + dh >= 2:
                    ctx.wait(nc.sync, tg + "om", (p * 2 + dh - 1) * 32)
                for ht in range(NHT):
                    ctx.dma(nc.sync, tg + "dnd%d" % dpar,
                            dnT[dpar][:, ht * 1024:(ht + 1) * 1024],
                            dn_src[dh * 1024:(dh + 1) * 1024,
                                   dn_coloff + ht * 128: dn_coloff + (ht + 1) * 128],
                            transpose=True)
                dnd_cnt[dpar] = ctx.cnt(tg + "dnd%d" % dpar)
                ctx.wait(nc.tensor, tg + "dnd%d" % dpar, dnd_cnt[dpar])
                for dto in range(8):
                    dt = dh * 8 + dto
                    for tq in range(4):
                        kk = p * 64 + dt * 4 + tq
                        slot = kk % 2
                        ctx.wait(nc.tensor, tg + "oev", o_slot_ev[slot])
                        for ht in range(NHT):
                            mmo = nc.tensor.matmul(out=ps_o[slot][:],
                                                   lhsT=dnT[dpar][:, ht * 1024 + dto * 128: ht * 1024 + (dto + 1) * 128],
                                                   rhs=act[:, ht * TL + tq * 512: ht * TL + (tq + 1) * 512],
                                                   start=(ht == 0), stop=(ht == NHT - 1))
                        om = ctx.inc(mmo, tg + "om", 1)
                        ctx.wait(v, tg + "om", om)
                        ctx.wait(v, tg + "acq%d" % slot, acq_cnt[slot])
                        cp = v.tensor_copy(out=stg[slot][:], in_=ps_o[slot][:])
                        o_slot_ev[slot] = ctx.inc(cp, tg + "oev", 1)
                        ctx.wait(g, tg + "oev", o_slot_ev[slot])
                        ad = g.dma_start(out=ds["acc"][dt * 128:(dt + 1) * 128, tq * 512:(tq + 1) * 512],
                                         in_=stg[slot][:],
                                         accum_op=(ALU.bypass if p == 0 else ALU.add))
                        acq_cnt[slot] = ctx.inc(ad, tg + "acq%d" % slot, 16)
        ctx.wait(g, tg + "acq0")
        ctx.wait(g, tg + "acq1")
        if "act" in ctx.taps and tg == "F1":
            ctx.wait(g, tg + "guevS")
            ctx.wait(g, tg + "guevV")
            ctx.inc(g.dma_start(out=ctx.taps["act"], in_=act[:]), tg + "tact", 16)
            ctx.wait(g, tg + "tact")


def _residual_stage(ctx, src, dst, tag, tap):
    nc = ctx.nc
    tg = tag
    ds = ctx.ds
    v, s = nc.vector, nc.scalar
    with ExitStack() as es:
        xs = [es.enter_context(nc.sbuf_tensor(f"{tg}_x{i}", [128, D], F32)) for i in range(2)]
        at = [es.enter_context(nc.sbuf_tensor(f"{tg}_at{i}", [128, 128], F32)) for i in range(2)]
        xn = [es.enter_context(nc.sbuf_tensor(f"{tg}_xn{i}", [128, D], F32)) for i in range(2)]
        ps_r = es.enter_context(nc.psum_tensor(tg + "_ps_r", [128, D], F32))
        st_counts = []
        for tt in range(NT):
            bp = tt % 2
            if tt >= 2:
                ctx.wait(nc.sync, tg + "add", tt - 1)
            ctx.dma(nc.sync, tg + "x%d" % bp, xs[bp][:], src[tt * 128:(tt + 1) * 128, :])
            for dt in range(NDT):
                gidx = tt * NDT + dt
                ap = dt % 2
                if gidx >= 2:
                    ctx.wait(nc.sync, tg + "tp", gidx - 1)
                dd = nc.sync.dma_start(out=at[ap][:, 0:128],
                                       in_=ds["acc"][dt * 128:(dt + 1) * 128, tt * 128:(tt + 1) * 128])
                abn = ctx.inc(dd, tg + "ab%d" % ap, 16)
                ctx.wait(nc.tensor, tg + "ab%d" % ap, abn)
                if dt == 0 and tt >= 1:
                    ctx.wait(nc.tensor, tg + "add", tt)
                mm = nc.tensor.matmul(out=ps_r[:, dt * 128:(dt + 1) * 128], lhsT=at[ap][:, 0:128],
                                      rhs=ctx.id_f32, start=True, stop=True)
                ctx.inc(mm, tg + "tp", 1)
            ctx.wait(v, tg + "tp")
            ctx.wait(v, tg + "x%d" % bp)
            if tt >= 2:
                ctx.wait(v, tg + "st%d" % bp, st_counts[tt - 2])
            add = v.tensor_tensor(out=xn[bp][:], in0=ps_r[:], in1=xs[bp][:], op=ALU.add)
            ctx.inc(add, tg + "add", 1)
            ctx.wait(nc.sync, tg + "add")
            ctx.inc(nc.sync.dma_start(out=dst[tt * 128:(tt + 1) * 128, :], in_=xn[bp][:]), tg + "st%d" % bp, 16)
            if tap is not None and tap in ctx.taps:
                ctx.inc(nc.sync.dma_start(out=ctx.taps[tap][tt * 128:(tt + 1) * 128, :], in_=xn[bp][:]),
                        tg + "st%d" % bp, 16)
            st_counts.append(ctx.cnt(tg + "st%d" % bp))
        ctx.wait(nc.sync, tg + "st0")
        ctx.wait(nc.sync, tg + "st1")
        ctx.wait(nc.vector, tg + "st0")
        ctx.wait(nc.vector, tg + "st1")



def _attention(ctx):
    nc = ctx.nc
    ds, di = ctx.ds, ctx.di
    v, s, g = nc.vector, nc.scalar, nc.gpsimd

    ctx.flush_sems()
    _ln_router_stage(ctx, src=ds["x1"], tag="L2", tap=None, do_router=False)
    ctx.barrier()

    with ExitStack() as es:
        ee = es.enter_context
        ctx.gt = ee(nc.sbuf_tensor("A_gt", [128, NCH * NH], F32))
        ctx.ngt = ee(nc.sbuf_tensor("A_ngt", [128, NCH * NH], F32))
        ctx.bt = ee(nc.sbuf_tensor("A_bt", [128, NCH * NH], F32))
        ctx.aCtab = ee(nc.sbuf_tensor("A_aCtab", [1, (NCH + 1) * NH], F32))
        ctx.prodtab = ee(nc.sbuf_tensor("A_prodtab", [1, (NCH + 1) * NH], F32))
        ctx.acbc = ee(nc.sbuf_tensor("A_acbc", [128, NCH * NH], F32))
        ctx.prodbc = ee(nc.sbuf_tensor("A_prodbc", [128, NCH * NH], F32))
        ctx.lamdiag = ee(nc.sbuf_tensor("A_lamdiag", [128, NH * HD], BF16))
        ctx.lambc = ee(nc.sbuf_tensor("A_lambc", [128, NH * HD], BF16))
        ctx.flush_sems()
        _attn_proj(ctx)
        ctx.barrier()
        ctx.flush_sems()
        _attn_phase_a(ctx)
        ctx.barrier()
        ctx.flush_sems()
        _attn_phase_b(ctx)
        ctx.barrier()


def _attn_proj(ctx):
    nc = ctx.nc
    ds, di = ctx.ds, ctx.di
    v, s, g = nc.vector, nc.scalar, nc.gpsimd
    lnT = ctx.lnT

    with ExitStack() as es:
        ee = es.enter_context
        qh_sb = ee(nc.sbuf_tensor("A_qh", [128, TL], BF16))
        qtok = ee(nc.sbuf_tensor("A_qtok", [128, 128], BF16))
        ktf = ee(nc.sbuf_tensor("A_ktf", [128, 128], BF16))
        ssb = ee(nc.sbuf_tensor("A_ss", [128, 8], F32))
        junk = ee(nc.sbuf_tensor("A_junk", [128, 128], F32))
        ps_p = [ee(nc.psum_tensor(f"A_psp{i}", [128, 512], F32)) for i in range(2)]
        ps_t2 = ee(nc.psum_tensor("A_pst2", [128, 128], F32))
        ps_t3 = ee(nc.psum_tensor("A_pst3", [128, 128], F32))
        ps_bg = ee(nc.psum_tensor("A_psbg", [16, 512], F32))
        ps_sm = ee(nc.psum_tensor("A_pssm", [128, 16], F32))

        pe_slot = [None, None]
        for kind, W, tokdst in (("q", di["Wq"], ds["bq"]), ("k", di["Wk"], ds["knat"]),
                                ("v", di["Wv"], ds["vnat"])):
            tgw = "Awl" + kind
            with nc.sbuf_tensor("A_wbuf" + kind, [128, NDT * D], BF16) as wq_sb:
                ctx.dma(g, tgw, wq_sb[:].rearrange("p (a c) -> p a c", a=16),
                        W.rearrange("(a p) c -> p a c", p=128))
                ctx.wait(nc.tensor, tgw)
                for h in range(NH):
                    for tq in range(4):
                        slot = (h * 4 + tq) % 2
                        if pe_slot[slot] is not None:
                            ctx.wait(nc.tensor, pe_slot[slot][0], pe_slot[slot][1])
                        for din in range(NDT):
                            mm = nc.tensor.matmul(out=ps_p[slot][:],
                                                  lhsT=wq_sb[:, din * D + h * 128: din * D + (h + 1) * 128],
                                                  rhs=lnT[:, din * TL + tq * 512: din * TL + (tq + 1) * 512],
                                                  start=(din == 0), stop=(din == NDT - 1))
                        pm = ctx.inc(mm, "Apm", 1)
                        ctx.wait(s, "Apm", pm)
                        if tq == 0:
                            ctx.wait(s, "At2m", ctx.cnt("At2m"))
                            ctx.wait(s, "Avt", ctx.cnt("Avt"))
                        ev = s.activation(out=qh_sb[:, tq * 512:(tq + 1) * 512], in_=ps_p[slot][:],
                                          func=AF.Silu)
                        pe_slot[slot] = ("ApevS", ctx.inc(ev, "ApevS", 1))
                    if kind == "v":
                        ctx.wait(g, "ApevS")
                        ctx.dma(g, "Avt", ds["vT"][h * 128:(h + 1) * 128, :], qh_sb[:])
                    for c in range(NCH):
                        ctx.wait(nc.tensor, "ApevS")
                        ctx.wait(nc.tensor, "Aqt", ctx.cnt("Aqt"))
                        mm2 = nc.tensor.matmul(out=ps_t2[:], lhsT=qh_sb[:, c * 128:(c + 1) * 128],
                                               rhs=ctx.id_bf, start=True, stop=True)
                        ctx.inc(mm2, "At2m", 1)
                        ctx.wait(s, "At2m")
                        if kind in ("q", "k"):
                            a1 = s.activation(out=junk[:], in_=ps_t2[:], func=AF.Square, accum_out=ssb[:, 0:1])
                            ctx.inc(a1, "Asq", 1)
                            ctx.wait(s, "Asq")
                            a2 = s.activation(out=ssb[:, 1:2], in_=ssb[:, 0:1], func=AF.Sqrt)
                            ctx.inc(a2, "Asq", 1)
                            ctx.wait(v, "Asq")
                            ctx.edge(v, v.tensor_scalar(out=ssb[:, 2:3], in0=ssb[:, 1:2], scalar1=1e-12,
                                                        scalar2=None, op0=ALU.max))
                            ctx.inc(v.reciprocal(out=ssb[:, 3:4], in_=ssb[:, 2:3]), "Arn", 1)
                            ctx.wait(s, "Arn")
                            a3 = s.activation(out=qtok[:], in_=ps_t2[:], func=AF.Identity, scale=ssb[:, 3:4])
                            ctx.inc(a3, "Aqt", 1)
                        else:
                            a3 = s.activation(out=qtok[:], in_=ps_t2[:], func=AF.Copy)
                            ctx.inc(a3, "Aqt", 1)
                        ctx.wait(g, "Aqt")
                        ctx.dma(g, "Ast", tokdst[c * 128:(c + 1) * 128, h * 128:(h + 1) * 128], qtok[:])
                        if kind == "k":
                            ctx.wait(nc.tensor, "Aqt")
                            ctx.wait(nc.tensor, "At3e", ctx.cnt("At3e"))
                            mm3 = nc.tensor.matmul(out=ps_t3[:], lhsT=qtok[:], rhs=ctx.id_bf,
                                                   start=True, stop=True)
                            ctx.inc(mm3, "At3m", 1)
                            ctx.wait(v, "At3m")
                            ctx.inc(v.tensor_copy(out=ktf[:], in_=ps_t3[:]), "At3e", 1)
                            ctx.wait(g, "At3e")
                            ctx.dma(g, "Ast2", ds["knT"][h * 128:(h + 1) * 128, c * 128:(c + 1) * 128],
                                    ktf[:])
                        ctx.wait(s, "Ast")
                        ctx.wait(s, "Ast2")
                ctx.wait(nc.sync, "Ast")
                ctx.wait(nc.sync, "Ast2")
                ctx.wait(nc.sync, "Avt")
                ctx.wait(nc.scalar, "Ast")
                ctx.barrier()

        ctx.barrier()
        with nc.sbuf_tensor("A_wab", [128, NDT * 2 * NH], BF16) as wab, \
             nc.sbuf_tensor("A_basb", [NH, 2], F32) as basb, \
             nc.sbuf_tensor("A_bg16", [NH, 2 * TL], F32) as bg16, \
             nc.sbuf_tensor("A_bgt", [NH, TL], F32) as bgt:
            ctx.dma(g, "Abw", wab[:].rearrange("p (a c) -> p a c", a=16)[:, :, 0:NH],
                    di["Wa"].rearrange("(a p) c -> p a c", p=128))
            ctx.dma(g, "Abw", wab[:].rearrange("p (a c) -> p a c", a=16)[:, :, NH:2 * NH],
                    di["Wb"].rearrange("(a p) c -> p a c", p=128))
            ctx.dma(nc.sync, "Abw2", basb[:, 0:1], di["ba"].rearrange("o h -> h o"),
                    allow_slow_non_contiguous=True)
            ctx.dma(nc.sync, "Abw2", basb[:, 1:2], di["bb"].rearrange("o h -> h o"),
                    allow_slow_non_contiguous=True)
            ctx.wait(nc.tensor, "Abw")
            ctx.wait(nc.scalar, "Abw2")
            for col, nm in ((0, "ga"), (NH, "bb")):
                for tq in range(4):
                    ctx.wait(nc.tensor, "Abge", ctx.cnt("Abge"))
                    for din in range(NDT):
                        mmb = nc.tensor.matmul(out=ps_bg[:],
                                               lhsT=wab[:, din * 2 * NH + col: din * 2 * NH + col + NH],
                                               rhs=lnT[:, din * TL + tq * 512: din * TL + (tq + 1) * 512],
                                               start=(din == 0), stop=(din == NDT - 1))
                    ctx.inc(mmb, "Abgm", 1)
                    ctx.wait(s, "Abgm")
                    bcol = basb[:, 0:1] if col == 0 else basb[:, 1:2]
                    ev = s.activation(out=bg16[:, (0 if col == 0 else TL) + tq * 512:
                                               (0 if col == 0 else TL) + (tq + 1) * 512],
                                      in_=ps_bg[:], func=AF.Identity, bias=bcol)
                    ctx.inc(ev, "Abge", 1)
            ctx.wait(s, "Abge")
            ctx.inc(s.activation(out=bg16[:, TL:2 * TL], in_=bg16[:, TL:2 * TL], func=AF.Sigmoid), "Abs", 1)
            ctx.inc(s.activation(out=bgt[:], in_=bg16[:, 0:TL], func=AF.Abs), "Abs", 1)
            ctx.wait(v, "Abs")
            ctx.inc(v.tensor_scalar(out=bgt[:], in0=bgt[:], scalar1=-1.0, scalar2=None, op0=ALU.mult), "Abv", 1)
            ctx.wait(s, "Abv")
            ctx.inc(s.activation(out=bgt[:], in_=bgt[:], func=AF.Exp), "Abs", 1)
            ctx.wait(v, "Abs")
            ctx.inc(v.tensor_scalar(out=bgt[:], in0=bgt[:], scalar1=1.0, scalar2=None, op0=ALU.add), "Abv", 1)
            ctx.wait(s, "Abv")
            ctx.inc(s.activation(out=bgt[:], in_=bgt[:], func=AF.Ln), "Abs", 1)
            ctx.wait(v, "Abs")
            ctx.edge(v, v.tensor_scalar(out=junk[0:NH, 0:TL] if False else bg16[:, 0:TL],
                                        in0=bg16[:, 0:TL], scalar1=0.0, scalar2=None, op0=ALU.min))
            ctx.inc(v.tensor_tensor(out=bg16[:, 0:TL], in0=bg16[:, 0:TL], in1=bgt[:], op=ALU.subtract),
                    "Abv", 1)
            ctx.wait(nc.tensor, "Abv")
            ctx.wait(nc.tensor, "Abs")
            for c in range(NCH):
                for which, tab in ((0, None), (1, ctx.bt)):
                    srcsl = bg16[:, which * TL + c * 128: which * TL + (c + 1) * 128]
                    ctx.wait(nc.tensor, "Asme", ctx.cnt("Asme"))
                    mmt = nc.tensor.matmul(out=ps_sm[:], lhsT=srcsl, rhs=ctx.id16_f32,
                                           start=True, stop=True)
                    ctx.inc(mmt, "Asmm", 1)
                    ctx.wait(v, "Asmm")
                    dst = (ctx.gt if which == 0 else ctx.bt)[:, c * NH:(c + 1) * NH]
                    if which == 0:
                        ctx.inc(v.tensor_copy(out=ctx.ngt[:, c * NH:(c + 1) * NH], in_=ps_sm[:]), "Asme", 1)
                    else:
                        ctx.inc(v.tensor_copy(out=dst, in_=ps_sm[:]), "Asme", 1)
            ctx.wait(nc.tensor, "Asme")
            for c in range(NCH):
                ctx.wait(nc.tensor, "Agce", ctx.cnt("Agce"))
                mmc = nc.tensor.matmul(out=ps_sm[:], lhsT=ctx.tri_f32,
                                       rhs=ctx.ngt[:, c * NH:(c + 1) * NH], start=True, stop=True)
                ctx.inc(mmc, "Agcm", 1)
                ctx.wait(v, "Agcm")
                ctx.inc(v.tensor_copy(out=ctx.gt[:, c * NH:(c + 1) * NH], in_=ps_sm[:]), "Agce", 1)
            ctx.wait(v, "Agce")
            ctx.inc(v.tensor_scalar(out=ctx.ngt[:], in0=ctx.gt[:], scalar1=-1.0, scalar2=None,
                                    op0=ALU.mult), "Angt", 1)

        tabs = ctx.tabs
        for h in range(NH):
            ctx.edge(v, v.tensor_scalar(out=ctx.lamdiag[:, h * 128:(h + 1) * 128], in0=ctx.id_bf,
                                        scalar1=tabs[:, h:h + 1], scalar2=None, op0=ALU.mult))
            ctx.wait(nc.tensor, "Alr", ctx.cnt("Alr"))
            mml = nc.tensor.matmul(out=ps_t2[0:1, :], lhsT=tabs[:, h:h + 1], rhs=ctx.id_f32,
                                   start=True, stop=True)
            ctx.inc(mml, "Alm", 1)
            ctx.wait(v, "Alm")
            ctx.inc(v.tensor_copy(out=junk[0:1, :], in_=ps_t2[0:1, :]), "Alr", 1)
            ctx.wait(nc.tensor, "Alr")
            mml2 = nc.tensor.matmul(out=ps_t3[:], lhsT=ctx.ones_f32, rhs=junk[0:1, :],
                                    start=True, stop=True)
            ctx.inc(mml2, "Alm2", 1)
            ctx.wait(v, "Alm2")
            ctx.inc(v.tensor_scalar(out=ctx.lambc[:, h * 128:(h + 1) * 128], in0=ps_t3[:],
                                    scalar1=1e-8, scalar2=None, op0=ALU.add), "Alr2", 1)
            ctx.wait(nc.tensor, "Alr2")

        ctx.barrier()
        with nc.sbuf_tensor("A_wotmp", [128, D], F32) as wot:
            for din in range(NDT):
                if din >= 1:
                    ctx.wait(nc.sync, "Awos", ctx.cnt("Awos"))
                ctx.dma(nc.sync, "Awol", wot[:], di["Wo"][din * 128:(din + 1) * 128, :])
                ctx.wait(v, "Awol")
                sc = v.tensor_scalar(out=lnT[:, din * D:(din + 1) * D], in0=wot[:],
                                     scalar1=tabs[:, 16:17], scalar2=None, op0=ALU.mult)
                ctx.inc(sc, "Awos", 1)
        ctx.wait(nc.sync, "Awos")
        ctx.barrier()





def _attn_phase_a(ctx):
    nc = ctx.nc
    ds = ctx.ds
    v, s, g = nc.vector, nc.scalar, nc.gpsimd

    with ExitStack() as es:
        ee = es.enter_context
        knc = ee(nc.sbuf_tensor("Pa_knc", [128, NH * HD], BF16))
        vnc = ee(nc.sbuf_tensor("Pa_vnc", [128, NH * HD], BF16))
        grow = ee(nc.sbuf_tensor("Pa_grow", [1, 128], F32))
        arow = ee(nc.sbuf_tensor("Pa_arow", [1, 128], F32))
        w0 = ee(nc.sbuf_tensor("Pa_w0", [128, 128], F32))
        btri = ee(nc.sbuf_tensor("Pa_btri", [128, 128], F32))
        wtb = ee(nc.sbuf_tensor("Pa_wtb", [128, 128], BF16))
        wcol = ee(nc.sbuf_tensor("Pa_wcol", [128, 1], F32))
        kw = ee(nc.sbuf_tensor("Pa_kw", [128, 128], BF16))
        pstage = ee(nc.sbuf_tensor("Pa_pstage", [128, 256], F32))
        Lkk = ee(nc.sbuf_tensor("Pa_Lkk", [128, NH * HD], F32))
        Lkv = ee(nc.sbuf_tensor("Pa_Lkv", [128, NH * HD], F32))
        zrow = ee(nc.sbuf_tensor("Pa_zrow", [1, NH * HD], F32))
        hstage = ee(nc.sbuf_tensor("Pa_hstage", [128, NH * HD], F32))
        hstage2 = ee(nc.sbuf_tensor("Pa_hstage2", [128, NH * HD], F32))
        hbf = ee(nc.sbuf_tensor("Pa_hbf", [128, NH * HD], BF16))
        drow = ee(nc.sbuf_tensor("Pa_drow", [1, 128], F32))
        ps_r = ee(nc.psum_tensor("Pa_psr", [1, 128], F32))
        ps_b = ee(nc.psum_tensor("Pa_psb", [128, 128], F32))
        ps_p1 = ee(nc.psum_tensor("Pa_psp1", [128, 128], F32))
        ps_p2 = ee(nc.psum_tensor("Pa_psp2", [128, 128], F32))
        ps_ac = ee(nc.psum_tensor("Pa_psac", [128, NH], F32))

        ctx.inc(v.memset(Lkk[:], 0.0), "Painit", 1)
        ctx.inc(v.memset(Lkv[:], 0.0), "Painit", 1)
        ctx.inc(v.memset(zrow[:], 0.0), "Painit", 1)
        ctx.inc(v.memset(ctx.prodtab[0:1, 0:NH], 1.0), "Painit", 1)
        ctx.wait(v, "Painit")

        for c in range(NCH):
            if c >= 1:
                ctx.wait(nc.sync, "Pakk", ctx.cnt("Pakk"))
            ctx.dma(nc.sync, "Paldk", knc[:], ds["knat"][c * 128:(c + 1) * 128, :])
            ctx.dma(nc.sync, "Paldv", vnc[:], ds["vnat"][c * 128:(c + 1) * 128, :])
            ctx.wait(g, "Painit")
            ctx.wait(g, "Palu", ctx.cnt("Palu"))
            ctx.dma(g, "Palp0", ds["lpref"][c, 0], Lkk[:])
            ctx.dma(g, "Palp1", ds["lpref"][c, 1], Lkv[:])
            for h in range(NH):
                col = c * NH + h
                ctx.wait(nc.tensor, "Page", ctx.cnt("Page"))
                mmr = nc.tensor.matmul(out=ps_r[0:1, :], lhsT=ctx.gt[:, col:col + 1], rhs=ctx.id_f32,
                                       start=True, stop=True)
                ctx.inc(mmr, "Pagm", 1)
                ctx.wait(v, "Pagm")
                ctx.wait(v, "Paae")
                ctx.inc(v.tensor_copy(out=grow[0:1, :], in_=ps_r[0:1, :]), "Page", 1)
                ctx.wait(s, "Page")
                ctx.inc(s.activation(out=arow[0:1, :], in_=grow[0:1, :], func=AF.Exp), "Paae", 1)
                ctx.wait(v, "Paae")
                ctx.edge(v, v.tensor_copy(out=ctx.aCtab[0:1, col:col + 1], in_=arow[0:1, 127:128]))
                ctx.wait(nc.tensor, "Paw0", ctx.cnt("Paw0"))
                ctx.wait(nc.tensor, "Page")
                mmb = nc.tensor.matmul(out=ps_b[:], lhsT=ctx.ones_f32, rhs=grow[0:1, :],
                                       start=True, stop=True)
                ctx.inc(mmb, "Pabm", 1)
                ctx.wait(v, "Pabm")
                ctx.wait(v, "Paw0", ctx.cnt("Paw0"))
                dgm = v.tensor_scalar(out=w0[:], in0=ps_b[:], scalar1=ctx.ngt[:, col:col + 1],
                                      scalar2=0.0, op0=ALU.add, op1=ALU.min)
                ctx.inc(dgm, "Padg", 1)
                ctx.wait(s, "Padg")
                a0 = s.activation(out=w0[:], in_=w0[:], func=AF.Exp)
                ctx.inc(a0, "Paw0", 1)
                ctx.wait(v, "Paw0")
                ctx.edge(v, v.tensor_scalar(out=btri[:], in0=ctx.tri_f32, scalar1=ctx.bt[:, col:col + 1],
                                            scalar2=None, op0=ALU.mult))
                ctx.wait(v, "Pamse_d", ctx.cnt("Pamse_d"))
                ctx.inc(v.tensor_tensor(out=wtb[:], in0=w0[:], in1=btri[:], op=ALU.mult), "Pamw", 1)
                ctx.edge(v, v.tensor_tensor(out=wcol[:], in0=w0[:, 127:128], in1=btri[:, 127:128],
                                            op=ALU.mult))
                ctx.wait(v, "Paldk")
                ctx.inc(v.tensor_scalar(out=kw[:], in0=knc[:, h * 128:(h + 1) * 128],
                                        scalar1=wcol[:], scalar2=None, op0=ALU.mult), "Pakwv", 1)
                ctx.wait(g, "Pamw")
                ctx.dma(g, "Pamse_d", ds["masks"][c, h], wtb[:])
                ctx.wait(nc.tensor, "Pakwv")
                ctx.wait(nc.tensor, "Paldv")
                ctx.wait(nc.tensor, "Pape", ctx.cnt("Pape"))
                m1 = nc.tensor.matmul(out=ps_p1[:], lhsT=knc[:, h * 128:(h + 1) * 128], rhs=kw[:],
                                      start=True, stop=True)
                ctx.inc(m1, "Pakk", 1)
                m2 = nc.tensor.matmul(out=ps_p2[:], lhsT=vnc[:, h * 128:(h + 1) * 128], rhs=kw[:],
                                      start=True, stop=True)
                ctx.inc(m2, "Pakk", 1)
                ctx.wait(v, "Pakk")
                ctx.wait(v, "Papd0", ctx.cnt("Papd0"))
                ctx.wait(v, "Papd1", ctx.cnt("Papd1"))
                ctx.inc(v.tensor_copy(out=pstage[:, 0:128], in_=ps_p1[:]), "Pape", 1)
                ctx.inc(v.tensor_copy(out=pstage[:, 128:256], in_=ps_p2[:]), "Pape", 1)
                ctx.wait(g, "Pape")
                ctx.dma(g, "Papd0", ds["partials"][c, 0, :, h * 128:(h + 1) * 128], pstage[:, 0:128])
                ctx.dma(g, "Papd1", ds["partials"][c, 1, :, h * 128:(h + 1) * 128], pstage[:, 128:256])
            ctx.wait(nc.tensor, "Paace", ctx.cnt("Paace"))
            ctx.wait(nc.tensor, "se_BassVectorEngine", ctx.cnt("se_BassVectorEngine"))
            mac = nc.tensor.matmul(out=ps_ac[:], lhsT=ctx.ones_f32,
                                   rhs=ctx.aCtab[0:1, c * NH:(c + 1) * NH], start=True, stop=True)
            ctx.inc(mac, "Paacm", 1)
            ctx.wait(v, "Paacm")
            ctx.inc(v.tensor_copy(out=ctx.acbc[:, c * NH:(c + 1) * NH], in_=ps_ac[:]), "Paace", 1)
            ctx.edge(v, v.tensor_tensor(out=ctx.prodtab[0:1, (c + 1) * NH:(c + 2) * NH],
                                        in0=ctx.prodtab[0:1, c * NH:(c + 1) * NH],
                                        in1=ctx.aCtab[0:1, c * NH:(c + 1) * NH], op=ALU.mult))
            ctx.wait(v, "Palp0")
            ctx.wait(v, "Palp1")
            ctx.wait(nc.sync, "Papd0")
            ctx.wait(nc.sync, "Papd1")
            ctx.dma(nc.sync, "Palr0", hstage[:], ds["partials"][c, 0])
            ctx.dma(nc.sync, "Palr1", hstage2[:], ds["partials"][c, 1])
            ctx.wait(v, "Palr0")
            ctx.wait(v, "Palr1")
            for h in range(NH):
                sl = slice(h * 128, (h + 1) * 128)
                accol = ctx.acbc[:, c * NH + h: c * NH + h + 1]
                e1 = v.scalar_tensor_tensor(out=Lkk[:, sl], in0=Lkk[:, sl], scalar=accol,
                                            in1=hstage[:, sl], op0=ALU.mult, op1=ALU.add)
                ctx.inc(e1, "Palu", 1)
                e2 = v.scalar_tensor_tensor(out=Lkv[:, sl], in0=Lkv[:, sl], scalar=accol,
                                            in1=hstage2[:, sl], op0=ALU.mult, op1=ALU.add)
                ctx.inc(e2, "Palu", 1)
            ctx.wait(v, "Palu")

        ctx.wait(g, "Palu")
        ctx.wait(g, "se_BassVectorEngine")
        ctx.dma(g, "Pacc", ds["cc_in"][0:HD, :], Lkk[:])
        ctx.dma(g, "Pacc", ds["cc_in"][HD:2 * HD, :], Lkv[:])
        ctx.dma(g, "Pacc", ds["cc_in"][2 * HD:2 * HD + 1, :], zrow[0:1, :])
        ctx.dma(g, "Pacc", ds["cc_in"][2 * HD:2 * HD + 1, 0:NH],
                ctx.prodtab[0:1, NCH * NH:(NCH + 1) * NH])
        ctx.wait(g, "Pacc")
        cc = g.collective_compute("AllGather", ALU.bypass, replica_groups=[list(range(NCORES))],
                                  ins=[ds["cc_in"].opt()],
                                  outs=[ds["cc_out"].rearrange("r p c -> (r p) c").opt()])
        ctx.inc(cc, "Paccd", 1)
        ctx.wait(g, "Paccd")
        ctx.wait(nc.sync, "Paccd")

        for j in range(NCORES):
            ctx.dma(nc.sync, "Padr", drow[0:1, j * NH:(j + 1) * NH],
                    ds["cc_out"][j, 2 * HD:2 * HD + 1, 0:NH])
        ctx.wait(nc.tensor, "Padr")
        mdb = nc.tensor.matmul(out=ps_b[:], lhsT=ctx.ones_f32, rhs=drow[0:1, :], start=True, stop=True)
        ctx.inc(mdb, "Padbm", 1)
        ctx.wait(v, "Padbm")
        dbc = ee(nc.sbuf_tensor("Pa_dbc", [128, 128], F32))
        ctx.inc(v.tensor_copy(out=dbc[:], in_=ps_b[:]), "Padbe", 1)
        ctx.inc(v.memset(Lkk[:], 0.0), "Pagz", 1)
        ctx.inc(v.memset(Lkv[:], 0.0), "Pagz", 1)
        ctx.wait(g, "Pagz")
        ctx.dma(g, "Papw0", ds["prefixes"][0, 0:HD, :], Lkk[:])
        ctx.dma(g, "Papw1", ds["prefixes"][0, HD:2 * HD, :], Lkv[:])
        for j in range(NCORES - 1):
            ctx.wait(nc.sync, "Pagu", ctx.cnt("Pagu"))
            ctx.dma(nc.sync, "Pagl0", hstage[:], ds["cc_out"][j, 0:HD, :])
            ctx.dma(nc.sync, "Pagl1", hstage2[:], ds["cc_out"][j, HD:2 * HD, :])
            ctx.wait(v, "Pagl0")
            ctx.wait(v, "Pagl1")
            ctx.wait(v, "Padbe")
            ctx.wait(v, "Papw0")
            ctx.wait(v, "Papw1")
            for h in range(NH):
                sl = slice(h * 128, (h + 1) * 128)
                dcol = dbc[:, j * NH + h: j * NH + h + 1]
                e1 = v.scalar_tensor_tensor(out=Lkk[:, sl], in0=Lkk[:, sl], scalar=dcol,
                                            in1=hstage[:, sl], op0=ALU.mult, op1=ALU.add)
                ctx.inc(e1, "Pagu", 1)
                e2 = v.scalar_tensor_tensor(out=Lkv[:, sl], in0=Lkv[:, sl], scalar=dcol,
                                            in1=hstage2[:, sl], op0=ALU.mult, op1=ALU.add)
                ctx.inc(e2, "Pagu", 1)
            ctx.wait(g, "Pagu")
            ctx.dma(g, "Papw0", ds["prefixes"][j + 1, 0:HD, :], Lkk[:])
            ctx.dma(g, "Papw1", ds["prefixes"][j + 1, HD:2 * HD, :], Lkv[:])
        ctx.wait(nc.sync, "Papw0")
        ctx.wait(nc.sync, "Papw1")
        pid = nc.sync.partition_id()
        ctx.dma(nc.sync, "Pahs", hstage[:], ds["prefixes"][pid, 0:HD, :])
        ctx.dma(nc.sync, "Pahs", hstage2[:], ds["prefixes"][pid, HD:2 * HD, :])

        for c in range(NCH):
            ctx.wait(nc.tensor, "Papbe", ctx.cnt("Papbe"))
            mpb = nc.tensor.matmul(out=ps_ac[:], lhsT=ctx.ones_f32,
                                   rhs=ctx.prodtab[0:1, c * NH:(c + 1) * NH], start=True, stop=True)
            ctx.inc(mpb, "Papbm", 1)
            ctx.wait(v, "Papbm")
            ctx.inc(v.tensor_copy(out=ctx.prodbc[:, c * NH:(c + 1) * NH], in_=ps_ac[:]), "Papbe", 1)

        for c in range(NCH):
            for st_i in range(2):
                hs = hstage if st_i == 0 else hstage2
                ctx.wait(nc.sync, "Pahb", ctx.cnt("Pahb"))
                ctx.dma(nc.sync, "Pahl", Lkk[:], ds["lpref"][c, st_i])
                ctx.wait(v, "Pahl")
                ctx.wait(v, "Pahs")
                for h in range(NH):
                    sl = slice(h * 128, (h + 1) * 128)
                    pcol = ctx.prodbc[:, c * NH + h: c * NH + h + 1]
                    e1 = v.scalar_tensor_tensor(out=Lkv[:, sl], in0=hs[:, sl], scalar=pcol,
                                                in1=Lkk[:, sl], op0=ALU.mult, op1=ALU.add)
                    ctx.inc(e1, "Pahf", 1)
                ctx.wait(v, "Pahf")
                ctx.wait(v, "Pahd", ctx.cnt("Pahd"))
                ctx.inc(v.tensor_copy(out=hbf[:], in_=Lkv[:]), "Pahb", 1)
                ctx.wait(g, "Pahb")
                ctx.dma(g, "Pahd", ds["h0"][c, st_i], hbf[:])
        ctx.wait(g, "Pahd")

def _attn_phase_b(ctx):
    nc = ctx.nc
    ds = ctx.ds
    v, s, g = nc.vector, nc.scalar, nc.gpsimd
    wo = ctx.lnT
    NQ = 10
    Q_D, Q_PQ, Q_DN, Q_DONE, Q_AL, Q_NAL, Q_BE, Q_T0, Q_T1, Q_T2 = range(NQ)

    with ExitStack() as es:
        ee = es.enter_context
        h0c = [ee(nc.sbuf_tensor("Pb_h0%d" % i, [128, NH * HD], BF16)) for i in range(2)]
        hvc = ee(nc.sbuf_tensor("Pb_hv", [128, NH * HD], BF16))
        knc = [ee(nc.sbuf_tensor("Pb_knat%d" % i, [128, NH * HD], BF16)) for i in range(2)]
        ktc = [ee(nc.sbuf_tensor("Pb_knT%d" % i, [128, NH * HD], BF16)) for i in range(2)]
        vtc = ee(nc.sbuf_tensor("Pb_vT", [128, NH * HD], BF16))
        msk = [ee(nc.sbuf_tensor("Pb_msk%d" % i, [128, NH * HD], BF16)) for i in range(2)]
        bq = ee(nc.sbuf_tensor("Pb_bq", [128, NH * HD], BF16))
        dga = [ee(nc.sbuf_tensor("Pb_dga%d" % i, [128, NH * HD], BF16)) for i in range(2)]
        acol = ee(nc.sbuf_tensor("Pb_acol", [128, 2 * NH], F32))
        xst = [ee(nc.sbuf_tensor("Pb_x%d" % i, [128, NH * HD], F32)) for i in range(2)]
        rst = [ee(nc.sbuf_tensor("Pb_r%d" % i, [128, NH * HD], BF16)) for i in range(2)]
        pst = [ee(nc.sbuf_tensor("Pb_p%d" % i, [128, NH * HD], BF16)) for i in range(2)]
        qtk = [ee(nc.sbuf_tensor("Pb_qtk%d" % i, [128, NH * HD], BF16)) for i in range(2)]
        prod = ee(nc.sbuf_tensor("Pb_prod", [128, NH * HD], BF16))
        pfeat = ee(nc.sbuf_tensor("Pb_pfeat", [128, NH * HD], BF16))
        apfeat = ee(nc.sbuf_tensor("Pb_apfeat", [128, NH * HD], BF16))
        qfeat = ee(nc.sbuf_tensor("Pb_qfeat", [128, NH * HD], BF16))
        msb = ee(nc.sbuf_tensor("Pb_msb", [128, NH * HD], BF16))
        sc = [ee(nc.sbuf_tensor("Pb_sc%d" % i, [128, NQ * NH], F32)) for i in range(2)]
        den = ee(nc.sbuf_tensor("Pb_den", [128, NH * HD], F32))
        kk2 = ee(nc.sbuf_tensor("Pb_kk2", [128, NH * HD], BF16))
        otok = den
        junk = kk2
        x1c = den
        x2c = x1c
        psPT = ee(nc.psum_tensor("Pb_psPT", [128, 1024], F32))
        psSS = ee(nc.psum_tensor("Pb_psSS", [128, 1024], F32))
        psQQ = ee(nc.psum_tensor("Pb_psQQ", [128, 1024], F32))
        psQT = ee(nc.psum_tensor("Pb_psQT", [128, 1024], F32))

        def qcolsm(m, q, gh=None, h=None):
            if h is not None:
                return sc[m][:, q * NH + h: q * NH + h + 1]
            return sc[m][:, q * NH + gh * 8: q * NH + gh * 8 + 8]

        def junkm(m):
            return junk if m == 0 else den

        def hsl(h):
            return slice(h * 128, (h + 1) * 128)

        def halfsl(gh):
            return slice(gh * 8 * 128, (gh + 1) * 8 * 128)

        def mv_round(m, src_tok, dgam, h0m, kncm, ktcm, mskm, qtkm):
            for gh in range(2):
                heads = range(gh * 8, gh * 8 + 8)
                ctx.wait(nc.tensor, "Bpw%d%d" % (m, gh), ctx.cnt("Bpw%d%d" % (m, gh)))
                ctx.wait(nc.tensor, "BpeS", ctx.cnt("BpeS"))
                for i, h in enumerate(heads):
                    mm = nc.tensor.matmul(out=psPT[:, i * 128:(i + 1) * 128], lhsT=src_tok[:, hsl(h)],
                                          rhs=ctx.id_bf, start=True, stop=True)
                ctx.inc(mm, "Bptm", 1)
                ctx.wait(s, "Bptm")
                ctx.inc(s.activation(out=pfeat[:, halfsl(gh)], in_=psPT[:], func=AF.Copy), "BpeS", 1)
                ctx.wait(nc.tensor, "BpeS")
                for i, h in enumerate(heads):
                    mm = nc.tensor.matmul(out=psPT[:, i * 128:(i + 1) * 128], lhsT=src_tok[:, hsl(h)],
                                          rhs=dgam[:, hsl(h)], start=True, stop=True)
                ctx.inc(mm, "Bptm", 1)
                ctx.wait(s, "Bptm")
                ctx.inc(s.activation(out=apfeat[:, halfsl(gh)], in_=psPT[:], func=AF.Copy), "BpeS", 1)
                ctx.wait(nc.tensor, "Bmm", ctx.cnt("Bmm"))
                ctx.wait(nc.tensor, "Bden2", ctx.cnt("Bden2"))
                ctx.wait(nc.tensor, "BpeS")
                for i, h in enumerate(heads):
                    mm = nc.tensor.matmul(out=psSS[:, i * 128:(i + 1) * 128], lhsT=ktcm[:, hsl(h)],
                                          rhs=pfeat[:, hsl(h)], start=True, stop=True)
                ctx.inc(mm, "Bsm", 1)
                ctx.wait(v, "Bsm")
                mmv = v.tensor_tensor(out=msb[:, halfsl(gh)], in0=mskm[:, halfsl(gh)], in1=psSS[:],
                                      op=ALU.mult)
                ctx.inc(mmv, "Bmm", 1)
                ctx.wait(nc.tensor, "BqeS", ctx.cnt("BqeS"))
                ctx.wait(nc.tensor, "Bmm")
                for i, h in enumerate(heads):
                    osl = psQQ[:, i * 128:(i + 1) * 128]
                    nc.tensor.matmul(out=osl, lhsT=h0m[:, hsl(h)], rhs=apfeat[:, hsl(h)],
                                     start=True, stop=False)
                    nc.tensor.matmul(out=osl, lhsT=ctx.lamdiag[:, hsl(h)], rhs=pfeat[:, hsl(h)],
                                     start=False, stop=False)
                    mm = nc.tensor.matmul(out=osl, lhsT=kncm[:, hsl(h)], rhs=msb[:, hsl(h)],
                                          start=False, stop=True)
                ctx.inc(mm, "Bqm", 1)
                ctx.wait(s, "Bqm")
                ctx.inc(s.activation(out=qfeat[:, halfsl(gh)], in_=psQQ[:], func=AF.Copy), "BqeS", 1)
                ctx.wait(nc.tensor, "BqteS", ctx.cnt("BqteS"))
                ctx.wait(nc.tensor, "BqeS")
                for i, h in enumerate(heads):
                    mm = nc.tensor.matmul(out=psQT[:, i * 128:(i + 1) * 128], lhsT=qfeat[:, hsl(h)],
                                          rhs=ctx.id_bf, start=True, stop=True)
                ctx.inc(mm, "Bqtm", 1)
                ctx.wait(s, "Bqtm")
                ctx.wait(s, "Brw%d%d" % (m, gh), ctx.cnt("Brw%d%d" % (m, gh)))
                ctx.wait(s, "Bofm", ctx.cnt("Bofm"))
                ctx.inc(s.activation(out=qtkm[:, halfsl(gh)], in_=psQT[:], func=AF.Copy), "BqteS", 1)

        NPAIR = NCH // 2
        for cp in range(NPAIR):
            ctx.wait(nc.sync, "Bx2s", ctx.cnt("Bx2s"))
            for m in range(2):
                c = 2 * cp + m
                ctx.dma(nc.sync, "Bld0", h0c[m][:], ds["h0"][c, 0])
                ctx.dma(nc.sync, "Bld2", knc[m][:], ds["knat"][c * 128:(c + 1) * 128, :])
                ctx.dma(nc.sync, "Bld4", ktc[m][:].rearrange("p (h t) -> p h t", h=NH),
                        ds["knT"].rearrange("(h p) t -> p h t", p=128)[:, :, c * 128:(c + 1) * 128])
                ctx.dma(nc.sync, "Bld6", msk[m][:].rearrange("p (h t) -> p h t", h=NH),
                        ds["masks"][c].rearrange("h p t -> p h t"))
            ctx.wait(s, "Bwom", ctx.cnt("Bwom"))
            ctx.wait(v, "Bwom", ctx.cnt("Bwom"))
            ctx.inc(s.activation(out=acol[:, 0:NH], in_=ctx.gt[:, 2 * cp * NH:(2 * cp + 1) * NH],
                                 func=AF.Exp), "Bac", 1)
            ctx.inc(s.activation(out=acol[:, NH:2 * NH], in_=ctx.gt[:, (2 * cp + 1) * NH:(2 * cp + 2) * NH],
                                 func=AF.Exp), "Bac", 1)
            ctx.wait(v, "Bac")
            for m in range(2):
                for h in range(NH):
                    dd = v.tensor_scalar(out=dga[m][:, hsl(h)], in0=ctx.id_bf,
                                         scalar1=acol[:, m * NH + h: m * NH + h + 1],
                                         scalar2=None, op0=ALU.mult)
                    ctx.inc(dd, "Bdga", 1)
            for eng in (v, s, nc.tensor):
                for i in (0, 2, 4, 6):
                    ctx.wait(eng, "Bld%d" % i)
            ctx.wait(nc.tensor, "Bdga")

            for m in range(2):
                c = 2 * cp + m
                for mm_ in range(2):
                    for gg_ in range(2):
                        ctx.wait(nc.sync, "Bpw%d%d" % (mm_, gg_), ctx.cnt("Bpw%d%d" % (mm_, gg_)))
                ctx.dma(nc.sync, "Bld3", bq[:], ds["bq"][c * 128:(c + 1) * 128, :])
                ctx.wait(v, "Bld3")
                ctx.wait(v, "Bwtm", ctx.cnt("Bwtm"))
                ctx.wait(v, "Bd0", ctx.cnt("Bd0"))
                ctx.inc(v.tensor_tensor(out=kk2[:], in0=knc[m][:], in1=knc[m][:], op=ALU.mult), "Bkk2", 1)
                ctx.wait(v, "Bdbm", ctx.cnt("Bdbm"))
                for h in range(NH):
                    hh = v.tensor_tensor(out=prod[:, hsl(h)], in0=h0c[m][:, hsl(h)], in1=ctx.id_bf,
                                         op=ALU.mult)
                    ctx.inc(hh, "Bhdi", 1)
                ctx.wait(nc.tensor, "Bhdi")
                ctx.wait(nc.tensor, "Bkk2")
                ctx.wait(nc.tensor, "Bx2v", ctx.cnt("Bx2v"))
                for gh in range(2):
                    heads = range(gh * 8, gh * 8 + 8)
                    ctx.wait(nc.tensor, "BdbS", ctx.cnt("BdbS"))
                    ctx.wait(nc.tensor, "BqeS", ctx.cnt("BqeS"))
                    for i, h in enumerate(heads):
                        mm = nc.tensor.matmul(out=psQQ[:, i * 128:(i + 1) * 128], lhsT=ctx.onesmat_bf,
                                              rhs=prod[:, hsl(h)], start=True, stop=True)
                    ctx.inc(mm, "Bdbm", 1)
                    ctx.wait(s, "Bdbm")
                    ctx.inc(s.activation(out=den[:, halfsl(gh)], in_=psQQ[:], func=AF.Copy), "BdbS", 1)
                    ctx.wait(nc.tensor, "Bden2", ctx.cnt("Bden2"))
                    ctx.wait(nc.tensor, "Bmm", ctx.cnt("Bmm"))
                    for i, h in enumerate(heads):
                        mm = nc.tensor.matmul(out=psSS[:, i * 128:(i + 1) * 128], lhsT=msk[m][:, hsl(h)],
                                              rhs=kk2[:, hsl(h)], start=True, stop=True)
                    ctx.inc(mm, "Bwtm", 1)
                    ctx.wait(v, "Bwtm")
                    ctx.wait(v, "BdbS")
                    for i, h in enumerate(heads):
                        e1 = v.scalar_tensor_tensor(out=den[:, hsl(h)], in0=den[:, hsl(h)],
                                                    scalar=acol[:, m * NH + h: m * NH + h + 1],
                                                    in1=psSS[:, i * 128:(i + 1) * 128],
                                                    op0=ALU.mult, op1=ALU.add)
                        ctx.inc(e1, "Bden2", 1)
                    ctx.wait(v, "Bden2")
                    e2 = v.tensor_tensor(out=den[:, halfsl(gh)], in0=den[:, halfsl(gh)],
                                         in1=ctx.lambc[:, halfsl(gh)], op=ALU.add)
                    ctx.inc(e2, "Bden3", 1)
                    ctx.wait(v, "Bden3")
                    e3 = v.reciprocal(out=den[:, halfsl(gh)], in_=den[:, halfsl(gh)])
                    ctx.inc(e3, "Bden4", 1)
                    ctx.wait(v, "Bden4")
                    e4 = v.tensor_tensor(out=xst[m][:, halfsl(gh)], in0=bq[:, halfsl(gh)],
                                         in1=den[:, halfsl(gh)], op=ALU.mult)
                    ctx.inc(e4, "Bx0m", 1)
                    ctx.wait(v, "Bx0m")
                    e5 = v.tensor_copy(out=pst[m][:, halfsl(gh)], in_=xst[m][:, halfsl(gh)])
                    ctx.inc(e5, "Bpw%d%d" % (m, gh), 1)
                mv_round(m, pst[m], dga[m], h0c[m], knc[m], ktc[m], msk[m], qtk[m])
                ctx.wait(v, "BqteS")
                ctx.inc(v.memset(sc[m][:], 0.0), "Bsc0", 1)
                for gh in range(2):
                    e6 = v.tensor_tensor(out=rst[m][:, halfsl(gh)], in0=bq[:, halfsl(gh)],
                                         in1=qtk[m][:, halfsl(gh)], op=ALU.subtract)
                    ctx.inc(e6, "Brw%d%d" % (m, gh), 1)
                ctx.wait(s, "Brw%d0" % m)
                ctx.wait(s, "Brw%d1" % m)
                ctx.wait(s, "Bsc0")
                for h in range(NH):
                    a1 = s.activation(out=junkm(m)[:, hsl(h)], in_=rst[m][:, hsl(h)], func=AF.Square,
                                      accum_out=qcolsm(m, Q_D, h=h))
                    ctx.inc(a1, "Bd0", 1)
                ctx.wait(v, "Bd0")
                for gh in range(2):
                    e7 = v.tensor_copy(out=pst[m][:, halfsl(gh)], in_=rst[m][:, halfsl(gh)])
                    ctx.inc(e7, "Bpw%d%d" % (m, gh), 1)

            def chain(m, qv):
                for gh in range(2):
                    heads = range(gh * 8, gh * 8 + 8)
                    ctx.wait(v, "BqteS", qv - 1 + gh)
                    e1 = v.tensor_tensor(out=prod[:, halfsl(gh)], in0=pst[m][:, halfsl(gh)],
                                         in1=qtk[m][:, halfsl(gh)], op=ALU.mult)
                    ctx.edge(v, e1)
                    e2 = v.tensor_reduce(out=qcolsm(m, Q_PQ, gh=gh)[:, :, None],
                                         in_=prod[:, halfsl(gh)].rearrange("p (h d) -> p h d", h=8),
                                         axis=AX.X, op=ALU.add)
                    ctx.edge(v, e2)
                    qq = lambda q: qcolsm(m, q, gh=gh)
                    ctx.edge(v, v.tensor_scalar(out=qq(Q_T0), in0=qq(Q_D), scalar1=1e-10,
                                                scalar2=None, op0=ALU.is_lt))
                    ctx.edge(v, v.tensor_tensor(out=qq(Q_DONE), in0=qq(Q_DONE), in1=qq(Q_T0),
                                                op=ALU.max))
                    ctx.edge(v, v.tensor_tensor(out=qq(Q_T1), in0=qq(Q_PQ), in1=qq(Q_PQ),
                                                op=ALU.mult))
                    ctx.edge(v, v.tensor_scalar(out=qq(Q_T1), in0=qq(Q_T1), scalar1=1e-20,
                                                scalar2=None, op0=ALU.is_lt))
                    ctx.edge(v, v.tensor_tensor(out=qq(Q_DONE), in0=qq(Q_DONE), in1=qq(Q_T1),
                                                op=ALU.max))
                    ctx.edge(v, v.tensor_scalar(out=qq(Q_T0), in0=qq(Q_DONE), scalar1=-1.0,
                                                scalar2=1.0, op0=ALU.mult, op1=ALU.add))
                    ctx.edge(v, v.tensor_tensor(out=qq(Q_T1), in0=qq(Q_PQ), in1=qq(Q_T0),
                                                op=ALU.mult))
                    ctx.edge(v, v.tensor_tensor(out=qq(Q_T1), in0=qq(Q_T1), in1=qq(Q_DONE),
                                                op=ALU.add))
                    ctx.edge(v, v.reciprocal(out=qq(Q_T2), in_=qq(Q_T1)))
                    ctx.edge(v, v.tensor_tensor(out=qq(Q_AL), in0=qq(Q_D), in1=qq(Q_T2),
                                                op=ALU.mult))
                    ctx.edge(v, v.tensor_tensor(out=qq(Q_AL), in0=qq(Q_AL), in1=qq(Q_T0),
                                                op=ALU.mult))
                    ctx.edge(v, v.tensor_scalar(out=qq(Q_NAL), in0=qq(Q_AL), scalar1=-1.0,
                                                scalar2=None, op0=ALU.mult))
                    for h in heads:
                        xu = v.scalar_tensor_tensor(out=xst[m][:, hsl(h)], in0=pst[m][:, hsl(h)],
                                                    scalar=qcolsm(m, Q_AL, h=h),
                                                    in1=xst[m][:, hsl(h)],
                                                    op0=ALU.mult, op1=ALU.add)
                        ctx.inc(xu, "Bxu", 1)
                    for h in heads:
                        rr = v.scalar_tensor_tensor(out=rst[m][:, hsl(h)], in0=qtk[m][:, hsl(h)],
                                                    scalar=qcolsm(m, Q_NAL, h=h),
                                                    in1=rst[m][:, hsl(h)],
                                                    op0=ALU.mult, op1=ALU.add)
                        ctx.inc(rr, "Brw%d%d" % (m, gh), 1)
                    ctx.wait(s, "Brw%d%d" % (m, gh))
                    for h in heads:
                        a1 = s.activation(out=junkm(m)[:, hsl(h)], in_=rst[m][:, hsl(h)],
                                          func=AF.Square, accum_out=qcolsm(m, Q_DN, h=h))
                        ctx.inc(a1, "Bdn", 1)
                    ctx.wait(v, "Bdn")
                    ctx.edge(v, v.tensor_tensor(out=qq(Q_T1), in0=qq(Q_D), in1=qq(Q_T0),
                                                op=ALU.mult))
                    ctx.edge(v, v.tensor_tensor(out=qq(Q_T1), in0=qq(Q_T1), in1=qq(Q_DONE),
                                                op=ALU.add))
                    ctx.edge(v, v.reciprocal(out=qq(Q_T2), in_=qq(Q_T1)))
                    ctx.edge(v, v.tensor_tensor(out=qq(Q_BE), in0=qq(Q_DN), in1=qq(Q_T2),
                                                op=ALU.mult))
                    ctx.edge(v, v.tensor_tensor(out=qq(Q_BE), in0=qq(Q_BE), in1=qq(Q_T0),
                                                op=ALU.mult))
                    ctx.edge(v, v.tensor_copy(out=qq(Q_D), in_=qq(Q_DN)))
                    for h in heads:
                        pp = v.scalar_tensor_tensor(out=pst[m][:, hsl(h)], in0=pst[m][:, hsl(h)],
                                                    scalar=qcolsm(m, Q_BE, h=h),
                                                    in1=rst[m][:, hsl(h)],
                                                    op0=ALU.mult, op1=ALU.add)
                        ctx.inc(pp, "Bpw%d%d" % (m, gh), 1)

            qv = [None, None]
            for it in range(CGI):
                if it > 0:
                    chain(1, qv[1])
                mv_round(0, pst[0], dga[0], h0c[0], knc[0], ktc[0], msk[0], qtk[0])
                qv[0] = ctx.cnt("BqteS")
                mv_round(1, pst[1], dga[1], h0c[1], knc[1], ktc[1], msk[1], qtk[1])
                qv[1] = ctx.cnt("BqteS")
                chain(0, qv[0])
            chain(1, qv[1])

            for m in range(2):
                c = 2 * cp + m
                ctx.wait(nc.sync, "Bsm", ctx.cnt("Bsm"))
                ctx.wait(nc.sync, "Bqm", ctx.cnt("Bqm"))
                ctx.dma(nc.sync, "Bld1", hvc[:], ds["h0"][c, 1])
                ctx.dma(nc.sync, "Bld5", vtc[:].rearrange("p (h t) -> p h t", h=NH),
                        ds["vT"].rearrange("(h p) t -> p h t", p=128)[:, :, c * 128:(c + 1) * 128])
                ctx.wait(v, "Bpw%d0" % m)
                ctx.wait(v, "Bpw%d1" % m)
                ctx.wait(v, "Bx2v", ctx.cnt("Bx2v"))
                ctx.inc(v.tensor_copy(out=prod[:], in_=xst[m][:]), "Bxb", 1)
                for eng in (v, s, nc.tensor):
                    ctx.wait(eng, "Bld1")
                    ctx.wait(eng, "Bld5")
                ctx.wait(nc.tensor, "Bxb")
                ctx.wait(s, "Bx2s", ctx.cnt("Bx2s"))
                mv_out(ctx, prod, dga[m], h0c[m], hvc, knc[m], ktc[m], vtc, msk[m], msb,
                       pfeat, apfeat, qfeat, otok, psPT, psSS, psQQ, psQT, hsl, halfsl)
                ctx.wait(s, "BqteS")
                ctx.wait(s, "Bos", ctx.cnt("Bos"))
                for h in range(NH):
                    a1 = s.activation(out=junk[:, hsl(h)], in_=otok[:, hsl(h)], func=AF.Square,
                                      accum_out=qcolsm(m, Q_T0, h=h))
                    ctx.inc(a1, "Bos", 1)
                ctx.wait(v, "Bos")
                scm = sc[m]
                ctx.edge(v, v.tensor_scalar(out=scm[:, Q_T1 * NH:(Q_T1 + 1) * NH],
                                            in0=scm[:, Q_T0 * NH:(Q_T0 + 1) * NH],
                                            scalar1=1.0 / HD, scalar2=1e-6, op0=ALU.mult, op1=ALU.add))
                ctx.inc(v.nop(), "Bov", 1)
                ctx.wait(s, "Bov")
                a2 = s.activation(out=scm[:, Q_T2 * NH:(Q_T2 + 1) * NH],
                                  in_=scm[:, Q_T1 * NH:(Q_T1 + 1) * NH], func=AF.Sqrt)
                ctx.inc(a2, "Bos2", 1)
                ctx.wait(v, "Bos2")
                ctx.inc(v.reciprocal(out=scm[:, Q_T1 * NH:(Q_T1 + 1) * NH],
                                     in_=scm[:, Q_T2 * NH:(Q_T2 + 1) * NH]), "Bov2", 1)
                ctx.wait(s, "Bov2")
                for h in range(NH):
                    a3 = s.activation(out=qtk[m][:, hsl(h)], in_=otok[:, hsl(h)], func=AF.Identity,
                                      scale=qcolsm(m, Q_T1, h=h))
                    ctx.inc(a3, "BonS", 1)
                ctx.wait(nc.sync, "BonS")
                ctx.dma(nc.sync, "Bld7", x1c[:], ds["x1"][c * 128:(c + 1) * 128, :])
                ctx.wait(nc.tensor, "BonS")
                for gh in range(2):
                    ctx.wait(nc.tensor, "BofS", ctx.cnt("BofS"))
                    for i, h in enumerate(range(gh * 8, gh * 8 + 8)):
                        mm = nc.tensor.matmul(out=psPT[:, i * 128:(i + 1) * 128],
                                              lhsT=qtk[m][:, hsl(h)],
                                              rhs=ctx.id_bf, start=True, stop=True)
                    ctx.inc(mm, "Bofm", 1)
                    ctx.wait(s, "Bofm")
                    ctx.inc(s.activation(out=pfeat[:, halfsl(gh)], in_=psPT[:], func=AF.Copy),
                            "BofS", 1)
                ctx.wait(nc.tensor, "BofS")
                ctx.wait(nc.tensor, "Bx2v", ctx.cnt("Bx2v"))
                for dq in range(4):
                    psout = (psQQ if dq < 2 else psQT)
                    osl = psout[:, (dq % 2) * 512:(dq % 2) * 512 + 512]
                    for h in range(NH):
                        mm = nc.tensor.matmul(out=osl, lhsT=pfeat[:, hsl(h)],
                                              rhs=wo[:, h * D + dq * 512: h * D + (dq + 1) * 512],
                                              start=(h == 0), stop=(h == NH - 1))
                    ctx.inc(mm, "Bwom", 1)
                ctx.wait(v, "Bwom")
                ctx.wait(v, "Bld7")
                ctx.wait(v, "Bx2s", ctx.cnt("Bx2s"))
                for dq in range(4):
                    psout = (psQQ if dq < 2 else psQT)
                    osl = psout[:, (dq % 2) * 512:(dq % 2) * 512 + 512]
                    ad = v.tensor_tensor(out=x2c[:, dq * 512:(dq + 1) * 512], in0=osl,
                                         in1=x1c[:, dq * 512:(dq + 1) * 512], op=ALU.add)
                    ctx.inc(ad, "Bx2v", 1)
                ctx.wait(g, "Bx2v")
                ctx.dma(g, "Bx2s", ds["x2"][c * 128:(c + 1) * 128, :], x2c[:])
        ctx.wait(g, "Bx2s")


def mv_out(ctx, xb, dga, h0c, hvc, knc, ktc, vtc, msk, msb, pfeat, apfeat, qfeat,
           otok, psPT, psSS, psQQ, psQT, hsl, halfsl):
    nc = ctx.nc
    v, s = nc.vector, nc.scalar
    ctx.wait(nc.tensor, "Bx2v", ctx.cnt("Bx2v"))
    for gh in range(2):
        heads = range(gh * 8, gh * 8 + 8)
        ctx.wait(nc.tensor, "BpeS", ctx.cnt("BpeS"))
        for i, h in enumerate(heads):
            mm = nc.tensor.matmul(out=psPT[:, i * 128:(i + 1) * 128], lhsT=xb[:, hsl(h)],
                                  rhs=ctx.id_bf, start=True, stop=True)
        ctx.inc(mm, "Bptm", 1)
        ctx.wait(s, "Bptm")
        ctx.inc(s.activation(out=pfeat[:, halfsl(gh)], in_=psPT[:], func=AF.Copy), "BpeS", 1)
        ctx.wait(nc.tensor, "BpeS")
        for i, h in enumerate(heads):
            mm = nc.tensor.matmul(out=psPT[:, i * 128:(i + 1) * 128], lhsT=xb[:, hsl(h)],
                                  rhs=dga[:, hsl(h)], start=True, stop=True)
        ctx.inc(mm, "Bptm", 1)
        ctx.wait(s, "Bptm")
        ctx.inc(s.activation(out=apfeat[:, halfsl(gh)], in_=psPT[:], func=AF.Copy), "BpeS", 1)
        ctx.wait(nc.tensor, "Bmm", ctx.cnt("Bmm"))
        ctx.wait(nc.tensor, "BpeS")
        for i, h in enumerate(heads):
            mm = nc.tensor.matmul(out=psSS[:, i * 128:(i + 1) * 128], lhsT=vtc[:, hsl(h)],
                                  rhs=pfeat[:, hsl(h)], start=True, stop=True)
        ctx.inc(mm, "Bsm", 1)
        ctx.wait(v, "Bsm")
        ctx.inc(v.tensor_tensor(out=msb[:, halfsl(gh)], in0=msk[:, halfsl(gh)], in1=psSS[:],
                                op=ALU.mult), "Bmm", 1)
        ctx.wait(nc.tensor, "BqeS", ctx.cnt("BqeS"))
        ctx.wait(nc.tensor, "Bmm")
        for i, h in enumerate(heads):
            osl = psQQ[:, i * 128:(i + 1) * 128]
            nc.tensor.matmul(out=osl, lhsT=hvc[:, hsl(h)], rhs=apfeat[:, hsl(h)],
                             start=True, stop=False)
            mm = nc.tensor.matmul(out=osl, lhsT=knc[:, hsl(h)], rhs=msb[:, hsl(h)],
                                  start=False, stop=True)
        ctx.inc(mm, "Bqm", 1)
        ctx.wait(s, "Bqm")
        ctx.inc(s.activation(out=qfeat[:, halfsl(gh)], in_=psQQ[:], func=AF.Copy), "BqeS", 1)
        ctx.wait(nc.tensor, "BqteS", ctx.cnt("BqteS"))
        ctx.wait(nc.tensor, "BqeS")
        for i, h in enumerate(heads):
            mm = nc.tensor.matmul(out=psQT[:, i * 128:(i + 1) * 128], lhsT=qfeat[:, hsl(h)],
                                  rhs=ctx.id_bf, start=True, stop=True)
        ctx.inc(mm, "Bqtm", 1)
        ctx.wait(s, "Bqtm")
        ao = s.activation(out=otok[:, halfsl(gh)], in_=psQT[:], func=AF.Copy)
        ctx.inc(ao, "BqteS", 1)


_PROGRAM_CACHE = {}


def _get_program(do_attn, taps=()):
    key = (do_attn, tuple(sorted(taps)))
    if key not in _PROGRAM_CACHE:
        _PROGRAM_CACHE[key] = build_program(do_attn=do_attn, taps=taps)
    return _PROGRAM_CACHE[key]


def make_in_maps(inputs):
    x_full = np.ascontiguousarray(inputs["L0_attn_x"].reshape(T, D))
    base = {
        "gate_up": np.ascontiguousarray(inputs["L0_ffn_gate_up_proj"]),
        "down": np.ascontiguousarray(inputs["L0_ffn_down_proj"]),
        "sh_gate": np.ascontiguousarray(inputs["L0_ffn_sh_gate_w"]),
        "sh_up": np.ascontiguousarray(inputs["L0_ffn_sh_up_w"]),
        "sh_down": np.ascontiguousarray(inputs["L0_ffn_sh_down_w"]),
        "shared_gate": np.ascontiguousarray(inputs["L0_ffn_shared_gate_w"].reshape(1, D)),
        "router": np.ascontiguousarray(inputs["L0_ffn_router_weight"]),
        "Wq": np.ascontiguousarray(inputs["L0_attn_Wq"]),
        "Wk": np.ascontiguousarray(inputs["L0_attn_Wk"]),
        "Wv": np.ascontiguousarray(inputs["L0_attn_Wv"]),
        "Wo": np.ascontiguousarray(inputs["L0_attn_Wo"]),
        "Wa": np.ascontiguousarray(inputs["L0_attn_Wa"]),
        "Wb": np.ascontiguousarray(inputs["L0_attn_Wb"]),
        "ba": np.ascontiguousarray(inputs["L0_attn_ba"].reshape(1, NH)),
        "bb": np.ascontiguousarray(inputs["L0_attn_bb"].reshape(1, NH)),
        "lamb": np.ascontiguousarray(inputs["L0_attn_lambda_params"].reshape(NH, HD)),
        "onw": np.ascontiguousarray(inputs["L0_attn_o_norm_weight"].reshape(HD, 1)),
    }
    in_maps = []
    for c in range(NCORES):
        m = dict(base)
        m["x"] = np.ascontiguousarray(x_full[c * TL:(c + 1) * TL])
        in_maps.append(m)
    return in_maps


def kernel(**inputs):
    nc = _get_program(DO_ATTN)
    in_maps = make_in_maps(inputs)
    res = run_bass_kernel_spmd(nc, in_maps, core_ids=list(range(NCORES)))
    y = np.concatenate([res.results[c]["y"] for c in range(NCORES)], axis=0)
    return y.reshape(1, T, D)


# version 09
# speedup vs baseline: 1.0611x; 1.0611x over previous
import numpy as np
import ml_dtypes
import concourse.bass as bass
import concourse.mybir as mybir
from concourse.bass_utils import run_bass_kernel_spmd
from contextlib import ExitStack

F32 = mybir.dt.float32
BF16 = mybir.dt.bfloat16
ALU = mybir.AluOpType
AF = mybir.ActivationFunctionType
AX = mybir.AxisListType

NCORES = 8
T = 16384
TL = T // NCORES
D = 2048
NH = 16
HD = 128
CT = 128
NCH = TL // CT
CGI = 30
E = 8
IM = 1408
ISH = 5632
NT = TL // 128
NDT = D // 128
LAMB_LB = 0.25

BF = ml_dtypes.bfloat16
DO_ATTN = True


class Ctx:
    pass


def _emit_common(ctx):
    nc = ctx.nc
    ee = ctx.stack.enter_context
    ctx.sems = {}
    ctx.semcnt = {}
    ctx.semgen = [0]

    def sem(name):
        if name not in ctx.sems:
            ctx.sems[name] = ee(nc.semaphore(f"{name}_g{ctx.semgen[0]}"))
            ctx.semcnt[name] = 0
        return ctx.sems[name]

    def inc(instr, name, k=1):
        instr.then_inc(sem(name), k)
        ctx.semcnt[name] += k
        return ctx.semcnt[name]

    def wait(eng, name, val=None):
        s = sem(name)
        eng.wait_ge(s, ctx.semcnt[name] if val is None else val)

    def cnt(name):
        sem(name)
        return ctx.semcnt[name]

    def barrier():
        nc.all_engine_barrier(sem_only=True)

    def flush_sems():
        names = list(ctx.sems)
        if not names:
            return
        for n in names:
            nc.gpsimd.wait_ge(ctx.sems[n], ctx.semcnt[n])
        barrier()
        nc.clear_and_free_semaphores([ctx.sems[n] for n in names])
        barrier()
        for n in names:
            del ctx.sems[n]
            del ctx.semcnt[n]
        ctx.semgen[0] += 1
    ctx.flush_sems = flush_sems

    _ENG_NAME = {}

    def edge(eng, instr):
        nm = "selfedge_" + str(id(eng) if eng not in _ENG_NAME else _ENG_NAME[eng])
        nm = "se_" + type(eng).__name__
        inc(instr, nm, 1)
        wait(eng, nm)
        return instr

    def dma(eng, name, out, in_, **kw):
        wait(eng, name)
        d = eng.dma_start(out=out, in_=in_, **kw)
        inc(d, name, 16)
        return d
    ctx.dma = dma

    ctx.sem, ctx.inc, ctx.wait, ctx.cnt, ctx.barrier, ctx.edge = sem, inc, wait, cnt, barrier, edge


def build_program(do_attn=DO_ATTN, taps=(), stages=99):
    nc = bass.Bass("TRN2", target_bir_lowering=False)
    ctx = Ctx()
    ctx.nc = nc
    ctx.stack = ExitStack()
    ctx.stack.__enter__()
    _emit_common(ctx)
    ee = ctx.stack.enter_context

    id_bf_d = nc.inline_tensor(np.eye(128, dtype=BF), "id_bf_d").ap()
    id_f32_d = nc.inline_tensor(np.eye(128, dtype=np.float32), "id_f32_d").ap()
    id16_f32_d = nc.inline_tensor(np.eye(16, dtype=np.float32), "id16_f32_d").ap()
    tri = (np.arange(128)[:, None] <= np.arange(128)[None, :]).astype(np.float32)
    tri_f32_d = nc.inline_tensor(tri, "tri_f32_d").ap()
    ones_f32_d = nc.inline_tensor(np.ones((1, 128), dtype=np.float32), "ones_f32_d").ap()
    onescol_f32_d = nc.inline_tensor(np.ones((128, 1), dtype=np.float32), "onescol_f32_d").ap()
    onesmat_bf_d = nc.inline_tensor(np.ones((128, 128), dtype=BF), "onesmat_bf_d").ap()
    ctx.id_bf = ee(nc.sbuf_tensor("id_bf", [128, 128], BF16))[:]
    ctx.id_f32 = ee(nc.sbuf_tensor("id_f32", [128, 128], F32))[:]
    ctx.id16_f32 = ee(nc.sbuf_tensor("id16_f32", [16, 16], F32))[:]
    ctx.tri_f32 = ee(nc.sbuf_tensor("tri_f32", [128, 128], F32))[:]
    ctx.ones_f32 = ee(nc.sbuf_tensor("ones_f32", [1, 128], F32))[:]
    ctx.onescol_f32 = ee(nc.sbuf_tensor("onescol_f32", [128, 1], F32))[:]
    ctx.onesmat_bf = ee(nc.sbuf_tensor("onesmat_bf", [128, 128], BF16))[:]

    di = {}
    di["x"] = nc.dram_tensor("x", [TL, D], F32, kind="ExternalInput").ap()
    di["gate_up"] = nc.dram_tensor("gate_up", [E, 2 * IM, D], F32, kind="ExternalInput").ap()
    di["down"] = nc.dram_tensor("down", [E, D, IM], F32, kind="ExternalInput").ap()
    di["sh_gate"] = nc.dram_tensor("sh_gate", [ISH, D], F32, kind="ExternalInput").ap()
    di["sh_up"] = nc.dram_tensor("sh_up", [ISH, D], F32, kind="ExternalInput").ap()
    di["sh_down"] = nc.dram_tensor("sh_down", [D, ISH], F32, kind="ExternalInput").ap()
    di["shared_gate"] = nc.dram_tensor("shared_gate", [1, D], F32, kind="ExternalInput").ap()
    di["router"] = nc.dram_tensor("router", [E, D], F32, kind="ExternalInput").ap()
    di["Wq"] = nc.dram_tensor("Wq", [D, D], F32, kind="ExternalInput").ap()
    di["Wk"] = nc.dram_tensor("Wk", [D, D], F32, kind="ExternalInput").ap()
    di["Wv"] = nc.dram_tensor("Wv", [D, D], F32, kind="ExternalInput").ap()
    di["Wo"] = nc.dram_tensor("Wo", [D, D], F32, kind="ExternalInput").ap()
    di["Wa"] = nc.dram_tensor("Wa", [D, NH], F32, kind="ExternalInput").ap()
    di["Wb"] = nc.dram_tensor("Wb", [D, NH], F32, kind="ExternalInput").ap()
    di["ba"] = nc.dram_tensor("ba", [1, NH], F32, kind="ExternalInput").ap()
    di["bb"] = nc.dram_tensor("bb", [1, NH], F32, kind="ExternalInput").ap()
    di["lamb"] = nc.dram_tensor("lamb", [NH, HD], F32, kind="ExternalInput").ap()
    di["onw"] = nc.dram_tensor("onw", [HD, 1], F32, kind="ExternalInput").ap()
    ctx.di = di

    ctx.y = nc.dram_tensor("y", [TL, D], F32, kind="ExternalOutput").ap()

    ds = {}
    ds["gu_bf"] = nc.dram_tensor("gu_bf", [E, 2 * IM, D], BF16).ap()
    ds["down_bf"] = nc.dram_tensor("down_bf", [E, D, IM], BF16).ap()
    ds["shg_bf"] = nc.dram_tensor("shg_bf", [ISH, D], BF16).ap()
    ds["shu_bf"] = nc.dram_tensor("shu_bf", [ISH, D], BF16).ap()
    ds["shd_bf"] = nc.dram_tensor("shd_bf", [D, ISH], BF16).ap()
    ds["acc"] = nc.dram_tensor("acc", [D, TL], F32).ap()
    ds["x1"] = nc.dram_tensor("x1", [TL, D], F32).ap()
    ds["x2"] = nc.dram_tensor("x2", [TL, D], F32).ap()
    ds["crow"] = nc.dram_tensor("crow", [1, 9 * TL], F32).ap()
    ds["bq"] = nc.dram_tensor("bq", [TL, NH * HD], BF16).ap()
    ds["knat"] = nc.dram_tensor("knat_s", [TL, NH * HD], BF16).ap()
    ds["vnat"] = nc.dram_tensor("vnat_s", [TL, NH * HD], BF16).ap()
    ds["knT"] = nc.dram_tensor("knT_s", [NH * HD, TL], BF16).ap()
    ds["vT"] = nc.dram_tensor("vT_s", [NH * HD, TL], BF16).ap()
    ds["masks"] = nc.dram_tensor("masks_s", [NCH, NH, CT, CT], BF16).ap()
    ds["partials"] = nc.dram_tensor("partials_s", [NCH, 2, HD, NH * HD], F32).ap()
    ds["lpref"] = nc.dram_tensor("lpref_s", [NCH, 2, HD, NH * HD], F32).ap()
    ds["h0"] = nc.dram_tensor("h0_s", [NCH, 2, HD, NH * HD], BF16).ap()
    ds["cc_in"] = nc.dram_tensor("cc_in", [2 * HD + 1, NH * HD], F32).ap()
    ds["cc_out"] = nc.dram_tensor("cc_out", [NCORES, 2 * HD + 1, NH * HD], F32).ap()
    ds["prefixes"] = nc.dram_tensor("prefixes_s", [NCORES, 2 * HD, NH * HD], F32).ap()
    ctx.ds = ds

    ctx.taps = {}
    for name in taps:
        if name in ("ln1T", "ln3T"):
            ctx.taps[name] = nc.dram_tensor("tap_" + name, [128, NDT * TL], BF16, kind="ExternalOutput").ap()
        elif name == "comb":
            ctx.taps[name] = nc.dram_tensor("tap_" + name, [1, 9 * TL], F32, kind="ExternalOutput").ap()
        elif name in ("x1",):
            ctx.taps[name] = nc.dram_tensor("tap_" + name, [TL, D], F32, kind="ExternalOutput").ap()
        elif name == "acc1":
            ctx.taps[name] = nc.dram_tensor("tap_" + name, [D, TL], F32, kind="ExternalOutput").ap()
        elif name == "act":
            ctx.taps[name] = nc.dram_tensor("tap_" + name, [128, 11 * TL], BF16, kind="ExternalOutput").ap()

    ctx.lnT = ee(nc.sbuf_tensor("lnT", [128, NDT * TL], BF16))
    ctx.rt9 = ee(nc.sbuf_tensor("rt9", [128, NDT * 9], F32))
    ctx.tabs = ee(nc.sbuf_tensor("tabs", [128, 32], F32))

    for dstc, srcc in ((ctx.id_bf, id_bf_d), (ctx.id_f32, id_f32_d), (ctx.id16_f32, id16_f32_d),
                       (ctx.tri_f32, tri_f32_d), (ctx.ones_f32, ones_f32_d),
                       (ctx.onescol_f32, onescol_f32_d), (ctx.onesmat_bf, onesmat_bf_d)):
        ctx.dma(nc.sync, "cst", dstc, srcc)
    for eng in (nc.tensor, nc.vector, nc.scalar, nc.gpsimd, nc.sync):
        ctx.wait(eng, "cst")
    _weights_prologue(ctx)
    _build_router_T(ctx)
    _build_small_tables(ctx)
    ctx.wait(nc.gpsimd, "wcast")
    ctx.barrier()

    if stages < 2:
        ctx.stack.__exit__(None, None, None)
        return nc
    ctx.flush_sems()
    _ln_router_stage(ctx, src=di["x"], tag="L1", tap="ln1T")
    ctx.barrier()
    if "comb" in ctx.taps:
        ctx.inc(nc.gpsimd.dma_start(out=ctx.taps["comb"], in_=ds["crow"]), "tapc", 16)
        ctx.wait(nc.gpsimd, "tapc")
    ctx.barrier()
    if stages < 3:
        ctx.stack.__exit__(None, None, None)
        return nc
    ctx.flush_sems()
    _ffn_experts(ctx, tag="F1")
    ctx.barrier()
    if "acc1" in ctx.taps:
        ctx.inc(nc.gpsimd.dma_start(out=ctx.taps["acc1"], in_=ds["acc"]), "tapa", 16)
        ctx.wait(nc.gpsimd, "tapa")
    if stages < 4:
        ctx.stack.__exit__(None, None, None)
        return nc
    ctx.flush_sems()
    _residual_stage(ctx, src=di["x"], dst=ds["x1"], tag="R1", tap="x1")
    ctx.barrier()

    if stages < 5:
        ctx.stack.__exit__(None, None, None)
        return nc
    if do_attn:
        _attention(ctx)
    else:
        ctx.inc(nc.gpsimd.dma_start(out=ds["x2"], in_=ds["x1"]), "xcp", 16)
        ctx.wait(nc.gpsimd, "xcp")
    ctx.barrier()

    if stages >= 6:
        ctx.flush_sems()
        _ln_router_stage(ctx, src=ds["x2"], tag="L3", tap="ln3T")
        ctx.barrier()
    if stages >= 7:
        ctx.flush_sems()
        _ffn_experts(ctx, tag="F2")
        ctx.barrier()
    if stages >= 8:
        ctx.flush_sems()
        _residual_stage(ctx, src=ds["x2"], dst=ctx.y, tag="R2", tap=None)
        ctx.barrier()

    ctx.stack.__exit__(None, None, None)
    return nc


def _weights_prologue(ctx):
    nc = ctx.nc
    g = nc.gpsimd
    di, ds = ctx.di, ctx.ds
    for e in range(E):
        ctx.dma(g, "wcast", ds["gu_bf"][e], di["gate_up"][e])
        ctx.dma(g, "wcast", ds["down_bf"][e], di["down"][e])
    ctx.dma(g, "wcast", ds["shg_bf"], di["sh_gate"])
    ctx.dma(g, "wcast", ds["shu_bf"], di["sh_up"])
    ctx.dma(g, "wcast", ds["shd_bf"], di["sh_down"])


def _build_router_T(ctx):
    nc = ctx.nc
    with ExitStack() as es:
        stage = es.enter_context(nc.sbuf_tensor("rt_stage", [16, D], F32))
        ps = es.enter_context(nc.psum_tensor("ps_rt", [128, 16], F32))
        ctx.dma(nc.sync, "rtld", stage[0:E, :], ctx.di["router"])
        ctx.dma(nc.sync, "rtld", stage[E:E + 1, :], ctx.di["shared_gate"])
        ctx.wait(nc.tensor, "rtld")
        for dt in range(NDT):
            if dt >= 1:
                ctx.wait(nc.tensor, "rtev", dt)
            mm = nc.tensor.matmul(out=ps[:, 0:9], lhsT=stage[0:9, dt * 128:(dt + 1) * 128],
                                  rhs=ctx.id16_f32[0:9, 0:9], start=True, stop=True)
            ctx.inc(mm, "rtmm", 1)
            ctx.wait(nc.vector, "rtmm")
            ctx.inc(nc.vector.tensor_copy(out=ctx.rt9[:, dt * 9:(dt + 1) * 9], in_=ps[:, 0:9]), "rtev", 1)
        ctx.wait(nc.tensor, "rtev")
        ctx.wait(nc.sync, "rtev")


def _build_small_tables(ctx):
    nc = ctx.nc
    tabs = ctx.tabs
    with ExitStack() as es:
        lp = es.enter_context(nc.sbuf_tensor("lp_stage", [NH, HD], F32))
        t1 = es.enter_context(nc.sbuf_tensor("lp_t1", [NH, HD], F32))
        sp = es.enter_context(nc.sbuf_tensor("lp_sp", [NH, HD], F32))
        ps = es.enter_context(nc.psum_tensor("ps_lp", [128, 16], F32))
        ctx.dma(nc.sync, "lpld", lp[:], ctx.di["lamb"])
        ctx.dma(nc.sync, "lpld", tabs[0:HD, 16:17], ctx.di["onw"])
        v, s = nc.vector, nc.scalar
        ctx.wait(s, "lpld")
        ctx.inc(s.activation(out=t1[:], in_=lp[:], func=AF.Abs), "lps", 1)
        ctx.wait(v, "lps")
        ctx.inc(v.tensor_scalar(out=t1[:], in0=t1[:], scalar1=-1.0, scalar2=None, op0=ALU.mult), "lpv", 1)
        ctx.wait(s, "lpv")
        ctx.inc(s.activation(out=t1[:], in_=t1[:], func=AF.Exp), "lps", 1)
        ctx.wait(v, "lps")
        ctx.inc(v.tensor_scalar(out=t1[:], in0=t1[:], scalar1=1.0, scalar2=None, op0=ALU.add), "lpv", 1)
        ctx.wait(s, "lpv")
        ctx.inc(s.activation(out=t1[:], in_=t1[:], func=AF.Ln), "lps", 1)
        ctx.wait(v, "lps")
        ctx.edge(v, v.tensor_scalar(out=sp[:], in0=lp[:], scalar1=0.0, scalar2=None, op0=ALU.max))
        ctx.edge(v, v.tensor_tensor(out=sp[:], in0=sp[:], in1=t1[:], op=ALU.add))
        ctx.inc(v.tensor_scalar(out=sp[:], in0=sp[:], scalar1=LAMB_LB, scalar2=None, op0=ALU.add), "lpv", 1)
        ctx.wait(nc.tensor, "lpv")
        ctx.inc(nc.tensor.matmul(out=ps[:, 0:16], lhsT=sp[:], rhs=ctx.id16_f32, start=True, stop=True), "lpmm", 1)
        ctx.wait(v, "lpmm")
        ctx.inc(v.tensor_copy(out=tabs[:, 0:16], in_=ps[:, 0:16]), "lpev", 1)
        ctx.wait(nc.tensor, "lpev")
        ctx.wait(nc.sync, "lpev")


def _ln_router_stage(ctx, src, tag, tap, do_router=True):
    import os
    RMODE = int(os.environ.get("RMODE", "2")) if do_router else 0
    TMODE = int(os.environ.get("TMODE", "2"))
    LTILES = int(os.environ.get("LTILES", str(NT)))
    """LN(src) -> ctx.lnT (bf16 feature-major [:, dt*TL+tok]); fp32 router -> ctx.crow
    (cols c*TL+tok: c 0..7 = 0.5*combine_c, c 8 = 0.5*sigmoid(shared))."""
    nc = ctx.nc
    lnT = ctx.lnT
    crow = ctx.ds["crow"]
    tg = tag
    v, s = nc.vector, nc.scalar
    with ExitStack() as es:
        xs = [es.enter_context(nc.sbuf_tensor(f"{tg}_x{i}", [128, D], F32)) for i in range(2)]
        lntok = es.enter_context(nc.sbuf_tensor(tg + "_lntok", [128, D], F32))
        lnf = [es.enter_context(nc.sbuf_tensor(f"{tg}_lnf{i}", [128, 128], F32)) for i in range(2)]
        st = es.enter_context(nc.sbuf_tensor(tg + "_st", [128, 8], F32))
        pr = es.enter_context(nc.sbuf_tensor(tg + "_pr", [128, 16], F32))
        q2 = es.enter_context(nc.sbuf_tensor(tg + "_q2", [128, 16], F32))
        comb = es.enter_context(nc.sbuf_tensor(tg + "_comb", [128, 16], F32))
        ps_t = [es.enter_context(nc.psum_tensor(f"{tg}_ps_t{i}", [128, 128], F32)) for i in range(2)]
        ps_lg = es.enter_context(nc.psum_tensor(tg + "_ps_lg", [128, 16], F32))
        ps_cr = es.enter_context(nc.psum_tensor(tg + "_ps_cr", [1, 128], F32))
        crs = es.enter_context(nc.sbuf_tensor(tg + "_crs", [1, 128], F32))

        for tt in range(LTILES):
            buf = xs[tt % 2]
            if tt >= 2:
                ctx.wait(nc.sync, tg + "app", tt - 1)
            ctx.dma(nc.sync, tg + "x%d" % (tt % 2), buf[:], src[tt * 128:(tt + 1) * 128, :])
            if tt >= 1:
                ctx.wait(s, tg + "t", tt * NDT)
            ctx.wait(s, tg + "x%d" % (tt % 2))
            a1 = s.activation(out=lntok[:], in_=buf[:], func=AF.Copy, accum_out=st[:, 0:1])
            ctx.inc(a1, tg + "s", 1)
            ctx.wait(s, tg + "s")
            ctx.inc(s.activation(out=lntok[:], in_=buf[:], func=AF.Square, accum_out=st[:, 1:2]), tg + "s", 1)
            ctx.wait(v, tg + "s")
            v.tensor_scalar(out=st[:, 2:3], in0=st[:, 0:1], scalar1=1.0 / D, scalar2=None, op0=ALU.mult)
            ctx.edge(v, v.tensor_scalar(out=st[:, 3:4], in0=st[:, 1:2], scalar1=1.0 / D, scalar2=None, op0=ALU.mult))
            ctx.edge(v, v.scalar_tensor_tensor(out=st[:, 4:5], in0=st[:, 2:3], scalar=st[:, 2:3], in1=st[:, 3:4],
                                               op0=ALU.mult, op1=ALU.subtract))
            ctx.inc(v.tensor_scalar(out=st[:, 4:5], in0=st[:, 4:5], scalar1=-1.0, scalar2=1e-5,
                                    op0=ALU.mult, op1=ALU.add), tg + "v", 1)
            ctx.wait(s, tg + "v")
            ctx.inc(s.activation(out=st[:, 5:6], in_=st[:, 4:5], func=AF.Sqrt), tg + "s", 1)
            if TMODE < 1:
                ctx.inc(v.nop(), tg + "app", 1)
                continue
            ctx.wait(v, tg + "s")
            v.reciprocal(out=st[:, 6:7], in_=st[:, 5:6])
            ctx.edge(v, v.tensor_scalar(out=st[:, 7:8], in0=st[:, 2:3], scalar1=-1.0, scalar2=None, op0=ALU.mult))
            app = v.tensor_scalar(out=lntok[:], in0=buf[:], scalar1=st[:, 7:8], scalar2=st[:, 6:7],
                                  op0=ALU.add, op1=ALU.mult)
            ctx.inc(app, tg + "app", 1)
            ctx.wait(nc.tensor, tg + "app")
            if TMODE < 2:
                continue
            for dt in range(NDT):
                gidx = tt * NDT + dt
                half = dt % 2
                if gidx >= 2:
                    ctx.wait(nc.tensor, tg + "ev", (gidx - 1) * 2)
                mm = nc.tensor.matmul(out=ps_t[half][:],
                                      lhsT=lntok[:, dt * 128:(dt + 1) * 128],
                                      rhs=ctx.id_f32, start=True, stop=True)
                ctx.inc(mm, tg + "t", 1)
                if RMODE >= 1 and gidx >= 2:
                    ctx.wait(v, tg + "lgm", gidx - 1)
                ctx.wait(v, tg + "t")
                c1 = v.tensor_copy(out=lnf[half][:], in_=ps_t[half][:])
                ctx.inc(c1, tg + "ev", 1)
                ctx.wait(v, tg + "ev")
                ctx.inc(v.tensor_copy(out=lnT[:, dt * TL + tt * 128: dt * TL + (tt + 1) * 128],
                                      in_=lnf[half][:]), tg + "ev", 1)
                if RMODE >= 1:
                    if dt == 0 and tt >= 1:
                        ctx.wait(nc.tensor, tg + "prc", tt)
                    ctx.wait(nc.tensor, tg + "ev")
                    mm2 = nc.tensor.matmul(out=ps_lg[:, 0:9], lhsT=lnf[half][:],
                                           rhs=ctx.rt9[:, dt * 9:(dt + 1) * 9],
                                           start=(dt == 0), stop=(dt == NDT - 1))
                    ctx.inc(mm2, tg + "lgm", 1)
            if RMODE < 1:
                continue
            if tt >= 1:
                ctx.wait(v, tg + "crmm", tt * 9)
            ctx.wait(v, tg + "lgm")
            ctx.wait(v, tg + "sg")
            ctx.inc(v.tensor_copy(out=pr[:, 0:9], in_=ps_lg[:, 0:9]), tg + "prc", 1)
            ctx.wait(s, tg + "prc")
            ctx.inc(s.activation(out=pr[:, 9:10], in_=pr[:, 8:9], func=AF.Sigmoid), tg + "sg", 1)
            ctx.wait(v, tg + "prc")
            ctx.edge(v, v.reduce_max(out=q2[:, 0:1], in_=pr[:, 0:8], axis=AX.X))
            ctx.inc(v.tensor_scalar(out=q2[:, 1:2], in0=q2[:, 0:1], scalar1=-1.0, scalar2=None, op0=ALU.mult),
                    tg + "pv", 1)
            ctx.wait(s, tg + "pv")
            ctx.inc(s.activation(out=pr[:, 0:8], in_=pr[:, 0:8], func=AF.Exp, bias=q2[:, 1:2]), tg + "sg", 1)
            ctx.wait(v, tg + "sg")
            ctx.edge(v, v.tensor_scalar(out=pr[:, 0:8], in0=pr[:, 0:8], scalar1=1.0, scalar2=0.0,
                                         op0=ALU.mult, op1=ALU.add, accum_out=q2[:, 2:3]))
            ctx.edge(v, v.reciprocal(out=q2[:, 3:4], in_=q2[:, 2:3]))
            ctx.edge(v, v.tensor_scalar(out=pr[:, 0:8], in0=pr[:, 0:8], scalar1=q2[:, 3:4], scalar2=None, op0=ALU.mult))
            ctx.edge(v, v.reduce_max(out=q2[:, 4:5], in_=pr[:, 0:8], axis=AX.X))
            ctx.edge(v, v.tensor_scalar(out=q2[:, 8:16], in0=pr[:, 0:8], scalar1=q2[:, 4:5], scalar2=None, op0=ALU.is_equal))
            ctx.edge(v, v.scalar_tensor_tensor(out=comb[:, 8:16], in0=q2[:, 8:16], scalar=-1e30, in1=pr[:, 0:8],
                                               op0=ALU.mult, op1=ALU.add))
            ctx.edge(v, v.reduce_max(out=q2[:, 5:6], in_=comb[:, 8:16], axis=AX.X))
            v.tensor_scalar(out=comb[:, 0:8], in0=comb[:, 8:16], scalar1=q2[:, 5:6], scalar2=None, op0=ALU.is_equal)
            ctx.edge(v, v.tensor_tensor(out=q2[:, 6:7], in0=q2[:, 4:5], in1=q2[:, 5:6], op=ALU.add))
            ctx.edge(v, v.reciprocal(out=q2[:, 7:8], in_=q2[:, 6:7]))
            ctx.edge(v, v.tensor_tensor(out=comb[:, 0:8], in0=comb[:, 0:8], in1=q2[:, 8:16], op=ALU.add))
            ctx.edge(v, v.tensor_tensor(out=comb[:, 0:8], in0=comb[:, 0:8], in1=pr[:, 0:8], op=ALU.mult))
            ctx.edge(v, v.tensor_scalar(out=comb[:, 0:8], in0=comb[:, 0:8], scalar1=q2[:, 7:8], scalar2=0.5,
                                        op0=ALU.mult, op1=ALU.mult))
            ctx.wait(v, tg + "sg")
            ctx.inc(v.tensor_scalar(out=comb[:, 8:9], in0=pr[:, 9:10], scalar1=0.5, scalar2=None, op0=ALU.mult),
                    tg + "cmb", 1)
            if RMODE < 2:
                continue
            ctx.wait(nc.tensor, tg + "cmb")
            for c in range(9):
                cidx = tt * 9 + c
                if cidx >= 1:
                    ctx.wait(nc.tensor, tg + "crev", cidx)
                mmc = nc.tensor.matmul(out=ps_cr[0:1, :], lhsT=comb[:, c:c + 1], rhs=ctx.id_f32,
                                       start=True, stop=True)
                ctx.inc(mmc, tg + "crmm", 1)
                if cidx >= 1:
                    ctx.wait(v, tg + "crst", 16 * cidx)
                ctx.wait(v, tg + "crmm")
                ctx.inc(v.tensor_copy(out=crs[0:1, :], in_=ps_cr[0:1, :]), tg + "crev", 1)
                ctx.wait(nc.sync, tg + "crev")
                ctx.inc(nc.sync.dma_start(out=crow[0:1, c * TL + tt * 128: c * TL + (tt + 1) * 128],
                                          in_=crs[0:1, :]), tg + "crst", 16)
        ctx.wait(nc.sync, tg + "ev")
        if RMODE >= 2:
            ctx.wait(nc.sync, tg + "crst")
    if tap in ctx.taps:
        ctx.inc(nc.sync.dma_start(out=ctx.taps[tap], in_=lnT[:]), tg + "tap", 16)
        ctx.wait(nc.sync, tg + "tap")


def _ffn_experts(ctx, tag):
    nc = ctx.nc
    tg = tag
    lnT, ds = ctx.lnT, ctx.ds
    crow = ds["crow"]
    v, s, g = nc.vector, nc.scalar, nc.gpsimd
    import os
    NPE = int(os.environ.get("NPE", "0"))
    NHT = 11
    pes = []
    for j in range(4):
        gblocks = []
        for srcname in ("shg_bf", "shu_bf"):
            base = j * IM
            for off, nt_ in ((0, 4), (512, 4), (1024, 3)):
                gblocks.append((ds[srcname], base + off, nt_))
        pes.append((gblocks, ds["shd_bf"], j * IM, 8))
    for e in range(E):
        gblocks = []
        for off, nt_ in ((0, 4), (512, 4), (1024, 3), (IM, 4), (IM + 512, 4), (IM + 1024, 3)):
            gblocks.append((ds["gu_bf"][e], off, nt_))
        pes.append((gblocks, ds["down_bf"][e], 0, e))
    if NPE:
        pes = pes[:NPE]

    with ExitStack() as es:
        guT = [es.enter_context(nc.sbuf_tensor(f"{tg}_guT{i}", [128, 16 * 512], BF16)) for i in range(2)]
        dnT = [es.enter_context(nc.sbuf_tensor(f"{tg}_dnT{i}", [128, NHT * 1024], BF16)) for i in range(2)]
        crow_sb = es.enter_context(nc.sbuf_tensor(tg + "_crow_sb", [1, TL], F32))
        act = es.enter_context(nc.sbuf_tensor(tg + "_act", [128, NHT * TL], BF16))
        cbc = es.enter_context(nc.sbuf_tensor(tg + "_cbc", [128, TL], BF16))
        stg = [es.enter_context(nc.sbuf_tensor(f"{tg}_stg{i}", [128, 512], F32)) for i in range(2)]
        ps_gu = [es.enter_context(nc.psum_tensor(f"{tg}_psgu{i}", [128, 512], F32)) for i in range(2)]
        ps_o = [es.enter_context(nc.psum_tensor(f"{tg}_pso{i}", [128, 512], F32)) for i in range(2)]

        gu_slot_ev = [None, None]
        o_slot_ev = [0, 0]
        acq_cnt = [0, 0]
        gud_cnt = [0, 0]
        dnd_cnt = [0, 0]
        blk_ev_need = [0, 0]

        for p, (gblocks, dn_src, dn_coloff, c_idx) in enumerate(pes):
            if p >= 1:
                ctx.wait(s, tg + "om", p * 64)
                ctx.wait(v, tg + "om", p * 64)
            if p >= 1:
                ctx.wait(nc.gpsimd, tg + "cbm", p * 4)
            ctx.inc(nc.gpsimd.dma_start(out=crow_sb[0:1, :], in_=crow[0:1, c_idx * TL:(c_idx + 1) * TL]),
                    tg + "crl", 16)
            ctx.wait(nc.tensor, tg + "crl", 16 * (p + 1))
            if gu_slot_ev[0] is not None:
                ctx.wait(nc.tensor, gu_slot_ev[0][0], gu_slot_ev[0][1])
            ctx.wait(s, tg + "guevV")
            for hh in range(4):
                mcb = nc.tensor.matmul(out=ps_gu[0][:], lhsT=ctx.ones_f32,
                                       rhs=crow_sb[0:1, hh * 512:(hh + 1) * 512],
                                       start=True, stop=True)
                ctx.inc(mcb, tg + "cbm", 1)
                ctx.wait(s, tg + "cbm")
                ctx.inc(s.activation(out=cbc[:, hh * 512:(hh + 1) * 512], in_=ps_gu[0][:], func=AF.Copy),
                        tg + "cbev", 1)
                ctx.wait(nc.tensor, tg + "cbev")


            ht_base = 0
            blki = 0
            ht_global = 0
            for (gsrc, row_off, ntile) in gblocks:
                par = blki % 2
                if os.environ.get("GUSER", "0") == "1":
                    ctx.wait(nc.sync, tg + "gum", ctx.cnt(tg + "gum"))
                else:
                    ctx.wait(nc.sync, tg + "gum", blk_ev_need[par])
                for dt in range(NDT):
                    ctx.dma(nc.sync, tg + "gud%d" % par,
                            guT[par][:, dt * 512: dt * 512 + ntile * 128],
                            gsrc[row_off:row_off + ntile * 128, dt * 128:(dt + 1) * 128],
                            transpose=True)
                gud_cnt[par] = ctx.cnt(tg + "gud%d" % par)
                ctx.wait(nc.tensor, tg + "gud%d" % par, gud_cnt[par])
                for htb in range(ntile):
                    ht = ht_global
                    ht_global += 1
                    for tq in range(4):
                        slot = (ht * 4 + tq) % 2
                        if gu_slot_ev[slot] is not None:
                            ctx.wait(nc.tensor, gu_slot_ev[slot][0], gu_slot_ev[slot][1])
                        for dt in range(NDT):
                            mmg = nc.tensor.matmul(out=ps_gu[slot][:],
                                                   lhsT=guT[par][:, dt * 512 + htb * 128: dt * 512 + (htb + 1) * 128],
                                                   rhs=lnT[:, dt * TL + tq * 512: dt * TL + (tq + 1) * 512],
                                                   start=(dt == 0), stop=(dt == NDT - 1))
                        gum = ctx.inc(mmg, tg + "gum", 1)
                        if ht < NHT:
                            ctx.wait(s, tg + "gum", gum)
                            ev = s.activation(out=act[:, ht * TL + tq * 512: ht * TL + (tq + 1) * 512],
                                              in_=ps_gu[slot][:], func=AF.Silu)
                            gu_slot_ev[slot] = (tg + "guevS", ctx.inc(ev, tg + "guevS", 1))
                        else:
                            jslot = ht - NHT
                            asl = act[:, jslot * TL + tq * 512: jslot * TL + (tq + 1) * 512]
                            ctx.wait(v, tg + "gum", gum)
                            ctx.edge(v, v.tensor_tensor(out=asl, in0=asl, in1=cbc[:, tq * 512:(tq + 1) * 512],
                                                        op=ALU.mult))
                            ev = v.tensor_tensor(out=asl, in0=asl, in1=ps_gu[slot][:], op=ALU.mult)
                            gu_slot_ev[slot] = (tg + "guevV", ctx.inc(ev, tg + "guevV", 1))
                blki += 1
                if blki >= 2:
                    pass
                blk_ev_need[par] = ctx.cnt(tg + "gum")

            if os.environ.get("NODND", "0") == "1":
                continue
            ctx.wait(nc.tensor, tg + "guevS")
            ctx.wait(nc.tensor, tg + "guevV")
            for dh in range(2):
                dpar = dh % 2
                if p * 2 + dh >= 2:
                    ctx.wait(nc.sync, tg + "om", (p * 2 + dh - 1) * 32)
                for ht in range(NHT):
                    ctx.dma(nc.sync, tg + "dnd%d" % dpar,
                            dnT[dpar][:, ht * 1024:(ht + 1) * 1024],
                            dn_src[dh * 1024:(dh + 1) * 1024,
                                   dn_coloff + ht * 128: dn_coloff + (ht + 1) * 128],
                            transpose=True)
                dnd_cnt[dpar] = ctx.cnt(tg + "dnd%d" % dpar)
                ctx.wait(nc.tensor, tg + "dnd%d" % dpar, dnd_cnt[dpar])
                for dto in range(8):
                    dt = dh * 8 + dto
                    for tq in range(4):
                        kk = p * 64 + dt * 4 + tq
                        slot = kk % 2
                        ctx.wait(nc.tensor, tg + "oev", o_slot_ev[slot])
                        for ht in range(NHT):
                            mmo = nc.tensor.matmul(out=ps_o[slot][:],
                                                   lhsT=dnT[dpar][:, ht * 1024 + dto * 128: ht * 1024 + (dto + 1) * 128],
                                                   rhs=act[:, ht * TL + tq * 512: ht * TL + (tq + 1) * 512],
                                                   start=(ht == 0), stop=(ht == NHT - 1))
                        om = ctx.inc(mmo, tg + "om", 1)
                        ctx.wait(v, tg + "om", om)
                        ctx.wait(v, tg + "acq%d" % slot, acq_cnt[slot])
                        cp = v.tensor_copy(out=stg[slot][:], in_=ps_o[slot][:])
                        o_slot_ev[slot] = ctx.inc(cp, tg + "oev", 1)
                        ctx.wait(g, tg + "oev", o_slot_ev[slot])
                        ad = g.dma_start(out=ds["acc"][dt * 128:(dt + 1) * 128, tq * 512:(tq + 1) * 512],
                                         in_=stg[slot][:],
                                         accum_op=(ALU.bypass if p == 0 else ALU.add))
                        acq_cnt[slot] = ctx.inc(ad, tg + "acq%d" % slot, 16)
        ctx.wait(g, tg + "acq0")
        ctx.wait(g, tg + "acq1")
        if "act" in ctx.taps and tg == "F1":
            ctx.wait(g, tg + "guevS")
            ctx.wait(g, tg + "guevV")
            ctx.inc(g.dma_start(out=ctx.taps["act"], in_=act[:]), tg + "tact", 16)
            ctx.wait(g, tg + "tact")


def _residual_stage(ctx, src, dst, tag, tap):
    nc = ctx.nc
    tg = tag
    ds = ctx.ds
    v, s = nc.vector, nc.scalar
    with ExitStack() as es:
        xs = [es.enter_context(nc.sbuf_tensor(f"{tg}_x{i}", [128, D], F32)) for i in range(2)]
        at = [es.enter_context(nc.sbuf_tensor(f"{tg}_at{i}", [128, 128], F32)) for i in range(2)]
        xn = [es.enter_context(nc.sbuf_tensor(f"{tg}_xn{i}", [128, D], F32)) for i in range(2)]
        ps_r = es.enter_context(nc.psum_tensor(tg + "_ps_r", [128, D], F32))
        st_counts = []
        for tt in range(NT):
            bp = tt % 2
            if tt >= 2:
                ctx.wait(nc.sync, tg + "add", tt - 1)
            ctx.dma(nc.sync, tg + "x%d" % bp, xs[bp][:], src[tt * 128:(tt + 1) * 128, :])
            for dt in range(NDT):
                gidx = tt * NDT + dt
                ap = dt % 2
                if gidx >= 2:
                    ctx.wait(nc.sync, tg + "tp", gidx - 1)
                dd = nc.sync.dma_start(out=at[ap][:, 0:128],
                                       in_=ds["acc"][dt * 128:(dt + 1) * 128, tt * 128:(tt + 1) * 128])
                abn = ctx.inc(dd, tg + "ab%d" % ap, 16)
                ctx.wait(nc.tensor, tg + "ab%d" % ap, abn)
                if dt == 0 and tt >= 1:
                    ctx.wait(nc.tensor, tg + "add", tt)
                mm = nc.tensor.matmul(out=ps_r[:, dt * 128:(dt + 1) * 128], lhsT=at[ap][:, 0:128],
                                      rhs=ctx.id_f32, start=True, stop=True)
                ctx.inc(mm, tg + "tp", 1)
            ctx.wait(v, tg + "tp")
            ctx.wait(v, tg + "x%d" % bp)
            if tt >= 2:
                ctx.wait(v, tg + "st%d" % bp, st_counts[tt - 2])
            add = v.tensor_tensor(out=xn[bp][:], in0=ps_r[:], in1=xs[bp][:], op=ALU.add)
            ctx.inc(add, tg + "add", 1)
            ctx.wait(nc.sync, tg + "add")
            ctx.inc(nc.sync.dma_start(out=dst[tt * 128:(tt + 1) * 128, :], in_=xn[bp][:]), tg + "st%d" % bp, 16)
            if tap is not None and tap in ctx.taps:
                ctx.inc(nc.sync.dma_start(out=ctx.taps[tap][tt * 128:(tt + 1) * 128, :], in_=xn[bp][:]),
                        tg + "st%d" % bp, 16)
            st_counts.append(ctx.cnt(tg + "st%d" % bp))
        ctx.wait(nc.sync, tg + "st0")
        ctx.wait(nc.sync, tg + "st1")
        ctx.wait(nc.vector, tg + "st0")
        ctx.wait(nc.vector, tg + "st1")



def _attention(ctx):
    nc = ctx.nc
    ds, di = ctx.ds, ctx.di
    v, s, g = nc.vector, nc.scalar, nc.gpsimd

    ctx.flush_sems()
    _ln_router_stage(ctx, src=ds["x1"], tag="L2", tap=None, do_router=False)
    ctx.barrier()

    with ExitStack() as es:
        ee = es.enter_context
        ctx.gt = ee(nc.sbuf_tensor("A_gt", [128, NCH * NH], F32))
        ctx.ngt = ee(nc.sbuf_tensor("A_ngt", [128, NCH * NH], F32))
        ctx.bt = ee(nc.sbuf_tensor("A_bt", [128, NCH * NH], F32))
        ctx.aCtab = ee(nc.sbuf_tensor("A_aCtab", [1, (NCH + 1) * NH], F32))
        ctx.prodtab = ee(nc.sbuf_tensor("A_prodtab", [1, (NCH + 1) * NH], F32))
        ctx.acbc = ee(nc.sbuf_tensor("A_acbc", [128, NCH * NH], F32))
        ctx.prodbc = ee(nc.sbuf_tensor("A_prodbc", [128, NCH * NH], F32))
        ctx.lamdiag = ee(nc.sbuf_tensor("A_lamdiag", [128, NH * HD], BF16))
        ctx.lambc = ee(nc.sbuf_tensor("A_lambc", [128, NH * HD], BF16))
        ctx.flush_sems()
        _attn_proj(ctx)
        ctx.barrier()
        ctx.flush_sems()
        _attn_phase_a(ctx)
        ctx.barrier()
        ctx.flush_sems()
        _attn_phase_b(ctx)
        ctx.barrier()


def _attn_proj(ctx):
    nc = ctx.nc
    ds, di = ctx.ds, ctx.di
    v, s, g = nc.vector, nc.scalar, nc.gpsimd
    lnT = ctx.lnT

    with ExitStack() as es:
        ee = es.enter_context
        qh_sb = ee(nc.sbuf_tensor("A_qh", [128, TL], BF16))
        qtok = ee(nc.sbuf_tensor("A_qtok", [128, 128], BF16))
        ktf = ee(nc.sbuf_tensor("A_ktf", [128, 128], BF16))
        ssb = ee(nc.sbuf_tensor("A_ss", [128, 8], F32))
        junk = ee(nc.sbuf_tensor("A_junk", [128, 128], F32))
        ps_p = [ee(nc.psum_tensor(f"A_psp{i}", [128, 512], F32)) for i in range(2)]
        ps_t2 = ee(nc.psum_tensor("A_pst2", [128, 128], F32))
        ps_t3 = ee(nc.psum_tensor("A_pst3", [128, 128], F32))
        ps_bg = ee(nc.psum_tensor("A_psbg", [16, 512], F32))
        ps_sm = ee(nc.psum_tensor("A_pssm", [128, 16], F32))

        pe_slot = [None, None]
        for kind, W, tokdst in (("q", di["Wq"], ds["bq"]), ("k", di["Wk"], ds["knat"]),
                                ("v", di["Wv"], ds["vnat"])):
            tgw = "Awl" + kind
            with nc.sbuf_tensor("A_wbuf" + kind, [128, NDT * D], BF16) as wq_sb:
                ctx.dma(g, tgw, wq_sb[:].rearrange("p (a c) -> p a c", a=16),
                        W.rearrange("(a p) c -> p a c", p=128))
                ctx.wait(nc.tensor, tgw)
                for h in range(NH):
                    for tq in range(4):
                        slot = (h * 4 + tq) % 2
                        if pe_slot[slot] is not None:
                            ctx.wait(nc.tensor, pe_slot[slot][0], pe_slot[slot][1])
                        for din in range(NDT):
                            mm = nc.tensor.matmul(out=ps_p[slot][:],
                                                  lhsT=wq_sb[:, din * D + h * 128: din * D + (h + 1) * 128],
                                                  rhs=lnT[:, din * TL + tq * 512: din * TL + (tq + 1) * 512],
                                                  start=(din == 0), stop=(din == NDT - 1))
                        pm = ctx.inc(mm, "Apm", 1)
                        ctx.wait(s, "Apm", pm)
                        if tq == 0:
                            ctx.wait(s, "At2m", ctx.cnt("At2m"))
                            ctx.wait(s, "Avt", ctx.cnt("Avt"))
                        ev = s.activation(out=qh_sb[:, tq * 512:(tq + 1) * 512], in_=ps_p[slot][:],
                                          func=AF.Silu)
                        pe_slot[slot] = ("ApevS", ctx.inc(ev, "ApevS", 1))
                    if kind == "v":
                        ctx.wait(g, "ApevS")
                        ctx.dma(g, "Avt", ds["vT"][h * 128:(h + 1) * 128, :], qh_sb[:])
                    for c in range(NCH):
                        ctx.wait(nc.tensor, "ApevS")
                        ctx.wait(nc.tensor, "Aqt", ctx.cnt("Aqt"))
                        mm2 = nc.tensor.matmul(out=ps_t2[:], lhsT=qh_sb[:, c * 128:(c + 1) * 128],
                                               rhs=ctx.id_bf, start=True, stop=True)
                        ctx.inc(mm2, "At2m", 1)
                        ctx.wait(s, "At2m")
                        if kind in ("q", "k"):
                            a1 = s.activation(out=junk[:], in_=ps_t2[:], func=AF.Square, accum_out=ssb[:, 0:1])
                            ctx.inc(a1, "Asq", 1)
                            ctx.wait(s, "Asq")
                            a2 = s.activation(out=ssb[:, 1:2], in_=ssb[:, 0:1], func=AF.Sqrt)
                            ctx.inc(a2, "Asq", 1)
                            ctx.wait(v, "Asq")
                            ctx.edge(v, v.tensor_scalar(out=ssb[:, 2:3], in0=ssb[:, 1:2], scalar1=1e-12,
                                                        scalar2=None, op0=ALU.max))
                            ctx.inc(v.reciprocal(out=ssb[:, 3:4], in_=ssb[:, 2:3]), "Arn", 1)
                            ctx.wait(s, "Arn")
                            a3 = s.activation(out=qtok[:], in_=ps_t2[:], func=AF.Identity, scale=ssb[:, 3:4])
                            ctx.inc(a3, "Aqt", 1)
                        else:
                            a3 = s.activation(out=qtok[:], in_=ps_t2[:], func=AF.Copy)
                            ctx.inc(a3, "Aqt", 1)
                        ctx.wait(g, "Aqt")
                        ctx.dma(g, "Ast", tokdst[c * 128:(c + 1) * 128, h * 128:(h + 1) * 128], qtok[:])
                        if kind == "k":
                            ctx.wait(nc.tensor, "Aqt")
                            ctx.wait(nc.tensor, "At3e", ctx.cnt("At3e"))
                            mm3 = nc.tensor.matmul(out=ps_t3[:], lhsT=qtok[:], rhs=ctx.id_bf,
                                                   start=True, stop=True)
                            ctx.inc(mm3, "At3m", 1)
                            ctx.wait(v, "At3m")
                            ctx.inc(v.tensor_copy(out=ktf[:], in_=ps_t3[:]), "At3e", 1)
                            ctx.wait(g, "At3e")
                            ctx.dma(g, "Ast2", ds["knT"][h * 128:(h + 1) * 128, c * 128:(c + 1) * 128],
                                    ktf[:])
                        ctx.wait(s, "Ast")
                        ctx.wait(s, "Ast2")
                ctx.wait(nc.sync, "Ast")
                ctx.wait(nc.sync, "Ast2")
                ctx.wait(nc.sync, "Avt")
                ctx.wait(nc.scalar, "Ast")
                ctx.barrier()

        ctx.barrier()
        with nc.sbuf_tensor("A_wab", [128, NDT * 2 * NH], BF16) as wab, \
             nc.sbuf_tensor("A_basb", [NH, 2], F32) as basb, \
             nc.sbuf_tensor("A_bg16", [NH, 2 * TL], F32) as bg16, \
             nc.sbuf_tensor("A_bgt", [NH, TL], F32) as bgt:
            ctx.dma(g, "Abw", wab[:].rearrange("p (a c) -> p a c", a=16)[:, :, 0:NH],
                    di["Wa"].rearrange("(a p) c -> p a c", p=128))
            ctx.dma(g, "Abw", wab[:].rearrange("p (a c) -> p a c", a=16)[:, :, NH:2 * NH],
                    di["Wb"].rearrange("(a p) c -> p a c", p=128))
            ctx.dma(nc.sync, "Abw2", basb[:, 0:1], di["ba"].rearrange("o h -> h o"),
                    allow_slow_non_contiguous=True)
            ctx.dma(nc.sync, "Abw2", basb[:, 1:2], di["bb"].rearrange("o h -> h o"),
                    allow_slow_non_contiguous=True)
            ctx.wait(nc.tensor, "Abw")
            ctx.wait(nc.scalar, "Abw2")
            for col, nm in ((0, "ga"), (NH, "bb")):
                for tq in range(4):
                    ctx.wait(nc.tensor, "Abge", ctx.cnt("Abge"))
                    for din in range(NDT):
                        mmb = nc.tensor.matmul(out=ps_bg[:],
                                               lhsT=wab[:, din * 2 * NH + col: din * 2 * NH + col + NH],
                                               rhs=lnT[:, din * TL + tq * 512: din * TL + (tq + 1) * 512],
                                               start=(din == 0), stop=(din == NDT - 1))
                    ctx.inc(mmb, "Abgm", 1)
                    ctx.wait(s, "Abgm")
                    bcol = basb[:, 0:1] if col == 0 else basb[:, 1:2]
                    ev = s.activation(out=bg16[:, (0 if col == 0 else TL) + tq * 512:
                                               (0 if col == 0 else TL) + (tq + 1) * 512],
                                      in_=ps_bg[:], func=AF.Identity, bias=bcol)
                    ctx.inc(ev, "Abge", 1)
            ctx.wait(s, "Abge")
            ctx.inc(s.activation(out=bg16[:, TL:2 * TL], in_=bg16[:, TL:2 * TL], func=AF.Sigmoid), "Abs", 1)
            ctx.inc(s.activation(out=bgt[:], in_=bg16[:, 0:TL], func=AF.Abs), "Abs", 1)
            ctx.wait(v, "Abs")
            ctx.inc(v.tensor_scalar(out=bgt[:], in0=bgt[:], scalar1=-1.0, scalar2=None, op0=ALU.mult), "Abv", 1)
            ctx.wait(s, "Abv")
            ctx.inc(s.activation(out=bgt[:], in_=bgt[:], func=AF.Exp), "Abs", 1)
            ctx.wait(v, "Abs")
            ctx.inc(v.tensor_scalar(out=bgt[:], in0=bgt[:], scalar1=1.0, scalar2=None, op0=ALU.add), "Abv", 1)
            ctx.wait(s, "Abv")
            ctx.inc(s.activation(out=bgt[:], in_=bgt[:], func=AF.Ln), "Abs", 1)
            ctx.wait(v, "Abs")
            ctx.edge(v, v.tensor_scalar(out=junk[0:NH, 0:TL] if False else bg16[:, 0:TL],
                                        in0=bg16[:, 0:TL], scalar1=0.0, scalar2=None, op0=ALU.min))
            ctx.inc(v.tensor_tensor(out=bg16[:, 0:TL], in0=bg16[:, 0:TL], in1=bgt[:], op=ALU.subtract),
                    "Abv", 1)
            ctx.wait(nc.tensor, "Abv")
            ctx.wait(nc.tensor, "Abs")
            for c in range(NCH):
                for which, tab in ((0, None), (1, ctx.bt)):
                    srcsl = bg16[:, which * TL + c * 128: which * TL + (c + 1) * 128]
                    ctx.wait(nc.tensor, "Asme", ctx.cnt("Asme"))
                    mmt = nc.tensor.matmul(out=ps_sm[:], lhsT=srcsl, rhs=ctx.id16_f32,
                                           start=True, stop=True)
                    ctx.inc(mmt, "Asmm", 1)
                    ctx.wait(v, "Asmm")
                    dst = (ctx.gt if which == 0 else ctx.bt)[:, c * NH:(c + 1) * NH]
                    if which == 0:
                        ctx.inc(v.tensor_copy(out=ctx.ngt[:, c * NH:(c + 1) * NH], in_=ps_sm[:]), "Asme", 1)
                    else:
                        ctx.inc(v.tensor_copy(out=dst, in_=ps_sm[:]), "Asme", 1)
            ctx.wait(nc.tensor, "Asme")
            for c in range(NCH):
                ctx.wait(nc.tensor, "Agce", ctx.cnt("Agce"))
                mmc = nc.tensor.matmul(out=ps_sm[:], lhsT=ctx.tri_f32,
                                       rhs=ctx.ngt[:, c * NH:(c + 1) * NH], start=True, stop=True)
                ctx.inc(mmc, "Agcm", 1)
                ctx.wait(v, "Agcm")
                ctx.inc(v.tensor_copy(out=ctx.gt[:, c * NH:(c + 1) * NH], in_=ps_sm[:]), "Agce", 1)
            ctx.wait(v, "Agce")
            ctx.inc(v.tensor_scalar(out=ctx.ngt[:], in0=ctx.gt[:], scalar1=-1.0, scalar2=None,
                                    op0=ALU.mult), "Angt", 1)

        tabs = ctx.tabs
        for h in range(NH):
            ctx.edge(v, v.tensor_scalar(out=ctx.lamdiag[:, h * 128:(h + 1) * 128], in0=ctx.id_bf,
                                        scalar1=tabs[:, h:h + 1], scalar2=None, op0=ALU.mult))
            ctx.wait(nc.tensor, "Alr", ctx.cnt("Alr"))
            mml = nc.tensor.matmul(out=ps_t2[0:1, :], lhsT=tabs[:, h:h + 1], rhs=ctx.id_f32,
                                   start=True, stop=True)
            ctx.inc(mml, "Alm", 1)
            ctx.wait(v, "Alm")
            ctx.inc(v.tensor_copy(out=junk[0:1, :], in_=ps_t2[0:1, :]), "Alr", 1)
            ctx.wait(nc.tensor, "Alr")
            mml2 = nc.tensor.matmul(out=ps_t3[:], lhsT=ctx.ones_f32, rhs=junk[0:1, :],
                                    start=True, stop=True)
            ctx.inc(mml2, "Alm2", 1)
            ctx.wait(v, "Alm2")
            ctx.inc(v.tensor_scalar(out=ctx.lambc[:, h * 128:(h + 1) * 128], in0=ps_t3[:],
                                    scalar1=1e-8, scalar2=None, op0=ALU.add), "Alr2", 1)
            ctx.wait(nc.tensor, "Alr2")

        ctx.barrier()
        with nc.sbuf_tensor("A_wotmp", [128, D], F32) as wot:
            for din in range(NDT):
                if din >= 1:
                    ctx.wait(nc.sync, "Awos", ctx.cnt("Awos"))
                ctx.dma(nc.sync, "Awol", wot[:], di["Wo"][din * 128:(din + 1) * 128, :])
                ctx.wait(v, "Awol")
                sc = v.tensor_scalar(out=lnT[:, din * D:(din + 1) * D], in0=wot[:],
                                     scalar1=tabs[:, 16:17], scalar2=None, op0=ALU.mult)
                ctx.inc(sc, "Awos", 1)
        ctx.wait(nc.sync, "Awos")
        ctx.barrier()





def _attn_phase_a(ctx):
    nc = ctx.nc
    ds = ctx.ds
    v, s, g = nc.vector, nc.scalar, nc.gpsimd

    with ExitStack() as es:
        ee = es.enter_context
        knc = ee(nc.sbuf_tensor("Pa_knc", [128, NH * HD], BF16))
        vnc = ee(nc.sbuf_tensor("Pa_vnc", [128, NH * HD], BF16))
        grow = ee(nc.sbuf_tensor("Pa_grow", [1, 128], F32))
        arow = ee(nc.sbuf_tensor("Pa_arow", [1, 128], F32))
        w0 = ee(nc.sbuf_tensor("Pa_w0", [128, 128], F32))
        btri = ee(nc.sbuf_tensor("Pa_btri", [128, 128], F32))
        wtb = ee(nc.sbuf_tensor("Pa_wtb", [128, 128], BF16))
        wcol = ee(nc.sbuf_tensor("Pa_wcol", [128, 1], F32))
        kw = ee(nc.sbuf_tensor("Pa_kw", [128, 128], BF16))
        pstage = ee(nc.sbuf_tensor("Pa_pstage", [128, 256], F32))
        Lkk = ee(nc.sbuf_tensor("Pa_Lkk", [128, NH * HD], F32))
        Lkv = ee(nc.sbuf_tensor("Pa_Lkv", [128, NH * HD], F32))
        zrow = ee(nc.sbuf_tensor("Pa_zrow", [1, NH * HD], F32))
        hstage = ee(nc.sbuf_tensor("Pa_hstage", [128, NH * HD], F32))
        hstage2 = ee(nc.sbuf_tensor("Pa_hstage2", [128, NH * HD], F32))
        hbf = ee(nc.sbuf_tensor("Pa_hbf", [128, NH * HD], BF16))
        drow = ee(nc.sbuf_tensor("Pa_drow", [1, 128], F32))
        ps_r = ee(nc.psum_tensor("Pa_psr", [1, 128], F32))
        ps_b = ee(nc.psum_tensor("Pa_psb", [128, 128], F32))
        ps_p1 = ee(nc.psum_tensor("Pa_psp1", [128, 128], F32))
        ps_p2 = ee(nc.psum_tensor("Pa_psp2", [128, 128], F32))
        ps_ac = ee(nc.psum_tensor("Pa_psac", [128, NH], F32))

        ctx.inc(v.memset(Lkk[:], 0.0), "Painit", 1)
        ctx.inc(v.memset(Lkv[:], 0.0), "Painit", 1)
        ctx.inc(v.memset(zrow[:], 0.0), "Painit", 1)
        ctx.inc(v.memset(ctx.prodtab[0:1, 0:NH], 1.0), "Painit", 1)
        ctx.wait(v, "Painit")

        for c in range(NCH):
            if c >= 1:
                ctx.wait(nc.sync, "Pakk", ctx.cnt("Pakk"))
            ctx.dma(nc.sync, "Paldk", knc[:], ds["knat"][c * 128:(c + 1) * 128, :])
            ctx.dma(nc.sync, "Paldv", vnc[:], ds["vnat"][c * 128:(c + 1) * 128, :])
            ctx.wait(g, "Painit")
            ctx.wait(g, "Palu", ctx.cnt("Palu"))
            ctx.dma(g, "Palp0", ds["lpref"][c, 0], Lkk[:])
            ctx.dma(g, "Palp1", ds["lpref"][c, 1], Lkv[:])
            for h in range(NH):
                col = c * NH + h
                ctx.wait(nc.tensor, "Page", ctx.cnt("Page"))
                mmr = nc.tensor.matmul(out=ps_r[0:1, :], lhsT=ctx.gt[:, col:col + 1], rhs=ctx.id_f32,
                                       start=True, stop=True)
                ctx.inc(mmr, "Pagm", 1)
                ctx.wait(v, "Pagm")
                ctx.wait(v, "Paae")
                ctx.inc(v.tensor_copy(out=grow[0:1, :], in_=ps_r[0:1, :]), "Page", 1)
                ctx.wait(s, "Page")
                ctx.inc(s.activation(out=arow[0:1, :], in_=grow[0:1, :], func=AF.Exp), "Paae", 1)
                ctx.wait(v, "Paae")
                ctx.edge(v, v.tensor_copy(out=ctx.aCtab[0:1, col:col + 1], in_=arow[0:1, 127:128]))
                ctx.wait(nc.tensor, "Paw0", ctx.cnt("Paw0"))
                ctx.wait(nc.tensor, "Page")
                mmb = nc.tensor.matmul(out=ps_b[:], lhsT=ctx.ones_f32, rhs=grow[0:1, :],
                                       start=True, stop=True)
                ctx.inc(mmb, "Pabm", 1)
                ctx.wait(v, "Pabm")
                ctx.wait(v, "Paw0", ctx.cnt("Paw0"))
                dgm = v.tensor_scalar(out=w0[:], in0=ps_b[:], scalar1=ctx.ngt[:, col:col + 1],
                                      scalar2=0.0, op0=ALU.add, op1=ALU.min)
                ctx.inc(dgm, "Padg", 1)
                ctx.wait(s, "Padg")
                a0 = s.activation(out=w0[:], in_=w0[:], func=AF.Exp)
                ctx.inc(a0, "Paw0", 1)
                ctx.wait(v, "Paw0")
                ctx.edge(v, v.tensor_scalar(out=btri[:], in0=ctx.tri_f32, scalar1=ctx.bt[:, col:col + 1],
                                            scalar2=None, op0=ALU.mult))
                ctx.wait(v, "Pamse_d", ctx.cnt("Pamse_d"))
                ctx.inc(v.tensor_tensor(out=wtb[:], in0=w0[:], in1=btri[:], op=ALU.mult), "Pamw", 1)
                ctx.edge(v, v.tensor_tensor(out=wcol[:], in0=w0[:, 127:128], in1=btri[:, 127:128],
                                            op=ALU.mult))
                ctx.wait(v, "Paldk")
                ctx.inc(v.tensor_scalar(out=kw[:], in0=knc[:, h * 128:(h + 1) * 128],
                                        scalar1=wcol[:], scalar2=None, op0=ALU.mult), "Pakwv", 1)
                ctx.wait(g, "Pamw")
                ctx.dma(g, "Pamse_d", ds["masks"][c, h], wtb[:])
                ctx.wait(nc.tensor, "Pakwv")
                ctx.wait(nc.tensor, "Paldv")
                ctx.wait(nc.tensor, "Pape", ctx.cnt("Pape"))
                m1 = nc.tensor.matmul(out=ps_p1[:], lhsT=knc[:, h * 128:(h + 1) * 128], rhs=kw[:],
                                      start=True, stop=True)
                ctx.inc(m1, "Pakk", 1)
                m2 = nc.tensor.matmul(out=ps_p2[:], lhsT=vnc[:, h * 128:(h + 1) * 128], rhs=kw[:],
                                      start=True, stop=True)
                ctx.inc(m2, "Pakk", 1)
                ctx.wait(v, "Pakk")
                ctx.wait(v, "Papd0", ctx.cnt("Papd0"))
                ctx.wait(v, "Papd1", ctx.cnt("Papd1"))
                ctx.inc(v.tensor_copy(out=pstage[:, 0:128], in_=ps_p1[:]), "Pape", 1)
                ctx.inc(v.tensor_copy(out=pstage[:, 128:256], in_=ps_p2[:]), "Pape", 1)
                ctx.wait(g, "Pape")
                ctx.dma(g, "Papd0", ds["partials"][c, 0, :, h * 128:(h + 1) * 128], pstage[:, 0:128])
                ctx.dma(g, "Papd1", ds["partials"][c, 1, :, h * 128:(h + 1) * 128], pstage[:, 128:256])
            ctx.wait(nc.tensor, "Paace", ctx.cnt("Paace"))
            ctx.wait(nc.tensor, "se_BassVectorEngine", ctx.cnt("se_BassVectorEngine"))
            mac = nc.tensor.matmul(out=ps_ac[:], lhsT=ctx.ones_f32,
                                   rhs=ctx.aCtab[0:1, c * NH:(c + 1) * NH], start=True, stop=True)
            ctx.inc(mac, "Paacm", 1)
            ctx.wait(v, "Paacm")
            ctx.inc(v.tensor_copy(out=ctx.acbc[:, c * NH:(c + 1) * NH], in_=ps_ac[:]), "Paace", 1)
            ctx.edge(v, v.tensor_tensor(out=ctx.prodtab[0:1, (c + 1) * NH:(c + 2) * NH],
                                        in0=ctx.prodtab[0:1, c * NH:(c + 1) * NH],
                                        in1=ctx.aCtab[0:1, c * NH:(c + 1) * NH], op=ALU.mult))
            ctx.wait(v, "Palp0")
            ctx.wait(v, "Palp1")
            ctx.wait(nc.sync, "Papd0")
            ctx.wait(nc.sync, "Papd1")
            ctx.dma(nc.sync, "Palr0", hstage[:], ds["partials"][c, 0])
            ctx.dma(nc.sync, "Palr1", hstage2[:], ds["partials"][c, 1])
            ctx.wait(v, "Palr0")
            ctx.wait(v, "Palr1")
            for h in range(NH):
                sl = slice(h * 128, (h + 1) * 128)
                accol = ctx.acbc[:, c * NH + h: c * NH + h + 1]
                e1 = v.scalar_tensor_tensor(out=Lkk[:, sl], in0=Lkk[:, sl], scalar=accol,
                                            in1=hstage[:, sl], op0=ALU.mult, op1=ALU.add)
                ctx.inc(e1, "Palu", 1)
                e2 = v.scalar_tensor_tensor(out=Lkv[:, sl], in0=Lkv[:, sl], scalar=accol,
                                            in1=hstage2[:, sl], op0=ALU.mult, op1=ALU.add)
                ctx.inc(e2, "Palu", 1)
            ctx.wait(v, "Palu")

        ctx.wait(g, "Palu")
        ctx.wait(g, "se_BassVectorEngine")
        ctx.dma(g, "Pacc", ds["cc_in"][0:HD, :], Lkk[:])
        ctx.dma(g, "Pacc", ds["cc_in"][HD:2 * HD, :], Lkv[:])
        ctx.dma(g, "Pacc", ds["cc_in"][2 * HD:2 * HD + 1, :], zrow[0:1, :])
        ctx.dma(g, "Pacc", ds["cc_in"][2 * HD:2 * HD + 1, 0:NH],
                ctx.prodtab[0:1, NCH * NH:(NCH + 1) * NH])
        ctx.wait(g, "Pacc")
        cc = g.collective_compute("AllGather", ALU.bypass, replica_groups=[list(range(NCORES))],
                                  ins=[ds["cc_in"].opt()],
                                  outs=[ds["cc_out"].rearrange("r p c -> (r p) c").opt()])
        ctx.inc(cc, "Paccd", 1)
        ctx.wait(g, "Paccd")
        ctx.wait(nc.sync, "Paccd")

        for j in range(NCORES):
            ctx.dma(nc.sync, "Padr", drow[0:1, j * NH:(j + 1) * NH],
                    ds["cc_out"][j, 2 * HD:2 * HD + 1, 0:NH])
        ctx.wait(nc.tensor, "Padr")
        mdb = nc.tensor.matmul(out=ps_b[:], lhsT=ctx.ones_f32, rhs=drow[0:1, :], start=True, stop=True)
        ctx.inc(mdb, "Padbm", 1)
        ctx.wait(v, "Padbm")
        dbc = ee(nc.sbuf_tensor("Pa_dbc", [128, 128], F32))
        ctx.inc(v.tensor_copy(out=dbc[:], in_=ps_b[:]), "Padbe", 1)
        ctx.inc(v.memset(Lkk[:], 0.0), "Pagz", 1)
        ctx.inc(v.memset(Lkv[:], 0.0), "Pagz", 1)
        ctx.wait(g, "Pagz")
        ctx.dma(g, "Papw0", ds["prefixes"][0, 0:HD, :], Lkk[:])
        ctx.dma(g, "Papw1", ds["prefixes"][0, HD:2 * HD, :], Lkv[:])
        for j in range(NCORES - 1):
            ctx.wait(nc.sync, "Pagu", ctx.cnt("Pagu"))
            ctx.dma(nc.sync, "Pagl0", hstage[:], ds["cc_out"][j, 0:HD, :])
            ctx.dma(nc.sync, "Pagl1", hstage2[:], ds["cc_out"][j, HD:2 * HD, :])
            ctx.wait(v, "Pagl0")
            ctx.wait(v, "Pagl1")
            ctx.wait(v, "Padbe")
            ctx.wait(v, "Papw0")
            ctx.wait(v, "Papw1")
            for h in range(NH):
                sl = slice(h * 128, (h + 1) * 128)
                dcol = dbc[:, j * NH + h: j * NH + h + 1]
                e1 = v.scalar_tensor_tensor(out=Lkk[:, sl], in0=Lkk[:, sl], scalar=dcol,
                                            in1=hstage[:, sl], op0=ALU.mult, op1=ALU.add)
                ctx.inc(e1, "Pagu", 1)
                e2 = v.scalar_tensor_tensor(out=Lkv[:, sl], in0=Lkv[:, sl], scalar=dcol,
                                            in1=hstage2[:, sl], op0=ALU.mult, op1=ALU.add)
                ctx.inc(e2, "Pagu", 1)
            ctx.wait(g, "Pagu")
            ctx.dma(g, "Papw0", ds["prefixes"][j + 1, 0:HD, :], Lkk[:])
            ctx.dma(g, "Papw1", ds["prefixes"][j + 1, HD:2 * HD, :], Lkv[:])
        ctx.wait(nc.sync, "Papw0")
        ctx.wait(nc.sync, "Papw1")
        pid = nc.sync.partition_id()
        ctx.dma(nc.sync, "Pahs", hstage[:], ds["prefixes"][pid, 0:HD, :])
        ctx.dma(nc.sync, "Pahs", hstage2[:], ds["prefixes"][pid, HD:2 * HD, :])

        for c in range(NCH):
            ctx.wait(nc.tensor, "Papbe", ctx.cnt("Papbe"))
            mpb = nc.tensor.matmul(out=ps_ac[:], lhsT=ctx.ones_f32,
                                   rhs=ctx.prodtab[0:1, c * NH:(c + 1) * NH], start=True, stop=True)
            ctx.inc(mpb, "Papbm", 1)
            ctx.wait(v, "Papbm")
            ctx.inc(v.tensor_copy(out=ctx.prodbc[:, c * NH:(c + 1) * NH], in_=ps_ac[:]), "Papbe", 1)

        for c in range(NCH):
            for st_i in range(2):
                hs = hstage if st_i == 0 else hstage2
                ctx.wait(nc.sync, "Pahb", ctx.cnt("Pahb"))
                ctx.dma(nc.sync, "Pahl", Lkk[:], ds["lpref"][c, st_i])
                ctx.wait(v, "Pahl")
                ctx.wait(v, "Pahs")
                for h in range(NH):
                    sl = slice(h * 128, (h + 1) * 128)
                    pcol = ctx.prodbc[:, c * NH + h: c * NH + h + 1]
                    e1 = v.scalar_tensor_tensor(out=Lkv[:, sl], in0=hs[:, sl], scalar=pcol,
                                                in1=Lkk[:, sl], op0=ALU.mult, op1=ALU.add)
                    ctx.inc(e1, "Pahf", 1)
                ctx.wait(v, "Pahf")
                ctx.wait(v, "Pahd", ctx.cnt("Pahd"))
                ctx.inc(v.tensor_copy(out=hbf[:], in_=Lkv[:]), "Pahb", 1)
                ctx.wait(g, "Pahb")
                ctx.dma(g, "Pahd", ds["h0"][c, st_i], hbf[:])
        ctx.wait(g, "Pahd")

def _attn_phase_b(ctx):
    nc = ctx.nc
    ds = ctx.ds
    v, s, g = nc.vector, nc.scalar, nc.gpsimd
    wo = ctx.lnT
    NQ = 10
    Q_D, Q_PQ, Q_DN, Q_DONE, Q_AL, Q_NAL, Q_BE, Q_T0, Q_T1, Q_T2 = range(NQ)

    with ExitStack() as es:
        ee = es.enter_context
        h0c = [ee(nc.sbuf_tensor("Pb_h0%d" % i, [128, NH * HD], BF16)) for i in range(2)]
        hvc = ee(nc.sbuf_tensor("Pb_hv", [128, NH * HD], BF16))
        knc = [ee(nc.sbuf_tensor("Pb_knat%d" % i, [128, NH * HD], BF16)) for i in range(2)]
        ktc = [ee(nc.sbuf_tensor("Pb_knT%d" % i, [128, NH * HD], BF16)) for i in range(2)]
        vtc = ee(nc.sbuf_tensor("Pb_vT", [128, NH * HD], BF16))
        msk = [ee(nc.sbuf_tensor("Pb_msk%d" % i, [128, NH * HD], BF16)) for i in range(2)]
        bq = ee(nc.sbuf_tensor("Pb_bq", [128, NH * HD], BF16))
        dga = [ee(nc.sbuf_tensor("Pb_dga%d" % i, [128, NH * HD], BF16)) for i in range(2)]
        acol = ee(nc.sbuf_tensor("Pb_acol", [128, 2 * NH], F32))
        xst = [ee(nc.sbuf_tensor("Pb_x%d" % i, [128, NH * HD], F32)) for i in range(2)]
        rst = [ee(nc.sbuf_tensor("Pb_r%d" % i, [128, NH * HD], BF16)) for i in range(2)]
        pst = [ee(nc.sbuf_tensor("Pb_p%d" % i, [128, NH * HD], BF16)) for i in range(2)]
        qtk = [ee(nc.sbuf_tensor("Pb_qtk%d" % i, [128, NH * HD], BF16)) for i in range(2)]
        prod = ee(nc.sbuf_tensor("Pb_prod", [128, NH * HD], BF16))
        pfeat = ee(nc.sbuf_tensor("Pb_pfeat", [128, NH * HD], BF16))
        apfeat = ee(nc.sbuf_tensor("Pb_apfeat", [128, NH * HD], BF16))
        qfeat = ee(nc.sbuf_tensor("Pb_qfeat", [128, NH * HD], BF16))
        msb = ee(nc.sbuf_tensor("Pb_msb", [128, NH * HD], BF16))
        sc = [ee(nc.sbuf_tensor("Pb_sc%d" % i, [128, NQ * NH], F32)) for i in range(2)]
        den = ee(nc.sbuf_tensor("Pb_den", [128, NH * HD], F32))
        kk2 = ee(nc.sbuf_tensor("Pb_kk2", [128, NH * HD], BF16))
        otok = den
        junk = kk2
        x1c = den
        x2c = x1c
        psPT = ee(nc.psum_tensor("Pb_psPT", [128, 1024], F32))
        psSS = ee(nc.psum_tensor("Pb_psSS", [128, 1024], F32))
        psQQ = ee(nc.psum_tensor("Pb_psQQ", [128, 1024], F32))
        psQT = ee(nc.psum_tensor("Pb_psQT", [128, 1024], F32))

        def qcolsm(m, q, gh=None, h=None):
            if h is not None:
                return sc[m][:, q * NH + h: q * NH + h + 1]
            return sc[m][:, q * NH + gh * 8: q * NH + gh * 8 + 8]

        def junkm(m):
            return junk if m == 0 else den

        def hsl(h):
            return slice(h * 128, (h + 1) * 128)

        def halfsl(gh):
            return slice(gh * 8 * 128, (gh + 1) * 8 * 128)

        def mv_round(m, src_tok, dgam, h0m, kncm, ktcm, mskm, qtkm):
            for gh in range(2):
                heads = range(gh * 8, gh * 8 + 8)
                ctx.wait(nc.tensor, "Bpw%d%d" % (m, gh), ctx.cnt("Bpw%d%d" % (m, gh)))
                ctx.wait(nc.tensor, "BpeS", ctx.cnt("BpeS"))
                for i, h in enumerate(heads):
                    mm = nc.tensor.matmul(out=psPT[:, i * 128:(i + 1) * 128], lhsT=src_tok[:, hsl(h)],
                                          rhs=ctx.id_bf, start=True, stop=True)
                ctx.inc(mm, "Bptm", 1)
                ctx.wait(s, "Bptm")
                ctx.inc(s.activation(out=pfeat[:, halfsl(gh)], in_=psPT[:], func=AF.Copy), "BpeS", 1)
                ctx.wait(nc.tensor, "BpeS")
                for i, h in enumerate(heads):
                    mm = nc.tensor.matmul(out=psPT[:, i * 128:(i + 1) * 128], lhsT=src_tok[:, hsl(h)],
                                          rhs=dgam[:, hsl(h)], start=True, stop=True)
                ctx.inc(mm, "Bptm", 1)
                ctx.wait(s, "Bptm")
                ctx.inc(s.activation(out=apfeat[:, halfsl(gh)], in_=psPT[:], func=AF.Copy), "BpeS", 1)
                ctx.wait(nc.tensor, "BssE", ctx.cnt("BssE"))
                ctx.wait(nc.tensor, "Bden2", ctx.cnt("Bden2"))
                ctx.wait(nc.tensor, "BpeS")
                for i, h in enumerate(heads):
                    mm = nc.tensor.matmul(out=psSS[:, i * 128:(i + 1) * 128], lhsT=ktcm[:, hsl(h)],
                                          rhs=pfeat[:, hsl(h)], start=True, stop=True)
                ctx.inc(mm, "Bsm", 1)
                ctx.wait(s, "Bsm")
                ctx.wait(s, "Bmm", ctx.cnt("Bmm"))
                ctx.inc(s.activation(out=qfeat[:, halfsl(1 - gh)], in_=psSS[:], func=AF.Copy),
                        "BssE", 1)
                ctx.wait(g, "BssE")
                mmv = g.tensor_tensor(out=msb[:, halfsl(gh)], in0=mskm[:, halfsl(gh)],
                                      in1=qfeat[:, halfsl(1 - gh)], op=ALU.mult)
                ctx.inc(mmv, "Bmm", 1)
                ctx.wait(nc.tensor, "BqeS", ctx.cnt("BqeS"))
                ctx.wait(nc.tensor, "Bmm")
                for i, h in enumerate(heads):
                    osl = psQQ[:, i * 128:(i + 1) * 128]
                    nc.tensor.matmul(out=osl, lhsT=h0m[:, hsl(h)], rhs=apfeat[:, hsl(h)],
                                     start=True, stop=False)
                    nc.tensor.matmul(out=osl, lhsT=ctx.lamdiag[:, hsl(h)], rhs=pfeat[:, hsl(h)],
                                     start=False, stop=False)
                    mm = nc.tensor.matmul(out=osl, lhsT=kncm[:, hsl(h)], rhs=msb[:, hsl(h)],
                                          start=False, stop=True)
                ctx.inc(mm, "Bqm", 1)
                ctx.wait(s, "Bqm")
                ctx.inc(s.activation(out=qfeat[:, halfsl(gh)], in_=psQQ[:], func=AF.Copy), "BqeS", 1)
                ctx.wait(nc.tensor, "BqteS", ctx.cnt("BqteS"))
                ctx.wait(nc.tensor, "BqeS")
                for i, h in enumerate(heads):
                    mm = nc.tensor.matmul(out=psQT[:, i * 128:(i + 1) * 128], lhsT=qfeat[:, hsl(h)],
                                          rhs=ctx.id_bf, start=True, stop=True)
                ctx.inc(mm, "Bqtm", 1)
                ctx.wait(s, "Bqtm")
                ctx.wait(s, "Brw%d%d" % (m, gh), ctx.cnt("Brw%d%d" % (m, gh)))
                ctx.wait(s, "Bofm", ctx.cnt("Bofm"))
                ctx.inc(s.activation(out=qtkm[:, halfsl(gh)], in_=psQT[:], func=AF.Copy), "BqteS", 1)

        NPAIR = NCH // 2
        for cp in range(NPAIR):
            ctx.wait(nc.sync, "Bx2s", ctx.cnt("Bx2s"))
            for m in range(2):
                c = 2 * cp + m
                ctx.dma(nc.sync, "Bld0", h0c[m][:], ds["h0"][c, 0])
                ctx.dma(nc.sync, "Bld2", knc[m][:], ds["knat"][c * 128:(c + 1) * 128, :])
                ctx.dma(nc.sync, "Bld4", ktc[m][:].rearrange("p (h t) -> p h t", h=NH),
                        ds["knT"].rearrange("(h p) t -> p h t", p=128)[:, :, c * 128:(c + 1) * 128])
                ctx.dma(nc.sync, "Bld6", msk[m][:].rearrange("p (h t) -> p h t", h=NH),
                        ds["masks"][c].rearrange("h p t -> p h t"))
            ctx.wait(s, "Bwom", ctx.cnt("Bwom"))
            ctx.wait(v, "Bwom", ctx.cnt("Bwom"))
            ctx.inc(s.activation(out=acol[:, 0:NH], in_=ctx.gt[:, 2 * cp * NH:(2 * cp + 1) * NH],
                                 func=AF.Exp), "Bac", 1)
            ctx.inc(s.activation(out=acol[:, NH:2 * NH], in_=ctx.gt[:, (2 * cp + 1) * NH:(2 * cp + 2) * NH],
                                 func=AF.Exp), "Bac", 1)
            ctx.wait(v, "Bac")
            for m in range(2):
                for h in range(NH):
                    dd = v.tensor_scalar(out=dga[m][:, hsl(h)], in0=ctx.id_bf,
                                         scalar1=acol[:, m * NH + h: m * NH + h + 1],
                                         scalar2=None, op0=ALU.mult)
                    ctx.inc(dd, "Bdga", 1)
            for eng in (v, s, nc.tensor):
                for i in (0, 2, 4, 6):
                    ctx.wait(eng, "Bld%d" % i)
            ctx.wait(nc.tensor, "Bdga")

            for m in range(2):
                c = 2 * cp + m
                for mm_ in range(2):
                    for gg_ in range(2):
                        ctx.wait(nc.sync, "Bpw%d%d" % (mm_, gg_), ctx.cnt("Bpw%d%d" % (mm_, gg_)))
                ctx.dma(nc.sync, "Bld3", bq[:], ds["bq"][c * 128:(c + 1) * 128, :])
                ctx.wait(v, "Bld3")
                ctx.wait(v, "Bwtm", ctx.cnt("Bwtm"))
                ctx.wait(v, "Bd0", ctx.cnt("Bd0"))
                ctx.inc(v.tensor_tensor(out=kk2[:], in0=knc[m][:], in1=knc[m][:], op=ALU.mult), "Bkk2", 1)
                ctx.wait(v, "Bdbm", ctx.cnt("Bdbm"))
                for h in range(NH):
                    hh = v.tensor_tensor(out=prod[:, hsl(h)], in0=h0c[m][:, hsl(h)], in1=ctx.id_bf,
                                         op=ALU.mult)
                    ctx.inc(hh, "Bhdi", 1)
                ctx.wait(nc.tensor, "Bhdi")
                ctx.wait(nc.tensor, "Bkk2")
                ctx.wait(nc.tensor, "Bx2v", ctx.cnt("Bx2v"))
                for gh in range(2):
                    heads = range(gh * 8, gh * 8 + 8)
                    ctx.wait(nc.tensor, "BdbS", ctx.cnt("BdbS"))
                    ctx.wait(nc.tensor, "BqeS", ctx.cnt("BqeS"))
                    for i, h in enumerate(heads):
                        mm = nc.tensor.matmul(out=psQQ[:, i * 128:(i + 1) * 128], lhsT=ctx.onesmat_bf,
                                              rhs=prod[:, hsl(h)], start=True, stop=True)
                    ctx.inc(mm, "Bdbm", 1)
                    ctx.wait(s, "Bdbm")
                    ctx.inc(s.activation(out=den[:, halfsl(gh)], in_=psQQ[:], func=AF.Copy), "BdbS", 1)
                    ctx.wait(nc.tensor, "Bden2", ctx.cnt("Bden2"))
                    ctx.wait(nc.tensor, "BssE", ctx.cnt("BssE"))
                    for i, h in enumerate(heads):
                        mm = nc.tensor.matmul(out=psSS[:, i * 128:(i + 1) * 128], lhsT=msk[m][:, hsl(h)],
                                              rhs=kk2[:, hsl(h)], start=True, stop=True)
                    ctx.inc(mm, "Bwtm", 1)
                    ctx.wait(v, "Bwtm")
                    ctx.wait(v, "BdbS")
                    for i, h in enumerate(heads):
                        e1 = v.scalar_tensor_tensor(out=den[:, hsl(h)], in0=den[:, hsl(h)],
                                                    scalar=acol[:, m * NH + h: m * NH + h + 1],
                                                    in1=psSS[:, i * 128:(i + 1) * 128],
                                                    op0=ALU.mult, op1=ALU.add)
                        ctx.inc(e1, "Bden2", 1)
                    ctx.wait(v, "Bden2")
                    e2 = v.tensor_tensor(out=den[:, halfsl(gh)], in0=den[:, halfsl(gh)],
                                         in1=ctx.lambc[:, halfsl(gh)], op=ALU.add)
                    ctx.inc(e2, "Bden3", 1)
                    ctx.wait(v, "Bden3")
                    e3 = v.reciprocal(out=den[:, halfsl(gh)], in_=den[:, halfsl(gh)])
                    ctx.inc(e3, "Bden4", 1)
                    ctx.wait(v, "Bden4")
                    e4 = v.tensor_tensor(out=xst[m][:, halfsl(gh)], in0=bq[:, halfsl(gh)],
                                         in1=den[:, halfsl(gh)], op=ALU.mult)
                    ctx.inc(e4, "Bx0m", 1)
                    ctx.wait(v, "Bx0m")
                    e5 = v.tensor_copy(out=pst[m][:, halfsl(gh)], in_=xst[m][:, halfsl(gh)])
                    ctx.inc(e5, "Bpw%d%d" % (m, gh), 1)
                mv_round(m, pst[m], dga[m], h0c[m], knc[m], ktc[m], msk[m], qtk[m])
                ctx.wait(v, "BqteS")
                ctx.inc(v.memset(sc[m][:], 0.0), "Bsc0", 1)
                for gh in range(2):
                    e6 = v.tensor_tensor(out=rst[m][:, halfsl(gh)], in0=bq[:, halfsl(gh)],
                                         in1=qtk[m][:, halfsl(gh)], op=ALU.subtract)
                    ctx.inc(e6, "Brw%d%d" % (m, gh), 1)
                ctx.wait(s, "Brw%d0" % m)
                ctx.wait(s, "Brw%d1" % m)
                ctx.wait(s, "Bsc0")
                for h in range(NH):
                    a1 = s.activation(out=junkm(m)[:, hsl(h)], in_=rst[m][:, hsl(h)], func=AF.Square,
                                      accum_out=qcolsm(m, Q_D, h=h))
                    ctx.inc(a1, "Bd0", 1)
                ctx.wait(v, "Bd0")
                for gh in range(2):
                    e7 = v.tensor_copy(out=pst[m][:, halfsl(gh)], in_=rst[m][:, halfsl(gh)])
                    ctx.inc(e7, "Bpw%d%d" % (m, gh), 1)

            def chain(m, qv):
                q16 = lambda q: sc[m][:, q * NH:(q + 1) * NH]
                for gh in range(2):
                    ctx.wait(v, "BqteS", qv - 1 + gh)
                    e1 = v.tensor_tensor(out=prod[:, halfsl(gh)], in0=pst[m][:, halfsl(gh)],
                                         in1=qtk[m][:, halfsl(gh)], op=ALU.mult)
                    ctx.edge(v, e1)
                    e2 = v.tensor_reduce(out=qcolsm(m, Q_PQ, gh=gh)[:, :, None],
                                         in_=prod[:, halfsl(gh)].rearrange("p (h d) -> p h d", h=8),
                                         axis=AX.X, op=ALU.add)
                    ctx.edge(v, e2)
                ctx.edge(v, v.tensor_scalar(out=q16(Q_T0), in0=q16(Q_D), scalar1=1e-10,
                                            scalar2=None, op0=ALU.is_lt))
                ctx.edge(v, v.tensor_tensor(out=q16(Q_DONE), in0=q16(Q_DONE), in1=q16(Q_T0),
                                            op=ALU.max))
                ctx.edge(v, v.tensor_tensor(out=q16(Q_T1), in0=q16(Q_PQ), in1=q16(Q_PQ),
                                            op=ALU.mult))
                ctx.edge(v, v.tensor_scalar(out=q16(Q_T1), in0=q16(Q_T1), scalar1=1e-20,
                                            scalar2=None, op0=ALU.is_lt))
                ctx.edge(v, v.tensor_tensor(out=q16(Q_DONE), in0=q16(Q_DONE), in1=q16(Q_T1),
                                            op=ALU.max))
                ctx.edge(v, v.tensor_scalar(out=q16(Q_T0), in0=q16(Q_DONE), scalar1=-1.0,
                                            scalar2=1.0, op0=ALU.mult, op1=ALU.add))
                ctx.edge(v, v.tensor_tensor(out=q16(Q_T1), in0=q16(Q_PQ), in1=q16(Q_T0),
                                            op=ALU.mult))
                ctx.edge(v, v.tensor_tensor(out=q16(Q_T1), in0=q16(Q_T1), in1=q16(Q_DONE),
                                            op=ALU.add))
                ctx.edge(v, v.reciprocal(out=q16(Q_T2), in_=q16(Q_T1)))
                ctx.edge(v, v.tensor_tensor(out=q16(Q_AL), in0=q16(Q_D), in1=q16(Q_T2),
                                            op=ALU.mult))
                ctx.edge(v, v.tensor_tensor(out=q16(Q_AL), in0=q16(Q_AL), in1=q16(Q_T0),
                                            op=ALU.mult))
                ctx.inc(v.tensor_scalar(out=q16(Q_NAL), in0=q16(Q_AL), scalar1=-1.0,
                                        scalar2=None, op0=ALU.mult), "Bal", 1)
                ctx.wait(v, "Bal")
                for h in range(NH):
                    xu = v.scalar_tensor_tensor(out=xst[m][:, hsl(h)], in0=pst[m][:, hsl(h)],
                                                scalar=qcolsm(m, Q_AL, h=h),
                                                in1=xst[m][:, hsl(h)],
                                                op0=ALU.mult, op1=ALU.add)
                    ctx.inc(xu, "Bxu", 1)
                for h in range(NH):
                    rr = v.scalar_tensor_tensor(out=rst[m][:, hsl(h)], in0=qtk[m][:, hsl(h)],
                                                scalar=qcolsm(m, Q_NAL, h=h),
                                                in1=rst[m][:, hsl(h)],
                                                op0=ALU.mult, op1=ALU.add)
                    ctx.inc(rr, "Brw%d%d" % (m, h // 8), 1)
                ctx.wait(s, "Brw%d0" % m)
                ctx.wait(s, "Brw%d1" % m)
                for h in range(NH):
                    a1 = s.activation(out=junkm(m)[:, hsl(h)], in_=rst[m][:, hsl(h)],
                                      func=AF.Square, accum_out=qcolsm(m, Q_DN, h=h))
                    ctx.inc(a1, "Bdn", 1)
                ctx.wait(v, "Bdn")
                ctx.edge(v, v.tensor_tensor(out=q16(Q_T1), in0=q16(Q_D), in1=q16(Q_T0),
                                            op=ALU.mult))
                ctx.edge(v, v.tensor_tensor(out=q16(Q_T1), in0=q16(Q_T1), in1=q16(Q_DONE),
                                            op=ALU.add))
                ctx.edge(v, v.reciprocal(out=q16(Q_T2), in_=q16(Q_T1)))
                ctx.edge(v, v.tensor_tensor(out=q16(Q_BE), in0=q16(Q_DN), in1=q16(Q_T2),
                                            op=ALU.mult))
                ctx.edge(v, v.tensor_tensor(out=q16(Q_BE), in0=q16(Q_BE), in1=q16(Q_T0),
                                            op=ALU.mult))
                ctx.edge(v, v.tensor_copy(out=q16(Q_D), in_=q16(Q_DN)))
                ctx.wait(v, "Bxu")
                for h in range(NH):
                    pp = v.scalar_tensor_tensor(out=pst[m][:, hsl(h)], in0=pst[m][:, hsl(h)],
                                                scalar=qcolsm(m, Q_BE, h=h),
                                                in1=rst[m][:, hsl(h)],
                                                op0=ALU.mult, op1=ALU.add)
                    ctx.inc(pp, "Bpw%d%d" % (m, h // 8), 1)

            qv = [None, None]
            for it in range(CGI):
                if it > 0:
                    chain(1, qv[1])
                mv_round(0, pst[0], dga[0], h0c[0], knc[0], ktc[0], msk[0], qtk[0])
                qv[0] = ctx.cnt("BqteS")
                mv_round(1, pst[1], dga[1], h0c[1], knc[1], ktc[1], msk[1], qtk[1])
                qv[1] = ctx.cnt("BqteS")
                chain(0, qv[0])
            chain(1, qv[1])

            for m in range(2):
                c = 2 * cp + m
                ctx.wait(nc.sync, "Bsm", ctx.cnt("Bsm"))
                ctx.wait(nc.sync, "Bqm", ctx.cnt("Bqm"))
                ctx.dma(nc.sync, "Bld1", hvc[:], ds["h0"][c, 1])
                ctx.dma(nc.sync, "Bld5", vtc[:].rearrange("p (h t) -> p h t", h=NH),
                        ds["vT"].rearrange("(h p) t -> p h t", p=128)[:, :, c * 128:(c + 1) * 128])
                ctx.wait(v, "Bpw%d0" % m)
                ctx.wait(v, "Bpw%d1" % m)
                ctx.wait(v, "Bxu")
                ctx.wait(v, "Bx2v", ctx.cnt("Bx2v"))
                ctx.inc(v.tensor_copy(out=prod[:], in_=xst[m][:]), "Bxb", 1)
                for eng in (v, s, nc.tensor):
                    ctx.wait(eng, "Bld1")
                    ctx.wait(eng, "Bld5")
                ctx.wait(nc.tensor, "Bxb")
                ctx.wait(s, "Bx2s", ctx.cnt("Bx2s"))
                mv_out(ctx, prod, dga[m], h0c[m], hvc, knc[m], ktc[m], vtc, msk[m], msb,
                       pfeat, apfeat, qfeat, otok, psPT, psSS, psQQ, psQT, hsl, halfsl)
                ctx.wait(s, "BqteS")
                ctx.wait(s, "Bos", ctx.cnt("Bos"))
                for h in range(NH):
                    a1 = s.activation(out=junk[:, hsl(h)], in_=otok[:, hsl(h)], func=AF.Square,
                                      accum_out=qcolsm(m, Q_T0, h=h))
                    ctx.inc(a1, "Bos", 1)
                ctx.wait(v, "Bos")
                scm = sc[m]
                ctx.edge(v, v.tensor_scalar(out=scm[:, Q_T1 * NH:(Q_T1 + 1) * NH],
                                            in0=scm[:, Q_T0 * NH:(Q_T0 + 1) * NH],
                                            scalar1=1.0 / HD, scalar2=1e-6, op0=ALU.mult, op1=ALU.add))
                ctx.inc(v.nop(), "Bov", 1)
                ctx.wait(s, "Bov")
                a2 = s.activation(out=scm[:, Q_T2 * NH:(Q_T2 + 1) * NH],
                                  in_=scm[:, Q_T1 * NH:(Q_T1 + 1) * NH], func=AF.Sqrt)
                ctx.inc(a2, "Bos2", 1)
                ctx.wait(v, "Bos2")
                ctx.inc(v.reciprocal(out=scm[:, Q_T1 * NH:(Q_T1 + 1) * NH],
                                     in_=scm[:, Q_T2 * NH:(Q_T2 + 1) * NH]), "Bov2", 1)
                ctx.wait(s, "Bov2")
                for h in range(NH):
                    a3 = s.activation(out=qtk[m][:, hsl(h)], in_=otok[:, hsl(h)], func=AF.Identity,
                                      scale=qcolsm(m, Q_T1, h=h))
                    ctx.inc(a3, "BonS", 1)
                ctx.wait(nc.sync, "BonS")
                ctx.dma(nc.sync, "Bld7", x1c[:], ds["x1"][c * 128:(c + 1) * 128, :])
                ctx.wait(nc.tensor, "BonS")
                for gh in range(2):
                    ctx.wait(nc.tensor, "BofS", ctx.cnt("BofS"))
                    for i, h in enumerate(range(gh * 8, gh * 8 + 8)):
                        mm = nc.tensor.matmul(out=psPT[:, i * 128:(i + 1) * 128],
                                              lhsT=qtk[m][:, hsl(h)],
                                              rhs=ctx.id_bf, start=True, stop=True)
                    ctx.inc(mm, "Bofm", 1)
                    ctx.wait(s, "Bofm")
                    ctx.inc(s.activation(out=pfeat[:, halfsl(gh)], in_=psPT[:], func=AF.Copy),
                            "BofS", 1)
                ctx.wait(nc.tensor, "BofS")
                ctx.wait(nc.tensor, "Bx2v", ctx.cnt("Bx2v"))
                for dq in range(4):
                    psout = (psQQ if dq < 2 else psQT)
                    osl = psout[:, (dq % 2) * 512:(dq % 2) * 512 + 512]
                    for h in range(NH):
                        mm = nc.tensor.matmul(out=osl, lhsT=pfeat[:, hsl(h)],
                                              rhs=wo[:, h * D + dq * 512: h * D + (dq + 1) * 512],
                                              start=(h == 0), stop=(h == NH - 1))
                    ctx.inc(mm, "Bwom", 1)
                ctx.wait(v, "Bwom")
                ctx.wait(v, "Bld7")
                ctx.wait(v, "Bx2s", ctx.cnt("Bx2s"))
                for dq in range(4):
                    psout = (psQQ if dq < 2 else psQT)
                    osl = psout[:, (dq % 2) * 512:(dq % 2) * 512 + 512]
                    ad = v.tensor_tensor(out=x2c[:, dq * 512:(dq + 1) * 512], in0=osl,
                                         in1=x1c[:, dq * 512:(dq + 1) * 512], op=ALU.add)
                    ctx.inc(ad, "Bx2v", 1)
                ctx.wait(g, "Bx2v")
                ctx.dma(g, "Bx2s", ds["x2"][c * 128:(c + 1) * 128, :], x2c[:])
        ctx.wait(g, "Bx2s")


def mv_out(ctx, xb, dga, h0c, hvc, knc, ktc, vtc, msk, msb, pfeat, apfeat, qfeat,
           otok, psPT, psSS, psQQ, psQT, hsl, halfsl):
    nc = ctx.nc
    v, s = nc.vector, nc.scalar
    ctx.wait(nc.tensor, "Bx2v", ctx.cnt("Bx2v"))
    for gh in range(2):
        heads = range(gh * 8, gh * 8 + 8)
        ctx.wait(nc.tensor, "BpeS", ctx.cnt("BpeS"))
        for i, h in enumerate(heads):
            mm = nc.tensor.matmul(out=psPT[:, i * 128:(i + 1) * 128], lhsT=xb[:, hsl(h)],
                                  rhs=ctx.id_bf, start=True, stop=True)
        ctx.inc(mm, "Bptm", 1)
        ctx.wait(s, "Bptm")
        ctx.inc(s.activation(out=pfeat[:, halfsl(gh)], in_=psPT[:], func=AF.Copy), "BpeS", 1)
        ctx.wait(nc.tensor, "BpeS")
        for i, h in enumerate(heads):
            mm = nc.tensor.matmul(out=psPT[:, i * 128:(i + 1) * 128], lhsT=xb[:, hsl(h)],
                                  rhs=dga[:, hsl(h)], start=True, stop=True)
        ctx.inc(mm, "Bptm", 1)
        ctx.wait(s, "Bptm")
        ctx.inc(s.activation(out=apfeat[:, halfsl(gh)], in_=psPT[:], func=AF.Copy), "BpeS", 1)
        ctx.wait(nc.tensor, "BssE", ctx.cnt("BssE"))
        ctx.wait(nc.tensor, "BpeS")
        for i, h in enumerate(heads):
            mm = nc.tensor.matmul(out=psSS[:, i * 128:(i + 1) * 128], lhsT=vtc[:, hsl(h)],
                                  rhs=pfeat[:, hsl(h)], start=True, stop=True)
        ctx.inc(mm, "Bsm", 1)
        ctx.wait(s, "Bsm")
        ctx.wait(s, "Bmm", ctx.cnt("Bmm"))
        ctx.inc(s.activation(out=qfeat[:, halfsl(1 - gh)], in_=psSS[:], func=AF.Copy), "BssE", 1)
        ctx.wait(nc.gpsimd, "BssE")
        ctx.inc(nc.gpsimd.tensor_tensor(out=msb[:, halfsl(gh)], in0=msk[:, halfsl(gh)],
                                        in1=qfeat[:, halfsl(1 - gh)], op=ALU.mult), "Bmm", 1)
        ctx.wait(nc.tensor, "BqeS", ctx.cnt("BqeS"))
        ctx.wait(nc.tensor, "Bmm")
        for i, h in enumerate(heads):
            osl = psQQ[:, i * 128:(i + 1) * 128]
            nc.tensor.matmul(out=osl, lhsT=hvc[:, hsl(h)], rhs=apfeat[:, hsl(h)],
                             start=True, stop=False)
            mm = nc.tensor.matmul(out=osl, lhsT=knc[:, hsl(h)], rhs=msb[:, hsl(h)],
                                  start=False, stop=True)
        ctx.inc(mm, "Bqm", 1)
        ctx.wait(s, "Bqm")
        ctx.inc(s.activation(out=qfeat[:, halfsl(gh)], in_=psQQ[:], func=AF.Copy), "BqeS", 1)
        ctx.wait(nc.tensor, "BqteS", ctx.cnt("BqteS"))
        ctx.wait(nc.tensor, "BqeS")
        for i, h in enumerate(heads):
            mm = nc.tensor.matmul(out=psQT[:, i * 128:(i + 1) * 128], lhsT=qfeat[:, hsl(h)],
                                  rhs=ctx.id_bf, start=True, stop=True)
        ctx.inc(mm, "Bqtm", 1)
        ctx.wait(s, "Bqtm")
        ao = s.activation(out=otok[:, halfsl(gh)], in_=psQT[:], func=AF.Copy)
        ctx.inc(ao, "BqteS", 1)


_PROGRAM_CACHE = {}


def _get_program(do_attn, taps=()):
    key = (do_attn, tuple(sorted(taps)))
    if key not in _PROGRAM_CACHE:
        _PROGRAM_CACHE[key] = build_program(do_attn=do_attn, taps=taps)
    return _PROGRAM_CACHE[key]


def make_in_maps(inputs):
    x_full = np.ascontiguousarray(inputs["L0_attn_x"].reshape(T, D))
    base = {
        "gate_up": np.ascontiguousarray(inputs["L0_ffn_gate_up_proj"]),
        "down": np.ascontiguousarray(inputs["L0_ffn_down_proj"]),
        "sh_gate": np.ascontiguousarray(inputs["L0_ffn_sh_gate_w"]),
        "sh_up": np.ascontiguousarray(inputs["L0_ffn_sh_up_w"]),
        "sh_down": np.ascontiguousarray(inputs["L0_ffn_sh_down_w"]),
        "shared_gate": np.ascontiguousarray(inputs["L0_ffn_shared_gate_w"].reshape(1, D)),
        "router": np.ascontiguousarray(inputs["L0_ffn_router_weight"]),
        "Wq": np.ascontiguousarray(inputs["L0_attn_Wq"]),
        "Wk": np.ascontiguousarray(inputs["L0_attn_Wk"]),
        "Wv": np.ascontiguousarray(inputs["L0_attn_Wv"]),
        "Wo": np.ascontiguousarray(inputs["L0_attn_Wo"]),
        "Wa": np.ascontiguousarray(inputs["L0_attn_Wa"]),
        "Wb": np.ascontiguousarray(inputs["L0_attn_Wb"]),
        "ba": np.ascontiguousarray(inputs["L0_attn_ba"].reshape(1, NH)),
        "bb": np.ascontiguousarray(inputs["L0_attn_bb"].reshape(1, NH)),
        "lamb": np.ascontiguousarray(inputs["L0_attn_lambda_params"].reshape(NH, HD)),
        "onw": np.ascontiguousarray(inputs["L0_attn_o_norm_weight"].reshape(HD, 1)),
    }
    in_maps = []
    for c in range(NCORES):
        m = dict(base)
        m["x"] = np.ascontiguousarray(x_full[c * TL:(c + 1) * TL])
        in_maps.append(m)
    return in_maps


def kernel(**inputs):
    nc = _get_program(DO_ATTN)
    in_maps = make_in_maps(inputs)
    res = run_bass_kernel_spmd(nc, in_maps, core_ids=list(range(NCORES)))
    y = np.concatenate([res.results[c]["y"] for c in range(NCORES)], axis=0)
    return y.reshape(1, T, D)


# version 12
# speedup vs baseline: 1.1233x; 1.0586x over previous
import numpy as np
import ml_dtypes
import concourse.bass as bass
import concourse.mybir as mybir
from concourse.bass_utils import run_bass_kernel_spmd
from contextlib import ExitStack

F32 = mybir.dt.float32
BF16 = mybir.dt.bfloat16
ALU = mybir.AluOpType
AF = mybir.ActivationFunctionType
AX = mybir.AxisListType

NCORES = 8
T = 16384
TL = T // NCORES
D = 2048
NH = 16
HD = 128
CT = 128
NCH = TL // CT
CGI = 30
E = 8
IM = 1408
ISH = 5632
NT = TL // 128
NDT = D // 128
LAMB_LB = 0.25

BF = ml_dtypes.bfloat16
DO_ATTN = True


class Ctx:
    pass


def _emit_common(ctx):
    nc = ctx.nc
    ee = ctx.stack.enter_context
    ctx.sems = {}
    ctx.semcnt = {}
    ctx.semgen = [0]

    def sem(name):
        if name not in ctx.sems:
            ctx.sems[name] = ee(nc.semaphore(f"{name}_g{ctx.semgen[0]}"))
            ctx.semcnt[name] = 0
        return ctx.sems[name]

    def inc(instr, name, k=1):
        instr.then_inc(sem(name), k)
        ctx.semcnt[name] += k
        return ctx.semcnt[name]

    def wait(eng, name, val=None):
        s = sem(name)
        eng.wait_ge(s, ctx.semcnt[name] if val is None else val)

    def cnt(name):
        sem(name)
        return ctx.semcnt[name]

    def barrier():
        nc.all_engine_barrier(sem_only=True)

    def flush_sems():
        names = list(ctx.sems)
        if not names:
            return
        for n in names:
            nc.gpsimd.wait_ge(ctx.sems[n], ctx.semcnt[n])
        barrier()
        nc.clear_and_free_semaphores([ctx.sems[n] for n in names])
        barrier()
        for n in names:
            del ctx.sems[n]
            del ctx.semcnt[n]
        ctx.semgen[0] += 1
    ctx.flush_sems = flush_sems

    _ENG_NAME = {}

    def edge(eng, instr):
        nm = "selfedge_" + str(id(eng) if eng not in _ENG_NAME else _ENG_NAME[eng])
        nm = "se_" + type(eng).__name__
        inc(instr, nm, 1)
        wait(eng, nm)
        return instr

    def dma(eng, name, out, in_, **kw):
        wait(eng, name)
        d = eng.dma_start(out=out, in_=in_, **kw)
        inc(d, name, 16)
        return d
    ctx.dma = dma

    ctx.sem, ctx.inc, ctx.wait, ctx.cnt, ctx.barrier, ctx.edge = sem, inc, wait, cnt, barrier, edge


def build_program(do_attn=DO_ATTN, taps=(), stages=99):
    nc = bass.Bass("TRN2", target_bir_lowering=False)
    ctx = Ctx()
    ctx.nc = nc
    ctx.stack = ExitStack()
    ctx.stack.__enter__()
    _emit_common(ctx)
    ee = ctx.stack.enter_context

    id_bf_d = nc.inline_tensor(np.eye(128, dtype=BF), "id_bf_d").ap()
    id_f32_d = nc.inline_tensor(np.eye(128, dtype=np.float32), "id_f32_d").ap()
    id16_f32_d = nc.inline_tensor(np.eye(16, dtype=np.float32), "id16_f32_d").ap()
    tri = (np.arange(128)[:, None] <= np.arange(128)[None, :]).astype(np.float32)
    tri_f32_d = nc.inline_tensor(tri, "tri_f32_d").ap()
    ones_f32_d = nc.inline_tensor(np.ones((1, 128), dtype=np.float32), "ones_f32_d").ap()
    onescol_f32_d = nc.inline_tensor(np.ones((128, 1), dtype=np.float32), "onescol_f32_d").ap()
    onesmat_bf_d = nc.inline_tensor(np.ones((128, 128), dtype=BF), "onesmat_bf_d").ap()
    ctx.id_bf = ee(nc.sbuf_tensor("id_bf", [128, 128], BF16))[:]
    ctx.id_f32 = ee(nc.sbuf_tensor("id_f32", [128, 128], F32))[:]
    ctx.id16_f32 = ee(nc.sbuf_tensor("id16_f32", [16, 16], F32))[:]
    ctx.tri_f32 = ee(nc.sbuf_tensor("tri_f32", [128, 128], F32))[:]
    ctx.ones_f32 = ee(nc.sbuf_tensor("ones_f32", [1, 128], F32))[:]
    ctx.onescol_f32 = ee(nc.sbuf_tensor("onescol_f32", [128, 1], F32))[:]
    ctx.onesmat_bf = ee(nc.sbuf_tensor("onesmat_bf", [128, 128], BF16))[:]

    di = {}
    di["x"] = nc.dram_tensor("x", [TL, D], F32, kind="ExternalInput").ap()
    di["gate_up"] = nc.dram_tensor("gate_up", [E, 2 * IM, D], F32, kind="ExternalInput").ap()
    di["down"] = nc.dram_tensor("down", [E, D, IM], F32, kind="ExternalInput").ap()
    di["sh_gate"] = nc.dram_tensor("sh_gate", [ISH, D], F32, kind="ExternalInput").ap()
    di["sh_up"] = nc.dram_tensor("sh_up", [ISH, D], F32, kind="ExternalInput").ap()
    di["sh_down"] = nc.dram_tensor("sh_down", [D, ISH], F32, kind="ExternalInput").ap()
    di["shared_gate"] = nc.dram_tensor("shared_gate", [1, D], F32, kind="ExternalInput").ap()
    di["router"] = nc.dram_tensor("router", [E, D], F32, kind="ExternalInput").ap()
    di["Wq"] = nc.dram_tensor("Wq", [D, D], F32, kind="ExternalInput").ap()
    di["Wk"] = nc.dram_tensor("Wk", [D, D], F32, kind="ExternalInput").ap()
    di["Wv"] = nc.dram_tensor("Wv", [D, D], F32, kind="ExternalInput").ap()
    di["Wo"] = nc.dram_tensor("Wo", [D, D], F32, kind="ExternalInput").ap()
    di["Wa"] = nc.dram_tensor("Wa", [D, NH], F32, kind="ExternalInput").ap()
    di["Wb"] = nc.dram_tensor("Wb", [D, NH], F32, kind="ExternalInput").ap()
    di["ba"] = nc.dram_tensor("ba", [1, NH], F32, kind="ExternalInput").ap()
    di["bb"] = nc.dram_tensor("bb", [1, NH], F32, kind="ExternalInput").ap()
    di["lamb"] = nc.dram_tensor("lamb", [NH, HD], F32, kind="ExternalInput").ap()
    di["onw"] = nc.dram_tensor("onw", [HD, 1], F32, kind="ExternalInput").ap()
    ctx.di = di

    ctx.y = nc.dram_tensor("y", [TL, D], F32, kind="ExternalOutput").ap()

    ds = {}
    ds["gu_bf"] = nc.dram_tensor("gu_bf", [E, 2 * IM, D], BF16).ap()
    ds["down_bf"] = nc.dram_tensor("down_bf", [E, D, IM], BF16).ap()
    ds["shg_bf"] = nc.dram_tensor("shg_bf", [ISH, D], BF16).ap()
    ds["shu_bf"] = nc.dram_tensor("shu_bf", [ISH, D], BF16).ap()
    ds["shd_bf"] = nc.dram_tensor("shd_bf", [D, ISH], BF16).ap()
    ds["acc"] = nc.dram_tensor("acc", [D, TL], F32).ap()
    ds["x1"] = nc.dram_tensor("x1", [TL, D], F32).ap()
    ds["x2"] = nc.dram_tensor("x2", [TL, D], F32).ap()
    ds["crow"] = nc.dram_tensor("crow", [1, 9 * TL], F32).ap()
    ds["bq"] = nc.dram_tensor("bq", [TL, NH * HD], BF16).ap()
    ds["knat"] = nc.dram_tensor("knat_s", [TL, NH * HD], BF16).ap()
    ds["vnat"] = nc.dram_tensor("vnat_s", [TL, NH * HD], BF16).ap()
    ds["knT"] = nc.dram_tensor("knT_s", [NH * HD, TL], BF16).ap()
    ds["vT"] = nc.dram_tensor("vT_s", [NH * HD, TL], BF16).ap()
    ds["masks"] = nc.dram_tensor("masks_s", [NCH, NH, CT, CT], BF16).ap()
    ds["partials"] = nc.dram_tensor("partials_s", [NCH, 2, HD, NH * HD], F32).ap()
    ds["lpref"] = nc.dram_tensor("lpref_s", [NCH, 2, HD, NH * HD], F32).ap()
    ds["h0"] = nc.dram_tensor("h0_s", [NCH, 2, HD, NH * HD], BF16).ap()
    ds["cc_in"] = nc.dram_tensor("cc_in", [2 * HD + 1, NH * HD], F32).ap()
    ds["cc_out"] = nc.dram_tensor("cc_out", [NCORES, 2 * HD + 1, NH * HD], F32).ap()
    ds["prefixes"] = nc.dram_tensor("prefixes_s", [NCORES, 2 * HD, NH * HD], F32).ap()
    ctx.ds = ds

    ctx.taps = {}
    for name in taps:
        if name in ("ln1T", "ln3T"):
            ctx.taps[name] = nc.dram_tensor("tap_" + name, [128, NDT * TL], BF16, kind="ExternalOutput").ap()
        elif name == "comb":
            ctx.taps[name] = nc.dram_tensor("tap_" + name, [1, 9 * TL], F32, kind="ExternalOutput").ap()
        elif name in ("x1",):
            ctx.taps[name] = nc.dram_tensor("tap_" + name, [TL, D], F32, kind="ExternalOutput").ap()
        elif name == "acc1":
            ctx.taps[name] = nc.dram_tensor("tap_" + name, [D, TL], F32, kind="ExternalOutput").ap()
        elif name == "act":
            ctx.taps[name] = nc.dram_tensor("tap_" + name, [128, 11 * TL], BF16, kind="ExternalOutput").ap()

    ctx.lnT = ee(nc.sbuf_tensor("lnT", [128, NDT * TL], BF16))
    ctx.rt9 = ee(nc.sbuf_tensor("rt9", [128, NDT * 9], F32))
    ctx.tabs = ee(nc.sbuf_tensor("tabs", [128, 32], F32))

    for dstc, srcc in ((ctx.id_bf, id_bf_d), (ctx.id_f32, id_f32_d), (ctx.id16_f32, id16_f32_d),
                       (ctx.tri_f32, tri_f32_d), (ctx.ones_f32, ones_f32_d),
                       (ctx.onescol_f32, onescol_f32_d), (ctx.onesmat_bf, onesmat_bf_d)):
        ctx.dma(nc.sync, "cst", dstc, srcc)
    for eng in (nc.tensor, nc.vector, nc.scalar, nc.gpsimd, nc.sync):
        ctx.wait(eng, "cst")
    _weights_prologue(ctx)
    _build_router_T(ctx)
    _build_small_tables(ctx)
    ctx.wait(nc.gpsimd, "wcast")
    ctx.barrier()

    if stages < 2:
        ctx.stack.__exit__(None, None, None)
        return nc
    ctx.flush_sems()
    _ln_router_stage(ctx, src=di["x"], tag="L1", tap="ln1T")
    ctx.barrier()
    if "comb" in ctx.taps:
        ctx.inc(nc.gpsimd.dma_start(out=ctx.taps["comb"], in_=ds["crow"]), "tapc", 16)
        ctx.wait(nc.gpsimd, "tapc")
    ctx.barrier()
    if stages < 3:
        ctx.stack.__exit__(None, None, None)
        return nc
    ctx.flush_sems()
    _ffn_experts(ctx, tag="F1")
    ctx.barrier()
    if "acc1" in ctx.taps:
        ctx.inc(nc.gpsimd.dma_start(out=ctx.taps["acc1"], in_=ds["acc"]), "tapa", 16)
        ctx.wait(nc.gpsimd, "tapa")
    if stages < 4:
        ctx.stack.__exit__(None, None, None)
        return nc
    ctx.flush_sems()
    _residual_stage(ctx, src=di["x"], dst=ds["x1"], tag="R1", tap="x1")
    ctx.barrier()

    if stages < 5:
        ctx.stack.__exit__(None, None, None)
        return nc
    if do_attn:
        _attention(ctx)
    else:
        ctx.inc(nc.gpsimd.dma_start(out=ds["x2"], in_=ds["x1"]), "xcp", 16)
        ctx.wait(nc.gpsimd, "xcp")
    ctx.barrier()

    if stages >= 6:
        ctx.flush_sems()
        _ln_router_stage(ctx, src=ds["x2"], tag="L3", tap="ln3T")
        ctx.barrier()
    if stages >= 7:
        ctx.flush_sems()
        _ffn_experts(ctx, tag="F2")
        ctx.barrier()
    if stages >= 8:
        ctx.flush_sems()
        _residual_stage(ctx, src=ds["x2"], dst=ctx.y, tag="R2", tap=None)
        ctx.barrier()

    ctx.stack.__exit__(None, None, None)
    return nc


def _weights_prologue(ctx):
    nc = ctx.nc
    g = nc.gpsimd
    di, ds = ctx.di, ctx.ds
    for e in range(E):
        ctx.dma(g, "wcast", ds["gu_bf"][e], di["gate_up"][e])
        ctx.dma(g, "wcast", ds["down_bf"][e], di["down"][e])
    ctx.dma(g, "wcast", ds["shg_bf"], di["sh_gate"])
    ctx.dma(g, "wcast", ds["shu_bf"], di["sh_up"])
    ctx.dma(g, "wcast", ds["shd_bf"], di["sh_down"])


def _build_router_T(ctx):
    nc = ctx.nc
    with ExitStack() as es:
        stage = es.enter_context(nc.sbuf_tensor("rt_stage", [16, D], F32))
        ps = es.enter_context(nc.psum_tensor("ps_rt", [128, 16], F32))
        ctx.dma(nc.sync, "rtld", stage[0:E, :], ctx.di["router"])
        ctx.dma(nc.sync, "rtld", stage[E:E + 1, :], ctx.di["shared_gate"])
        ctx.wait(nc.tensor, "rtld")
        for dt in range(NDT):
            if dt >= 1:
                ctx.wait(nc.tensor, "rtev", dt)
            mm = nc.tensor.matmul(out=ps[:, 0:9], lhsT=stage[0:9, dt * 128:(dt + 1) * 128],
                                  rhs=ctx.id16_f32[0:9, 0:9], start=True, stop=True)
            ctx.inc(mm, "rtmm", 1)
            ctx.wait(nc.vector, "rtmm")
            ctx.inc(nc.vector.tensor_copy(out=ctx.rt9[:, dt * 9:(dt + 1) * 9], in_=ps[:, 0:9]), "rtev", 1)
        ctx.wait(nc.tensor, "rtev")
        ctx.wait(nc.sync, "rtev")


def _build_small_tables(ctx):
    nc = ctx.nc
    tabs = ctx.tabs
    with ExitStack() as es:
        lp = es.enter_context(nc.sbuf_tensor("lp_stage", [NH, HD], F32))
        t1 = es.enter_context(nc.sbuf_tensor("lp_t1", [NH, HD], F32))
        sp = es.enter_context(nc.sbuf_tensor("lp_sp", [NH, HD], F32))
        ps = es.enter_context(nc.psum_tensor("ps_lp", [128, 16], F32))
        ctx.dma(nc.sync, "lpld", lp[:], ctx.di["lamb"])
        ctx.dma(nc.sync, "lpld", tabs[0:HD, 16:17], ctx.di["onw"])
        v, s = nc.vector, nc.scalar
        ctx.wait(s, "lpld")
        ctx.inc(s.activation(out=t1[:], in_=lp[:], func=AF.Abs), "lps", 1)
        ctx.wait(v, "lps")
        ctx.inc(v.tensor_scalar(out=t1[:], in0=t1[:], scalar1=-1.0, scalar2=None, op0=ALU.mult), "lpv", 1)
        ctx.wait(s, "lpv")
        ctx.inc(s.activation(out=t1[:], in_=t1[:], func=AF.Exp), "lps", 1)
        ctx.wait(v, "lps")
        ctx.inc(v.tensor_scalar(out=t1[:], in0=t1[:], scalar1=1.0, scalar2=None, op0=ALU.add), "lpv", 1)
        ctx.wait(s, "lpv")
        ctx.inc(s.activation(out=t1[:], in_=t1[:], func=AF.Ln), "lps", 1)
        ctx.wait(v, "lps")
        ctx.edge(v, v.tensor_scalar(out=sp[:], in0=lp[:], scalar1=0.0, scalar2=None, op0=ALU.max))
        ctx.edge(v, v.tensor_tensor(out=sp[:], in0=sp[:], in1=t1[:], op=ALU.add))
        ctx.inc(v.tensor_scalar(out=sp[:], in0=sp[:], scalar1=LAMB_LB, scalar2=None, op0=ALU.add), "lpv", 1)
        ctx.wait(nc.tensor, "lpv")
        ctx.inc(nc.tensor.matmul(out=ps[:, 0:16], lhsT=sp[:], rhs=ctx.id16_f32, start=True, stop=True), "lpmm", 1)
        ctx.wait(v, "lpmm")
        ctx.inc(v.tensor_copy(out=tabs[:, 0:16], in_=ps[:, 0:16]), "lpev", 1)
        ctx.wait(nc.tensor, "lpev")
        ctx.wait(nc.sync, "lpev")


def _ln_router_stage(ctx, src, tag, tap, do_router=True):
    import os
    RMODE = int(os.environ.get("RMODE", "2")) if do_router else 0
    TMODE = int(os.environ.get("TMODE", "2"))
    LTILES = int(os.environ.get("LTILES", str(NT)))
    """LN(src) -> ctx.lnT (bf16 feature-major [:, dt*TL+tok]); fp32 router -> ctx.crow
    (cols c*TL+tok: c 0..7 = 0.5*combine_c, c 8 = 0.5*sigmoid(shared))."""
    nc = ctx.nc
    lnT = ctx.lnT
    crow = ctx.ds["crow"]
    tg = tag
    v, s = nc.vector, nc.scalar
    with ExitStack() as es:
        xs = [es.enter_context(nc.sbuf_tensor(f"{tg}_x{i}", [128, D], F32)) for i in range(2)]
        lntok = es.enter_context(nc.sbuf_tensor(tg + "_lntok", [128, D], F32))
        lnf = [es.enter_context(nc.sbuf_tensor(f"{tg}_lnf{i}", [128, 128], F32)) for i in range(2)]
        st = es.enter_context(nc.sbuf_tensor(tg + "_st", [128, 8], F32))
        pr = es.enter_context(nc.sbuf_tensor(tg + "_pr", [128, 16], F32))
        q2 = es.enter_context(nc.sbuf_tensor(tg + "_q2", [128, 16], F32))
        comb = es.enter_context(nc.sbuf_tensor(tg + "_comb", [128, 16], F32))
        ps_t = [es.enter_context(nc.psum_tensor(f"{tg}_ps_t{i}", [128, 128], F32)) for i in range(2)]
        ps_lg = es.enter_context(nc.psum_tensor(tg + "_ps_lg", [128, 16], F32))
        ps_cr = es.enter_context(nc.psum_tensor(tg + "_ps_cr", [1, 128], F32))
        crs = es.enter_context(nc.sbuf_tensor(tg + "_crs", [1, 128], F32))

        for tt in range(LTILES):
            buf = xs[tt % 2]
            if tt >= 2:
                ctx.wait(nc.sync, tg + "app", tt - 1)
            ctx.dma(nc.sync, tg + "x%d" % (tt % 2), buf[:], src[tt * 128:(tt + 1) * 128, :])
            if tt >= 1:
                ctx.wait(s, tg + "t", tt * NDT)
            ctx.wait(s, tg + "x%d" % (tt % 2))
            a1 = s.activation(out=lntok[:], in_=buf[:], func=AF.Copy, accum_out=st[:, 0:1])
            ctx.inc(a1, tg + "s", 1)
            ctx.wait(s, tg + "s")
            ctx.inc(s.activation(out=lntok[:], in_=buf[:], func=AF.Square, accum_out=st[:, 1:2]), tg + "s", 1)
            ctx.wait(v, tg + "s")
            v.tensor_scalar(out=st[:, 2:3], in0=st[:, 0:1], scalar1=1.0 / D, scalar2=None, op0=ALU.mult)
            ctx.edge(v, v.tensor_scalar(out=st[:, 3:4], in0=st[:, 1:2], scalar1=1.0 / D, scalar2=None, op0=ALU.mult))
            ctx.edge(v, v.scalar_tensor_tensor(out=st[:, 4:5], in0=st[:, 2:3], scalar=st[:, 2:3], in1=st[:, 3:4],
                                               op0=ALU.mult, op1=ALU.subtract))
            ctx.inc(v.tensor_scalar(out=st[:, 4:5], in0=st[:, 4:5], scalar1=-1.0, scalar2=1e-5,
                                    op0=ALU.mult, op1=ALU.add), tg + "v", 1)
            ctx.wait(s, tg + "v")
            ctx.inc(s.activation(out=st[:, 5:6], in_=st[:, 4:5], func=AF.Sqrt), tg + "s", 1)
            if TMODE < 1:
                ctx.inc(v.nop(), tg + "app", 1)
                continue
            ctx.wait(v, tg + "s")
            v.reciprocal(out=st[:, 6:7], in_=st[:, 5:6])
            ctx.edge(v, v.tensor_scalar(out=st[:, 7:8], in0=st[:, 2:3], scalar1=-1.0, scalar2=None, op0=ALU.mult))
            app = v.tensor_scalar(out=lntok[:], in0=buf[:], scalar1=st[:, 7:8], scalar2=st[:, 6:7],
                                  op0=ALU.add, op1=ALU.mult)
            ctx.inc(app, tg + "app", 1)
            ctx.wait(nc.tensor, tg + "app")
            if TMODE < 2:
                continue
            for dt in range(NDT):
                gidx = tt * NDT + dt
                half = dt % 2
                if gidx >= 2:
                    ctx.wait(nc.tensor, tg + "ev", (gidx - 1) * 2)
                mm = nc.tensor.matmul(out=ps_t[half][:],
                                      lhsT=lntok[:, dt * 128:(dt + 1) * 128],
                                      rhs=ctx.id_f32, start=True, stop=True)
                ctx.inc(mm, tg + "t", 1)
                if RMODE >= 1 and gidx >= 2:
                    ctx.wait(v, tg + "lgm", gidx - 1)
                ctx.wait(v, tg + "t")
                c1 = v.tensor_copy(out=lnf[half][:], in_=ps_t[half][:])
                ctx.inc(c1, tg + "ev", 1)
                ctx.wait(v, tg + "ev")
                ctx.inc(v.tensor_copy(out=lnT[:, dt * TL + tt * 128: dt * TL + (tt + 1) * 128],
                                      in_=lnf[half][:]), tg + "ev", 1)
                if RMODE >= 1:
                    if dt == 0 and tt >= 1:
                        ctx.wait(nc.tensor, tg + "prc", tt)
                    ctx.wait(nc.tensor, tg + "ev")
                    mm2 = nc.tensor.matmul(out=ps_lg[:, 0:9], lhsT=lnf[half][:],
                                           rhs=ctx.rt9[:, dt * 9:(dt + 1) * 9],
                                           start=(dt == 0), stop=(dt == NDT - 1))
                    ctx.inc(mm2, tg + "lgm", 1)
            if RMODE < 1:
                continue
            if tt >= 1:
                ctx.wait(v, tg + "crmm", tt * 9)
            ctx.wait(v, tg + "lgm")
            ctx.wait(v, tg + "sg")
            ctx.inc(v.tensor_copy(out=pr[:, 0:9], in_=ps_lg[:, 0:9]), tg + "prc", 1)
            ctx.wait(s, tg + "prc")
            ctx.inc(s.activation(out=pr[:, 9:10], in_=pr[:, 8:9], func=AF.Sigmoid), tg + "sg", 1)
            ctx.wait(v, tg + "prc")
            ctx.edge(v, v.reduce_max(out=q2[:, 0:1], in_=pr[:, 0:8], axis=AX.X))
            ctx.inc(v.tensor_scalar(out=q2[:, 1:2], in0=q2[:, 0:1], scalar1=-1.0, scalar2=None, op0=ALU.mult),
                    tg + "pv", 1)
            ctx.wait(s, tg + "pv")
            ctx.inc(s.activation(out=pr[:, 0:8], in_=pr[:, 0:8], func=AF.Exp, bias=q2[:, 1:2]), tg + "sg", 1)
            ctx.wait(v, tg + "sg")
            ctx.edge(v, v.tensor_scalar(out=pr[:, 0:8], in0=pr[:, 0:8], scalar1=1.0, scalar2=0.0,
                                         op0=ALU.mult, op1=ALU.add, accum_out=q2[:, 2:3]))
            ctx.edge(v, v.reciprocal(out=q2[:, 3:4], in_=q2[:, 2:3]))
            ctx.edge(v, v.tensor_scalar(out=pr[:, 0:8], in0=pr[:, 0:8], scalar1=q2[:, 3:4], scalar2=None, op0=ALU.mult))
            ctx.edge(v, v.reduce_max(out=q2[:, 4:5], in_=pr[:, 0:8], axis=AX.X))
            ctx.edge(v, v.tensor_scalar(out=q2[:, 8:16], in0=pr[:, 0:8], scalar1=q2[:, 4:5], scalar2=None, op0=ALU.is_equal))
            ctx.edge(v, v.scalar_tensor_tensor(out=comb[:, 8:16], in0=q2[:, 8:16], scalar=-1e30, in1=pr[:, 0:8],
                                               op0=ALU.mult, op1=ALU.add))
            ctx.edge(v, v.reduce_max(out=q2[:, 5:6], in_=comb[:, 8:16], axis=AX.X))
            v.tensor_scalar(out=comb[:, 0:8], in0=comb[:, 8:16], scalar1=q2[:, 5:6], scalar2=None, op0=ALU.is_equal)
            ctx.edge(v, v.tensor_tensor(out=q2[:, 6:7], in0=q2[:, 4:5], in1=q2[:, 5:6], op=ALU.add))
            ctx.edge(v, v.reciprocal(out=q2[:, 7:8], in_=q2[:, 6:7]))
            ctx.edge(v, v.tensor_tensor(out=comb[:, 0:8], in0=comb[:, 0:8], in1=q2[:, 8:16], op=ALU.add))
            ctx.edge(v, v.tensor_tensor(out=comb[:, 0:8], in0=comb[:, 0:8], in1=pr[:, 0:8], op=ALU.mult))
            ctx.edge(v, v.tensor_scalar(out=comb[:, 0:8], in0=comb[:, 0:8], scalar1=q2[:, 7:8], scalar2=0.5,
                                        op0=ALU.mult, op1=ALU.mult))
            ctx.wait(v, tg + "sg")
            ctx.inc(v.tensor_scalar(out=comb[:, 8:9], in0=pr[:, 9:10], scalar1=0.5, scalar2=None, op0=ALU.mult),
                    tg + "cmb", 1)
            if RMODE < 2:
                continue
            ctx.wait(nc.tensor, tg + "cmb")
            for c in range(9):
                cidx = tt * 9 + c
                if cidx >= 1:
                    ctx.wait(nc.tensor, tg + "crev", cidx)
                mmc = nc.tensor.matmul(out=ps_cr[0:1, :], lhsT=comb[:, c:c + 1], rhs=ctx.id_f32,
                                       start=True, stop=True)
                ctx.inc(mmc, tg + "crmm", 1)
                if cidx >= 1:
                    ctx.wait(v, tg + "crst", 16 * cidx)
                ctx.wait(v, tg + "crmm")
                ctx.inc(v.tensor_copy(out=crs[0:1, :], in_=ps_cr[0:1, :]), tg + "crev", 1)
                ctx.wait(nc.sync, tg + "crev")
                ctx.inc(nc.sync.dma_start(out=crow[0:1, c * TL + tt * 128: c * TL + (tt + 1) * 128],
                                          in_=crs[0:1, :]), tg + "crst", 16)
        ctx.wait(nc.sync, tg + "ev")
        if RMODE >= 2:
            ctx.wait(nc.sync, tg + "crst")
    if tap in ctx.taps:
        ctx.inc(nc.sync.dma_start(out=ctx.taps[tap], in_=lnT[:]), tg + "tap", 16)
        ctx.wait(nc.sync, tg + "tap")


def _ffn_experts(ctx, tag):
    nc = ctx.nc
    tg = tag
    lnT, ds = ctx.lnT, ctx.ds
    crow = ds["crow"]
    v, s, g = nc.vector, nc.scalar, nc.gpsimd
    import os
    NPE = int(os.environ.get("NPE", "0"))
    NHT = 11
    pes = []
    for j in range(4):
        gblocks = []
        for srcname in ("shg_bf", "shu_bf"):
            base = j * IM
            for off, nt_ in ((0, 4), (512, 4), (1024, 3)):
                gblocks.append((ds[srcname], base + off, nt_))
        pes.append((gblocks, ds["shd_bf"], j * IM, 8))
    for e in range(E):
        gblocks = []
        for off, nt_ in ((0, 4), (512, 4), (1024, 3), (IM, 4), (IM + 512, 4), (IM + 1024, 3)):
            gblocks.append((ds["gu_bf"][e], off, nt_))
        pes.append((gblocks, ds["down_bf"][e], 0, e))
    if NPE:
        pes = pes[:NPE]

    with ExitStack() as es:
        guT = [es.enter_context(nc.sbuf_tensor(f"{tg}_guT{i}", [128, 16 * 512], BF16)) for i in range(2)]
        dnT = [es.enter_context(nc.sbuf_tensor(f"{tg}_dnT{i}", [128, NHT * 1024], BF16)) for i in range(2)]
        crow_sb = es.enter_context(nc.sbuf_tensor(tg + "_crow_sb", [1, TL], F32))
        act = es.enter_context(nc.sbuf_tensor(tg + "_act", [128, NHT * TL], BF16))
        cbc = es.enter_context(nc.sbuf_tensor(tg + "_cbc", [128, TL], BF16))
        stg = [es.enter_context(nc.sbuf_tensor(f"{tg}_stg{i}", [128, 512], F32)) for i in range(2)]
        ps_gu = [es.enter_context(nc.psum_tensor(f"{tg}_psgu{i}", [128, 512], F32)) for i in range(2)]
        ps_o = [es.enter_context(nc.psum_tensor(f"{tg}_pso{i}", [128, 512], F32)) for i in range(2)]

        gu_slot_ev = [None, None]
        o_slot_ev = [0, 0]
        acq_cnt = [0, 0]
        gud_cnt = [0, 0]
        dnd_cnt = [0, 0]
        blk_ev_need = [0, 0]

        for p, (gblocks, dn_src, dn_coloff, c_idx) in enumerate(pes):
            if p >= 1:
                ctx.wait(s, tg + "om", p * 64)
                ctx.wait(v, tg + "om", p * 64)
            if p >= 1:
                ctx.wait(nc.gpsimd, tg + "cbm", p * 4)
            ctx.inc(nc.gpsimd.dma_start(out=crow_sb[0:1, :], in_=crow[0:1, c_idx * TL:(c_idx + 1) * TL]),
                    tg + "crl", 16)
            ctx.wait(nc.tensor, tg + "crl", 16 * (p + 1))
            if gu_slot_ev[0] is not None:
                ctx.wait(nc.tensor, gu_slot_ev[0][0], gu_slot_ev[0][1])
            ctx.wait(s, tg + "guevV")
            for hh in range(4):
                mcb = nc.tensor.matmul(out=ps_gu[0][:], lhsT=ctx.ones_f32,
                                       rhs=crow_sb[0:1, hh * 512:(hh + 1) * 512],
                                       start=True, stop=True)
                ctx.inc(mcb, tg + "cbm", 1)
                ctx.wait(s, tg + "cbm")
                ctx.inc(s.activation(out=cbc[:, hh * 512:(hh + 1) * 512], in_=ps_gu[0][:], func=AF.Copy),
                        tg + "cbev", 1)
                ctx.wait(nc.tensor, tg + "cbev")


            ht_base = 0
            blki = 0
            ht_global = 0
            for (gsrc, row_off, ntile) in gblocks:
                par = blki % 2
                if os.environ.get("GUSER", "0") == "1":
                    ctx.wait(nc.sync, tg + "gum", ctx.cnt(tg + "gum"))
                else:
                    ctx.wait(nc.sync, tg + "gum", blk_ev_need[par])
                for dt in range(NDT):
                    ctx.dma(nc.sync, tg + "gud%d" % par,
                            guT[par][:, dt * 512: dt * 512 + ntile * 128],
                            gsrc[row_off:row_off + ntile * 128, dt * 128:(dt + 1) * 128],
                            transpose=True)
                gud_cnt[par] = ctx.cnt(tg + "gud%d" % par)
                ctx.wait(nc.tensor, tg + "gud%d" % par, gud_cnt[par])
                for htb in range(ntile):
                    ht = ht_global
                    ht_global += 1
                    for tq in range(4):
                        slot = (ht * 4 + tq) % 2
                        if gu_slot_ev[slot] is not None:
                            ctx.wait(nc.tensor, gu_slot_ev[slot][0], gu_slot_ev[slot][1])
                        for dt in range(NDT):
                            mmg = nc.tensor.matmul(out=ps_gu[slot][:],
                                                   lhsT=guT[par][:, dt * 512 + htb * 128: dt * 512 + (htb + 1) * 128],
                                                   rhs=lnT[:, dt * TL + tq * 512: dt * TL + (tq + 1) * 512],
                                                   start=(dt == 0), stop=(dt == NDT - 1))
                        gum = ctx.inc(mmg, tg + "gum", 1)
                        if ht < NHT:
                            ctx.wait(s, tg + "gum", gum)
                            ev = s.activation(out=act[:, ht * TL + tq * 512: ht * TL + (tq + 1) * 512],
                                              in_=ps_gu[slot][:], func=AF.Silu)
                            gu_slot_ev[slot] = (tg + "guevS", ctx.inc(ev, tg + "guevS", 1))
                        else:
                            jslot = ht - NHT
                            asl = act[:, jslot * TL + tq * 512: jslot * TL + (tq + 1) * 512]
                            ctx.wait(v, tg + "gum", gum)
                            ctx.edge(v, v.tensor_tensor(out=asl, in0=asl, in1=cbc[:, tq * 512:(tq + 1) * 512],
                                                        op=ALU.mult))
                            ev = v.tensor_tensor(out=asl, in0=asl, in1=ps_gu[slot][:], op=ALU.mult)
                            gu_slot_ev[slot] = (tg + "guevV", ctx.inc(ev, tg + "guevV", 1))
                blki += 1
                if blki >= 2:
                    pass
                blk_ev_need[par] = ctx.cnt(tg + "gum")

            if os.environ.get("NODND", "0") == "1":
                continue
            ctx.wait(nc.tensor, tg + "guevS")
            ctx.wait(nc.tensor, tg + "guevV")
            for dh in range(2):
                dpar = dh % 2
                if p * 2 + dh >= 2:
                    ctx.wait(nc.sync, tg + "om", (p * 2 + dh - 1) * 32)
                for ht in range(NHT):
                    ctx.dma(nc.sync, tg + "dnd%d" % dpar,
                            dnT[dpar][:, ht * 1024:(ht + 1) * 1024],
                            dn_src[dh * 1024:(dh + 1) * 1024,
                                   dn_coloff + ht * 128: dn_coloff + (ht + 1) * 128],
                            transpose=True)
                dnd_cnt[dpar] = ctx.cnt(tg + "dnd%d" % dpar)
                ctx.wait(nc.tensor, tg + "dnd%d" % dpar, dnd_cnt[dpar])
                for dto in range(8):
                    dt = dh * 8 + dto
                    for tq in range(4):
                        kk = p * 64 + dt * 4 + tq
                        slot = kk % 2
                        ctx.wait(nc.tensor, tg + "oev", o_slot_ev[slot])
                        for ht in range(NHT):
                            mmo = nc.tensor.matmul(out=ps_o[slot][:],
                                                   lhsT=dnT[dpar][:, ht * 1024 + dto * 128: ht * 1024 + (dto + 1) * 128],
                                                   rhs=act[:, ht * TL + tq * 512: ht * TL + (tq + 1) * 512],
                                                   start=(ht == 0), stop=(ht == NHT - 1))
                        om = ctx.inc(mmo, tg + "om", 1)
                        ctx.wait(v, tg + "om", om)
                        ctx.wait(v, tg + "acq%d" % slot, acq_cnt[slot])
                        cp = v.tensor_copy(out=stg[slot][:], in_=ps_o[slot][:])
                        o_slot_ev[slot] = ctx.inc(cp, tg + "oev", 1)
                        ctx.wait(g, tg + "oev", o_slot_ev[slot])
                        ad = g.dma_start(out=ds["acc"][dt * 128:(dt + 1) * 128, tq * 512:(tq + 1) * 512],
                                         in_=stg[slot][:],
                                         accum_op=(ALU.bypass if p == 0 else ALU.add))
                        acq_cnt[slot] = ctx.inc(ad, tg + "acq%d" % slot, 16)
        ctx.wait(g, tg + "acq0")
        ctx.wait(g, tg + "acq1")
        if "act" in ctx.taps and tg == "F1":
            ctx.wait(g, tg + "guevS")
            ctx.wait(g, tg + "guevV")
            ctx.inc(g.dma_start(out=ctx.taps["act"], in_=act[:]), tg + "tact", 16)
            ctx.wait(g, tg + "tact")


def _residual_stage(ctx, src, dst, tag, tap):
    nc = ctx.nc
    tg = tag
    ds = ctx.ds
    v, s = nc.vector, nc.scalar
    with ExitStack() as es:
        xs = [es.enter_context(nc.sbuf_tensor(f"{tg}_x{i}", [128, D], F32)) for i in range(2)]
        at = [es.enter_context(nc.sbuf_tensor(f"{tg}_at{i}", [128, 128], F32)) for i in range(2)]
        xn = [es.enter_context(nc.sbuf_tensor(f"{tg}_xn{i}", [128, D], F32)) for i in range(2)]
        ps_r = es.enter_context(nc.psum_tensor(tg + "_ps_r", [128, D], F32))
        st_counts = []
        for tt in range(NT):
            bp = tt % 2
            if tt >= 2:
                ctx.wait(nc.sync, tg + "add", tt - 1)
            ctx.dma(nc.sync, tg + "x%d" % bp, xs[bp][:], src[tt * 128:(tt + 1) * 128, :])
            for dt in range(NDT):
                gidx = tt * NDT + dt
                ap = dt % 2
                if gidx >= 2:
                    ctx.wait(nc.sync, tg + "tp", gidx - 1)
                dd = nc.sync.dma_start(out=at[ap][:, 0:128],
                                       in_=ds["acc"][dt * 128:(dt + 1) * 128, tt * 128:(tt + 1) * 128])
                abn = ctx.inc(dd, tg + "ab%d" % ap, 16)
                ctx.wait(nc.tensor, tg + "ab%d" % ap, abn)
                if dt == 0 and tt >= 1:
                    ctx.wait(nc.tensor, tg + "add", tt)
                mm = nc.tensor.matmul(out=ps_r[:, dt * 128:(dt + 1) * 128], lhsT=at[ap][:, 0:128],
                                      rhs=ctx.id_f32, start=True, stop=True)
                ctx.inc(mm, tg + "tp", 1)
            ctx.wait(v, tg + "tp")
            ctx.wait(v, tg + "x%d" % bp)
            if tt >= 2:
                ctx.wait(v, tg + "st%d" % bp, st_counts[tt - 2])
            add = v.tensor_tensor(out=xn[bp][:], in0=ps_r[:], in1=xs[bp][:], op=ALU.add)
            ctx.inc(add, tg + "add", 1)
            ctx.wait(nc.sync, tg + "add")
            ctx.inc(nc.sync.dma_start(out=dst[tt * 128:(tt + 1) * 128, :], in_=xn[bp][:]), tg + "st%d" % bp, 16)
            if tap is not None and tap in ctx.taps:
                ctx.inc(nc.sync.dma_start(out=ctx.taps[tap][tt * 128:(tt + 1) * 128, :], in_=xn[bp][:]),
                        tg + "st%d" % bp, 16)
            st_counts.append(ctx.cnt(tg + "st%d" % bp))
        ctx.wait(nc.sync, tg + "st0")
        ctx.wait(nc.sync, tg + "st1")
        ctx.wait(nc.vector, tg + "st0")
        ctx.wait(nc.vector, tg + "st1")



def _attention(ctx):
    nc = ctx.nc
    ds, di = ctx.ds, ctx.di
    v, s, g = nc.vector, nc.scalar, nc.gpsimd

    ctx.flush_sems()
    _ln_router_stage(ctx, src=ds["x1"], tag="L2", tap=None, do_router=False)
    ctx.barrier()

    with ExitStack() as es:
        ee = es.enter_context
        ctx.gt = ee(nc.sbuf_tensor("A_gt", [128, NCH * NH], F32))
        ctx.ngt = ee(nc.sbuf_tensor("A_ngt", [128, NCH * NH], F32))
        ctx.bt = ee(nc.sbuf_tensor("A_bt", [128, NCH * NH], F32))
        ctx.aCtab = ee(nc.sbuf_tensor("A_aCtab", [1, (NCH + 1) * NH], F32))
        ctx.prodtab = ee(nc.sbuf_tensor("A_prodtab", [1, (NCH + 1) * NH], F32))
        ctx.acbc = ee(nc.sbuf_tensor("A_acbc", [128, NCH * NH], F32))
        ctx.prodbc = ee(nc.sbuf_tensor("A_prodbc", [128, NCH * NH], F32))
        ctx.lamdiag = ee(nc.sbuf_tensor("A_lamdiag", [128, NH * HD], BF16))
        ctx.lambc = ee(nc.sbuf_tensor("A_lambc", [128, NH * HD], BF16))
        ctx.flush_sems()
        _attn_proj(ctx)
        ctx.barrier()
        ctx.flush_sems()
        _attn_phase_a(ctx)
        ctx.barrier()
        ctx.flush_sems()
        _attn_phase_b(ctx)
        ctx.barrier()


def _attn_proj(ctx):
    nc = ctx.nc
    ds, di = ctx.ds, ctx.di
    v, s, g = nc.vector, nc.scalar, nc.gpsimd
    lnT = ctx.lnT

    with ExitStack() as es:
        ee = es.enter_context
        qh_sb = ee(nc.sbuf_tensor("A_qh", [128, TL], BF16))
        qtok = ee(nc.sbuf_tensor("A_qtok", [128, 128], BF16))
        ktf = ee(nc.sbuf_tensor("A_ktf", [128, 128], BF16))
        ssb = ee(nc.sbuf_tensor("A_ss", [128, 8], F32))
        junk = ee(nc.sbuf_tensor("A_junk", [128, 128], F32))
        ps_p = [ee(nc.psum_tensor(f"A_psp{i}", [128, 512], F32)) for i in range(2)]
        ps_t2 = ee(nc.psum_tensor("A_pst2", [128, 128], F32))
        ps_t3 = ee(nc.psum_tensor("A_pst3", [128, 128], F32))
        ps_bg = ee(nc.psum_tensor("A_psbg", [16, 512], F32))
        ps_sm = ee(nc.psum_tensor("A_pssm", [128, 16], F32))

        pe_slot = [None, None]
        for kind, W, tokdst in (("q", di["Wq"], ds["bq"]), ("k", di["Wk"], ds["knat"]),
                                ("v", di["Wv"], ds["vnat"])):
            tgw = "Awl" + kind
            with nc.sbuf_tensor("A_wbuf" + kind, [128, NDT * D], BF16) as wq_sb:
                ctx.dma(g, tgw, wq_sb[:].rearrange("p (a c) -> p a c", a=16),
                        W.rearrange("(a p) c -> p a c", p=128))
                ctx.wait(nc.tensor, tgw)
                for h in range(NH):
                    for tq in range(4):
                        slot = (h * 4 + tq) % 2
                        if pe_slot[slot] is not None:
                            ctx.wait(nc.tensor, pe_slot[slot][0], pe_slot[slot][1])
                        for din in range(NDT):
                            mm = nc.tensor.matmul(out=ps_p[slot][:],
                                                  lhsT=wq_sb[:, din * D + h * 128: din * D + (h + 1) * 128],
                                                  rhs=lnT[:, din * TL + tq * 512: din * TL + (tq + 1) * 512],
                                                  start=(din == 0), stop=(din == NDT - 1))
                        pm = ctx.inc(mm, "Apm", 1)
                        ctx.wait(s, "Apm", pm)
                        if tq == 0:
                            ctx.wait(s, "At2m", ctx.cnt("At2m"))
                            ctx.wait(s, "Avt", ctx.cnt("Avt"))
                        ev = s.activation(out=qh_sb[:, tq * 512:(tq + 1) * 512], in_=ps_p[slot][:],
                                          func=AF.Silu)
                        pe_slot[slot] = ("ApevS", ctx.inc(ev, "ApevS", 1))
                    if kind == "v":
                        ctx.wait(g, "ApevS")
                        ctx.dma(g, "Avt", ds["vT"][h * 128:(h + 1) * 128, :], qh_sb[:])
                    for c in range(NCH):
                        ctx.wait(nc.tensor, "ApevS")
                        ctx.wait(nc.tensor, "Aqt", ctx.cnt("Aqt"))
                        mm2 = nc.tensor.matmul(out=ps_t2[:], lhsT=qh_sb[:, c * 128:(c + 1) * 128],
                                               rhs=ctx.id_bf, start=True, stop=True)
                        ctx.inc(mm2, "At2m", 1)
                        ctx.wait(s, "At2m")
                        if kind in ("q", "k"):
                            a1 = s.activation(out=junk[:], in_=ps_t2[:], func=AF.Square, accum_out=ssb[:, 0:1])
                            ctx.inc(a1, "Asq", 1)
                            ctx.wait(s, "Asq")
                            a2 = s.activation(out=ssb[:, 1:2], in_=ssb[:, 0:1], func=AF.Sqrt)
                            ctx.inc(a2, "Asq", 1)
                            ctx.wait(v, "Asq")
                            ctx.edge(v, v.tensor_scalar(out=ssb[:, 2:3], in0=ssb[:, 1:2], scalar1=1e-12,
                                                        scalar2=None, op0=ALU.max))
                            ctx.inc(v.reciprocal(out=ssb[:, 3:4], in_=ssb[:, 2:3]), "Arn", 1)
                            ctx.wait(s, "Arn")
                            a3 = s.activation(out=qtok[:], in_=ps_t2[:], func=AF.Identity, scale=ssb[:, 3:4])
                            ctx.inc(a3, "Aqt", 1)
                        else:
                            a3 = s.activation(out=qtok[:], in_=ps_t2[:], func=AF.Copy)
                            ctx.inc(a3, "Aqt", 1)
                        ctx.wait(g, "Aqt")
                        ctx.dma(g, "Ast", tokdst[c * 128:(c + 1) * 128, h * 128:(h + 1) * 128], qtok[:])
                        if kind == "k":
                            ctx.wait(nc.tensor, "Aqt")
                            ctx.wait(nc.tensor, "At3e", ctx.cnt("At3e"))
                            mm3 = nc.tensor.matmul(out=ps_t3[:], lhsT=qtok[:], rhs=ctx.id_bf,
                                                   start=True, stop=True)
                            ctx.inc(mm3, "At3m", 1)
                            ctx.wait(v, "At3m")
                            ctx.inc(v.tensor_copy(out=ktf[:], in_=ps_t3[:]), "At3e", 1)
                            ctx.wait(g, "At3e")
                            ctx.dma(g, "Ast2", ds["knT"][h * 128:(h + 1) * 128, c * 128:(c + 1) * 128],
                                    ktf[:])
                        ctx.wait(s, "Ast")
                        ctx.wait(s, "Ast2")
                ctx.wait(nc.sync, "Ast")
                ctx.wait(nc.sync, "Ast2")
                ctx.wait(nc.sync, "Avt")
                ctx.wait(nc.scalar, "Ast")
                ctx.barrier()

        ctx.barrier()
        with nc.sbuf_tensor("A_wab", [128, NDT * 2 * NH], BF16) as wab, \
             nc.sbuf_tensor("A_basb", [NH, 2], F32) as basb, \
             nc.sbuf_tensor("A_bg16", [NH, 2 * TL], F32) as bg16, \
             nc.sbuf_tensor("A_bgt", [NH, TL], F32) as bgt:
            ctx.dma(g, "Abw", wab[:].rearrange("p (a c) -> p a c", a=16)[:, :, 0:NH],
                    di["Wa"].rearrange("(a p) c -> p a c", p=128))
            ctx.dma(g, "Abw", wab[:].rearrange("p (a c) -> p a c", a=16)[:, :, NH:2 * NH],
                    di["Wb"].rearrange("(a p) c -> p a c", p=128))
            ctx.dma(nc.sync, "Abw2", basb[:, 0:1], di["ba"].rearrange("o h -> h o"),
                    allow_slow_non_contiguous=True)
            ctx.dma(nc.sync, "Abw2", basb[:, 1:2], di["bb"].rearrange("o h -> h o"),
                    allow_slow_non_contiguous=True)
            ctx.wait(nc.tensor, "Abw")
            ctx.wait(nc.scalar, "Abw2")
            for col, nm in ((0, "ga"), (NH, "bb")):
                for tq in range(4):
                    ctx.wait(nc.tensor, "Abge", ctx.cnt("Abge"))
                    for din in range(NDT):
                        mmb = nc.tensor.matmul(out=ps_bg[:],
                                               lhsT=wab[:, din * 2 * NH + col: din * 2 * NH + col + NH],
                                               rhs=lnT[:, din * TL + tq * 512: din * TL + (tq + 1) * 512],
                                               start=(din == 0), stop=(din == NDT - 1))
                    ctx.inc(mmb, "Abgm", 1)
                    ctx.wait(s, "Abgm")
                    bcol = basb[:, 0:1] if col == 0 else basb[:, 1:2]
                    ev = s.activation(out=bg16[:, (0 if col == 0 else TL) + tq * 512:
                                               (0 if col == 0 else TL) + (tq + 1) * 512],
                                      in_=ps_bg[:], func=AF.Identity, bias=bcol)
                    ctx.inc(ev, "Abge", 1)
            ctx.wait(s, "Abge")
            ctx.inc(s.activation(out=bg16[:, TL:2 * TL], in_=bg16[:, TL:2 * TL], func=AF.Sigmoid), "Abs", 1)
            ctx.inc(s.activation(out=bgt[:], in_=bg16[:, 0:TL], func=AF.Abs), "Abs", 1)
            ctx.wait(v, "Abs")
            ctx.inc(v.tensor_scalar(out=bgt[:], in0=bgt[:], scalar1=-1.0, scalar2=None, op0=ALU.mult), "Abv", 1)
            ctx.wait(s, "Abv")
            ctx.inc(s.activation(out=bgt[:], in_=bgt[:], func=AF.Exp), "Abs", 1)
            ctx.wait(v, "Abs")
            ctx.inc(v.tensor_scalar(out=bgt[:], in0=bgt[:], scalar1=1.0, scalar2=None, op0=ALU.add), "Abv", 1)
            ctx.wait(s, "Abv")
            ctx.inc(s.activation(out=bgt[:], in_=bgt[:], func=AF.Ln), "Abs", 1)
            ctx.wait(v, "Abs")
            ctx.edge(v, v.tensor_scalar(out=junk[0:NH, 0:TL] if False else bg16[:, 0:TL],
                                        in0=bg16[:, 0:TL], scalar1=0.0, scalar2=None, op0=ALU.min))
            ctx.inc(v.tensor_tensor(out=bg16[:, 0:TL], in0=bg16[:, 0:TL], in1=bgt[:], op=ALU.subtract),
                    "Abv", 1)
            ctx.wait(nc.tensor, "Abv")
            ctx.wait(nc.tensor, "Abs")
            for c in range(NCH):
                for which, tab in ((0, None), (1, ctx.bt)):
                    srcsl = bg16[:, which * TL + c * 128: which * TL + (c + 1) * 128]
                    ctx.wait(nc.tensor, "Asme", ctx.cnt("Asme"))
                    mmt = nc.tensor.matmul(out=ps_sm[:], lhsT=srcsl, rhs=ctx.id16_f32,
                                           start=True, stop=True)
                    ctx.inc(mmt, "Asmm", 1)
                    ctx.wait(v, "Asmm")
                    dst = (ctx.gt if which == 0 else ctx.bt)[:, c * NH:(c + 1) * NH]
                    if which == 0:
                        ctx.inc(v.tensor_copy(out=ctx.ngt[:, c * NH:(c + 1) * NH], in_=ps_sm[:]), "Asme", 1)
                    else:
                        ctx.inc(v.tensor_copy(out=dst, in_=ps_sm[:]), "Asme", 1)
            ctx.wait(nc.tensor, "Asme")
            for c in range(NCH):
                ctx.wait(nc.tensor, "Agce", ctx.cnt("Agce"))
                mmc = nc.tensor.matmul(out=ps_sm[:], lhsT=ctx.tri_f32,
                                       rhs=ctx.ngt[:, c * NH:(c + 1) * NH], start=True, stop=True)
                ctx.inc(mmc, "Agcm", 1)
                ctx.wait(v, "Agcm")
                ctx.inc(v.tensor_copy(out=ctx.gt[:, c * NH:(c + 1) * NH], in_=ps_sm[:]), "Agce", 1)
            ctx.wait(v, "Agce")
            ctx.inc(v.tensor_scalar(out=ctx.ngt[:], in0=ctx.gt[:], scalar1=-1.0, scalar2=None,
                                    op0=ALU.mult), "Angt", 1)

        tabs = ctx.tabs
        for h in range(NH):
            ctx.edge(v, v.tensor_scalar(out=ctx.lamdiag[:, h * 128:(h + 1) * 128], in0=ctx.id_bf,
                                        scalar1=tabs[:, h:h + 1], scalar2=None, op0=ALU.mult))
            ctx.wait(nc.tensor, "Alr", ctx.cnt("Alr"))
            mml = nc.tensor.matmul(out=ps_t2[0:1, :], lhsT=tabs[:, h:h + 1], rhs=ctx.id_f32,
                                   start=True, stop=True)
            ctx.inc(mml, "Alm", 1)
            ctx.wait(v, "Alm")
            ctx.inc(v.tensor_copy(out=junk[0:1, :], in_=ps_t2[0:1, :]), "Alr", 1)
            ctx.wait(nc.tensor, "Alr")
            mml2 = nc.tensor.matmul(out=ps_t3[:], lhsT=ctx.ones_f32, rhs=junk[0:1, :],
                                    start=True, stop=True)
            ctx.inc(mml2, "Alm2", 1)
            ctx.wait(v, "Alm2")
            ctx.inc(v.tensor_scalar(out=ctx.lambc[:, h * 128:(h + 1) * 128], in0=ps_t3[:],
                                    scalar1=1e-8, scalar2=None, op0=ALU.add), "Alr2", 1)
            ctx.wait(nc.tensor, "Alr2")

        ctx.barrier()
        with nc.sbuf_tensor("A_wotmp", [128, D], F32) as wot:
            for din in range(NDT):
                if din >= 1:
                    ctx.wait(nc.sync, "Awos", ctx.cnt("Awos"))
                ctx.dma(nc.sync, "Awol", wot[:], di["Wo"][din * 128:(din + 1) * 128, :])
                ctx.wait(v, "Awol")
                sc = v.tensor_scalar(out=lnT[:, din * D:(din + 1) * D], in0=wot[:],
                                     scalar1=tabs[:, 16:17], scalar2=None, op0=ALU.mult)
                ctx.inc(sc, "Awos", 1)
        ctx.wait(nc.sync, "Awos")
        ctx.barrier()





def _attn_phase_a(ctx):
    nc = ctx.nc
    ds = ctx.ds
    v, s, g = nc.vector, nc.scalar, nc.gpsimd

    with ExitStack() as es:
        ee = es.enter_context
        knc = ee(nc.sbuf_tensor("Pa_knc", [128, NH * HD], BF16))
        vnc = ee(nc.sbuf_tensor("Pa_vnc", [128, NH * HD], BF16))
        grow = ee(nc.sbuf_tensor("Pa_grow", [1, 128], F32))
        arow = ee(nc.sbuf_tensor("Pa_arow", [1, 128], F32))
        w0 = ee(nc.sbuf_tensor("Pa_w0", [128, 128], F32))
        btri = ee(nc.sbuf_tensor("Pa_btri", [128, 128], F32))
        wtb = ee(nc.sbuf_tensor("Pa_wtb", [128, 128], BF16))
        wcol = ee(nc.sbuf_tensor("Pa_wcol", [128, 1], F32))
        kw = ee(nc.sbuf_tensor("Pa_kw", [128, 128], BF16))
        pstage = ee(nc.sbuf_tensor("Pa_pstage", [128, 256], F32))
        Lkk = ee(nc.sbuf_tensor("Pa_Lkk", [128, NH * HD], F32))
        Lkv = ee(nc.sbuf_tensor("Pa_Lkv", [128, NH * HD], F32))
        zrow = ee(nc.sbuf_tensor("Pa_zrow", [1, NH * HD], F32))
        hstage = ee(nc.sbuf_tensor("Pa_hstage", [128, NH * HD], F32))
        hstage2 = ee(nc.sbuf_tensor("Pa_hstage2", [128, NH * HD], F32))
        hbf = ee(nc.sbuf_tensor("Pa_hbf", [128, NH * HD], BF16))
        drow = ee(nc.sbuf_tensor("Pa_drow", [1, 128], F32))
        ps_r = ee(nc.psum_tensor("Pa_psr", [1, 128], F32))
        ps_b = ee(nc.psum_tensor("Pa_psb", [128, 128], F32))
        ps_p1 = ee(nc.psum_tensor("Pa_psp1", [128, 128], F32))
        ps_p2 = ee(nc.psum_tensor("Pa_psp2", [128, 128], F32))
        ps_ac = ee(nc.psum_tensor("Pa_psac", [128, NH], F32))

        ctx.inc(v.memset(Lkk[:], 0.0), "Painit", 1)
        ctx.inc(v.memset(Lkv[:], 0.0), "Painit", 1)
        ctx.inc(v.memset(zrow[:], 0.0), "Painit", 1)
        ctx.inc(v.memset(ctx.prodtab[0:1, 0:NH], 1.0), "Painit", 1)
        ctx.wait(v, "Painit")

        for c in range(NCH):
            if c >= 1:
                ctx.wait(nc.sync, "Pakk", ctx.cnt("Pakk"))
            ctx.dma(nc.sync, "Paldk", knc[:], ds["knat"][c * 128:(c + 1) * 128, :])
            ctx.dma(nc.sync, "Paldv", vnc[:], ds["vnat"][c * 128:(c + 1) * 128, :])
            ctx.wait(g, "Painit")
            ctx.wait(g, "Palu", ctx.cnt("Palu"))
            ctx.dma(g, "Palp0", ds["lpref"][c, 0], Lkk[:])
            ctx.dma(g, "Palp1", ds["lpref"][c, 1], Lkv[:])
            for h in range(NH):
                col = c * NH + h
                ctx.wait(nc.tensor, "Page", ctx.cnt("Page"))
                mmr = nc.tensor.matmul(out=ps_r[0:1, :], lhsT=ctx.gt[:, col:col + 1], rhs=ctx.id_f32,
                                       start=True, stop=True)
                ctx.inc(mmr, "Pagm", 1)
                ctx.wait(v, "Pagm")
                ctx.wait(v, "Paae")
                ctx.inc(v.tensor_copy(out=grow[0:1, :], in_=ps_r[0:1, :]), "Page", 1)
                ctx.wait(s, "Page")
                ctx.inc(s.activation(out=arow[0:1, :], in_=grow[0:1, :], func=AF.Exp), "Paae", 1)
                ctx.wait(v, "Paae")
                ctx.edge(v, v.tensor_copy(out=ctx.aCtab[0:1, col:col + 1], in_=arow[0:1, 127:128]))
                ctx.wait(nc.tensor, "Paw0", ctx.cnt("Paw0"))
                ctx.wait(nc.tensor, "Page")
                mmb = nc.tensor.matmul(out=ps_b[:], lhsT=ctx.ones_f32, rhs=grow[0:1, :],
                                       start=True, stop=True)
                ctx.inc(mmb, "Pabm", 1)
                ctx.wait(v, "Pabm")
                ctx.wait(v, "Paw0", ctx.cnt("Paw0"))
                dgm = v.tensor_scalar(out=w0[:], in0=ps_b[:], scalar1=ctx.ngt[:, col:col + 1],
                                      scalar2=0.0, op0=ALU.add, op1=ALU.min)
                ctx.inc(dgm, "Padg", 1)
                ctx.wait(s, "Padg")
                a0 = s.activation(out=w0[:], in_=w0[:], func=AF.Exp)
                ctx.inc(a0, "Paw0", 1)
                ctx.wait(v, "Paw0")
                ctx.edge(v, v.tensor_scalar(out=btri[:], in0=ctx.tri_f32, scalar1=ctx.bt[:, col:col + 1],
                                            scalar2=None, op0=ALU.mult))
                ctx.wait(v, "Pamse_d", ctx.cnt("Pamse_d"))
                ctx.inc(v.tensor_tensor(out=wtb[:], in0=w0[:], in1=btri[:], op=ALU.mult), "Pamw", 1)
                ctx.edge(v, v.tensor_tensor(out=wcol[:], in0=w0[:, 127:128], in1=btri[:, 127:128],
                                            op=ALU.mult))
                ctx.wait(v, "Paldk")
                ctx.inc(v.tensor_scalar(out=kw[:], in0=knc[:, h * 128:(h + 1) * 128],
                                        scalar1=wcol[:], scalar2=None, op0=ALU.mult), "Pakwv", 1)
                ctx.wait(g, "Pamw")
                ctx.dma(g, "Pamse_d", ds["masks"][c, h], wtb[:])
                ctx.wait(nc.tensor, "Pakwv")
                ctx.wait(nc.tensor, "Paldv")
                ctx.wait(nc.tensor, "Pape", ctx.cnt("Pape"))
                m1 = nc.tensor.matmul(out=ps_p1[:], lhsT=knc[:, h * 128:(h + 1) * 128], rhs=kw[:],
                                      start=True, stop=True)
                ctx.inc(m1, "Pakk", 1)
                m2 = nc.tensor.matmul(out=ps_p2[:], lhsT=vnc[:, h * 128:(h + 1) * 128], rhs=kw[:],
                                      start=True, stop=True)
                ctx.inc(m2, "Pakk", 1)
                ctx.wait(v, "Pakk")
                ctx.wait(v, "Papd0", ctx.cnt("Papd0"))
                ctx.wait(v, "Papd1", ctx.cnt("Papd1"))
                ctx.inc(v.tensor_copy(out=pstage[:, 0:128], in_=ps_p1[:]), "Pape", 1)
                ctx.inc(v.tensor_copy(out=pstage[:, 128:256], in_=ps_p2[:]), "Pape", 1)
                ctx.wait(g, "Pape")
                ctx.dma(g, "Papd0", ds["partials"][c, 0, :, h * 128:(h + 1) * 128], pstage[:, 0:128])
                ctx.dma(g, "Papd1", ds["partials"][c, 1, :, h * 128:(h + 1) * 128], pstage[:, 128:256])
            ctx.wait(nc.tensor, "Paace", ctx.cnt("Paace"))
            ctx.wait(nc.tensor, "se_BassVectorEngine", ctx.cnt("se_BassVectorEngine"))
            mac = nc.tensor.matmul(out=ps_ac[:], lhsT=ctx.ones_f32,
                                   rhs=ctx.aCtab[0:1, c * NH:(c + 1) * NH], start=True, stop=True)
            ctx.inc(mac, "Paacm", 1)
            ctx.wait(v, "Paacm")
            ctx.inc(v.tensor_copy(out=ctx.acbc[:, c * NH:(c + 1) * NH], in_=ps_ac[:]), "Paace", 1)
            ctx.edge(v, v.tensor_tensor(out=ctx.prodtab[0:1, (c + 1) * NH:(c + 2) * NH],
                                        in0=ctx.prodtab[0:1, c * NH:(c + 1) * NH],
                                        in1=ctx.aCtab[0:1, c * NH:(c + 1) * NH], op=ALU.mult))
            ctx.wait(v, "Palp0")
            ctx.wait(v, "Palp1")
            ctx.wait(nc.sync, "Papd0")
            ctx.wait(nc.sync, "Papd1")
            ctx.dma(nc.sync, "Palr0", hstage[:], ds["partials"][c, 0])
            ctx.dma(nc.sync, "Palr1", hstage2[:], ds["partials"][c, 1])
            ctx.wait(v, "Palr0")
            ctx.wait(v, "Palr1")
            for h in range(NH):
                sl = slice(h * 128, (h + 1) * 128)
                accol = ctx.acbc[:, c * NH + h: c * NH + h + 1]
                e1 = v.scalar_tensor_tensor(out=Lkk[:, sl], in0=Lkk[:, sl], scalar=accol,
                                            in1=hstage[:, sl], op0=ALU.mult, op1=ALU.add)
                ctx.inc(e1, "Palu", 1)
                e2 = v.scalar_tensor_tensor(out=Lkv[:, sl], in0=Lkv[:, sl], scalar=accol,
                                            in1=hstage2[:, sl], op0=ALU.mult, op1=ALU.add)
                ctx.inc(e2, "Palu", 1)
            ctx.wait(v, "Palu")

        ctx.wait(g, "Palu")
        ctx.wait(g, "se_BassVectorEngine")
        ctx.dma(g, "Pacc", ds["cc_in"][0:HD, :], Lkk[:])
        ctx.dma(g, "Pacc", ds["cc_in"][HD:2 * HD, :], Lkv[:])
        ctx.dma(g, "Pacc", ds["cc_in"][2 * HD:2 * HD + 1, :], zrow[0:1, :])
        ctx.dma(g, "Pacc", ds["cc_in"][2 * HD:2 * HD + 1, 0:NH],
                ctx.prodtab[0:1, NCH * NH:(NCH + 1) * NH])
        ctx.wait(g, "Pacc")
        cc = g.collective_compute("AllGather", ALU.bypass, replica_groups=[list(range(NCORES))],
                                  ins=[ds["cc_in"].opt()],
                                  outs=[ds["cc_out"].rearrange("r p c -> (r p) c").opt()])
        ctx.inc(cc, "Paccd", 1)
        ctx.wait(g, "Paccd")
        ctx.wait(nc.sync, "Paccd")

        for j in range(NCORES):
            ctx.dma(nc.sync, "Padr", drow[0:1, j * NH:(j + 1) * NH],
                    ds["cc_out"][j, 2 * HD:2 * HD + 1, 0:NH])
        ctx.wait(nc.tensor, "Padr")
        mdb = nc.tensor.matmul(out=ps_b[:], lhsT=ctx.ones_f32, rhs=drow[0:1, :], start=True, stop=True)
        ctx.inc(mdb, "Padbm", 1)
        ctx.wait(v, "Padbm")
        dbc = ee(nc.sbuf_tensor("Pa_dbc", [128, 128], F32))
        ctx.inc(v.tensor_copy(out=dbc[:], in_=ps_b[:]), "Padbe", 1)
        ctx.inc(v.memset(Lkk[:], 0.0), "Pagz", 1)
        ctx.inc(v.memset(Lkv[:], 0.0), "Pagz", 1)
        ctx.wait(g, "Pagz")
        ctx.dma(g, "Papw0", ds["prefixes"][0, 0:HD, :], Lkk[:])
        ctx.dma(g, "Papw1", ds["prefixes"][0, HD:2 * HD, :], Lkv[:])
        for j in range(NCORES - 1):
            ctx.wait(nc.sync, "Pagu", ctx.cnt("Pagu"))
            ctx.dma(nc.sync, "Pagl0", hstage[:], ds["cc_out"][j, 0:HD, :])
            ctx.dma(nc.sync, "Pagl1", hstage2[:], ds["cc_out"][j, HD:2 * HD, :])
            ctx.wait(v, "Pagl0")
            ctx.wait(v, "Pagl1")
            ctx.wait(v, "Padbe")
            ctx.wait(v, "Papw0")
            ctx.wait(v, "Papw1")
            for h in range(NH):
                sl = slice(h * 128, (h + 1) * 128)
                dcol = dbc[:, j * NH + h: j * NH + h + 1]
                e1 = v.scalar_tensor_tensor(out=Lkk[:, sl], in0=Lkk[:, sl], scalar=dcol,
                                            in1=hstage[:, sl], op0=ALU.mult, op1=ALU.add)
                ctx.inc(e1, "Pagu", 1)
                e2 = v.scalar_tensor_tensor(out=Lkv[:, sl], in0=Lkv[:, sl], scalar=dcol,
                                            in1=hstage2[:, sl], op0=ALU.mult, op1=ALU.add)
                ctx.inc(e2, "Pagu", 1)
            ctx.wait(g, "Pagu")
            ctx.dma(g, "Papw0", ds["prefixes"][j + 1, 0:HD, :], Lkk[:])
            ctx.dma(g, "Papw1", ds["prefixes"][j + 1, HD:2 * HD, :], Lkv[:])
        ctx.wait(nc.sync, "Papw0")
        ctx.wait(nc.sync, "Papw1")
        pid = nc.sync.partition_id()
        ctx.dma(nc.sync, "Pahs", hstage[:], ds["prefixes"][pid, 0:HD, :])
        ctx.dma(nc.sync, "Pahs", hstage2[:], ds["prefixes"][pid, HD:2 * HD, :])

        for c in range(NCH):
            ctx.wait(nc.tensor, "Papbe", ctx.cnt("Papbe"))
            mpb = nc.tensor.matmul(out=ps_ac[:], lhsT=ctx.ones_f32,
                                   rhs=ctx.prodtab[0:1, c * NH:(c + 1) * NH], start=True, stop=True)
            ctx.inc(mpb, "Papbm", 1)
            ctx.wait(v, "Papbm")
            ctx.inc(v.tensor_copy(out=ctx.prodbc[:, c * NH:(c + 1) * NH], in_=ps_ac[:]), "Papbe", 1)

        for c in range(NCH):
            for st_i in range(2):
                hs = hstage if st_i == 0 else hstage2
                ctx.wait(nc.sync, "Pahb", ctx.cnt("Pahb"))
                ctx.dma(nc.sync, "Pahl", Lkk[:], ds["lpref"][c, st_i])
                ctx.wait(v, "Pahl")
                ctx.wait(v, "Pahs")
                for h in range(NH):
                    sl = slice(h * 128, (h + 1) * 128)
                    pcol = ctx.prodbc[:, c * NH + h: c * NH + h + 1]
                    e1 = v.scalar_tensor_tensor(out=Lkv[:, sl], in0=hs[:, sl], scalar=pcol,
                                                in1=Lkk[:, sl], op0=ALU.mult, op1=ALU.add)
                    ctx.inc(e1, "Pahf", 1)
                ctx.wait(v, "Pahf")
                ctx.wait(v, "Pahd", ctx.cnt("Pahd"))
                ctx.inc(v.tensor_copy(out=hbf[:], in_=Lkv[:]), "Pahb", 1)
                ctx.wait(g, "Pahb")
                ctx.dma(g, "Pahd", ds["h0"][c, st_i], hbf[:])
        ctx.wait(g, "Pahd")

def _attn_phase_b(ctx):
    nc = ctx.nc
    ds = ctx.ds
    v, s, g = nc.vector, nc.scalar, nc.gpsimd
    wo = ctx.lnT
    NQ = 10
    Q_D, Q_PQ, Q_DN, Q_DONE, Q_AL, Q_NAL, Q_BE, Q_T0, Q_T1, Q_T2 = range(NQ)

    with ExitStack() as es:
        ee = es.enter_context
        h0c = [ee(nc.sbuf_tensor("Pb_h0%d" % i, [128, NH * HD], BF16)) for i in range(2)]
        hvc = ee(nc.sbuf_tensor("Pb_hv", [128, NH * HD], BF16))
        knc = [ee(nc.sbuf_tensor("Pb_knat%d" % i, [128, NH * HD], BF16)) for i in range(2)]
        ktc = [ee(nc.sbuf_tensor("Pb_knT%d" % i, [128, NH * HD], BF16)) for i in range(2)]
        vtc = ee(nc.sbuf_tensor("Pb_vT", [128, NH * HD], BF16))
        msk = [ee(nc.sbuf_tensor("Pb_msk%d" % i, [128, NH * HD], BF16)) for i in range(2)]
        bq = ee(nc.sbuf_tensor("Pb_bq", [128, NH * HD], BF16))
        dga = [ee(nc.sbuf_tensor("Pb_dga%d" % i, [128, NH * HD], BF16)) for i in range(2)]
        acol = ee(nc.sbuf_tensor("Pb_acol", [128, 2 * NH], F32))
        xst = [ee(nc.sbuf_tensor("Pb_x%d" % i, [128, NH * HD], F32)) for i in range(2)]
        rst = [ee(nc.sbuf_tensor("Pb_r%d" % i, [128, NH * HD], BF16)) for i in range(2)]
        pst = [ee(nc.sbuf_tensor("Pb_p%d" % i, [128, NH * HD], BF16)) for i in range(2)]
        qtk = [ee(nc.sbuf_tensor("Pb_qtk%d" % i, [128, NH * HD], BF16)) for i in range(2)]
        prod = ee(nc.sbuf_tensor("Pb_prod", [128, NH * HD], BF16))
        pfeat = ee(nc.sbuf_tensor("Pb_pfeat", [128, NH * HD], BF16))
        apfeat = ee(nc.sbuf_tensor("Pb_apfeat", [128, NH * HD], BF16))
        qfeat = ee(nc.sbuf_tensor("Pb_qfeat", [128, NH * HD], BF16))
        msb = ee(nc.sbuf_tensor("Pb_msb", [128, NH * HD], BF16))
        sc = [ee(nc.sbuf_tensor("Pb_sc%d" % i, [128, NQ * NH], F32)) for i in range(2)]
        den = ee(nc.sbuf_tensor("Pb_den", [128, NH * HD], F32))
        kk2 = ee(nc.sbuf_tensor("Pb_kk2", [128, NH * HD], BF16))
        otok = den
        junk = kk2
        x1c = den
        x2c = x1c
        psPT = ee(nc.psum_tensor("Pb_psPT", [128, 1024], F32))
        psSS = ee(nc.psum_tensor("Pb_psSS", [128, 1024], F32))
        psQQ = ee(nc.psum_tensor("Pb_psQQ", [128, 1024], F32))
        psQT = ee(nc.psum_tensor("Pb_psQT", [128, 1024], F32))

        def qcolsm(m, q, gh=None, h=None):
            if h is not None:
                return sc[m][:, q * NH + h: q * NH + h + 1]
            return sc[m][:, q * NH + gh * 8: q * NH + gh * 8 + 8]

        def junkm(m):
            return junk if m == 0 else den

        def hsl(h):
            return slice(h * 128, (h + 1) * 128)

        def halfsl(gh):
            return slice(gh * 8 * 128, (gh + 1) * 8 * 128)

        def mv_round(m, src_tok, dgam, h0m, kncm, ktcm, mskm, qtkm):
            for gh in range(2):
                heads = range(gh * 8, gh * 8 + 8)
                ctx.wait(nc.tensor, "Bpw%d%d" % (m, gh), ctx.cnt("Bpw%d%d" % (m, gh)))
                ctx.wait(nc.tensor, "BpeS", ctx.cnt("BpeS"))
                for i, h in enumerate(heads):
                    mm = nc.tensor.matmul(out=psPT[:, i * 128:(i + 1) * 128], lhsT=src_tok[:, hsl(h)],
                                          rhs=ctx.id_bf, start=True, stop=True)
                ctx.inc(mm, "Bptm", 1)
                ctx.wait(s, "Bptm")
                ctx.inc(s.activation(out=pfeat[:, halfsl(gh)], in_=psPT[:], func=AF.Copy), "BpeS", 1)
                ctx.wait(nc.tensor, "BpeS")
                for i, h in enumerate(heads):
                    mm = nc.tensor.matmul(out=psPT[:, i * 128:(i + 1) * 128], lhsT=src_tok[:, hsl(h)],
                                          rhs=dgam[:, hsl(h)], start=True, stop=True)
                ctx.inc(mm, "Bptm", 1)
                ctx.wait(s, "Bptm")
                ctx.inc(s.activation(out=apfeat[:, halfsl(gh)], in_=psPT[:], func=AF.Copy), "BpeS", 1)
                ctx.wait(nc.tensor, "BssE", ctx.cnt("BssE"))
                ctx.wait(nc.tensor, "Bden2", ctx.cnt("Bden2"))
                ctx.wait(nc.tensor, "BpeS")
                for i, h in enumerate(heads):
                    mm = nc.tensor.matmul(out=psSS[:, i * 128:(i + 1) * 128], lhsT=ktcm[:, hsl(h)],
                                          rhs=pfeat[:, hsl(h)], start=True, stop=True)
                ctx.inc(mm, "Bsm", 1)
                ctx.wait(s, "Bsm")
                ctx.wait(s, "Bmm", ctx.cnt("Bmm"))
                ctx.inc(s.activation(out=qfeat[:, halfsl(1 - gh)], in_=psSS[:], func=AF.Copy),
                        "BssE", 1)
                ctx.wait(g, "BssE")
                mmv = g.tensor_tensor(out=msb[:, halfsl(gh)], in0=mskm[:, halfsl(gh)],
                                      in1=qfeat[:, halfsl(1 - gh)], op=ALU.mult)
                ctx.inc(mmv, "Bmm", 1)
                ctx.wait(nc.tensor, "BqeS", ctx.cnt("BqeS"))
                ctx.wait(nc.tensor, "Bmm")
                for i, h in enumerate(heads):
                    osl = psQQ[:, i * 128:(i + 1) * 128]
                    nc.tensor.matmul(out=osl, lhsT=h0m[:, hsl(h)], rhs=apfeat[:, hsl(h)],
                                     start=True, stop=False)
                    nc.tensor.matmul(out=osl, lhsT=ctx.lamdiag[:, hsl(h)], rhs=pfeat[:, hsl(h)],
                                     start=False, stop=False)
                    mm = nc.tensor.matmul(out=osl, lhsT=kncm[:, hsl(h)], rhs=msb[:, hsl(h)],
                                          start=False, stop=True)
                ctx.inc(mm, "Bqm", 1)
                ctx.wait(s, "Bqm")
                ctx.inc(s.activation(out=qfeat[:, halfsl(gh)], in_=psQQ[:], func=AF.Copy), "BqeS", 1)
                ctx.wait(nc.tensor, "BqteS", ctx.cnt("BqteS"))
                ctx.wait(nc.tensor, "BqeS")
                for i, h in enumerate(heads):
                    mm = nc.tensor.matmul(out=psQT[:, i * 128:(i + 1) * 128], lhsT=qfeat[:, hsl(h)],
                                          rhs=ctx.id_bf, start=True, stop=True)
                ctx.inc(mm, "Bqtm", 1)
                ctx.wait(s, "Bqtm")
                ctx.wait(s, "Brw%d%d" % (m, gh), ctx.cnt("Brw%d%d" % (m, gh)))
                ctx.wait(s, "Bofm", ctx.cnt("Bofm"))
                ctx.inc(s.activation(out=qtkm[:, halfsl(gh)], in_=psQT[:], func=AF.Copy), "BqteS", 1)

        NPAIR = NCH // 2
        for cp in range(NPAIR):
            ctx.wait(nc.sync, "Bx2s", ctx.cnt("Bx2s"))
            for m in range(2):
                c = 2 * cp + m
                ctx.dma(nc.sync, "Bld0", h0c[m][:], ds["h0"][c, 0])
                ctx.dma(nc.sync, "Bld2", knc[m][:], ds["knat"][c * 128:(c + 1) * 128, :])
                ctx.dma(nc.sync, "Bld4", ktc[m][:].rearrange("p (h t) -> p h t", h=NH),
                        ds["knT"].rearrange("(h p) t -> p h t", p=128)[:, :, c * 128:(c + 1) * 128])
                ctx.dma(nc.sync, "Bld6", msk[m][:].rearrange("p (h t) -> p h t", h=NH),
                        ds["masks"][c].rearrange("h p t -> p h t"))
            ctx.wait(s, "Bwom", ctx.cnt("Bwom"))
            ctx.wait(v, "Bwom", ctx.cnt("Bwom"))
            ctx.inc(s.activation(out=acol[:, 0:NH], in_=ctx.gt[:, 2 * cp * NH:(2 * cp + 1) * NH],
                                 func=AF.Exp), "Bac", 1)
            ctx.inc(s.activation(out=acol[:, NH:2 * NH], in_=ctx.gt[:, (2 * cp + 1) * NH:(2 * cp + 2) * NH],
                                 func=AF.Exp), "Bac", 1)
            ctx.wait(v, "Bac")
            for m in range(2):
                for h in range(NH):
                    dd = v.tensor_scalar(out=dga[m][:, hsl(h)], in0=ctx.id_bf,
                                         scalar1=acol[:, m * NH + h: m * NH + h + 1],
                                         scalar2=None, op0=ALU.mult)
                    ctx.inc(dd, "Bdga", 1)
            for eng in (v, s, nc.tensor):
                for i in (0, 2, 4, 6):
                    ctx.wait(eng, "Bld%d" % i)
            ctx.wait(nc.tensor, "Bdga")

            for m in range(2):
                c = 2 * cp + m
                for mm_ in range(2):
                    for gg_ in range(2):
                        ctx.wait(nc.sync, "Bpw%d%d" % (mm_, gg_), ctx.cnt("Bpw%d%d" % (mm_, gg_)))
                ctx.dma(nc.sync, "Bld3", bq[:], ds["bq"][c * 128:(c + 1) * 128, :])
                ctx.wait(v, "Bld3")
                ctx.wait(v, "Bwtm", ctx.cnt("Bwtm"))
                ctx.wait(v, "Bd0", ctx.cnt("Bd0"))
                ctx.inc(v.tensor_tensor(out=kk2[:], in0=knc[m][:], in1=knc[m][:], op=ALU.mult), "Bkk2", 1)
                ctx.wait(v, "Bdbm", ctx.cnt("Bdbm"))
                for h in range(NH):
                    hh = v.tensor_tensor(out=prod[:, hsl(h)], in0=h0c[m][:, hsl(h)], in1=ctx.id_bf,
                                         op=ALU.mult)
                    ctx.inc(hh, "Bhdi", 1)
                ctx.wait(nc.tensor, "Bhdi")
                ctx.wait(nc.tensor, "Bkk2")
                ctx.wait(nc.tensor, "Bx2v", ctx.cnt("Bx2v"))
                for gh in range(2):
                    heads = range(gh * 8, gh * 8 + 8)
                    ctx.wait(nc.tensor, "BdbS", ctx.cnt("BdbS"))
                    ctx.wait(nc.tensor, "BqeS", ctx.cnt("BqeS"))
                    for i, h in enumerate(heads):
                        mm = nc.tensor.matmul(out=psQQ[:, i * 128:(i + 1) * 128], lhsT=ctx.onesmat_bf,
                                              rhs=prod[:, hsl(h)], start=True, stop=True)
                    ctx.inc(mm, "Bdbm", 1)
                    ctx.wait(s, "Bdbm")
                    ctx.inc(s.activation(out=den[:, halfsl(gh)], in_=psQQ[:], func=AF.Copy), "BdbS", 1)
                    ctx.wait(nc.tensor, "Bden2", ctx.cnt("Bden2"))
                    ctx.wait(nc.tensor, "BssE", ctx.cnt("BssE"))
                    for i, h in enumerate(heads):
                        mm = nc.tensor.matmul(out=psSS[:, i * 128:(i + 1) * 128], lhsT=msk[m][:, hsl(h)],
                                              rhs=kk2[:, hsl(h)], start=True, stop=True)
                    ctx.inc(mm, "Bwtm", 1)
                    ctx.wait(v, "Bwtm")
                    ctx.wait(v, "BdbS")
                    for i, h in enumerate(heads):
                        e1 = v.scalar_tensor_tensor(out=den[:, hsl(h)], in0=den[:, hsl(h)],
                                                    scalar=acol[:, m * NH + h: m * NH + h + 1],
                                                    in1=psSS[:, i * 128:(i + 1) * 128],
                                                    op0=ALU.mult, op1=ALU.add)
                        ctx.inc(e1, "Bden2", 1)
                    ctx.wait(v, "Bden2")
                    e2 = v.tensor_tensor(out=den[:, halfsl(gh)], in0=den[:, halfsl(gh)],
                                         in1=ctx.lambc[:, halfsl(gh)], op=ALU.add)
                    ctx.inc(e2, "Bden3", 1)
                    ctx.wait(v, "Bden3")
                    e3 = v.reciprocal(out=den[:, halfsl(gh)], in_=den[:, halfsl(gh)])
                    ctx.inc(e3, "Bden4", 1)
                    ctx.wait(v, "Bden4")
                    e4 = v.tensor_tensor(out=xst[m][:, halfsl(gh)], in0=bq[:, halfsl(gh)],
                                         in1=den[:, halfsl(gh)], op=ALU.mult)
                    ctx.inc(e4, "Bx0m", 1)
                    ctx.wait(v, "Bx0m")
                    e5 = v.tensor_copy(out=pst[m][:, halfsl(gh)], in_=xst[m][:, halfsl(gh)])
                    ctx.inc(e5, "Bpw%d%d" % (m, gh), 1)
                mv_round(m, pst[m], dga[m], h0c[m], knc[m], ktc[m], msk[m], qtk[m])
                ctx.wait(v, "BqteS")
                ctx.inc(v.memset(sc[m][:], 0.0), "Bsc0", 1)
                for gh in range(2):
                    e6 = v.tensor_tensor(out=rst[m][:, halfsl(gh)], in0=bq[:, halfsl(gh)],
                                         in1=qtk[m][:, halfsl(gh)], op=ALU.subtract)
                    ctx.inc(e6, "Brw%d%d" % (m, gh), 1)
                ctx.wait(s, "Brw%d0" % m)
                ctx.wait(s, "Brw%d1" % m)
                ctx.wait(s, "Bsc0")
                for h in range(NH):
                    a1 = s.activation(out=junkm(m)[:, hsl(h)], in_=rst[m][:, hsl(h)], func=AF.Square,
                                      accum_out=qcolsm(m, Q_D, h=h))
                    ctx.inc(a1, "Bd0", 1)
                ctx.wait(v, "Bd0")
                for gh in range(2):
                    e7 = v.tensor_copy(out=pst[m][:, halfsl(gh)], in_=rst[m][:, halfsl(gh)])
                    ctx.inc(e7, "Bpw%d%d" % (m, gh), 1)

            def chain(m, qv):
                q16 = lambda q: sc[m][:, q * NH:(q + 1) * NH]
                for gh in range(2):
                    ctx.wait(v, "BqteS", qv - 1 + gh)
                    e1 = v.tensor_tensor(out=prod[:, halfsl(gh)], in0=pst[m][:, halfsl(gh)],
                                         in1=qtk[m][:, halfsl(gh)], op=ALU.mult)
                    ctx.edge(v, e1)
                    e2 = v.tensor_reduce(out=qcolsm(m, Q_PQ, gh=gh)[:, :, None],
                                         in_=prod[:, halfsl(gh)].rearrange("p (h d) -> p h d", h=8),
                                         axis=AX.X, op=ALU.add)
                    ctx.edge(v, e2)
                ctx.edge(v, v.tensor_scalar(out=q16(Q_T0), in0=q16(Q_D), scalar1=1e-10,
                                            scalar2=None, op0=ALU.is_lt))
                ctx.edge(v, v.tensor_tensor(out=q16(Q_DONE), in0=q16(Q_DONE), in1=q16(Q_T0),
                                            op=ALU.max))
                ctx.edge(v, v.tensor_tensor(out=q16(Q_T1), in0=q16(Q_PQ), in1=q16(Q_PQ),
                                            op=ALU.mult))
                ctx.edge(v, v.tensor_scalar(out=q16(Q_T1), in0=q16(Q_T1), scalar1=1e-20,
                                            scalar2=None, op0=ALU.is_lt))
                ctx.edge(v, v.tensor_tensor(out=q16(Q_DONE), in0=q16(Q_DONE), in1=q16(Q_T1),
                                            op=ALU.max))
                ctx.edge(v, v.tensor_scalar(out=q16(Q_T0), in0=q16(Q_DONE), scalar1=-1.0,
                                            scalar2=1.0, op0=ALU.mult, op1=ALU.add))
                ctx.edge(v, v.tensor_tensor(out=q16(Q_T1), in0=q16(Q_PQ), in1=q16(Q_T0),
                                            op=ALU.mult))
                ctx.edge(v, v.tensor_tensor(out=q16(Q_T1), in0=q16(Q_T1), in1=q16(Q_DONE),
                                            op=ALU.add))
                ctx.edge(v, v.reciprocal(out=q16(Q_T2), in_=q16(Q_T1)))
                ctx.edge(v, v.tensor_tensor(out=q16(Q_AL), in0=q16(Q_D), in1=q16(Q_T2),
                                            op=ALU.mult))
                ctx.edge(v, v.tensor_tensor(out=q16(Q_AL), in0=q16(Q_AL), in1=q16(Q_T0),
                                            op=ALU.mult))
                ctx.inc(v.tensor_scalar(out=q16(Q_NAL), in0=q16(Q_AL), scalar1=-1.0,
                                        scalar2=None, op0=ALU.mult), "Bal", 1)
                ctx.wait(v, "Bal")
                for h in range(NH):
                    xu = v.scalar_tensor_tensor(out=xst[m][:, hsl(h)], in0=pst[m][:, hsl(h)],
                                                scalar=qcolsm(m, Q_AL, h=h),
                                                in1=xst[m][:, hsl(h)],
                                                op0=ALU.mult, op1=ALU.add)
                    ctx.inc(xu, "Bxu", 1)
                for h in range(NH):
                    rr = v.scalar_tensor_tensor(out=rst[m][:, hsl(h)], in0=qtk[m][:, hsl(h)],
                                                scalar=qcolsm(m, Q_NAL, h=h),
                                                in1=rst[m][:, hsl(h)],
                                                op0=ALU.mult, op1=ALU.add)
                    ctx.inc(rr, "Brw%d%d" % (m, h // 8), 1)
                ctx.wait(s, "Brw%d0" % m)
                ctx.wait(s, "Brw%d1" % m)
                for h in range(NH):
                    a1 = s.activation(out=junkm(m)[:, hsl(h)], in_=rst[m][:, hsl(h)],
                                      func=AF.Square, accum_out=qcolsm(m, Q_DN, h=h))
                    ctx.inc(a1, "Bdn", 1)
                ctx.wait(v, "Bdn")
                ctx.edge(v, v.tensor_tensor(out=q16(Q_T1), in0=q16(Q_D), in1=q16(Q_T0),
                                            op=ALU.mult))
                ctx.edge(v, v.tensor_tensor(out=q16(Q_T1), in0=q16(Q_T1), in1=q16(Q_DONE),
                                            op=ALU.add))
                ctx.edge(v, v.reciprocal(out=q16(Q_T2), in_=q16(Q_T1)))
                ctx.edge(v, v.tensor_tensor(out=q16(Q_BE), in0=q16(Q_DN), in1=q16(Q_T2),
                                            op=ALU.mult))
                ctx.edge(v, v.tensor_tensor(out=q16(Q_BE), in0=q16(Q_BE), in1=q16(Q_T0),
                                            op=ALU.mult))
                ctx.edge(v, v.tensor_copy(out=q16(Q_D), in_=q16(Q_DN)))
                ctx.wait(v, "Bxu")
                for h in range(NH):
                    pp = v.scalar_tensor_tensor(out=pst[m][:, hsl(h)], in0=pst[m][:, hsl(h)],
                                                scalar=qcolsm(m, Q_BE, h=h),
                                                in1=rst[m][:, hsl(h)],
                                                op0=ALU.mult, op1=ALU.add)
                    ctx.inc(pp, "Bpw%d%d" % (m, h // 8), 1)

            qv = [None, None]
            for it in range(CGI):
                mv_round(0, pst[0], dga[0], h0c[0], knc[0], ktc[0], msk[0], qtk[0])
                qv[0] = ctx.cnt("BqteS")
                if it > 0:
                    chain(1, qv[1])
                mv_round(1, pst[1], dga[1], h0c[1], knc[1], ktc[1], msk[1], qtk[1])
                qv[1] = ctx.cnt("BqteS")
                chain(0, qv[0])
            chain(1, qv[1])

            for m in range(2):
                c = 2 * cp + m
                ctx.wait(nc.sync, "Bsm", ctx.cnt("Bsm"))
                ctx.wait(nc.sync, "Bqm", ctx.cnt("Bqm"))
                ctx.dma(nc.sync, "Bld1", hvc[:], ds["h0"][c, 1])
                ctx.dma(nc.sync, "Bld5", vtc[:].rearrange("p (h t) -> p h t", h=NH),
                        ds["vT"].rearrange("(h p) t -> p h t", p=128)[:, :, c * 128:(c + 1) * 128])
                ctx.wait(v, "Bpw%d0" % m)
                ctx.wait(v, "Bpw%d1" % m)
                ctx.wait(v, "Bxu")
                ctx.wait(v, "Bx2v", ctx.cnt("Bx2v"))
                ctx.inc(v.tensor_copy(out=prod[:], in_=xst[m][:]), "Bxb", 1)
                for eng in (v, s, nc.tensor):
                    ctx.wait(eng, "Bld1")
                    ctx.wait(eng, "Bld5")
                ctx.wait(nc.tensor, "Bxb")
                ctx.wait(s, "Bx2s", ctx.cnt("Bx2s"))
                mv_out(ctx, prod, dga[m], h0c[m], hvc, knc[m], ktc[m], vtc, msk[m], msb,
                       pfeat, apfeat, qfeat, otok, psPT, psSS, psQQ, psQT, hsl, halfsl)
                ctx.wait(s, "BqteS")
                ctx.wait(s, "Bos", ctx.cnt("Bos"))
                for h in range(NH):
                    a1 = s.activation(out=junk[:, hsl(h)], in_=otok[:, hsl(h)], func=AF.Square,
                                      accum_out=qcolsm(m, Q_T0, h=h))
                    ctx.inc(a1, "Bos", 1)
                ctx.wait(v, "Bos")
                scm = sc[m]
                ctx.edge(v, v.tensor_scalar(out=scm[:, Q_T1 * NH:(Q_T1 + 1) * NH],
                                            in0=scm[:, Q_T0 * NH:(Q_T0 + 1) * NH],
                                            scalar1=1.0 / HD, scalar2=1e-6, op0=ALU.mult, op1=ALU.add))
                ctx.inc(v.nop(), "Bov", 1)
                ctx.wait(s, "Bov")
                a2 = s.activation(out=scm[:, Q_T2 * NH:(Q_T2 + 1) * NH],
                                  in_=scm[:, Q_T1 * NH:(Q_T1 + 1) * NH], func=AF.Sqrt)
                ctx.inc(a2, "Bos2", 1)
                ctx.wait(v, "Bos2")
                ctx.inc(v.reciprocal(out=scm[:, Q_T1 * NH:(Q_T1 + 1) * NH],
                                     in_=scm[:, Q_T2 * NH:(Q_T2 + 1) * NH]), "Bov2", 1)
                ctx.wait(s, "Bov2")
                for h in range(NH):
                    a3 = s.activation(out=qtk[m][:, hsl(h)], in_=otok[:, hsl(h)], func=AF.Identity,
                                      scale=qcolsm(m, Q_T1, h=h))
                    ctx.inc(a3, "BonS", 1)
                ctx.wait(nc.sync, "BonS")
                ctx.dma(nc.sync, "Bld7", x1c[:], ds["x1"][c * 128:(c + 1) * 128, :])
                ctx.wait(nc.tensor, "BonS")
                for gh in range(2):
                    ctx.wait(nc.tensor, "BofS", ctx.cnt("BofS"))
                    for i, h in enumerate(range(gh * 8, gh * 8 + 8)):
                        mm = nc.tensor.matmul(out=psPT[:, i * 128:(i + 1) * 128],
                                              lhsT=qtk[m][:, hsl(h)],
                                              rhs=ctx.id_bf, start=True, stop=True)
                    ctx.inc(mm, "Bofm", 1)
                    ctx.wait(s, "Bofm")
                    ctx.inc(s.activation(out=pfeat[:, halfsl(gh)], in_=psPT[:], func=AF.Copy),
                            "BofS", 1)
                ctx.wait(nc.tensor, "BofS")
                ctx.wait(nc.tensor, "Bx2v", ctx.cnt("Bx2v"))
                for dq in range(4):
                    psout = (psQQ if dq < 2 else psQT)
                    osl = psout[:, (dq % 2) * 512:(dq % 2) * 512 + 512]
                    for h in range(NH):
                        mm = nc.tensor.matmul(out=osl, lhsT=pfeat[:, hsl(h)],
                                              rhs=wo[:, h * D + dq * 512: h * D + (dq + 1) * 512],
                                              start=(h == 0), stop=(h == NH - 1))
                    ctx.inc(mm, "Bwom", 1)
                ctx.wait(v, "Bwom")
                ctx.wait(v, "Bld7")
                ctx.wait(v, "Bx2s", ctx.cnt("Bx2s"))
                for dq in range(4):
                    psout = (psQQ if dq < 2 else psQT)
                    osl = psout[:, (dq % 2) * 512:(dq % 2) * 512 + 512]
                    ad = v.tensor_tensor(out=x2c[:, dq * 512:(dq + 1) * 512], in0=osl,
                                         in1=x1c[:, dq * 512:(dq + 1) * 512], op=ALU.add)
                    ctx.inc(ad, "Bx2v", 1)
                ctx.wait(g, "Bx2v")
                ctx.dma(g, "Bx2s", ds["x2"][c * 128:(c + 1) * 128, :], x2c[:])
        ctx.wait(g, "Bx2s")


def mv_out(ctx, xb, dga, h0c, hvc, knc, ktc, vtc, msk, msb, pfeat, apfeat, qfeat,
           otok, psPT, psSS, psQQ, psQT, hsl, halfsl):
    nc = ctx.nc
    v, s = nc.vector, nc.scalar
    ctx.wait(nc.tensor, "Bx2v", ctx.cnt("Bx2v"))
    for gh in range(2):
        heads = range(gh * 8, gh * 8 + 8)
        ctx.wait(nc.tensor, "BpeS", ctx.cnt("BpeS"))
        for i, h in enumerate(heads):
            mm = nc.tensor.matmul(out=psPT[:, i * 128:(i + 1) * 128], lhsT=xb[:, hsl(h)],
                                  rhs=ctx.id_bf, start=True, stop=True)
        ctx.inc(mm, "Bptm", 1)
        ctx.wait(s, "Bptm")
        ctx.inc(s.activation(out=pfeat[:, halfsl(gh)], in_=psPT[:], func=AF.Copy), "BpeS", 1)
        ctx.wait(nc.tensor, "BpeS")
        for i, h in enumerate(heads):
            mm = nc.tensor.matmul(out=psPT[:, i * 128:(i + 1) * 128], lhsT=xb[:, hsl(h)],
                                  rhs=dga[:, hsl(h)], start=True, stop=True)
        ctx.inc(mm, "Bptm", 1)
        ctx.wait(s, "Bptm")
        ctx.inc(s.activation(out=apfeat[:, halfsl(gh)], in_=psPT[:], func=AF.Copy), "BpeS", 1)
        ctx.wait(nc.tensor, "BssE", ctx.cnt("BssE"))
        ctx.wait(nc.tensor, "BpeS")
        for i, h in enumerate(heads):
            mm = nc.tensor.matmul(out=psSS[:, i * 128:(i + 1) * 128], lhsT=vtc[:, hsl(h)],
                                  rhs=pfeat[:, hsl(h)], start=True, stop=True)
        ctx.inc(mm, "Bsm", 1)
        ctx.wait(s, "Bsm")
        ctx.wait(s, "Bmm", ctx.cnt("Bmm"))
        ctx.inc(s.activation(out=qfeat[:, halfsl(1 - gh)], in_=psSS[:], func=AF.Copy), "BssE", 1)
        ctx.wait(nc.gpsimd, "BssE")
        ctx.inc(nc.gpsimd.tensor_tensor(out=msb[:, halfsl(gh)], in0=msk[:, halfsl(gh)],
                                        in1=qfeat[:, halfsl(1 - gh)], op=ALU.mult), "Bmm", 1)
        ctx.wait(nc.tensor, "BqeS", ctx.cnt("BqeS"))
        ctx.wait(nc.tensor, "Bmm")
        for i, h in enumerate(heads):
            osl = psQQ[:, i * 128:(i + 1) * 128]
            nc.tensor.matmul(out=osl, lhsT=hvc[:, hsl(h)], rhs=apfeat[:, hsl(h)],
                             start=True, stop=False)
            mm = nc.tensor.matmul(out=osl, lhsT=knc[:, hsl(h)], rhs=msb[:, hsl(h)],
                                  start=False, stop=True)
        ctx.inc(mm, "Bqm", 1)
        ctx.wait(s, "Bqm")
        ctx.inc(s.activation(out=qfeat[:, halfsl(gh)], in_=psQQ[:], func=AF.Copy), "BqeS", 1)
        ctx.wait(nc.tensor, "BqteS", ctx.cnt("BqteS"))
        ctx.wait(nc.tensor, "BqeS")
        for i, h in enumerate(heads):
            mm = nc.tensor.matmul(out=psQT[:, i * 128:(i + 1) * 128], lhsT=qfeat[:, hsl(h)],
                                  rhs=ctx.id_bf, start=True, stop=True)
        ctx.inc(mm, "Bqtm", 1)
        ctx.wait(s, "Bqtm")
        ao = s.activation(out=otok[:, halfsl(gh)], in_=psQT[:], func=AF.Copy)
        ctx.inc(ao, "BqteS", 1)


_PROGRAM_CACHE = {}


def _get_program(do_attn, taps=()):
    key = (do_attn, tuple(sorted(taps)))
    if key not in _PROGRAM_CACHE:
        _PROGRAM_CACHE[key] = build_program(do_attn=do_attn, taps=taps)
    return _PROGRAM_CACHE[key]


def make_in_maps(inputs):
    x_full = np.ascontiguousarray(inputs["L0_attn_x"].reshape(T, D))
    base = {
        "gate_up": np.ascontiguousarray(inputs["L0_ffn_gate_up_proj"]),
        "down": np.ascontiguousarray(inputs["L0_ffn_down_proj"]),
        "sh_gate": np.ascontiguousarray(inputs["L0_ffn_sh_gate_w"]),
        "sh_up": np.ascontiguousarray(inputs["L0_ffn_sh_up_w"]),
        "sh_down": np.ascontiguousarray(inputs["L0_ffn_sh_down_w"]),
        "shared_gate": np.ascontiguousarray(inputs["L0_ffn_shared_gate_w"].reshape(1, D)),
        "router": np.ascontiguousarray(inputs["L0_ffn_router_weight"]),
        "Wq": np.ascontiguousarray(inputs["L0_attn_Wq"]),
        "Wk": np.ascontiguousarray(inputs["L0_attn_Wk"]),
        "Wv": np.ascontiguousarray(inputs["L0_attn_Wv"]),
        "Wo": np.ascontiguousarray(inputs["L0_attn_Wo"]),
        "Wa": np.ascontiguousarray(inputs["L0_attn_Wa"]),
        "Wb": np.ascontiguousarray(inputs["L0_attn_Wb"]),
        "ba": np.ascontiguousarray(inputs["L0_attn_ba"].reshape(1, NH)),
        "bb": np.ascontiguousarray(inputs["L0_attn_bb"].reshape(1, NH)),
        "lamb": np.ascontiguousarray(inputs["L0_attn_lambda_params"].reshape(NH, HD)),
        "onw": np.ascontiguousarray(inputs["L0_attn_o_norm_weight"].reshape(HD, 1)),
    }
    in_maps = []
    for c in range(NCORES):
        m = dict(base)
        m["x"] = np.ascontiguousarray(x_full[c * TL:(c + 1) * TL])
        in_maps.append(m)
    return in_maps


def kernel(**inputs):
    nc = _get_program(DO_ATTN)
    in_maps = make_in_maps(inputs)
    res = run_bass_kernel_spmd(nc, in_maps, core_ids=list(range(NCORES)))
    y = np.concatenate([res.results[c]["y"] for c in range(NCORES)], axis=0)
    return y.reshape(1, T, D)


# version 13 (final)
# speedup vs baseline: 1.2658x; 1.1268x over previous
import numpy as np
import ml_dtypes
import concourse.bass as bass
import concourse.mybir as mybir
from concourse.bass_utils import run_bass_kernel_spmd
from contextlib import ExitStack

F32 = mybir.dt.float32
BF16 = mybir.dt.bfloat16
ALU = mybir.AluOpType
AF = mybir.ActivationFunctionType
AX = mybir.AxisListType

NCORES = 8
T = 16384
TL = T // NCORES
D = 2048
NH = 16
HD = 128
CT = 128
NCH = TL // CT
CGI = 30
E = 8
IM = 1408
ISH = 5632
NT = TL // 128
NDT = D // 128
LAMB_LB = 0.25

BF = ml_dtypes.bfloat16
DO_ATTN = True


class Ctx:
    pass


def _emit_common(ctx):
    nc = ctx.nc
    ee = ctx.stack.enter_context
    ctx.sems = {}
    ctx.semcnt = {}
    ctx.semgen = [0]

    def sem(name):
        if name not in ctx.sems:
            ctx.sems[name] = ee(nc.semaphore(f"{name}_g{ctx.semgen[0]}"))
            ctx.semcnt[name] = 0
        return ctx.sems[name]

    def inc(instr, name, k=1):
        instr.then_inc(sem(name), k)
        ctx.semcnt[name] += k
        return ctx.semcnt[name]

    def wait(eng, name, val=None):
        s = sem(name)
        eng.wait_ge(s, ctx.semcnt[name] if val is None else val)

    def cnt(name):
        sem(name)
        return ctx.semcnt[name]

    def barrier():
        nc.all_engine_barrier(sem_only=True)

    def flush_sems():
        names = list(ctx.sems)
        if not names:
            return
        for n in names:
            nc.gpsimd.wait_ge(ctx.sems[n], ctx.semcnt[n])
        barrier()
        nc.clear_and_free_semaphores([ctx.sems[n] for n in names])
        barrier()
        for n in names:
            del ctx.sems[n]
            del ctx.semcnt[n]
        ctx.semgen[0] += 1
    ctx.flush_sems = flush_sems

    _ENG_NAME = {}

    def edge(eng, instr):
        nm = "selfedge_" + str(id(eng) if eng not in _ENG_NAME else _ENG_NAME[eng])
        nm = "se_" + type(eng).__name__
        inc(instr, nm, 1)
        wait(eng, nm)
        return instr

    def dma(eng, name, out, in_, **kw):
        wait(eng, name)
        d = eng.dma_start(out=out, in_=in_, **kw)
        inc(d, name, 16)
        return d
    ctx.dma = dma

    ctx.sem, ctx.inc, ctx.wait, ctx.cnt, ctx.barrier, ctx.edge = sem, inc, wait, cnt, barrier, edge


def build_program(do_attn=DO_ATTN, taps=(), stages=99):
    nc = bass.Bass("TRN2", target_bir_lowering=False)
    ctx = Ctx()
    ctx.nc = nc
    ctx.stack = ExitStack()
    ctx.stack.__enter__()
    _emit_common(ctx)
    ee = ctx.stack.enter_context

    id_bf_d = nc.inline_tensor(np.eye(128, dtype=BF), "id_bf_d").ap()
    id_f32_d = nc.inline_tensor(np.eye(128, dtype=np.float32), "id_f32_d").ap()
    id16_f32_d = nc.inline_tensor(np.eye(16, dtype=np.float32), "id16_f32_d").ap()
    tri = (np.arange(128)[:, None] <= np.arange(128)[None, :]).astype(np.float32)
    tri_f32_d = nc.inline_tensor(tri, "tri_f32_d").ap()
    ones_f32_d = nc.inline_tensor(np.ones((1, 128), dtype=np.float32), "ones_f32_d").ap()
    onescol_f32_d = nc.inline_tensor(np.ones((128, 1), dtype=np.float32), "onescol_f32_d").ap()
    onesmat_bf_d = nc.inline_tensor(np.ones((128, 128), dtype=BF), "onesmat_bf_d").ap()
    ctx.id_bf = ee(nc.sbuf_tensor("id_bf", [128, 128], BF16))[:]
    ctx.id_f32 = ee(nc.sbuf_tensor("id_f32", [128, 128], F32))[:]
    ctx.id16_f32 = ee(nc.sbuf_tensor("id16_f32", [16, 16], F32))[:]
    ctx.tri_f32 = ee(nc.sbuf_tensor("tri_f32", [128, 128], F32))[:]
    ctx.ones_f32 = ee(nc.sbuf_tensor("ones_f32", [1, 128], F32))[:]
    ctx.onescol_f32 = ee(nc.sbuf_tensor("onescol_f32", [128, 1], F32))[:]
    ctx.onesmat_bf = ee(nc.sbuf_tensor("onesmat_bf", [128, 128], BF16))[:]

    di = {}
    di["x"] = nc.dram_tensor("x", [TL, D], F32, kind="ExternalInput").ap()
    di["gate_up"] = nc.dram_tensor("gate_up", [E, 2 * IM, D], F32, kind="ExternalInput").ap()
    di["down"] = nc.dram_tensor("down", [E, D, IM], F32, kind="ExternalInput").ap()
    di["sh_gate"] = nc.dram_tensor("sh_gate", [ISH, D], F32, kind="ExternalInput").ap()
    di["sh_up"] = nc.dram_tensor("sh_up", [ISH, D], F32, kind="ExternalInput").ap()
    di["sh_down"] = nc.dram_tensor("sh_down", [D, ISH], F32, kind="ExternalInput").ap()
    di["shared_gate"] = nc.dram_tensor("shared_gate", [1, D], F32, kind="ExternalInput").ap()
    di["router"] = nc.dram_tensor("router", [E, D], F32, kind="ExternalInput").ap()
    di["Wq"] = nc.dram_tensor("Wq", [D, D], F32, kind="ExternalInput").ap()
    di["Wk"] = nc.dram_tensor("Wk", [D, D], F32, kind="ExternalInput").ap()
    di["Wv"] = nc.dram_tensor("Wv", [D, D], F32, kind="ExternalInput").ap()
    di["Wo"] = nc.dram_tensor("Wo", [D, D], F32, kind="ExternalInput").ap()
    di["Wa"] = nc.dram_tensor("Wa", [D, NH], F32, kind="ExternalInput").ap()
    di["Wb"] = nc.dram_tensor("Wb", [D, NH], F32, kind="ExternalInput").ap()
    di["ba"] = nc.dram_tensor("ba", [1, NH], F32, kind="ExternalInput").ap()
    di["bb"] = nc.dram_tensor("bb", [1, NH], F32, kind="ExternalInput").ap()
    di["lamb"] = nc.dram_tensor("lamb", [NH, HD], F32, kind="ExternalInput").ap()
    di["onw"] = nc.dram_tensor("onw", [HD, 1], F32, kind="ExternalInput").ap()
    ctx.di = di

    ctx.y = nc.dram_tensor("y", [TL, D], F32, kind="ExternalOutput").ap()

    ds = {}
    ds["gu_bf"] = nc.dram_tensor("gu_bf", [E, 2 * IM, D], BF16).ap()
    ds["down_bf"] = nc.dram_tensor("down_bf", [E, D, IM], BF16).ap()
    ds["shg_bf"] = nc.dram_tensor("shg_bf", [ISH, D], BF16).ap()
    ds["shu_bf"] = nc.dram_tensor("shu_bf", [ISH, D], BF16).ap()
    ds["shd_bf"] = nc.dram_tensor("shd_bf", [D, ISH], BF16).ap()
    ds["acc"] = nc.dram_tensor("acc", [D, TL], F32).ap()
    ds["x1"] = nc.dram_tensor("x1", [TL, D], F32).ap()
    ds["x2"] = nc.dram_tensor("x2", [TL, D], F32).ap()
    ds["crow"] = nc.dram_tensor("crow", [1, 9 * TL], F32).ap()
    ds["bq"] = nc.dram_tensor("bq", [TL, NH * HD], BF16).ap()
    ds["knat"] = nc.dram_tensor("knat_s", [TL, NH * HD], BF16).ap()
    ds["vnat"] = nc.dram_tensor("vnat_s", [TL, NH * HD], BF16).ap()
    ds["knT"] = nc.dram_tensor("knT_s", [NH * HD, TL], BF16).ap()
    ds["vT"] = nc.dram_tensor("vT_s", [NH * HD, TL], BF16).ap()
    ds["masks"] = nc.dram_tensor("masks_s", [NCH, NH, CT, CT], BF16).ap()
    ds["partials"] = nc.dram_tensor("partials_s", [NCH, 2, HD, NH * HD], F32).ap()
    ds["lpref"] = nc.dram_tensor("lpref_s", [NCH, 2, HD, NH * HD], F32).ap()
    ds["h0"] = nc.dram_tensor("h0_s", [NCH, 2, HD, NH * HD], BF16).ap()
    ds["cc_in"] = nc.dram_tensor("cc_in", [2 * HD + 1, NH * HD], F32).ap()
    ds["cc_out"] = nc.dram_tensor("cc_out", [NCORES, 2 * HD + 1, NH * HD], F32).ap()
    ds["prefixes"] = nc.dram_tensor("prefixes_s", [NCORES, 2 * HD, NH * HD], F32).ap()
    ctx.ds = ds

    ctx.taps = {}
    for name in taps:
        if name in ("ln1T", "ln3T"):
            ctx.taps[name] = nc.dram_tensor("tap_" + name, [128, NDT * TL], BF16, kind="ExternalOutput").ap()
        elif name == "comb":
            ctx.taps[name] = nc.dram_tensor("tap_" + name, [1, 9 * TL], F32, kind="ExternalOutput").ap()
        elif name in ("x1",):
            ctx.taps[name] = nc.dram_tensor("tap_" + name, [TL, D], F32, kind="ExternalOutput").ap()
        elif name == "acc1":
            ctx.taps[name] = nc.dram_tensor("tap_" + name, [D, TL], F32, kind="ExternalOutput").ap()
        elif name == "act":
            ctx.taps[name] = nc.dram_tensor("tap_" + name, [128, 11 * TL], BF16, kind="ExternalOutput").ap()

    ctx.lnT = ee(nc.sbuf_tensor("lnT", [128, NDT * TL], BF16))
    ctx.rt9 = ee(nc.sbuf_tensor("rt9", [128, NDT * 9], F32))
    ctx.tabs = ee(nc.sbuf_tensor("tabs", [128, 32], F32))

    for dstc, srcc in ((ctx.id_bf, id_bf_d), (ctx.id_f32, id_f32_d), (ctx.id16_f32, id16_f32_d),
                       (ctx.tri_f32, tri_f32_d), (ctx.ones_f32, ones_f32_d),
                       (ctx.onescol_f32, onescol_f32_d), (ctx.onesmat_bf, onesmat_bf_d)):
        ctx.dma(nc.sync, "cst", dstc, srcc)
    for eng in (nc.tensor, nc.vector, nc.scalar, nc.gpsimd, nc.sync):
        ctx.wait(eng, "cst")
    _weights_prologue(ctx)
    _build_router_T(ctx)
    _build_small_tables(ctx)
    ctx.wait(nc.gpsimd, "wcast")
    ctx.barrier()

    if stages < 2:
        ctx.stack.__exit__(None, None, None)
        return nc
    ctx.flush_sems()
    _ln_router_stage(ctx, src=di["x"], tag="L1", tap="ln1T")
    ctx.barrier()
    if "comb" in ctx.taps:
        ctx.inc(nc.gpsimd.dma_start(out=ctx.taps["comb"], in_=ds["crow"]), "tapc", 16)
        ctx.wait(nc.gpsimd, "tapc")
    ctx.barrier()
    if stages < 3:
        ctx.stack.__exit__(None, None, None)
        return nc
    ctx.flush_sems()
    _ffn_experts(ctx, tag="F1")
    ctx.barrier()
    if "acc1" in ctx.taps:
        ctx.inc(nc.gpsimd.dma_start(out=ctx.taps["acc1"], in_=ds["acc"]), "tapa", 16)
        ctx.wait(nc.gpsimd, "tapa")
    if stages < 4:
        ctx.stack.__exit__(None, None, None)
        return nc
    ctx.flush_sems()
    _residual_stage(ctx, src=di["x"], dst=ds["x1"], tag="R1", tap="x1")
    ctx.barrier()

    if stages < 5:
        ctx.stack.__exit__(None, None, None)
        return nc
    if do_attn:
        _attention(ctx)
    else:
        ctx.inc(nc.gpsimd.dma_start(out=ds["x2"], in_=ds["x1"]), "xcp", 16)
        ctx.wait(nc.gpsimd, "xcp")
    ctx.barrier()

    if stages >= 6:
        ctx.flush_sems()
        _ln_router_stage(ctx, src=ds["x2"], tag="L3", tap="ln3T")
        ctx.barrier()
    if stages >= 7:
        ctx.flush_sems()
        _ffn_experts(ctx, tag="F2")
        ctx.barrier()
    if stages >= 8:
        ctx.flush_sems()
        _residual_stage(ctx, src=ds["x2"], dst=ctx.y, tag="R2", tap=None)
        ctx.barrier()

    ctx.stack.__exit__(None, None, None)
    return nc


def _weights_prologue(ctx):
    nc = ctx.nc
    g = nc.gpsimd
    di, ds = ctx.di, ctx.ds
    for e in range(E):
        ctx.dma(g, "wcast", ds["gu_bf"][e], di["gate_up"][e])
        ctx.dma(g, "wcast", ds["down_bf"][e], di["down"][e])
    ctx.dma(g, "wcast", ds["shg_bf"], di["sh_gate"])
    ctx.dma(g, "wcast", ds["shu_bf"], di["sh_up"])
    ctx.dma(g, "wcast", ds["shd_bf"], di["sh_down"])


def _build_router_T(ctx):
    nc = ctx.nc
    with ExitStack() as es:
        stage = es.enter_context(nc.sbuf_tensor("rt_stage", [16, D], F32))
        ps = es.enter_context(nc.psum_tensor("ps_rt", [128, 16], F32))
        ctx.dma(nc.sync, "rtld", stage[0:E, :], ctx.di["router"])
        ctx.dma(nc.sync, "rtld", stage[E:E + 1, :], ctx.di["shared_gate"])
        ctx.wait(nc.tensor, "rtld")
        for dt in range(NDT):
            if dt >= 1:
                ctx.wait(nc.tensor, "rtev", dt)
            mm = nc.tensor.matmul(out=ps[:, 0:9], lhsT=stage[0:9, dt * 128:(dt + 1) * 128],
                                  rhs=ctx.id16_f32[0:9, 0:9], start=True, stop=True)
            ctx.inc(mm, "rtmm", 1)
            ctx.wait(nc.vector, "rtmm")
            ctx.inc(nc.vector.tensor_copy(out=ctx.rt9[:, dt * 9:(dt + 1) * 9], in_=ps[:, 0:9]), "rtev", 1)
        ctx.wait(nc.tensor, "rtev")
        ctx.wait(nc.sync, "rtev")


def _build_small_tables(ctx):
    nc = ctx.nc
    tabs = ctx.tabs
    with ExitStack() as es:
        lp = es.enter_context(nc.sbuf_tensor("lp_stage", [NH, HD], F32))
        t1 = es.enter_context(nc.sbuf_tensor("lp_t1", [NH, HD], F32))
        sp = es.enter_context(nc.sbuf_tensor("lp_sp", [NH, HD], F32))
        ps = es.enter_context(nc.psum_tensor("ps_lp", [128, 16], F32))
        ctx.dma(nc.sync, "lpld", lp[:], ctx.di["lamb"])
        ctx.dma(nc.sync, "lpld", tabs[0:HD, 16:17], ctx.di["onw"])
        v, s = nc.vector, nc.scalar
        ctx.wait(s, "lpld")
        ctx.inc(s.activation(out=t1[:], in_=lp[:], func=AF.Abs), "lps", 1)
        ctx.wait(v, "lps")
        ctx.inc(v.tensor_scalar(out=t1[:], in0=t1[:], scalar1=-1.0, scalar2=None, op0=ALU.mult), "lpv", 1)
        ctx.wait(s, "lpv")
        ctx.inc(s.activation(out=t1[:], in_=t1[:], func=AF.Exp), "lps", 1)
        ctx.wait(v, "lps")
        ctx.inc(v.tensor_scalar(out=t1[:], in0=t1[:], scalar1=1.0, scalar2=None, op0=ALU.add), "lpv", 1)
        ctx.wait(s, "lpv")
        ctx.inc(s.activation(out=t1[:], in_=t1[:], func=AF.Ln), "lps", 1)
        ctx.wait(v, "lps")
        ctx.edge(v, v.tensor_scalar(out=sp[:], in0=lp[:], scalar1=0.0, scalar2=None, op0=ALU.max))
        ctx.edge(v, v.tensor_tensor(out=sp[:], in0=sp[:], in1=t1[:], op=ALU.add))
        ctx.inc(v.tensor_scalar(out=sp[:], in0=sp[:], scalar1=LAMB_LB, scalar2=None, op0=ALU.add), "lpv", 1)
        ctx.wait(nc.tensor, "lpv")
        ctx.inc(nc.tensor.matmul(out=ps[:, 0:16], lhsT=sp[:], rhs=ctx.id16_f32, start=True, stop=True), "lpmm", 1)
        ctx.wait(v, "lpmm")
        ctx.inc(v.tensor_copy(out=tabs[:, 0:16], in_=ps[:, 0:16]), "lpev", 1)
        ctx.wait(nc.tensor, "lpev")
        ctx.wait(nc.sync, "lpev")


def _ln_router_stage(ctx, src, tag, tap, do_router=True):
    import os
    RMODE = int(os.environ.get("RMODE", "2")) if do_router else 0
    TMODE = int(os.environ.get("TMODE", "2"))
    LTILES = int(os.environ.get("LTILES", str(NT)))
    """LN(src) -> ctx.lnT (bf16 feature-major [:, dt*TL+tok]); fp32 router -> ctx.crow
    (cols c*TL+tok: c 0..7 = 0.5*combine_c, c 8 = 0.5*sigmoid(shared))."""
    nc = ctx.nc
    lnT = ctx.lnT
    crow = ctx.ds["crow"]
    tg = tag
    v, s = nc.vector, nc.scalar
    with ExitStack() as es:
        xs = [es.enter_context(nc.sbuf_tensor(f"{tg}_x{i}", [128, D], F32)) for i in range(2)]
        lntok = es.enter_context(nc.sbuf_tensor(tg + "_lntok", [128, D], F32))
        lnf = [es.enter_context(nc.sbuf_tensor(f"{tg}_lnf{i}", [128, 128], F32)) for i in range(2)]
        st = es.enter_context(nc.sbuf_tensor(tg + "_st", [128, 8], F32))
        pr = es.enter_context(nc.sbuf_tensor(tg + "_pr", [128, 16], F32))
        q2 = es.enter_context(nc.sbuf_tensor(tg + "_q2", [128, 16], F32))
        comb = es.enter_context(nc.sbuf_tensor(tg + "_comb", [128, 16], F32))
        ps_t = [es.enter_context(nc.psum_tensor(f"{tg}_ps_t{i}", [128, 128], F32)) for i in range(2)]
        ps_lg = es.enter_context(nc.psum_tensor(tg + "_ps_lg", [128, 16], F32))
        ps_cr = es.enter_context(nc.psum_tensor(tg + "_ps_cr", [1, 128], F32))
        crs = es.enter_context(nc.sbuf_tensor(tg + "_crs", [1, 128], F32))

        for tt in range(LTILES):
            buf = xs[tt % 2]
            if tt >= 2:
                ctx.wait(nc.sync, tg + "app", tt - 1)
            ctx.dma(nc.sync, tg + "x%d" % (tt % 2), buf[:], src[tt * 128:(tt + 1) * 128, :])
            if tt >= 1:
                ctx.wait(s, tg + "t", tt * NDT)
            ctx.wait(s, tg + "x%d" % (tt % 2))
            a1 = s.activation(out=lntok[:], in_=buf[:], func=AF.Copy, accum_out=st[:, 0:1])
            ctx.inc(a1, tg + "s", 1)
            ctx.wait(s, tg + "s")
            ctx.inc(s.activation(out=lntok[:], in_=buf[:], func=AF.Square, accum_out=st[:, 1:2]), tg + "s", 1)
            ctx.wait(v, tg + "s")
            v.tensor_scalar(out=st[:, 2:3], in0=st[:, 0:1], scalar1=1.0 / D, scalar2=None, op0=ALU.mult)
            ctx.edge(v, v.tensor_scalar(out=st[:, 3:4], in0=st[:, 1:2], scalar1=1.0 / D, scalar2=None, op0=ALU.mult))
            ctx.edge(v, v.scalar_tensor_tensor(out=st[:, 4:5], in0=st[:, 2:3], scalar=st[:, 2:3], in1=st[:, 3:4],
                                               op0=ALU.mult, op1=ALU.subtract))
            ctx.inc(v.tensor_scalar(out=st[:, 4:5], in0=st[:, 4:5], scalar1=-1.0, scalar2=1e-5,
                                    op0=ALU.mult, op1=ALU.add), tg + "v", 1)
            ctx.wait(s, tg + "v")
            ctx.inc(s.activation(out=st[:, 5:6], in_=st[:, 4:5], func=AF.Sqrt), tg + "s", 1)
            if TMODE < 1:
                ctx.inc(v.nop(), tg + "app", 1)
                continue
            ctx.wait(v, tg + "s")
            v.reciprocal(out=st[:, 6:7], in_=st[:, 5:6])
            ctx.edge(v, v.tensor_scalar(out=st[:, 7:8], in0=st[:, 2:3], scalar1=-1.0, scalar2=None, op0=ALU.mult))
            app = v.tensor_scalar(out=lntok[:], in0=buf[:], scalar1=st[:, 7:8], scalar2=st[:, 6:7],
                                  op0=ALU.add, op1=ALU.mult)
            ctx.inc(app, tg + "app", 1)
            ctx.wait(nc.tensor, tg + "app")
            if TMODE < 2:
                continue
            for dt in range(NDT):
                gidx = tt * NDT + dt
                half = dt % 2
                if gidx >= 2:
                    ctx.wait(nc.tensor, tg + "ev", (gidx - 1) * 2)
                mm = nc.tensor.matmul(out=ps_t[half][:],
                                      lhsT=lntok[:, dt * 128:(dt + 1) * 128],
                                      rhs=ctx.id_f32, start=True, stop=True)
                ctx.inc(mm, tg + "t", 1)
                if RMODE >= 1 and gidx >= 2:
                    ctx.wait(v, tg + "lgm", gidx - 1)
                ctx.wait(v, tg + "t")
                c1 = v.tensor_copy(out=lnf[half][:], in_=ps_t[half][:])
                ctx.inc(c1, tg + "ev", 1)
                ctx.wait(v, tg + "ev")
                ctx.inc(v.tensor_copy(out=lnT[:, dt * TL + tt * 128: dt * TL + (tt + 1) * 128],
                                      in_=lnf[half][:]), tg + "ev", 1)
                if RMODE >= 1:
                    if dt == 0 and tt >= 1:
                        ctx.wait(nc.tensor, tg + "prc", tt)
                    ctx.wait(nc.tensor, tg + "ev")
                    mm2 = nc.tensor.matmul(out=ps_lg[:, 0:9], lhsT=lnf[half][:],
                                           rhs=ctx.rt9[:, dt * 9:(dt + 1) * 9],
                                           start=(dt == 0), stop=(dt == NDT - 1))
                    ctx.inc(mm2, tg + "lgm", 1)
            if RMODE < 1:
                continue
            if tt >= 1:
                ctx.wait(v, tg + "crmm", tt * 9)
            ctx.wait(v, tg + "lgm")
            ctx.wait(v, tg + "sg")
            ctx.inc(v.tensor_copy(out=pr[:, 0:9], in_=ps_lg[:, 0:9]), tg + "prc", 1)
            ctx.wait(s, tg + "prc")
            ctx.inc(s.activation(out=pr[:, 9:10], in_=pr[:, 8:9], func=AF.Sigmoid), tg + "sg", 1)
            ctx.wait(v, tg + "prc")
            ctx.edge(v, v.reduce_max(out=q2[:, 0:1], in_=pr[:, 0:8], axis=AX.X))
            ctx.inc(v.tensor_scalar(out=q2[:, 1:2], in0=q2[:, 0:1], scalar1=-1.0, scalar2=None, op0=ALU.mult),
                    tg + "pv", 1)
            ctx.wait(s, tg + "pv")
            ctx.inc(s.activation(out=pr[:, 0:8], in_=pr[:, 0:8], func=AF.Exp, bias=q2[:, 1:2]), tg + "sg", 1)
            ctx.wait(v, tg + "sg")
            ctx.edge(v, v.tensor_scalar(out=pr[:, 0:8], in0=pr[:, 0:8], scalar1=1.0, scalar2=0.0,
                                         op0=ALU.mult, op1=ALU.add, accum_out=q2[:, 2:3]))
            ctx.edge(v, v.reciprocal(out=q2[:, 3:4], in_=q2[:, 2:3]))
            ctx.edge(v, v.tensor_scalar(out=pr[:, 0:8], in0=pr[:, 0:8], scalar1=q2[:, 3:4], scalar2=None, op0=ALU.mult))
            ctx.edge(v, v.reduce_max(out=q2[:, 4:5], in_=pr[:, 0:8], axis=AX.X))
            ctx.edge(v, v.tensor_scalar(out=q2[:, 8:16], in0=pr[:, 0:8], scalar1=q2[:, 4:5], scalar2=None, op0=ALU.is_equal))
            ctx.edge(v, v.scalar_tensor_tensor(out=comb[:, 8:16], in0=q2[:, 8:16], scalar=-1e30, in1=pr[:, 0:8],
                                               op0=ALU.mult, op1=ALU.add))
            ctx.edge(v, v.reduce_max(out=q2[:, 5:6], in_=comb[:, 8:16], axis=AX.X))
            v.tensor_scalar(out=comb[:, 0:8], in0=comb[:, 8:16], scalar1=q2[:, 5:6], scalar2=None, op0=ALU.is_equal)
            ctx.edge(v, v.tensor_tensor(out=q2[:, 6:7], in0=q2[:, 4:5], in1=q2[:, 5:6], op=ALU.add))
            ctx.edge(v, v.reciprocal(out=q2[:, 7:8], in_=q2[:, 6:7]))
            ctx.edge(v, v.tensor_tensor(out=comb[:, 0:8], in0=comb[:, 0:8], in1=q2[:, 8:16], op=ALU.add))
            ctx.edge(v, v.tensor_tensor(out=comb[:, 0:8], in0=comb[:, 0:8], in1=pr[:, 0:8], op=ALU.mult))
            ctx.edge(v, v.tensor_scalar(out=comb[:, 0:8], in0=comb[:, 0:8], scalar1=q2[:, 7:8], scalar2=0.5,
                                        op0=ALU.mult, op1=ALU.mult))
            ctx.wait(v, tg + "sg")
            ctx.inc(v.tensor_scalar(out=comb[:, 8:9], in0=pr[:, 9:10], scalar1=0.5, scalar2=None, op0=ALU.mult),
                    tg + "cmb", 1)
            if RMODE < 2:
                continue
            ctx.wait(nc.tensor, tg + "cmb")
            for c in range(9):
                cidx = tt * 9 + c
                if cidx >= 1:
                    ctx.wait(nc.tensor, tg + "crev", cidx)
                mmc = nc.tensor.matmul(out=ps_cr[0:1, :], lhsT=comb[:, c:c + 1], rhs=ctx.id_f32,
                                       start=True, stop=True)
                ctx.inc(mmc, tg + "crmm", 1)
                if cidx >= 1:
                    ctx.wait(v, tg + "crst", 16 * cidx)
                ctx.wait(v, tg + "crmm")
                ctx.inc(v.tensor_copy(out=crs[0:1, :], in_=ps_cr[0:1, :]), tg + "crev", 1)
                ctx.wait(nc.sync, tg + "crev")
                ctx.inc(nc.sync.dma_start(out=crow[0:1, c * TL + tt * 128: c * TL + (tt + 1) * 128],
                                          in_=crs[0:1, :]), tg + "crst", 16)
        ctx.wait(nc.sync, tg + "ev")
        if RMODE >= 2:
            ctx.wait(nc.sync, tg + "crst")
    if tap in ctx.taps:
        ctx.inc(nc.sync.dma_start(out=ctx.taps[tap], in_=lnT[:]), tg + "tap", 16)
        ctx.wait(nc.sync, tg + "tap")


def _ffn_experts(ctx, tag):
    nc = ctx.nc
    tg = tag
    lnT, ds = ctx.lnT, ctx.ds
    crow = ds["crow"]
    v, s, g = nc.vector, nc.scalar, nc.gpsimd
    import os
    NPE = int(os.environ.get("NPE", "0"))
    NHT = 11
    pes = []
    for j in range(4):
        gblocks = []
        for srcname in ("shg_bf", "shu_bf"):
            base = j * IM
            for off, nt_ in ((0, 4), (512, 4), (1024, 3)):
                gblocks.append((ds[srcname], base + off, nt_))
        pes.append((gblocks, ds["shd_bf"], j * IM, 8))
    for e in range(E):
        gblocks = []
        for off, nt_ in ((0, 4), (512, 4), (1024, 3), (IM, 4), (IM + 512, 4), (IM + 1024, 3)):
            gblocks.append((ds["gu_bf"][e], off, nt_))
        pes.append((gblocks, ds["down_bf"][e], 0, e))
    if NPE:
        pes = pes[:NPE]

    with ExitStack() as es:
        guT = [es.enter_context(nc.sbuf_tensor(f"{tg}_guT{i}", [128, 16 * 512], BF16)) for i in range(2)]
        dnT = [es.enter_context(nc.sbuf_tensor(f"{tg}_dnT{i}", [128, NHT * 1024], BF16)) for i in range(2)]
        crow_sb = es.enter_context(nc.sbuf_tensor(tg + "_crow_sb", [1, TL], F32))
        act = es.enter_context(nc.sbuf_tensor(tg + "_act", [128, NHT * TL], BF16))
        cbc = es.enter_context(nc.sbuf_tensor(tg + "_cbc", [128, TL], BF16))
        stg = [es.enter_context(nc.sbuf_tensor(f"{tg}_stg{i}", [128, 512], F32)) for i in range(2)]
        ps_gu = [es.enter_context(nc.psum_tensor(f"{tg}_psgu{i}", [128, 512], F32)) for i in range(2)]
        ps_o = [es.enter_context(nc.psum_tensor(f"{tg}_pso{i}", [128, 512], F32)) for i in range(2)]

        gu_slot_ev = [None, None]
        o_slot_ev = [0, 0]
        acq_cnt = [0, 0]
        gud_cnt = [0, 0]
        dnd_cnt = [0, 0]
        blk_ev_need = [0, 0]

        for p, (gblocks, dn_src, dn_coloff, c_idx) in enumerate(pes):
            if p >= 1:
                ctx.wait(s, tg + "om", p * 64)
                ctx.wait(v, tg + "om", p * 64)
            if p >= 1:
                ctx.wait(nc.gpsimd, tg + "cbm", p * 4)
            ctx.inc(nc.gpsimd.dma_start(out=crow_sb[0:1, :], in_=crow[0:1, c_idx * TL:(c_idx + 1) * TL]),
                    tg + "crl", 16)
            ctx.wait(nc.tensor, tg + "crl", 16 * (p + 1))
            if gu_slot_ev[0] is not None:
                ctx.wait(nc.tensor, gu_slot_ev[0][0], gu_slot_ev[0][1])
            ctx.wait(s, tg + "guevV")
            for hh in range(4):
                mcb = nc.tensor.matmul(out=ps_gu[0][:], lhsT=ctx.ones_f32,
                                       rhs=crow_sb[0:1, hh * 512:(hh + 1) * 512],
                                       start=True, stop=True)
                ctx.inc(mcb, tg + "cbm", 1)
                ctx.wait(s, tg + "cbm")
                ctx.inc(s.activation(out=cbc[:, hh * 512:(hh + 1) * 512], in_=ps_gu[0][:], func=AF.Copy),
                        tg + "cbev", 1)
                ctx.wait(nc.tensor, tg + "cbev")


            ht_base = 0
            blki = 0
            ht_global = 0
            for (gsrc, row_off, ntile) in gblocks:
                par = blki % 2
                if os.environ.get("GUSER", "0") == "1":
                    ctx.wait(nc.sync, tg + "gum", ctx.cnt(tg + "gum"))
                else:
                    ctx.wait(nc.sync, tg + "gum", blk_ev_need[par])
                for dt in range(NDT):
                    ctx.dma(nc.sync, tg + "gud%d" % par,
                            guT[par][:, dt * 512: dt * 512 + ntile * 128],
                            gsrc[row_off:row_off + ntile * 128, dt * 128:(dt + 1) * 128],
                            transpose=True)
                gud_cnt[par] = ctx.cnt(tg + "gud%d" % par)
                ctx.wait(nc.tensor, tg + "gud%d" % par, gud_cnt[par])
                for htb in range(ntile):
                    ht = ht_global
                    ht_global += 1
                    for tq in range(4):
                        slot = (ht * 4 + tq) % 2
                        if gu_slot_ev[slot] is not None:
                            ctx.wait(nc.tensor, gu_slot_ev[slot][0], gu_slot_ev[slot][1])
                        for dt in range(NDT):
                            mmg = nc.tensor.matmul(out=ps_gu[slot][:],
                                                   lhsT=guT[par][:, dt * 512 + htb * 128: dt * 512 + (htb + 1) * 128],
                                                   rhs=lnT[:, dt * TL + tq * 512: dt * TL + (tq + 1) * 512],
                                                   start=(dt == 0), stop=(dt == NDT - 1))
                        gum = ctx.inc(mmg, tg + "gum", 1)
                        if ht < NHT:
                            ctx.wait(s, tg + "gum", gum)
                            ev = s.activation(out=act[:, ht * TL + tq * 512: ht * TL + (tq + 1) * 512],
                                              in_=ps_gu[slot][:], func=AF.Silu)
                            gu_slot_ev[slot] = (tg + "guevS", ctx.inc(ev, tg + "guevS", 1))
                        else:
                            jslot = ht - NHT
                            asl = act[:, jslot * TL + tq * 512: jslot * TL + (tq + 1) * 512]
                            ctx.wait(v, tg + "gum", gum)
                            ctx.edge(v, v.tensor_tensor(out=asl, in0=asl, in1=cbc[:, tq * 512:(tq + 1) * 512],
                                                        op=ALU.mult))
                            ev = v.tensor_tensor(out=asl, in0=asl, in1=ps_gu[slot][:], op=ALU.mult)
                            gu_slot_ev[slot] = (tg + "guevV", ctx.inc(ev, tg + "guevV", 1))
                blki += 1
                if blki >= 2:
                    pass
                blk_ev_need[par] = ctx.cnt(tg + "gum")

            if os.environ.get("NODND", "0") == "1":
                continue
            ctx.wait(nc.tensor, tg + "guevS")
            ctx.wait(nc.tensor, tg + "guevV")
            for dh in range(2):
                dpar = dh % 2
                if p * 2 + dh >= 2:
                    ctx.wait(nc.sync, tg + "om", (p * 2 + dh - 1) * 32)
                for ht in range(NHT):
                    ctx.dma(nc.sync, tg + "dnd%d" % dpar,
                            dnT[dpar][:, ht * 1024:(ht + 1) * 1024],
                            dn_src[dh * 1024:(dh + 1) * 1024,
                                   dn_coloff + ht * 128: dn_coloff + (ht + 1) * 128],
                            transpose=True)
                dnd_cnt[dpar] = ctx.cnt(tg + "dnd%d" % dpar)
                ctx.wait(nc.tensor, tg + "dnd%d" % dpar, dnd_cnt[dpar])
                for dto in range(8):
                    dt = dh * 8 + dto
                    for tq in range(4):
                        kk = p * 64 + dt * 4 + tq
                        slot = kk % 2
                        ctx.wait(nc.tensor, tg + "oev", o_slot_ev[slot])
                        for ht in range(NHT):
                            mmo = nc.tensor.matmul(out=ps_o[slot][:],
                                                   lhsT=dnT[dpar][:, ht * 1024 + dto * 128: ht * 1024 + (dto + 1) * 128],
                                                   rhs=act[:, ht * TL + tq * 512: ht * TL + (tq + 1) * 512],
                                                   start=(ht == 0), stop=(ht == NHT - 1))
                        om = ctx.inc(mmo, tg + "om", 1)
                        ctx.wait(v, tg + "om", om)
                        ctx.wait(v, tg + "acq%d" % slot, acq_cnt[slot])
                        cp = v.tensor_copy(out=stg[slot][:], in_=ps_o[slot][:])
                        o_slot_ev[slot] = ctx.inc(cp, tg + "oev", 1)
                        ctx.wait(g, tg + "oev", o_slot_ev[slot])
                        ad = g.dma_start(out=ds["acc"][dt * 128:(dt + 1) * 128, tq * 512:(tq + 1) * 512],
                                         in_=stg[slot][:],
                                         accum_op=(ALU.bypass if p == 0 else ALU.add))
                        acq_cnt[slot] = ctx.inc(ad, tg + "acq%d" % slot, 16)
        ctx.wait(g, tg + "acq0")
        ctx.wait(g, tg + "acq1")
        if "act" in ctx.taps and tg == "F1":
            ctx.wait(g, tg + "guevS")
            ctx.wait(g, tg + "guevV")
            ctx.inc(g.dma_start(out=ctx.taps["act"], in_=act[:]), tg + "tact", 16)
            ctx.wait(g, tg + "tact")


def _residual_stage(ctx, src, dst, tag, tap):
    nc = ctx.nc
    tg = tag
    ds = ctx.ds
    v, s = nc.vector, nc.scalar
    with ExitStack() as es:
        xs = [es.enter_context(nc.sbuf_tensor(f"{tg}_x{i}", [128, D], F32)) for i in range(2)]
        at = [es.enter_context(nc.sbuf_tensor(f"{tg}_at{i}", [128, 128], F32)) for i in range(2)]
        xn = [es.enter_context(nc.sbuf_tensor(f"{tg}_xn{i}", [128, D], F32)) for i in range(2)]
        ps_r = es.enter_context(nc.psum_tensor(tg + "_ps_r", [128, D], F32))
        st_counts = []
        for tt in range(NT):
            bp = tt % 2
            if tt >= 2:
                ctx.wait(nc.sync, tg + "add", tt - 1)
            ctx.dma(nc.sync, tg + "x%d" % bp, xs[bp][:], src[tt * 128:(tt + 1) * 128, :])
            for dt in range(NDT):
                gidx = tt * NDT + dt
                ap = dt % 2
                if gidx >= 2:
                    ctx.wait(nc.sync, tg + "tp", gidx - 1)
                dd = nc.sync.dma_start(out=at[ap][:, 0:128],
                                       in_=ds["acc"][dt * 128:(dt + 1) * 128, tt * 128:(tt + 1) * 128])
                abn = ctx.inc(dd, tg + "ab%d" % ap, 16)
                ctx.wait(nc.tensor, tg + "ab%d" % ap, abn)
                if dt == 0 and tt >= 1:
                    ctx.wait(nc.tensor, tg + "add", tt)
                mm = nc.tensor.matmul(out=ps_r[:, dt * 128:(dt + 1) * 128], lhsT=at[ap][:, 0:128],
                                      rhs=ctx.id_f32, start=True, stop=True)
                ctx.inc(mm, tg + "tp", 1)
            ctx.wait(v, tg + "tp")
            ctx.wait(v, tg + "x%d" % bp)
            if tt >= 2:
                ctx.wait(v, tg + "st%d" % bp, st_counts[tt - 2])
            add = v.tensor_tensor(out=xn[bp][:], in0=ps_r[:], in1=xs[bp][:], op=ALU.add)
            ctx.inc(add, tg + "add", 1)
            ctx.wait(nc.sync, tg + "add")
            ctx.inc(nc.sync.dma_start(out=dst[tt * 128:(tt + 1) * 128, :], in_=xn[bp][:]), tg + "st%d" % bp, 16)
            if tap is not None and tap in ctx.taps:
                ctx.inc(nc.sync.dma_start(out=ctx.taps[tap][tt * 128:(tt + 1) * 128, :], in_=xn[bp][:]),
                        tg + "st%d" % bp, 16)
            st_counts.append(ctx.cnt(tg + "st%d" % bp))
        ctx.wait(nc.sync, tg + "st0")
        ctx.wait(nc.sync, tg + "st1")
        ctx.wait(nc.vector, tg + "st0")
        ctx.wait(nc.vector, tg + "st1")



def _attention(ctx):
    nc = ctx.nc
    ds, di = ctx.ds, ctx.di
    v, s, g = nc.vector, nc.scalar, nc.gpsimd

    ctx.flush_sems()
    _ln_router_stage(ctx, src=ds["x1"], tag="L2", tap=None, do_router=False)
    ctx.barrier()

    with ExitStack() as es:
        ee = es.enter_context
        ctx.gt = ee(nc.sbuf_tensor("A_gt", [128, NCH * NH], F32))
        ctx.ngt = ee(nc.sbuf_tensor("A_ngt", [128, NCH * NH], F32))
        ctx.bt = ee(nc.sbuf_tensor("A_bt", [128, NCH * NH], F32))
        ctx.aCtab = ee(nc.sbuf_tensor("A_aCtab", [1, (NCH + 1) * NH], F32))
        ctx.prodtab = ee(nc.sbuf_tensor("A_prodtab", [1, (NCH + 1) * NH], F32))
        ctx.acbc = ee(nc.sbuf_tensor("A_acbc", [128, NCH * NH], F32))
        ctx.prodbc = ee(nc.sbuf_tensor("A_prodbc", [128, NCH * NH], F32))
        ctx.lamdiag = ee(nc.sbuf_tensor("A_lamdiag", [128, NH * HD], BF16))
        ctx.lambc = ee(nc.sbuf_tensor("A_lambc", [128, NH * HD], BF16))
        ctx.flush_sems()
        _attn_proj(ctx)
        ctx.barrier()
        ctx.flush_sems()
        _attn_phase_a(ctx)
        ctx.barrier()
        ctx.flush_sems()
        _attn_phase_b(ctx)
        ctx.barrier()


def _attn_proj(ctx):
    nc = ctx.nc
    ds, di = ctx.ds, ctx.di
    v, s, g = nc.vector, nc.scalar, nc.gpsimd
    lnT = ctx.lnT

    with ExitStack() as es:
        ee = es.enter_context
        qh_sb = ee(nc.sbuf_tensor("A_qh", [128, TL], BF16))
        qtok = ee(nc.sbuf_tensor("A_qtok", [128, 128], BF16))
        ktf = ee(nc.sbuf_tensor("A_ktf", [128, 128], BF16))
        ssb = ee(nc.sbuf_tensor("A_ss", [128, 8], F32))
        junk = ee(nc.sbuf_tensor("A_junk", [128, 128], F32))
        ps_p = [ee(nc.psum_tensor(f"A_psp{i}", [128, 512], F32)) for i in range(2)]
        ps_t2 = ee(nc.psum_tensor("A_pst2", [128, 128], F32))
        ps_t3 = ee(nc.psum_tensor("A_pst3", [128, 128], F32))
        ps_bg = ee(nc.psum_tensor("A_psbg", [16, 512], F32))
        ps_sm = ee(nc.psum_tensor("A_pssm", [128, 16], F32))

        pe_slot = [None, None]
        for kind, W, tokdst in (("q", di["Wq"], ds["bq"]), ("k", di["Wk"], ds["knat"]),
                                ("v", di["Wv"], ds["vnat"])):
            tgw = "Awl" + kind
            with nc.sbuf_tensor("A_wbuf" + kind, [128, NDT * D], BF16) as wq_sb:
                ctx.dma(g, tgw, wq_sb[:].rearrange("p (a c) -> p a c", a=16),
                        W.rearrange("(a p) c -> p a c", p=128))
                ctx.wait(nc.tensor, tgw)
                for h in range(NH):
                    for tq in range(4):
                        slot = (h * 4 + tq) % 2
                        if pe_slot[slot] is not None:
                            ctx.wait(nc.tensor, pe_slot[slot][0], pe_slot[slot][1])
                        for din in range(NDT):
                            mm = nc.tensor.matmul(out=ps_p[slot][:],
                                                  lhsT=wq_sb[:, din * D + h * 128: din * D + (h + 1) * 128],
                                                  rhs=lnT[:, din * TL + tq * 512: din * TL + (tq + 1) * 512],
                                                  start=(din == 0), stop=(din == NDT - 1))
                        pm = ctx.inc(mm, "Apm", 1)
                        ctx.wait(s, "Apm", pm)
                        if tq == 0:
                            ctx.wait(s, "At2m", ctx.cnt("At2m"))
                            ctx.wait(s, "Avt", ctx.cnt("Avt"))
                        ev = s.activation(out=qh_sb[:, tq * 512:(tq + 1) * 512], in_=ps_p[slot][:],
                                          func=AF.Silu)
                        pe_slot[slot] = ("ApevS", ctx.inc(ev, "ApevS", 1))
                    if kind == "v":
                        ctx.wait(g, "ApevS")
                        ctx.dma(g, "Avt", ds["vT"][h * 128:(h + 1) * 128, :], qh_sb[:])
                    for c in range(NCH):
                        ctx.wait(nc.tensor, "ApevS")
                        ctx.wait(nc.tensor, "Aqt", ctx.cnt("Aqt"))
                        mm2 = nc.tensor.matmul(out=ps_t2[:], lhsT=qh_sb[:, c * 128:(c + 1) * 128],
                                               rhs=ctx.id_bf, start=True, stop=True)
                        ctx.inc(mm2, "At2m", 1)
                        ctx.wait(s, "At2m")
                        if kind in ("q", "k"):
                            a1 = s.activation(out=junk[:], in_=ps_t2[:], func=AF.Square, accum_out=ssb[:, 0:1])
                            ctx.inc(a1, "Asq", 1)
                            ctx.wait(s, "Asq")
                            a2 = s.activation(out=ssb[:, 1:2], in_=ssb[:, 0:1], func=AF.Sqrt)
                            ctx.inc(a2, "Asq", 1)
                            ctx.wait(v, "Asq")
                            ctx.edge(v, v.tensor_scalar(out=ssb[:, 2:3], in0=ssb[:, 1:2], scalar1=1e-12,
                                                        scalar2=None, op0=ALU.max))
                            ctx.inc(v.reciprocal(out=ssb[:, 3:4], in_=ssb[:, 2:3]), "Arn", 1)
                            ctx.wait(s, "Arn")
                            a3 = s.activation(out=qtok[:], in_=ps_t2[:], func=AF.Identity, scale=ssb[:, 3:4])
                            ctx.inc(a3, "Aqt", 1)
                        else:
                            a3 = s.activation(out=qtok[:], in_=ps_t2[:], func=AF.Copy)
                            ctx.inc(a3, "Aqt", 1)
                        ctx.wait(g, "Aqt")
                        ctx.dma(g, "Ast", tokdst[c * 128:(c + 1) * 128, h * 128:(h + 1) * 128], qtok[:])
                        if kind == "k":
                            ctx.wait(nc.tensor, "Aqt")
                            ctx.wait(nc.tensor, "At3e", ctx.cnt("At3e"))
                            mm3 = nc.tensor.matmul(out=ps_t3[:], lhsT=qtok[:], rhs=ctx.id_bf,
                                                   start=True, stop=True)
                            ctx.inc(mm3, "At3m", 1)
                            ctx.wait(v, "At3m")
                            ctx.inc(v.tensor_copy(out=ktf[:], in_=ps_t3[:]), "At3e", 1)
                            ctx.wait(g, "At3e")
                            ctx.dma(g, "Ast2", ds["knT"][h * 128:(h + 1) * 128, c * 128:(c + 1) * 128],
                                    ktf[:])
                        ctx.wait(s, "Ast")
                        ctx.wait(s, "Ast2")
                ctx.wait(nc.sync, "Ast")
                ctx.wait(nc.sync, "Ast2")
                ctx.wait(nc.sync, "Avt")
                ctx.wait(nc.scalar, "Ast")
                ctx.barrier()

        ctx.barrier()
        with nc.sbuf_tensor("A_wab", [128, NDT * 2 * NH], BF16) as wab, \
             nc.sbuf_tensor("A_basb", [NH, 2], F32) as basb, \
             nc.sbuf_tensor("A_bg16", [NH, 2 * TL], F32) as bg16, \
             nc.sbuf_tensor("A_bgt", [NH, TL], F32) as bgt:
            ctx.dma(g, "Abw", wab[:].rearrange("p (a c) -> p a c", a=16)[:, :, 0:NH],
                    di["Wa"].rearrange("(a p) c -> p a c", p=128))
            ctx.dma(g, "Abw", wab[:].rearrange("p (a c) -> p a c", a=16)[:, :, NH:2 * NH],
                    di["Wb"].rearrange("(a p) c -> p a c", p=128))
            ctx.dma(nc.sync, "Abw2", basb[:, 0:1], di["ba"].rearrange("o h -> h o"),
                    allow_slow_non_contiguous=True)
            ctx.dma(nc.sync, "Abw2", basb[:, 1:2], di["bb"].rearrange("o h -> h o"),
                    allow_slow_non_contiguous=True)
            ctx.wait(nc.tensor, "Abw")
            ctx.wait(nc.scalar, "Abw2")
            for col, nm in ((0, "ga"), (NH, "bb")):
                for tq in range(4):
                    ctx.wait(nc.tensor, "Abge", ctx.cnt("Abge"))
                    for din in range(NDT):
                        mmb = nc.tensor.matmul(out=ps_bg[:],
                                               lhsT=wab[:, din * 2 * NH + col: din * 2 * NH + col + NH],
                                               rhs=lnT[:, din * TL + tq * 512: din * TL + (tq + 1) * 512],
                                               start=(din == 0), stop=(din == NDT - 1))
                    ctx.inc(mmb, "Abgm", 1)
                    ctx.wait(s, "Abgm")
                    bcol = basb[:, 0:1] if col == 0 else basb[:, 1:2]
                    ev = s.activation(out=bg16[:, (0 if col == 0 else TL) + tq * 512:
                                               (0 if col == 0 else TL) + (tq + 1) * 512],
                                      in_=ps_bg[:], func=AF.Identity, bias=bcol)
                    ctx.inc(ev, "Abge", 1)
            ctx.wait(s, "Abge")
            ctx.inc(s.activation(out=bg16[:, TL:2 * TL], in_=bg16[:, TL:2 * TL], func=AF.Sigmoid), "Abs", 1)
            ctx.inc(s.activation(out=bgt[:], in_=bg16[:, 0:TL], func=AF.Abs), "Abs", 1)
            ctx.wait(v, "Abs")
            ctx.inc(v.tensor_scalar(out=bgt[:], in0=bgt[:], scalar1=-1.0, scalar2=None, op0=ALU.mult), "Abv", 1)
            ctx.wait(s, "Abv")
            ctx.inc(s.activation(out=bgt[:], in_=bgt[:], func=AF.Exp), "Abs", 1)
            ctx.wait(v, "Abs")
            ctx.inc(v.tensor_scalar(out=bgt[:], in0=bgt[:], scalar1=1.0, scalar2=None, op0=ALU.add), "Abv", 1)
            ctx.wait(s, "Abv")
            ctx.inc(s.activation(out=bgt[:], in_=bgt[:], func=AF.Ln), "Abs", 1)
            ctx.wait(v, "Abs")
            ctx.edge(v, v.tensor_scalar(out=junk[0:NH, 0:TL] if False else bg16[:, 0:TL],
                                        in0=bg16[:, 0:TL], scalar1=0.0, scalar2=None, op0=ALU.min))
            ctx.inc(v.tensor_tensor(out=bg16[:, 0:TL], in0=bg16[:, 0:TL], in1=bgt[:], op=ALU.subtract),
                    "Abv", 1)
            ctx.wait(nc.tensor, "Abv")
            ctx.wait(nc.tensor, "Abs")
            for c in range(NCH):
                for which, tab in ((0, None), (1, ctx.bt)):
                    srcsl = bg16[:, which * TL + c * 128: which * TL + (c + 1) * 128]
                    ctx.wait(nc.tensor, "Asme", ctx.cnt("Asme"))
                    mmt = nc.tensor.matmul(out=ps_sm[:], lhsT=srcsl, rhs=ctx.id16_f32,
                                           start=True, stop=True)
                    ctx.inc(mmt, "Asmm", 1)
                    ctx.wait(v, "Asmm")
                    dst = (ctx.gt if which == 0 else ctx.bt)[:, c * NH:(c + 1) * NH]
                    if which == 0:
                        ctx.inc(v.tensor_copy(out=ctx.ngt[:, c * NH:(c + 1) * NH], in_=ps_sm[:]), "Asme", 1)
                    else:
                        ctx.inc(v.tensor_copy(out=dst, in_=ps_sm[:]), "Asme", 1)
            ctx.wait(nc.tensor, "Asme")
            for c in range(NCH):
                ctx.wait(nc.tensor, "Agce", ctx.cnt("Agce"))
                mmc = nc.tensor.matmul(out=ps_sm[:], lhsT=ctx.tri_f32,
                                       rhs=ctx.ngt[:, c * NH:(c + 1) * NH], start=True, stop=True)
                ctx.inc(mmc, "Agcm", 1)
                ctx.wait(v, "Agcm")
                ctx.inc(v.tensor_copy(out=ctx.gt[:, c * NH:(c + 1) * NH], in_=ps_sm[:]), "Agce", 1)
            ctx.wait(v, "Agce")
            ctx.inc(v.tensor_scalar(out=ctx.ngt[:], in0=ctx.gt[:], scalar1=-1.0, scalar2=None,
                                    op0=ALU.mult), "Angt", 1)

        tabs = ctx.tabs
        for h in range(NH):
            ctx.edge(v, v.tensor_scalar(out=ctx.lamdiag[:, h * 128:(h + 1) * 128], in0=ctx.id_bf,
                                        scalar1=tabs[:, h:h + 1], scalar2=None, op0=ALU.mult))
            ctx.wait(nc.tensor, "Alr", ctx.cnt("Alr"))
            mml = nc.tensor.matmul(out=ps_t2[0:1, :], lhsT=tabs[:, h:h + 1], rhs=ctx.id_f32,
                                   start=True, stop=True)
            ctx.inc(mml, "Alm", 1)
            ctx.wait(v, "Alm")
            ctx.inc(v.tensor_copy(out=junk[0:1, :], in_=ps_t2[0:1, :]), "Alr", 1)
            ctx.wait(nc.tensor, "Alr")
            mml2 = nc.tensor.matmul(out=ps_t3[:], lhsT=ctx.ones_f32, rhs=junk[0:1, :],
                                    start=True, stop=True)
            ctx.inc(mml2, "Alm2", 1)
            ctx.wait(v, "Alm2")
            ctx.inc(v.tensor_scalar(out=ctx.lambc[:, h * 128:(h + 1) * 128], in0=ps_t3[:],
                                    scalar1=1e-8, scalar2=None, op0=ALU.add), "Alr2", 1)
            ctx.wait(nc.tensor, "Alr2")

        ctx.barrier()
        with nc.sbuf_tensor("A_wotmp", [128, D], F32) as wot:
            for din in range(NDT):
                if din >= 1:
                    ctx.wait(nc.sync, "Awos", ctx.cnt("Awos"))
                ctx.dma(nc.sync, "Awol", wot[:], di["Wo"][din * 128:(din + 1) * 128, :])
                ctx.wait(v, "Awol")
                sc = v.tensor_scalar(out=lnT[:, din * D:(din + 1) * D], in0=wot[:],
                                     scalar1=tabs[:, 16:17], scalar2=None, op0=ALU.mult)
                ctx.inc(sc, "Awos", 1)
        ctx.wait(nc.sync, "Awos")
        ctx.barrier()





def _attn_phase_a(ctx):
    nc = ctx.nc
    ds = ctx.ds
    v, s, g = nc.vector, nc.scalar, nc.gpsimd

    with ExitStack() as es:
        ee = es.enter_context
        knc = ee(nc.sbuf_tensor("Pa_knc", [128, NH * HD], BF16))
        vnc = ee(nc.sbuf_tensor("Pa_vnc", [128, NH * HD], BF16))
        grow = ee(nc.sbuf_tensor("Pa_grow", [1, 128], F32))
        arow = ee(nc.sbuf_tensor("Pa_arow", [1, 128], F32))
        w0 = ee(nc.sbuf_tensor("Pa_w0", [128, 128], F32))
        btri = ee(nc.sbuf_tensor("Pa_btri", [128, 128], F32))
        wtb = ee(nc.sbuf_tensor("Pa_wtb", [128, 128], BF16))
        wcol = ee(nc.sbuf_tensor("Pa_wcol", [128, 1], F32))
        kw = ee(nc.sbuf_tensor("Pa_kw", [128, 128], BF16))
        pstage = ee(nc.sbuf_tensor("Pa_pstage", [128, 256], F32))
        Lkk = ee(nc.sbuf_tensor("Pa_Lkk", [128, NH * HD], F32))
        Lkv = ee(nc.sbuf_tensor("Pa_Lkv", [128, NH * HD], F32))
        zrow = ee(nc.sbuf_tensor("Pa_zrow", [1, NH * HD], F32))
        hstage = ee(nc.sbuf_tensor("Pa_hstage", [128, NH * HD], F32))
        hstage2 = ee(nc.sbuf_tensor("Pa_hstage2", [128, NH * HD], F32))
        hbf = ee(nc.sbuf_tensor("Pa_hbf", [128, NH * HD], BF16))
        drow = ee(nc.sbuf_tensor("Pa_drow", [1, 128], F32))
        ps_r = ee(nc.psum_tensor("Pa_psr", [1, 128], F32))
        ps_b = ee(nc.psum_tensor("Pa_psb", [128, 128], F32))
        ps_p1 = ee(nc.psum_tensor("Pa_psp1", [128, 128], F32))
        ps_p2 = ee(nc.psum_tensor("Pa_psp2", [128, 128], F32))
        ps_ac = ee(nc.psum_tensor("Pa_psac", [128, NH], F32))

        ctx.inc(v.memset(Lkk[:], 0.0), "Painit", 1)
        ctx.inc(v.memset(Lkv[:], 0.0), "Painit", 1)
        ctx.inc(v.memset(zrow[:], 0.0), "Painit", 1)
        ctx.inc(v.memset(ctx.prodtab[0:1, 0:NH], 1.0), "Painit", 1)
        ctx.wait(v, "Painit")

        for c in range(NCH):
            if c >= 1:
                ctx.wait(nc.sync, "Pakk", ctx.cnt("Pakk"))
            ctx.dma(nc.sync, "Paldk", knc[:], ds["knat"][c * 128:(c + 1) * 128, :])
            ctx.dma(nc.sync, "Paldv", vnc[:], ds["vnat"][c * 128:(c + 1) * 128, :])
            ctx.wait(g, "Painit")
            ctx.wait(g, "Palu", ctx.cnt("Palu"))
            ctx.dma(g, "Palp0", ds["lpref"][c, 0], Lkk[:])
            ctx.dma(g, "Palp1", ds["lpref"][c, 1], Lkv[:])
            for h in range(NH):
                col = c * NH + h
                ctx.wait(nc.tensor, "Page", ctx.cnt("Page"))
                mmr = nc.tensor.matmul(out=ps_r[0:1, :], lhsT=ctx.gt[:, col:col + 1], rhs=ctx.id_f32,
                                       start=True, stop=True)
                ctx.inc(mmr, "Pagm", 1)
                ctx.wait(v, "Pagm")
                ctx.wait(v, "Paae")
                ctx.inc(v.tensor_copy(out=grow[0:1, :], in_=ps_r[0:1, :]), "Page", 1)
                ctx.wait(s, "Page")
                ctx.inc(s.activation(out=arow[0:1, :], in_=grow[0:1, :], func=AF.Exp), "Paae", 1)
                ctx.wait(v, "Paae")
                ctx.edge(v, v.tensor_copy(out=ctx.aCtab[0:1, col:col + 1], in_=arow[0:1, 127:128]))
                ctx.wait(nc.tensor, "Paw0", ctx.cnt("Paw0"))
                ctx.wait(nc.tensor, "Page")
                mmb = nc.tensor.matmul(out=ps_b[:], lhsT=ctx.ones_f32, rhs=grow[0:1, :],
                                       start=True, stop=True)
                ctx.inc(mmb, "Pabm", 1)
                ctx.wait(v, "Pabm")
                ctx.wait(v, "Paw0", ctx.cnt("Paw0"))
                dgm = v.tensor_scalar(out=w0[:], in0=ps_b[:], scalar1=ctx.ngt[:, col:col + 1],
                                      scalar2=0.0, op0=ALU.add, op1=ALU.min)
                ctx.inc(dgm, "Padg", 1)
                ctx.wait(s, "Padg")
                a0 = s.activation(out=w0[:], in_=w0[:], func=AF.Exp)
                ctx.inc(a0, "Paw0", 1)
                ctx.wait(v, "Paw0")
                ctx.edge(v, v.tensor_scalar(out=btri[:], in0=ctx.tri_f32, scalar1=ctx.bt[:, col:col + 1],
                                            scalar2=None, op0=ALU.mult))
                ctx.wait(v, "Pamse_d", ctx.cnt("Pamse_d"))
                ctx.inc(v.tensor_tensor(out=wtb[:], in0=w0[:], in1=btri[:], op=ALU.mult), "Pamw", 1)
                ctx.edge(v, v.tensor_tensor(out=wcol[:], in0=w0[:, 127:128], in1=btri[:, 127:128],
                                            op=ALU.mult))
                ctx.wait(v, "Paldk")
                ctx.inc(v.tensor_scalar(out=kw[:], in0=knc[:, h * 128:(h + 1) * 128],
                                        scalar1=wcol[:], scalar2=None, op0=ALU.mult), "Pakwv", 1)
                ctx.wait(g, "Pamw")
                ctx.dma(g, "Pamse_d", ds["masks"][c, h], wtb[:])
                ctx.wait(nc.tensor, "Pakwv")
                ctx.wait(nc.tensor, "Paldv")
                ctx.wait(nc.tensor, "Pape", ctx.cnt("Pape"))
                m1 = nc.tensor.matmul(out=ps_p1[:], lhsT=knc[:, h * 128:(h + 1) * 128], rhs=kw[:],
                                      start=True, stop=True)
                ctx.inc(m1, "Pakk", 1)
                m2 = nc.tensor.matmul(out=ps_p2[:], lhsT=vnc[:, h * 128:(h + 1) * 128], rhs=kw[:],
                                      start=True, stop=True)
                ctx.inc(m2, "Pakk", 1)
                ctx.wait(v, "Pakk")
                ctx.wait(v, "Papd0", ctx.cnt("Papd0"))
                ctx.wait(v, "Papd1", ctx.cnt("Papd1"))
                ctx.inc(v.tensor_copy(out=pstage[:, 0:128], in_=ps_p1[:]), "Pape", 1)
                ctx.inc(v.tensor_copy(out=pstage[:, 128:256], in_=ps_p2[:]), "Pape", 1)
                ctx.wait(g, "Pape")
                ctx.dma(g, "Papd0", ds["partials"][c, 0, :, h * 128:(h + 1) * 128], pstage[:, 0:128])
                ctx.dma(g, "Papd1", ds["partials"][c, 1, :, h * 128:(h + 1) * 128], pstage[:, 128:256])
            ctx.wait(nc.tensor, "Paace", ctx.cnt("Paace"))
            ctx.wait(nc.tensor, "se_BassVectorEngine", ctx.cnt("se_BassVectorEngine"))
            mac = nc.tensor.matmul(out=ps_ac[:], lhsT=ctx.ones_f32,
                                   rhs=ctx.aCtab[0:1, c * NH:(c + 1) * NH], start=True, stop=True)
            ctx.inc(mac, "Paacm", 1)
            ctx.wait(v, "Paacm")
            ctx.inc(v.tensor_copy(out=ctx.acbc[:, c * NH:(c + 1) * NH], in_=ps_ac[:]), "Paace", 1)
            ctx.edge(v, v.tensor_tensor(out=ctx.prodtab[0:1, (c + 1) * NH:(c + 2) * NH],
                                        in0=ctx.prodtab[0:1, c * NH:(c + 1) * NH],
                                        in1=ctx.aCtab[0:1, c * NH:(c + 1) * NH], op=ALU.mult))
            ctx.wait(v, "Palp0")
            ctx.wait(v, "Palp1")
            ctx.wait(nc.sync, "Papd0")
            ctx.wait(nc.sync, "Papd1")
            ctx.dma(nc.sync, "Palr0", hstage[:], ds["partials"][c, 0])
            ctx.dma(nc.sync, "Palr1", hstage2[:], ds["partials"][c, 1])
            ctx.wait(v, "Palr0")
            ctx.wait(v, "Palr1")
            for h in range(NH):
                sl = slice(h * 128, (h + 1) * 128)
                accol = ctx.acbc[:, c * NH + h: c * NH + h + 1]
                e1 = v.scalar_tensor_tensor(out=Lkk[:, sl], in0=Lkk[:, sl], scalar=accol,
                                            in1=hstage[:, sl], op0=ALU.mult, op1=ALU.add)
                ctx.inc(e1, "Palu", 1)
                e2 = v.scalar_tensor_tensor(out=Lkv[:, sl], in0=Lkv[:, sl], scalar=accol,
                                            in1=hstage2[:, sl], op0=ALU.mult, op1=ALU.add)
                ctx.inc(e2, "Palu", 1)
            ctx.wait(v, "Palu")

        ctx.wait(g, "Palu")
        ctx.wait(g, "se_BassVectorEngine")
        ctx.dma(g, "Pacc", ds["cc_in"][0:HD, :], Lkk[:])
        ctx.dma(g, "Pacc", ds["cc_in"][HD:2 * HD, :], Lkv[:])
        ctx.dma(g, "Pacc", ds["cc_in"][2 * HD:2 * HD + 1, :], zrow[0:1, :])
        ctx.dma(g, "Pacc", ds["cc_in"][2 * HD:2 * HD + 1, 0:NH],
                ctx.prodtab[0:1, NCH * NH:(NCH + 1) * NH])
        ctx.wait(g, "Pacc")
        cc = g.collective_compute("AllGather", ALU.bypass, replica_groups=[list(range(NCORES))],
                                  ins=[ds["cc_in"].opt()],
                                  outs=[ds["cc_out"].rearrange("r p c -> (r p) c").opt()])
        ctx.inc(cc, "Paccd", 1)
        ctx.wait(g, "Paccd")
        ctx.wait(nc.sync, "Paccd")

        for j in range(NCORES):
            ctx.dma(nc.sync, "Padr", drow[0:1, j * NH:(j + 1) * NH],
                    ds["cc_out"][j, 2 * HD:2 * HD + 1, 0:NH])
        ctx.wait(nc.tensor, "Padr")
        mdb = nc.tensor.matmul(out=ps_b[:], lhsT=ctx.ones_f32, rhs=drow[0:1, :], start=True, stop=True)
        ctx.inc(mdb, "Padbm", 1)
        ctx.wait(v, "Padbm")
        dbc = ee(nc.sbuf_tensor("Pa_dbc", [128, 128], F32))
        ctx.inc(v.tensor_copy(out=dbc[:], in_=ps_b[:]), "Padbe", 1)
        ctx.inc(v.memset(Lkk[:], 0.0), "Pagz", 1)
        ctx.inc(v.memset(Lkv[:], 0.0), "Pagz", 1)
        ctx.wait(g, "Pagz")
        ctx.dma(g, "Papw0", ds["prefixes"][0, 0:HD, :], Lkk[:])
        ctx.dma(g, "Papw1", ds["prefixes"][0, HD:2 * HD, :], Lkv[:])
        for j in range(NCORES - 1):
            ctx.wait(nc.sync, "Pagu", ctx.cnt("Pagu"))
            ctx.dma(nc.sync, "Pagl0", hstage[:], ds["cc_out"][j, 0:HD, :])
            ctx.dma(nc.sync, "Pagl1", hstage2[:], ds["cc_out"][j, HD:2 * HD, :])
            ctx.wait(v, "Pagl0")
            ctx.wait(v, "Pagl1")
            ctx.wait(v, "Padbe")
            ctx.wait(v, "Papw0")
            ctx.wait(v, "Papw1")
            for h in range(NH):
                sl = slice(h * 128, (h + 1) * 128)
                dcol = dbc[:, j * NH + h: j * NH + h + 1]
                e1 = v.scalar_tensor_tensor(out=Lkk[:, sl], in0=Lkk[:, sl], scalar=dcol,
                                            in1=hstage[:, sl], op0=ALU.mult, op1=ALU.add)
                ctx.inc(e1, "Pagu", 1)
                e2 = v.scalar_tensor_tensor(out=Lkv[:, sl], in0=Lkv[:, sl], scalar=dcol,
                                            in1=hstage2[:, sl], op0=ALU.mult, op1=ALU.add)
                ctx.inc(e2, "Pagu", 1)
            ctx.wait(g, "Pagu")
            ctx.dma(g, "Papw0", ds["prefixes"][j + 1, 0:HD, :], Lkk[:])
            ctx.dma(g, "Papw1", ds["prefixes"][j + 1, HD:2 * HD, :], Lkv[:])
        ctx.wait(nc.sync, "Papw0")
        ctx.wait(nc.sync, "Papw1")
        pid = nc.sync.partition_id()
        ctx.dma(nc.sync, "Pahs", hstage[:], ds["prefixes"][pid, 0:HD, :])
        ctx.dma(nc.sync, "Pahs", hstage2[:], ds["prefixes"][pid, HD:2 * HD, :])

        for c in range(NCH):
            ctx.wait(nc.tensor, "Papbe", ctx.cnt("Papbe"))
            mpb = nc.tensor.matmul(out=ps_ac[:], lhsT=ctx.ones_f32,
                                   rhs=ctx.prodtab[0:1, c * NH:(c + 1) * NH], start=True, stop=True)
            ctx.inc(mpb, "Papbm", 1)
            ctx.wait(v, "Papbm")
            ctx.inc(v.tensor_copy(out=ctx.prodbc[:, c * NH:(c + 1) * NH], in_=ps_ac[:]), "Papbe", 1)

        for c in range(NCH):
            for st_i in range(2):
                hs = hstage if st_i == 0 else hstage2
                ctx.wait(nc.sync, "Pahb", ctx.cnt("Pahb"))
                ctx.dma(nc.sync, "Pahl", Lkk[:], ds["lpref"][c, st_i])
                ctx.wait(v, "Pahl")
                ctx.wait(v, "Pahs")
                for h in range(NH):
                    sl = slice(h * 128, (h + 1) * 128)
                    pcol = ctx.prodbc[:, c * NH + h: c * NH + h + 1]
                    e1 = v.scalar_tensor_tensor(out=Lkv[:, sl], in0=hs[:, sl], scalar=pcol,
                                                in1=Lkk[:, sl], op0=ALU.mult, op1=ALU.add)
                    ctx.inc(e1, "Pahf", 1)
                ctx.wait(v, "Pahf")
                ctx.wait(v, "Pahd", ctx.cnt("Pahd"))
                ctx.inc(v.tensor_copy(out=hbf[:], in_=Lkv[:]), "Pahb", 1)
                ctx.wait(g, "Pahb")
                ctx.dma(g, "Pahd", ds["h0"][c, st_i], hbf[:])
        ctx.wait(g, "Pahd")

def _attn_phase_b(ctx):
    nc = ctx.nc
    ds = ctx.ds
    v, s, g = nc.vector, nc.scalar, nc.gpsimd
    wo = ctx.lnT
    NQ = 10
    Q_D, Q_PQ, Q_DN, Q_DONE, Q_AL, Q_NAL, Q_BE, Q_T0, Q_T1, Q_T2 = range(NQ)

    with ExitStack() as es:
        ee = es.enter_context
        h0c = [ee(nc.sbuf_tensor("Pb_h0%d" % i, [128, NH * HD], BF16)) for i in range(2)]
        hvc = ee(nc.sbuf_tensor("Pb_hv", [128, NH * HD], BF16))
        knc = [ee(nc.sbuf_tensor("Pb_knat%d" % i, [128, NH * HD], BF16)) for i in range(2)]
        ktc = [ee(nc.sbuf_tensor("Pb_knT%d" % i, [128, NH * HD], BF16)) for i in range(2)]
        vtc = ee(nc.sbuf_tensor("Pb_vT", [128, NH * HD], BF16))
        msk = [ee(nc.sbuf_tensor("Pb_msk%d" % i, [128, NH * HD], BF16)) for i in range(2)]
        bq = ee(nc.sbuf_tensor("Pb_bq", [128, NH * HD], BF16))
        dga = [ee(nc.sbuf_tensor("Pb_dga%d" % i, [128, NH * HD], BF16)) for i in range(2)]
        acol = ee(nc.sbuf_tensor("Pb_acol", [128, 2 * NH], F32))
        xst = [ee(nc.sbuf_tensor("Pb_x%d" % i, [128, NH * HD], F32)) for i in range(2)]
        rst = [ee(nc.sbuf_tensor("Pb_r%d" % i, [128, NH * HD], BF16)) for i in range(2)]
        pst = [ee(nc.sbuf_tensor("Pb_p%d" % i, [128, NH * HD], BF16)) for i in range(2)]
        qtk = [ee(nc.sbuf_tensor("Pb_qtk%d" % i, [128, NH * HD], BF16)) for i in range(2)]
        prod = ee(nc.sbuf_tensor("Pb_prod", [128, NH * HD], BF16))
        pfeat = ee(nc.sbuf_tensor("Pb_pfeat", [128, NH * HD], BF16))
        apfeat = ee(nc.sbuf_tensor("Pb_apfeat", [128, NH * HD], BF16))
        qfeat = ee(nc.sbuf_tensor("Pb_qfeat", [128, NH * HD], BF16))
        msb = ee(nc.sbuf_tensor("Pb_msb", [128, NH * HD], BF16))
        sc = [ee(nc.sbuf_tensor("Pb_sc%d" % i, [128, NQ * NH], F32)) for i in range(2)]
        den = ee(nc.sbuf_tensor("Pb_den", [128, NH * HD], F32))
        kk2 = ee(nc.sbuf_tensor("Pb_kk2", [128, NH * HD], BF16))
        otok = den
        junk = kk2
        x1c = den
        x2c = x1c
        psPT = ee(nc.psum_tensor("Pb_psPT", [128, 1024], F32))
        psSS = ee(nc.psum_tensor("Pb_psSS", [128, 1024], F32))
        psQQ = ee(nc.psum_tensor("Pb_psQQ", [128, 1024], F32))
        psQT = ee(nc.psum_tensor("Pb_psQT", [128, 1024], F32))

        def qcolsm(m, q, gh=None, h=None):
            if h is not None:
                return sc[m][:, q * NH + h: q * NH + h + 1]
            return sc[m][:, q * NH + gh * 8: q * NH + gh * 8 + 8]

        def junkm(m):
            return junk if m == 0 else den

        def hsl(h):
            return slice(h * 128, (h + 1) * 128)

        def halfsl(gh):
            return slice(gh * 8 * 128, (gh + 1) * 8 * 128)

        def mv_round(m, src_tok, dgam, h0m, kncm, ktcm, mskm, qtkm):
            for gh in range(2):
                heads = range(gh * 8, gh * 8 + 8)
                ctx.wait(nc.tensor, "Bpw%d%d" % (m, gh), ctx.cnt("Bpw%d%d" % (m, gh)))
                ctx.wait(nc.tensor, "BpeS", ctx.cnt("BpeS"))
                for i, h in enumerate(heads):
                    mm = nc.tensor.matmul(out=psPT[:, i * 128:(i + 1) * 128], lhsT=src_tok[:, hsl(h)],
                                          rhs=ctx.id_bf, start=True, stop=True)
                ctx.inc(mm, "Bptm", 1)
                ctx.wait(s, "Bptm")
                ctx.inc(s.activation(out=pfeat[:, halfsl(gh)], in_=psPT[:], func=AF.Copy), "BpeS", 1)
                ctx.wait(nc.tensor, "BpeS")
                for i, h in enumerate(heads):
                    mm = nc.tensor.matmul(out=psPT[:, i * 128:(i + 1) * 128], lhsT=src_tok[:, hsl(h)],
                                          rhs=dgam[:, hsl(h)], start=True, stop=True)
                ctx.inc(mm, "Bptm", 1)
                ctx.wait(s, "Bptm")
                ctx.inc(s.activation(out=apfeat[:, halfsl(gh)], in_=psPT[:], func=AF.Copy), "BpeS", 1)
                ctx.wait(nc.tensor, "BssE", ctx.cnt("BssE"))
                ctx.wait(nc.tensor, "Bden2", ctx.cnt("Bden2"))
                ctx.wait(nc.tensor, "BpeS")
                for i, h in enumerate(heads):
                    mm = nc.tensor.matmul(out=psSS[:, i * 128:(i + 1) * 128], lhsT=ktcm[:, hsl(h)],
                                          rhs=pfeat[:, hsl(h)], start=True, stop=True)
                ctx.inc(mm, "Bsm", 1)
                ctx.wait(s, "Bsm")
                ctx.wait(s, "Bmm", ctx.cnt("Bmm"))
                ctx.inc(s.activation(out=qfeat[:, halfsl(1 - gh)], in_=psSS[:], func=AF.Copy),
                        "BssE", 1)
                ctx.wait(g, "BssE")
                mmv = g.tensor_tensor(out=msb[:, halfsl(gh)], in0=mskm[:, halfsl(gh)],
                                      in1=qfeat[:, halfsl(1 - gh)], op=ALU.mult)
                ctx.inc(mmv, "Bmm", 1)
                ctx.wait(nc.tensor, "BqeS", ctx.cnt("BqeS"))
                ctx.wait(nc.tensor, "Bmm")
                for i, h in enumerate(heads):
                    osl = psQQ[:, i * 128:(i + 1) * 128]
                    nc.tensor.matmul(out=osl, lhsT=h0m[:, hsl(h)], rhs=apfeat[:, hsl(h)],
                                     start=True, stop=False)
                    nc.tensor.matmul(out=osl, lhsT=ctx.lamdiag[:, hsl(h)], rhs=pfeat[:, hsl(h)],
                                     start=False, stop=False)
                    mm = nc.tensor.matmul(out=osl, lhsT=kncm[:, hsl(h)], rhs=msb[:, hsl(h)],
                                          start=False, stop=True)
                ctx.inc(mm, "Bqm", 1)
                ctx.wait(s, "Bqm")
                ctx.inc(s.activation(out=qfeat[:, halfsl(gh)], in_=psQQ[:], func=AF.Copy), "BqeS", 1)
                ctx.wait(nc.tensor, "BqteS", ctx.cnt("BqteS"))
                ctx.wait(nc.tensor, "BqeS")
                for i, h in enumerate(heads):
                    mm = nc.tensor.matmul(out=psQT[:, i * 128:(i + 1) * 128], lhsT=qfeat[:, hsl(h)],
                                          rhs=ctx.id_bf, start=True, stop=True)
                ctx.inc(mm, "Bqtm", 1)
                ctx.wait(s, "Bqtm")
                ctx.wait(s, "Brw%d%d" % (m, gh), ctx.cnt("Brw%d%d" % (m, gh)))
                ctx.wait(s, "Bofm", ctx.cnt("Bofm"))
                ctx.inc(s.activation(out=qtkm[:, halfsl(gh)], in_=psQT[:], func=AF.Copy), "BqteS", 1)

        NPAIR = NCH // 2
        for cp in range(NPAIR):
            ctx.wait(nc.sync, "Bx2s", ctx.cnt("Bx2s"))
            for m in range(2):
                c = 2 * cp + m
                ctx.dma(nc.sync, "Bld0", h0c[m][:], ds["h0"][c, 0])
                ctx.dma(nc.sync, "Bld2", knc[m][:], ds["knat"][c * 128:(c + 1) * 128, :])
                ctx.dma(nc.sync, "Bld4", ktc[m][:].rearrange("p (h t) -> p h t", h=NH),
                        ds["knT"].rearrange("(h p) t -> p h t", p=128)[:, :, c * 128:(c + 1) * 128])
                ctx.dma(nc.sync, "Bld6", msk[m][:].rearrange("p (h t) -> p h t", h=NH),
                        ds["masks"][c].rearrange("h p t -> p h t"))
            ctx.wait(s, "Bwom", ctx.cnt("Bwom"))
            ctx.wait(v, "Bwom", ctx.cnt("Bwom"))
            ctx.inc(s.activation(out=acol[:, 0:NH], in_=ctx.gt[:, 2 * cp * NH:(2 * cp + 1) * NH],
                                 func=AF.Exp), "Bac", 1)
            ctx.inc(s.activation(out=acol[:, NH:2 * NH], in_=ctx.gt[:, (2 * cp + 1) * NH:(2 * cp + 2) * NH],
                                 func=AF.Exp), "Bac", 1)
            ctx.wait(v, "Bac")
            for m in range(2):
                for h in range(NH):
                    dd = v.tensor_scalar(out=dga[m][:, hsl(h)], in0=ctx.id_bf,
                                         scalar1=acol[:, m * NH + h: m * NH + h + 1],
                                         scalar2=None, op0=ALU.mult)
                    ctx.inc(dd, "Bdga", 1)
            for eng in (v, s, nc.tensor):
                for i in (0, 2, 4, 6):
                    ctx.wait(eng, "Bld%d" % i)
            ctx.wait(nc.tensor, "Bdga")

            for m in range(2):
                c = 2 * cp + m
                for mm_ in range(2):
                    for gg_ in range(2):
                        ctx.wait(nc.sync, "Bpw%d%d" % (mm_, gg_), ctx.cnt("Bpw%d%d" % (mm_, gg_)))
                ctx.dma(nc.sync, "Bld3", bq[:], ds["bq"][c * 128:(c + 1) * 128, :])
                ctx.wait(v, "Bld3")
                ctx.wait(v, "Bwtm", ctx.cnt("Bwtm"))
                ctx.wait(v, "Bd0", ctx.cnt("Bd0"))
                ctx.inc(v.tensor_tensor(out=kk2[:], in0=knc[m][:], in1=knc[m][:], op=ALU.mult), "Bkk2", 1)
                ctx.wait(v, "Bdbm", ctx.cnt("Bdbm"))
                for h in range(NH):
                    hh = v.tensor_tensor(out=prod[:, hsl(h)], in0=h0c[m][:, hsl(h)], in1=ctx.id_bf,
                                         op=ALU.mult)
                    ctx.inc(hh, "Bhdi", 1)
                ctx.wait(nc.tensor, "Bhdi")
                ctx.wait(nc.tensor, "Bkk2")
                ctx.wait(nc.tensor, "Bx2v", ctx.cnt("Bx2v"))
                for gh in range(2):
                    heads = range(gh * 8, gh * 8 + 8)
                    ctx.wait(nc.tensor, "BdbS", ctx.cnt("BdbS"))
                    ctx.wait(nc.tensor, "BqeS", ctx.cnt("BqeS"))
                    for i, h in enumerate(heads):
                        mm = nc.tensor.matmul(out=psQQ[:, i * 128:(i + 1) * 128], lhsT=ctx.onesmat_bf,
                                              rhs=prod[:, hsl(h)], start=True, stop=True)
                    ctx.inc(mm, "Bdbm", 1)
                    ctx.wait(s, "Bdbm")
                    ctx.inc(s.activation(out=den[:, halfsl(gh)], in_=psQQ[:], func=AF.Copy), "BdbS", 1)
                    ctx.wait(nc.tensor, "Bden2", ctx.cnt("Bden2"))
                    ctx.wait(nc.tensor, "BssE", ctx.cnt("BssE"))
                    for i, h in enumerate(heads):
                        mm = nc.tensor.matmul(out=psSS[:, i * 128:(i + 1) * 128], lhsT=msk[m][:, hsl(h)],
                                              rhs=kk2[:, hsl(h)], start=True, stop=True)
                    ctx.inc(mm, "Bwtm", 1)
                    ctx.wait(v, "Bwtm")
                    ctx.wait(v, "BdbS")
                    for i, h in enumerate(heads):
                        e1 = v.scalar_tensor_tensor(out=den[:, hsl(h)], in0=den[:, hsl(h)],
                                                    scalar=acol[:, m * NH + h: m * NH + h + 1],
                                                    in1=psSS[:, i * 128:(i + 1) * 128],
                                                    op0=ALU.mult, op1=ALU.add)
                        ctx.inc(e1, "Bden2", 1)
                    ctx.wait(v, "Bden2")
                    e2 = v.tensor_tensor(out=den[:, halfsl(gh)], in0=den[:, halfsl(gh)],
                                         in1=ctx.lambc[:, halfsl(gh)], op=ALU.add)
                    ctx.inc(e2, "Bden3", 1)
                    ctx.wait(v, "Bden3")
                    e3 = v.reciprocal(out=den[:, halfsl(gh)], in_=den[:, halfsl(gh)])
                    ctx.inc(e3, "Bden4", 1)
                    ctx.wait(v, "Bden4")
                    e4 = v.tensor_tensor(out=xst[m][:, halfsl(gh)], in0=bq[:, halfsl(gh)],
                                         in1=den[:, halfsl(gh)], op=ALU.mult)
                    ctx.inc(e4, "Bx0m", 1)
                    ctx.wait(v, "Bx0m")
                    e5 = v.tensor_copy(out=pst[m][:, halfsl(gh)], in_=xst[m][:, halfsl(gh)])
                    ctx.inc(e5, "Bpw%d%d" % (m, gh), 1)
                mv_round(m, pst[m], dga[m], h0c[m], knc[m], ktc[m], msk[m], qtk[m])
                ctx.wait(v, "BqteS")
                ctx.inc(v.memset(sc[m][:], 0.0), "Bsc0", 1)
                for gh in range(2):
                    e6 = v.tensor_tensor(out=rst[m][:, halfsl(gh)], in0=bq[:, halfsl(gh)],
                                         in1=qtk[m][:, halfsl(gh)], op=ALU.subtract)
                    ctx.inc(e6, "Brw%d%d" % (m, gh), 1)
                ctx.wait(s, "Brw%d0" % m)
                ctx.wait(s, "Brw%d1" % m)
                ctx.wait(s, "Bsc0")
                for h in range(NH):
                    a1 = s.activation(out=junkm(m)[:, hsl(h)], in_=rst[m][:, hsl(h)], func=AF.Square,
                                      accum_out=qcolsm(m, Q_D, h=h))
                    ctx.inc(a1, "Bd0", 1)
                ctx.wait(v, "Bd0")
                for gh in range(2):
                    e7 = v.tensor_copy(out=pst[m][:, halfsl(gh)], in_=rst[m][:, halfsl(gh)])
                    ctx.inc(e7, "Bpw%d%d" % (m, gh), 1)

            def chain(m, qv):
                q16 = lambda q: sc[m][:, q * NH:(q + 1) * NH]
                for gh in range(2):
                    ctx.wait(v, "BqteS", qv - 1 + gh)
                    e1 = v.tensor_tensor(out=prod[:, halfsl(gh)], in0=pst[m][:, halfsl(gh)],
                                         in1=qtk[m][:, halfsl(gh)], op=ALU.mult)
                    ctx.edge(v, e1)
                    e2 = v.tensor_reduce(out=qcolsm(m, Q_PQ, gh=gh)[:, :, None],
                                         in_=prod[:, halfsl(gh)].rearrange("p (h d) -> p h d", h=8),
                                         axis=AX.X, op=ALU.add)
                    ctx.edge(v, e2)
                ctx.edge(v, v.tensor_scalar(out=q16(Q_T0), in0=q16(Q_D), scalar1=1e-10,
                                            scalar2=None, op0=ALU.is_lt))
                ctx.edge(v, v.tensor_tensor(out=q16(Q_DONE), in0=q16(Q_DONE), in1=q16(Q_T0),
                                            op=ALU.max))
                ctx.edge(v, v.tensor_tensor(out=q16(Q_T1), in0=q16(Q_PQ), in1=q16(Q_PQ),
                                            op=ALU.mult))
                ctx.edge(v, v.tensor_scalar(out=q16(Q_T1), in0=q16(Q_T1), scalar1=1e-20,
                                            scalar2=None, op0=ALU.is_lt))
                ctx.edge(v, v.tensor_tensor(out=q16(Q_DONE), in0=q16(Q_DONE), in1=q16(Q_T1),
                                            op=ALU.max))
                ctx.edge(v, v.tensor_scalar(out=q16(Q_T0), in0=q16(Q_DONE), scalar1=-1.0,
                                            scalar2=1.0, op0=ALU.mult, op1=ALU.add))
                ctx.edge(v, v.tensor_tensor(out=q16(Q_T1), in0=q16(Q_PQ), in1=q16(Q_T0),
                                            op=ALU.mult))
                ctx.edge(v, v.tensor_tensor(out=q16(Q_T1), in0=q16(Q_T1), in1=q16(Q_DONE),
                                            op=ALU.add))
                ctx.edge(v, v.reciprocal(out=q16(Q_T2), in_=q16(Q_T1)))
                ctx.edge(v, v.tensor_tensor(out=q16(Q_AL), in0=q16(Q_D), in1=q16(Q_T2),
                                            op=ALU.mult))
                ctx.edge(v, v.tensor_tensor(out=q16(Q_AL), in0=q16(Q_AL), in1=q16(Q_T0),
                                            op=ALU.mult))
                ctx.inc(v.tensor_scalar(out=q16(Q_NAL), in0=q16(Q_AL), scalar1=-1.0,
                                        scalar2=None, op0=ALU.mult), "Bal", 1)
                ctx.wait(v, "Bal")
                for h in range(NH):
                    xu = v.scalar_tensor_tensor(out=xst[m][:, hsl(h)], in0=pst[m][:, hsl(h)],
                                                scalar=qcolsm(m, Q_AL, h=h),
                                                in1=xst[m][:, hsl(h)],
                                                op0=ALU.mult, op1=ALU.add)
                    ctx.inc(xu, "Bxu", 1)
                for h in range(NH):
                    rr = v.scalar_tensor_tensor(out=rst[m][:, hsl(h)], in0=qtk[m][:, hsl(h)],
                                                scalar=qcolsm(m, Q_NAL, h=h),
                                                in1=rst[m][:, hsl(h)],
                                                op0=ALU.mult, op1=ALU.add)
                    ctx.inc(rr, "Brw%d%d" % (m, h // 8), 1)
                for gh in range(2):
                    ctx.edge(v, v.tensor_tensor(out=den[:, halfsl(gh)], in0=rst[m][:, halfsl(gh)],
                                                in1=rst[m][:, halfsl(gh)], op=ALU.mult))
                    ctx.edge(v, v.tensor_reduce(out=qcolsm(m, Q_DN, gh=gh)[:, :, None],
                                                in_=den[:, halfsl(gh)].rearrange("p (h d) -> p h d", h=8),
                                                axis=AX.X, op=ALU.add))
                ctx.edge(v, v.tensor_tensor(out=q16(Q_T1), in0=q16(Q_D), in1=q16(Q_T0),
                                            op=ALU.mult))
                ctx.edge(v, v.tensor_tensor(out=q16(Q_T1), in0=q16(Q_T1), in1=q16(Q_DONE),
                                            op=ALU.add))
                ctx.edge(v, v.reciprocal(out=q16(Q_T2), in_=q16(Q_T1)))
                ctx.edge(v, v.tensor_tensor(out=q16(Q_BE), in0=q16(Q_DN), in1=q16(Q_T2),
                                            op=ALU.mult))
                ctx.edge(v, v.tensor_tensor(out=q16(Q_BE), in0=q16(Q_BE), in1=q16(Q_T0),
                                            op=ALU.mult))
                ctx.edge(v, v.tensor_copy(out=q16(Q_D), in_=q16(Q_DN)))
                ctx.wait(v, "Bxu")
                for h in range(NH):
                    pp = v.scalar_tensor_tensor(out=pst[m][:, hsl(h)], in0=pst[m][:, hsl(h)],
                                                scalar=qcolsm(m, Q_BE, h=h),
                                                in1=rst[m][:, hsl(h)],
                                                op0=ALU.mult, op1=ALU.add)
                    ctx.inc(pp, "Bpw%d%d" % (m, h // 8), 1)

            qv = [None, None]
            for it in range(CGI):
                mv_round(0, pst[0], dga[0], h0c[0], knc[0], ktc[0], msk[0], qtk[0])
                qv[0] = ctx.cnt("BqteS")
                if it > 0:
                    chain(1, qv[1])
                mv_round(1, pst[1], dga[1], h0c[1], knc[1], ktc[1], msk[1], qtk[1])
                qv[1] = ctx.cnt("BqteS")
                chain(0, qv[0])
            chain(1, qv[1])

            for m in range(2):
                c = 2 * cp + m
                ctx.wait(nc.sync, "Bsm", ctx.cnt("Bsm"))
                ctx.wait(nc.sync, "Bqm", ctx.cnt("Bqm"))
                ctx.dma(nc.sync, "Bld1", hvc[:], ds["h0"][c, 1])
                ctx.dma(nc.sync, "Bld5", vtc[:].rearrange("p (h t) -> p h t", h=NH),
                        ds["vT"].rearrange("(h p) t -> p h t", p=128)[:, :, c * 128:(c + 1) * 128])
                ctx.wait(v, "Bpw%d0" % m)
                ctx.wait(v, "Bpw%d1" % m)
                ctx.wait(v, "Bxu")
                ctx.wait(v, "Bx2v", ctx.cnt("Bx2v"))
                ctx.inc(v.tensor_copy(out=prod[:], in_=xst[m][:]), "Bxb", 1)
                for eng in (v, s, nc.tensor):
                    ctx.wait(eng, "Bld1")
                    ctx.wait(eng, "Bld5")
                ctx.wait(nc.tensor, "Bxb")
                ctx.wait(s, "Bx2s", ctx.cnt("Bx2s"))
                mv_out(ctx, prod, dga[m], h0c[m], hvc, knc[m], ktc[m], vtc, msk[m], msb,
                       pfeat, apfeat, qfeat, otok, psPT, psSS, psQQ, psQT, hsl, halfsl)
                ctx.wait(s, "BqteS")
                ctx.wait(s, "Bos", ctx.cnt("Bos"))
                for h in range(NH):
                    a1 = s.activation(out=junk[:, hsl(h)], in_=otok[:, hsl(h)], func=AF.Square,
                                      accum_out=qcolsm(m, Q_T0, h=h))
                    ctx.inc(a1, "Bos", 1)
                ctx.wait(v, "Bos")
                scm = sc[m]
                ctx.edge(v, v.tensor_scalar(out=scm[:, Q_T1 * NH:(Q_T1 + 1) * NH],
                                            in0=scm[:, Q_T0 * NH:(Q_T0 + 1) * NH],
                                            scalar1=1.0 / HD, scalar2=1e-6, op0=ALU.mult, op1=ALU.add))
                ctx.inc(v.nop(), "Bov", 1)
                ctx.wait(s, "Bov")
                a2 = s.activation(out=scm[:, Q_T2 * NH:(Q_T2 + 1) * NH],
                                  in_=scm[:, Q_T1 * NH:(Q_T1 + 1) * NH], func=AF.Sqrt)
                ctx.inc(a2, "Bos2", 1)
                ctx.wait(v, "Bos2")
                ctx.inc(v.reciprocal(out=scm[:, Q_T1 * NH:(Q_T1 + 1) * NH],
                                     in_=scm[:, Q_T2 * NH:(Q_T2 + 1) * NH]), "Bov2", 1)
                ctx.wait(s, "Bov2")
                for h in range(NH):
                    a3 = s.activation(out=qtk[m][:, hsl(h)], in_=otok[:, hsl(h)], func=AF.Identity,
                                      scale=qcolsm(m, Q_T1, h=h))
                    ctx.inc(a3, "BonS", 1)
                ctx.wait(nc.sync, "BonS")
                ctx.dma(nc.sync, "Bld7", x1c[:], ds["x1"][c * 128:(c + 1) * 128, :])
                ctx.wait(nc.tensor, "BonS")
                for gh in range(2):
                    ctx.wait(nc.tensor, "BofS", ctx.cnt("BofS"))
                    for i, h in enumerate(range(gh * 8, gh * 8 + 8)):
                        mm = nc.tensor.matmul(out=psPT[:, i * 128:(i + 1) * 128],
                                              lhsT=qtk[m][:, hsl(h)],
                                              rhs=ctx.id_bf, start=True, stop=True)
                    ctx.inc(mm, "Bofm", 1)
                    ctx.wait(s, "Bofm")
                    ctx.inc(s.activation(out=pfeat[:, halfsl(gh)], in_=psPT[:], func=AF.Copy),
                            "BofS", 1)
                ctx.wait(nc.tensor, "BofS")
                ctx.wait(nc.tensor, "Bx2v", ctx.cnt("Bx2v"))
                for dq in range(4):
                    psout = (psQQ if dq < 2 else psQT)
                    osl = psout[:, (dq % 2) * 512:(dq % 2) * 512 + 512]
                    for h in range(NH):
                        mm = nc.tensor.matmul(out=osl, lhsT=pfeat[:, hsl(h)],
                                              rhs=wo[:, h * D + dq * 512: h * D + (dq + 1) * 512],
                                              start=(h == 0), stop=(h == NH - 1))
                    ctx.inc(mm, "Bwom", 1)
                ctx.wait(v, "Bwom")
                ctx.wait(v, "Bld7")
                ctx.wait(v, "Bx2s", ctx.cnt("Bx2s"))
                for dq in range(4):
                    psout = (psQQ if dq < 2 else psQT)
                    osl = psout[:, (dq % 2) * 512:(dq % 2) * 512 + 512]
                    ad = v.tensor_tensor(out=x2c[:, dq * 512:(dq + 1) * 512], in0=osl,
                                         in1=x1c[:, dq * 512:(dq + 1) * 512], op=ALU.add)
                    ctx.inc(ad, "Bx2v", 1)
                ctx.wait(g, "Bx2v")
                ctx.dma(g, "Bx2s", ds["x2"][c * 128:(c + 1) * 128, :], x2c[:])
        ctx.wait(g, "Bx2s")


def mv_out(ctx, xb, dga, h0c, hvc, knc, ktc, vtc, msk, msb, pfeat, apfeat, qfeat,
           otok, psPT, psSS, psQQ, psQT, hsl, halfsl):
    nc = ctx.nc
    v, s = nc.vector, nc.scalar
    ctx.wait(nc.tensor, "Bx2v", ctx.cnt("Bx2v"))
    for gh in range(2):
        heads = range(gh * 8, gh * 8 + 8)
        ctx.wait(nc.tensor, "BpeS", ctx.cnt("BpeS"))
        for i, h in enumerate(heads):
            mm = nc.tensor.matmul(out=psPT[:, i * 128:(i + 1) * 128], lhsT=xb[:, hsl(h)],
                                  rhs=ctx.id_bf, start=True, stop=True)
        ctx.inc(mm, "Bptm", 1)
        ctx.wait(s, "Bptm")
        ctx.inc(s.activation(out=pfeat[:, halfsl(gh)], in_=psPT[:], func=AF.Copy), "BpeS", 1)
        ctx.wait(nc.tensor, "BpeS")
        for i, h in enumerate(heads):
            mm = nc.tensor.matmul(out=psPT[:, i * 128:(i + 1) * 128], lhsT=xb[:, hsl(h)],
                                  rhs=dga[:, hsl(h)], start=True, stop=True)
        ctx.inc(mm, "Bptm", 1)
        ctx.wait(s, "Bptm")
        ctx.inc(s.activation(out=apfeat[:, halfsl(gh)], in_=psPT[:], func=AF.Copy), "BpeS", 1)
        ctx.wait(nc.tensor, "BssE", ctx.cnt("BssE"))
        ctx.wait(nc.tensor, "BpeS")
        for i, h in enumerate(heads):
            mm = nc.tensor.matmul(out=psSS[:, i * 128:(i + 1) * 128], lhsT=vtc[:, hsl(h)],
                                  rhs=pfeat[:, hsl(h)], start=True, stop=True)
        ctx.inc(mm, "Bsm", 1)
        ctx.wait(s, "Bsm")
        ctx.wait(s, "Bmm", ctx.cnt("Bmm"))
        ctx.inc(s.activation(out=qfeat[:, halfsl(1 - gh)], in_=psSS[:], func=AF.Copy), "BssE", 1)
        ctx.wait(nc.gpsimd, "BssE")
        ctx.inc(nc.gpsimd.tensor_tensor(out=msb[:, halfsl(gh)], in0=msk[:, halfsl(gh)],
                                        in1=qfeat[:, halfsl(1 - gh)], op=ALU.mult), "Bmm", 1)
        ctx.wait(nc.tensor, "BqeS", ctx.cnt("BqeS"))
        ctx.wait(nc.tensor, "Bmm")
        for i, h in enumerate(heads):
            osl = psQQ[:, i * 128:(i + 1) * 128]
            nc.tensor.matmul(out=osl, lhsT=hvc[:, hsl(h)], rhs=apfeat[:, hsl(h)],
                             start=True, stop=False)
            mm = nc.tensor.matmul(out=osl, lhsT=knc[:, hsl(h)], rhs=msb[:, hsl(h)],
                                  start=False, stop=True)
        ctx.inc(mm, "Bqm", 1)
        ctx.wait(s, "Bqm")
        ctx.inc(s.activation(out=qfeat[:, halfsl(gh)], in_=psQQ[:], func=AF.Copy), "BqeS", 1)
        ctx.wait(nc.tensor, "BqteS", ctx.cnt("BqteS"))
        ctx.wait(nc.tensor, "BqeS")
        for i, h in enumerate(heads):
            mm = nc.tensor.matmul(out=psQT[:, i * 128:(i + 1) * 128], lhsT=qfeat[:, hsl(h)],
                                  rhs=ctx.id_bf, start=True, stop=True)
        ctx.inc(mm, "Bqtm", 1)
        ctx.wait(s, "Bqtm")
        ao = s.activation(out=otok[:, halfsl(gh)], in_=psQT[:], func=AF.Copy)
        ctx.inc(ao, "BqteS", 1)


_PROGRAM_CACHE = {}


def _get_program(do_attn, taps=()):
    key = (do_attn, tuple(sorted(taps)))
    if key not in _PROGRAM_CACHE:
        _PROGRAM_CACHE[key] = build_program(do_attn=do_attn, taps=taps)
    return _PROGRAM_CACHE[key]


def make_in_maps(inputs):
    x_full = np.ascontiguousarray(inputs["L0_attn_x"].reshape(T, D))
    base = {
        "gate_up": np.ascontiguousarray(inputs["L0_ffn_gate_up_proj"]),
        "down": np.ascontiguousarray(inputs["L0_ffn_down_proj"]),
        "sh_gate": np.ascontiguousarray(inputs["L0_ffn_sh_gate_w"]),
        "sh_up": np.ascontiguousarray(inputs["L0_ffn_sh_up_w"]),
        "sh_down": np.ascontiguousarray(inputs["L0_ffn_sh_down_w"]),
        "shared_gate": np.ascontiguousarray(inputs["L0_ffn_shared_gate_w"].reshape(1, D)),
        "router": np.ascontiguousarray(inputs["L0_ffn_router_weight"]),
        "Wq": np.ascontiguousarray(inputs["L0_attn_Wq"]),
        "Wk": np.ascontiguousarray(inputs["L0_attn_Wk"]),
        "Wv": np.ascontiguousarray(inputs["L0_attn_Wv"]),
        "Wo": np.ascontiguousarray(inputs["L0_attn_Wo"]),
        "Wa": np.ascontiguousarray(inputs["L0_attn_Wa"]),
        "Wb": np.ascontiguousarray(inputs["L0_attn_Wb"]),
        "ba": np.ascontiguousarray(inputs["L0_attn_ba"].reshape(1, NH)),
        "bb": np.ascontiguousarray(inputs["L0_attn_bb"].reshape(1, NH)),
        "lamb": np.ascontiguousarray(inputs["L0_attn_lambda_params"].reshape(NH, HD)),
        "onw": np.ascontiguousarray(inputs["L0_attn_o_norm_weight"].reshape(HD, 1)),
    }
    in_maps = []
    for c in range(NCORES):
        m = dict(base)
        m["x"] = np.ascontiguousarray(x_full[c * TL:(c + 1) * TL])
        in_maps.append(m)
    return in_maps


def kernel(**inputs):
    nc = _get_program(DO_ATTN)
    in_maps = make_in_maps(inputs)
    res = run_bass_kernel_spmd(nc, in_maps, core_ids=list(range(NCORES)))
    y = np.concatenate([res.results[c]["y"] for c in range(NCORES)], axis=0)
    return y.reshape(1, T, D)
